# version 07
# baseline (speedup 1.0000x reference)
import numpy as np
import ml_dtypes
import concourse.bass as bass
import concourse.mybir as mybir

F32 = mybir.dt.float32
BF16 = mybir.dt.bfloat16
AF = mybir.ActivationFunctionType
ALU = mybir.AluOpType
AX = mybir.AxisListType
BF = ml_dtypes.bfloat16

D = 1024
H = 16
HD = 64
TOK = 2048
NT = TOK // 128
NSEQ = 4096
NKT = NSEQ // 128
NEG = -30000.0
EPS = 1e-5


def own_rows(half):
    return np.r_[0:1024, 3072:4096] if half == 0 else np.r_[1024:3072]


def own_chunks(half):
    return (list(range(0, 8)) + list(range(24, 32))) if half == 0 else list(range(8, 24))


class Sem:
    def __init__(self, nc, name):
        self.h = nc.alloc_semaphore(name)
        self.v = 0
        nc._sem_objs = getattr(nc, "_sem_objs", [])
        nc._sem_objs.append(self)

    def bump(self, n):
        self.v += n
        return self.v


class Script:
    ENGINES = ("sync", "scalar", "vector", "tensor", "gpsimd")

    def __init__(self, nc):
        self.nc = nc
        self.items = []

    def op(self, engine, fn, waits=(), incs=()):
        rw = [(s, s.v if v is None else v) for (s, v) in waits]
        ri = []
        for (s, n) in incs:
            s.bump(n)
            ri.append((s, n))
        self.items.append((engine, fn, rw, ri))

    OWN_SEM = {"vector": "s_dv", "scalar": "s_ac", "tensor": "s_pe"}
    COMPUTE_SEMS = ("s_dv", "s_ac", "s_pe")

    def emit(self):
        nc = self.nc
        tr = getattr(nc, "_wait_track", None)
        if tr is None:
            tr = nc._wait_track = {e: {} for e in self.ENGINES}
        with nc.Block() as blk:
            for eng_name in self.ENGINES:
                mine = [it for it in self.items if it[0] == eng_name]
                if not mine:
                    continue

                def body(eng, mine=mine, eng_name=eng_name):
                    dedup = eng_name in self.OWN_SEM
                    known = tr[eng_name]
                    for (_, fn, waits, incs) in mine:
                        for (s, v) in waits:
                            if v <= 0:
                                continue
                            nm = s.h.name
                            if dedup and known.get(nm, 0) >= v:
                                continue
                            eng.wait_ge(s.h, v)
                            known[nm] = max(known.get(nm, 0), v)
                        if fn is None:
                            assert not incs
                            continue
                        inst = fn(eng)
                        for (s, n) in incs:
                            inst.then_inc(s.h, n)
                getattr(blk, eng_name)(body)
        sems = getattr(nc, "_sem_objs", [])
        cross = {}
        for e in self.ENGINES:
            for nm, v in tr[e].items():
                cross[nm] = max(cross.get(nm, 0), v)
        for s in sems:
            if s.h.name in self.COMPUTE_SEMS:
                cross[s.h.name] = max(cross.get(s.h.name, 0), s.v)
        for e in self.ENGINES:
            tr[e].update(cross)
        self.items = []


class Ctx:
    def __init__(self, nc):
        self.nc = nc
        self.ld = Sem(nc, "s_ld")
        self.ldg = Sem(nc, "s_ldg")
        self.lo = Sem(nc, "s_lo")
        self.st = Sem(nc, "s_st")
        self.pe = Sem(nc, "s_pe")
        self.dv = Sem(nc, "s_dv")
        self.ac = Sem(nc, "s_ac")
        self._consts = {}
        self._ctile = nc.alloc_sbuf_tensor("constcols", [128, 24], F32)
        self._cn = 0
        self.idb = nc.alloc_sbuf_tensor("identb", [128, 128], BF16)
        self.idf = nc.alloc_sbuf_tensor("identf", [128, 128], F32)
        self.ones_r = nc.alloc_sbuf_tensor("ones_r", [1, 128], F32)
        self.ones_rb = nc.alloc_sbuf_tensor("ones_rb", [1, 128], BF16)
        self._id_src = nc.inline_tensor(np.eye(128, dtype=BF).view(np.uint16),
                                        name="idb_src")
        self._idf_src = nc.inline_tensor(np.eye(128, dtype=np.float32),
                                         name="idf_src")

    def constcol(self, value):
        value = float(value)
        if value not in self._consts:
            assert self._cn < 24, "const pool full"
            self._consts[value] = self._cn
            self._cn += 1
        return self._ctile[:, self._consts[value]:self._consts[value] + 1]

    def emit_init(self, S, const_values=()):
        for v in const_values:
            self.constcol(v)
        for v, i in list(self._consts.items()):
            S.op("vector", lambda e, i=i, v=v: e.memset(self._ctile[:, i:i + 1], v),
                 incs=[(self.dv, 1)])
        S.op("vector", lambda e: e.memset(self.ones_r[:, :], 1.0), incs=[(self.dv, 1)])
        S.op("vector", lambda e: e.memset(self.ones_rb[:, :], 1.0), incs=[(self.dv, 1)])
        S.op("sync", lambda e: e.dma_start(out=self.idb[:, :],
                                           in_=self._id_src.ap().bitcast(BF16)),
             incs=[(self.ld, 16)])
        S.op("sync", lambda e: e.dma_start(out=self.idf[:, :], in_=self._idf_src.ap()),
             incs=[(self.ld, 16)])


def mm_group(S, ctx, psum_ap, pairs, waits_first=(), stop_last=True):
    n = len(pairs)
    for i, (l, r) in enumerate(pairs):
        stop = (i == n - 1) and stop_last
        S.op("tensor",
             lambda e, l=l, r=r, i=i, stop=stop: e.matmul(
                 psum_ap, l, r, start=(i == 0), stop=stop),
             waits=(list(waits_first) if i == 0 else []),
             incs=([(ctx.pe, 1)] if i == n - 1 else []))


def emit_ln(ctx, S, x_ap, out_ap, scr, x_waits, gain_bias=None):
    dv, ac = ctx.dv, ctx.ac
    sq, c0, c1, c2, c3 = scr["sq"], scr["c0"], scr["c1"], scr["c2"], scr["c3"]
    S.op("scalar", lambda e: e.activation(sq, x_ap, AF.Square),
         waits=list(x_waits) + [(dv, None)], incs=[(ac, 1)])
    S.op("vector", lambda e: e.reduce_sum(c0, x_ap, axis=AX.X),
         waits=list(x_waits) + [(dv, None)], incs=[(dv, 1)])
    S.op("vector", lambda e: e.reduce_sum(c1, sq, axis=AX.X),
         waits=[(ac, None), (dv, None)], incs=[(dv, 1)])
    S.op("vector", lambda e: e.tensor_scalar_mul(c0, c0, -1.0 / D),
         waits=[(dv, None)], incs=[(dv, 1)])
    S.op("vector", lambda e: e.tensor_scalar_mul(c1, c1, 1.0 / D),
         waits=[(dv, None)], incs=[(dv, 1)])
    S.op("vector", lambda e: e.tensor_tensor(c2, c0, c0, ALU.mult),
         waits=[(dv, None)], incs=[(dv, 1)])
    S.op("vector", lambda e: e.tensor_sub(c1, c1, c2),
         waits=[(dv, None)], incs=[(dv, 1)])
    S.op("scalar", lambda e: e.activation(c2, c1, AF.Sqrt, bias=ctx.constcol(EPS)),
         waits=[(dv, None)], incs=[(ac, 1)])
    S.op("vector", lambda e: e.reciprocal(c3, c2),
         waits=[(ac, None)], incs=[(dv, 1)])
    S.op("vector", lambda e: e.tensor_scalar(out_ap, x_ap, c0, c3, ALU.add, ALU.mult),
         waits=[(dv, None)], incs=[(dv, 1)])
    if gain_bias is not None:
        G, B = gain_bias
        S.op("vector", lambda e: e.tensor_tensor(out_ap, out_ap, G, ALU.mult),
             waits=[(dv, None)], incs=[(dv, 1)])
        S.op("vector", lambda e: e.tensor_tensor(out_ap, out_ap, B, ALU.add),
             waits=[(dv, None)], incs=[(dv, 1)])


def emit_tp(ctx, S, src_ap, dst_ap, psum_ap, waits=(), f32=False):
    ident = ctx.idf if f32 else ctx.idb
    S.op("tensor", lambda e: e.transpose(psum_ap, src_ap, ident[:, :]),
         waits=list(waits) + [(ctx.dv, None)], incs=[(ctx.pe, 1)])
    S.op("vector", lambda e: e.tensor_copy(dst_ap, psum_ap),
         waits=[(ctx.pe, None)], incs=[(ctx.dv, 1)])


def emit_rotary(ctx, S, x_ap, cos_ap, sin_ap, tmp_ap, tmp2_ap, out_ap, waits):
    dv = ctx.dv
    W = x_ap.shape[-1]
    S.op("vector", lambda e: e.tensor_tensor(tmp_ap, x_ap, cos_ap, ALU.mult),
         waits=list(waits) + [(dv, None)], incs=[(dv, 1)])
    S.op("vector", lambda e: e.tensor_tensor(tmp2_ap[:, 0:W:2], x_ap[:, 1:W:2],
                                             sin_ap[:, 0:W:2], ALU.mult),
         waits=[(dv, None)], incs=[(dv, 1)])
    S.op("vector", lambda e: e.tensor_sub(tmp_ap[:, 0:W:2], tmp_ap[:, 0:W:2],
                                          tmp2_ap[:, 0:W:2]),
         waits=[(dv, None)], incs=[(dv, 1)])
    S.op("vector", lambda e: e.tensor_tensor(tmp2_ap[:, 1:W:2], x_ap[:, 0:W:2],
                                             sin_ap[:, 1:W:2], ALU.mult),
         waits=[(dv, None)], incs=[(dv, 1)])
    S.op("vector", lambda e: e.tensor_tensor(tmp_ap[:, 1:W:2], tmp_ap[:, 1:W:2],
                                             tmp2_ap[:, 1:W:2], ALU.add),
         waits=[(dv, None)], incs=[(dv, 1)])
    S.op("vector", lambda e: e.tensor_copy(out_ap, tmp_ap),
         waits=[(dv, None)], incs=[(dv, 1)])



def build_prog1(stop_after=None):
    nc = bass.Bass("TRN2", target_bir_lowering=False)
    ctx = Ctx(nc)
    xs = nc.dram_tensor("xs", [TOK, D], F32, kind="ExternalInput")
    w_qkv = nc.dram_tensor("w_qkv", [D, 3 * D], F32, kind="ExternalInput")
    w_kv = nc.dram_tensor("w_kv", [D, 2 * D], F32, kind="ExternalInput")
    ropec = nc.dram_tensor("ropec", [TOK, D], F32, kind="ExternalInput")
    ropes = nc.dram_tensor("ropes", [TOK, D], F32, kind="ExternalInput")
    o_krt = nc.dram_tensor("KrT", [D, TOK], BF16, kind="ExternalOutput")
    o_v = nc.dram_tensor("V", [TOK, D], BF16, kind="ExternalOutput")
    o_qt = nc.dram_tensor("qT", [D, TOK], BF16, kind="ExternalOutput")
    o_kk = nc.dram_tensor("kk", [TOK, D], BF16, kind="ExternalOutput")
    o_vk = nc.dram_tensor("vk", [TOK, D], BF16, kind="ExternalOutput")
    o_pct = nc.dram_tensor("PcT", [64, NT * H * 64], F32, kind="ExternalOutput")

    sx = [nc.alloc_sbuf_tensor(f"sx{i}", [128, D], F32) for i in range(2)]
    sq = nc.alloc_sbuf_tensor("sq", [128, D], F32)
    cst = {k: nc.alloc_sbuf_tensor(f"c_{k}", [128, 1], F32) for k in
           ("c0", "c1", "c2", "c3")}
    xln = nc.alloc_sbuf_tensor("xln", [128, D], BF16)
    xlt = [nc.alloc_sbuf_tensor(f"xlt{c}", [128, TOK], BF16) for c in range(8)]
    wfm = [nc.alloc_sbuf_tensor(f"wfm{i}", [128, 8 * 128], BF16) for i in range(2)]
    wtm = [nc.alloc_sbuf_tensor(f"wtm{i}", [128, 8 * 512], BF16) for i in range(2)]
    ropci = [nc.alloc_sbuf_tensor(f"ropc{i}", [128, 512], F32) for i in range(2)]
    ropsi = [nc.alloc_sbuf_tensor(f"rops{i}", [128, 512], F32) for i in range(2)]
    kf32 = nc.alloc_sbuf_tensor("kf32", [128, 512], F32)
    tmp = nc.alloc_sbuf_tensor("tmpD", [128, 512], F32)
    tmp2 = nc.alloc_sbuf_tensor("tmpD2", [128, 512], F32)
    krt_sb = [nc.alloc_sbuf_tensor(f"krt{c}", [128, TOK], BF16) for c in range(8)]
    kkfull = nc.alloc_sbuf_tensor("kkfull", [128, NT * D], BF16)
    vkfull = nc.alloc_sbuf_tensor("vkfull", [128, NT * D], BF16)
    stage = [nc.alloc_sbuf_tensor(f"stg{i}", [128, 512], BF16) for i in range(4)]
    pcstage = nc.alloc_sbuf_tensor("pcstage", [64, H * 64], F32)
    pmm = [nc.alloc_psum_tensor(f"pmm{i}", [128, 512], F32) for i in range(2)]
    ptp = [nc.alloc_psum_tensor(f"ptp{i}", [128, 1024], BF16)[0:128, 0:128]
           for i in range(2)]
    ppc = [nc.alloc_psum_tensor(f"ppc{i}", [128, 512], F32)[0:64, 0:64]
           for i in range(2)]

    S = Script(nc)
    ctx.emit_init(S, (EPS,))
    ld, ldg, st, pe, dv, ac = ctx.ld, ctx.ldg, ctx.st, ctx.pe, ctx.dv, ctx.ac

    scr = dict(sq=sq[:, :], c0=cst["c0"][:, :], c1=cst["c1"][:, :],
               c2=cst["c2"][:, :], c3=cst["c3"][:, :])
    ntp = 0
    for t in range(NT):
        b = t % 2
        S.op("sync", lambda e, t=t, b=b: e.dma_start(
            out=sx[b][:, :], in_=xs[t * 128:(t + 1) * 128, :]),
            waits=[(dv, None)], incs=[(ld, 16)])
        emit_ln(ctx, S, sx[b][:, :], xln[:, :], scr, x_waits=[(ld, None)])
        for c in range(8):
            emit_tp(ctx, S, xln[:, c * 128:(c + 1) * 128],
                    xlt[c][:, t * 128:(t + 1) * 128], ptp[ntp % 2][:, :])
            ntp += 1
    S.emit()
    if stop_after == "A":
        return nc, ctx

    S = Script(nc)
    for oc in range(8):
        b = oc % 2
        for kc in range(8):
            S.op("gpsimd", lambda e, kc=kc, oc=oc, b=b: e.dma_start(
                out=wfm[b][:, kc * 128:(kc + 1) * 128],
                in_=w_qkv[kc * 128:(kc + 1) * 128, oc * 128:oc * 128 + 128]),
                waits=([(pe, None)] if kc == 0 else []), incs=[(ldg, 16)])
        for tb in range(4):
            pb = tb % 2
            pairs = [(wfm[b][:, kc * 128:(kc + 1) * 128],
                      xlt[kc][:, tb * 512:(tb + 1) * 512]) for kc in range(8)]
            mm_group(S, ctx, pmm[pb][:, :], pairs,
                     waits_first=[(ldg, None), (ac, None)])
            S.op("scalar", lambda e, pb=pb, tb=tb: e.activation(
                stage[tb][:, :], pmm[pb][:, :], AF.Copy),
                waits=[(pe, None), (st, None)], incs=[(ac, 1)])
            S.op("gpsimd", lambda e, oc=oc, tb=tb: e.dma_start(
                out=o_qt[oc * 128:(oc + 1) * 128, tb * 512:(tb + 1) * 512],
                in_=stage[tb][:, :]),
                waits=[(ac, None)], incs=[(st, 16)])
    S.emit()
    if stop_after == "B":
        return nc, ctx

    S = Script(nc)

    def load_wtm(wdram, col0, buf):
        for kc in range(8):
            S.op("gpsimd", lambda e, kc=kc, wdram=wdram, col0=col0, buf=buf:
                 e.dma_start(out=buf[:, kc * 512:(kc + 1) * 512],
                             in_=wdram[kc * 128:(kc + 1) * 128, col0:col0 + 512]),
                 waits=([(pe, None)] if kc == 0 else []), incs=[(ldg, 16)])

    def tm_pass(wdram, col_base, evac):
        for g in range(2):
            wb = wtm[g % 2]
            load_wtm(wdram, col_base + g * 512, wb)
            for t in range(NT):
                pb = t % 2
                pairs = [(xlt[kc][:, t * 128:(t + 1) * 128],
                          wb[:, kc * 512:(kc + 1) * 512]) for kc in range(8)]
                mm_group(S, ctx, pmm[pb][:, :], pairs,
                         waits_first=[(ldg, None), (ac, None), (dv, None)])
                evac(t, g, pmm[pb][:, :])

    ntp = 0

    def evac_k(t, g, psum):
        nonlocal ntp
        S.op("sync", lambda e: e.dma_start(
            out=ropci[t % 2][:, :], in_=ropec[t * 128:(t + 1) * 128,
                                             g * 512:(g + 1) * 512]),
            waits=[(dv, None)], incs=[(ld, 16)])
        S.op("sync", lambda e: e.dma_start(
            out=ropsi[t % 2][:, :], in_=ropes[t * 128:(t + 1) * 128,
                                             g * 512:(g + 1) * 512]),
            incs=[(ld, 16)])
        S.op("scalar", lambda e: e.activation(kf32[:, :], psum, AF.Copy),
             waits=[(pe, None), (dv, None)], incs=[(ac, 1)])
        emit_rotary(ctx, S, kf32[:, :], ropci[t % 2][:, :], ropsi[t % 2][:, :],
                    tmp[:, :], tmp2[:, :], stage[0][:, :],
                    waits=[(ac, None), (ld, None)])
        for j in range(4):
            c = g * 4 + j
            emit_tp(ctx, S, stage[0][:, j * 128:(j + 1) * 128],
                    krt_sb[c][:, t * 128:(t + 1) * 128], ptp[ntp % 2][:, :])
            ntp += 1

    tm_pass(w_qkv, D, evac_k)

    def evac_v(t, g, psum):
        sb = stage[2 + (t % 2)]
        S.op("scalar", lambda e: e.activation(sb[:, :], psum, AF.Copy),
             waits=[(pe, None), (st, None)], incs=[(ac, 1)])
        S.op("gpsimd", lambda e: e.dma_start(
            out=o_v[t * 128:(t + 1) * 128, g * 512:(g + 1) * 512], in_=sb[:, :]),
            waits=[(ac, None)], incs=[(st, 16)])

    tm_pass(w_qkv, 2 * D, evac_v)

    def mk_evac_full(full, odram):
        def evac(t, g, psum):
            dst = full[:, t * D + g * 512: t * D + (g + 1) * 512]
            S.op("scalar", lambda e: e.activation(dst, psum, AF.Copy),
                 waits=[(pe, None)], incs=[(ac, 1)])
            S.op("gpsimd", lambda e: e.dma_start(
                out=odram[t * 128:(t + 1) * 128, g * 512:(g + 1) * 512], in_=dst),
                waits=[(ac, None)], incs=[(st, 16)])
        return evac

    tm_pass(w_kv, 0, mk_evac_full(kkfull, o_kk))
    tm_pass(w_kv, D, mk_evac_full(vkfull, o_vk))
    S.emit()
    if stop_after == "C":
        return nc, ctx

    import os as _os
    _dpart = _os.environ.get("D_PART", "all")
    S = Script(nc)
    if _dpart in ("all", "krt"):
        for c in range(8):
            S.op("gpsimd", lambda e, c=c: e.dma_start(
                out=o_krt[c * 128:(c + 1) * 128, :], in_=krt_sb[c][:, :]),
                incs=[(st, 16)])
    for t in (range(NT) if _dpart in ("all", "pct") else []):
        for h in range(H):
            pb = (t * H + h) % 2
            S.op("tensor", lambda e, t=t, h=h, pb=pb: e.matmul(
                ppc[pb][:, :],
                vkfull[:, t * D + h * 64: t * D + (h + 1) * 64],
                kkfull[:, t * D + h * 64: t * D + (h + 1) * 64],
                start=True, stop=True),
                waits=[(ac, None)], incs=[(ctx.pe, 1)])
            S.op("scalar", lambda e, h=h, pb=pb: e.activation(
                pcstage[:, h * 64:(h + 1) * 64], ppc[pb][0:64, :], AF.Copy),
                waits=[(pe, None)] + ([(st, None)] if h == 0 else []),
                incs=[(ac, 1)])
        S.op("gpsimd", lambda e, t=t: e.dma_start(
            out=o_pct[:, t * H * 64:(t + 1) * H * 64], in_=pcstage[:, :]),
            waits=[(ac, None)], incs=[(st, 16)])
    S.emit()
    if stop_after == "D":
        return nc, ctx
    S = Script(nc)
    S.op("gpsimd", None, waits=[(ctx.st, None), (ctx.ldg, None)])
    S.op("sync", None, waits=[(ctx.st, None)])
    S.emit()
    return nc, ctx



def rope_tables(positions, scale=1.0):
    inv = 1.0 / (10000.0 ** (np.arange(0, HD, 2, dtype=np.float32) / HD))
    ang = positions[:, None].astype(np.float32) * inv[None, :]
    ang = np.repeat(ang, 2, axis=-1)
    cos = (np.cos(ang) * scale).astype(np.float32)
    sin = (np.sin(ang) * scale).astype(np.float32)
    return np.tile(cos, (1, H)), np.tile(sin, (1, H))



def mask_arrays():
    i = np.arange(128)
    tri = np.where(i[None, :] <= i[:, None], 0.0, NEG).astype(np.float32)
    trit = np.where(i[:, None] <= i[None, :], 0.0, NEG).astype(np.float32)
    lowt = (i[:, None] <= i[None, :]).astype(np.float32)
    low01 = (i[None, :] <= i[:, None]).astype(np.float32)
    id8 = (np.eye(64) * 8.0).astype(np.float32)
    return tri, trit, lowt, low01, id8



def build_prog2(half):
    nc = bass.Bass("TRN2", target_bir_lowering=False)
    ctx = Ctx(nc)
    ld, ldg, st, pe, dv, ac = ctx.ld, ctx.ldg, ctx.st, ctx.pe, ctx.dv, ctx.ac
    chunks = own_chunks(half)

    xs = nc.dram_tensor("xs", [TOK, D], F32, kind="ExternalInput")
    qTd = nc.dram_tensor("qT", [D, TOK], BF16, kind="ExternalInput")
    kkd = nc.dram_tensor("kk", [TOK, D], BF16, kind="ExternalInput")
    vkd = nc.dram_tensor("vk", [TOK, D], BF16, kind="ExternalInput")
    krtf = nc.dram_tensor("KrTf", [D, NSEQ], BF16, kind="ExternalInput")
    vf = nc.dram_tensor("Vf", [NSEQ, D], BF16, kind="ExternalInput")
    pctg = nc.dram_tensor("PcTg", [64, NKT * H * 64], F32, kind="ExternalInput")
    ropc = nc.dram_tensor("ropecq64", [TOK, HD], F32, kind="ExternalInput")
    rops = nc.dram_tensor("ropesq64", [TOK, HD], F32, kind="ExternalInput")
    woutd = nc.dram_tensor("w_out", [D, D], F32, kind="ExternalInput")
    boutd = nc.dram_tensor("b_out", [1, D], F32, kind="ExternalInput")
    wq2d = nc.dram_tensor("wq2", [D, D], F32, kind="ExternalInput")
    wk2d = nc.dram_tensor("wk2", [D, D], F32, kind="ExternalInput")
    wv2d = nc.dram_tensor("wv2", [D, D], F32, kind="ExternalInput")
    bq2d = nc.dram_tensor("bq2c", [D, 1], F32, kind="ExternalInput")
    bk2d = nc.dram_tensor("bk2c", [D, 1], F32, kind="ExternalInput")
    bv2d = nc.dram_tensor("bv2", [1, D], F32, kind="ExternalInput")
    o_x1 = nc.dram_tensor("x1", [TOK, D], F32, kind="ExternalOutput")
    o_qt2 = nc.dram_tensor("qT2", [D, TOK], BF16, kind="ExternalOutput")
    o_kt2 = nc.dram_tensor("kT2", [D, TOK], BF16, kind="ExternalOutput")
    o_v2 = nc.dram_tensor("V2", [TOK, D], BF16, kind="ExternalOutput")

    tri_a, trit_a, lowt_a, low01_a, id8_a = mask_arrays()
    tri_d = nc.inline_tensor(tri_a, name="tri_d")
    trit_d = nc.inline_tensor(trit_a, name="trit_d")
    lowt_d = nc.inline_tensor(lowt_a, name="lowt_d")
    low01_d = nc.inline_tensor(low01_a, name="low01_d")
    id8_d = nc.inline_tensor(id8_a, name="id8_d")

    tri = nc.alloc_sbuf_tensor("tri", [128, 128], F32)
    trit = nc.alloc_sbuf_tensor("trit", [128, 128], F32)
    lowt = nc.alloc_sbuf_tensor("lowt", [128, 128], F32)
    low01 = nc.alloc_sbuf_tensor("low01", [128, 128], F32)
    id8 = nc.alloc_sbuf_tensor("id8", [64, 64], F32)
    qhat = nc.alloc_sbuf_tensor("qhat", [128, TOK], BF16)
    khat = nc.alloc_sbuf_tensor("khat", [128, NSEQ], BF16)
    vhat = nc.alloc_sbuf_tensor("vhat", [128, NKT * 65], BF16)
    qTh = nc.alloc_sbuf_tensor("qTh", [64, TOK], BF16)
    kkh = nc.alloc_sbuf_tensor("kkh", [128, NT * 64], BF16)
    vkh = nc.alloc_sbuf_tensor("vkh", [128, NT * 64], BF16)
    kkTh = nc.alloc_sbuf_tensor("kkTh", [64, TOK], BF16)
    vkTh = nc.alloc_sbuf_tensor("vkTh", [64, TOK], BF16)
    sct = nc.alloc_sbuf_tensor("sct", [64, NT * 64], F32)
    sctb = nc.alloc_sbuf_tensor("sctb", [64, NT * 64], BF16)
    pcg = nc.alloc_sbuf_tensor("pcg", [64, NKT * 64], F32)
    run = nc.alloc_sbuf_tensor("run", [64, 64], F32)
    ssqcols = nc.alloc_sbuf_tensor("ssqcols", [64, NT], F32)
    ssqrow = nc.alloc_sbuf_tensor("ssqrow", [1, NT], F32)
    ssqb = nc.alloc_sbuf_tensor("ssqb", [128, NT], F32)
    sq64 = nc.alloc_sbuf_tensor("sq64", [64, 64], F32)
    c64 = nc.alloc_sbuf_tensor("c64", [64, 1], F32)
    g1 = nc.alloc_sbuf_tensor("g1", [128, 128], F32)
    gp = nc.alloc_sbuf_tensor("gp", [128, 128], F32)
    ucol = nc.alloc_sbuf_tensor("ucol", [128, 6], F32)
    uprod = nc.alloc_sbuf_tensor("uprod", [128, 64], F32)
    stdc = nc.alloc_sbuf_tensor("stdc", [128, 3], F32)
    qn32 = nc.alloc_sbuf_tensor("qn32", [128, 64], F32)
    rot_t = nc.alloc_sbuf_tensor("rot_t", [128, 64], F32)
    rot_t2 = nc.alloc_sbuf_tensor("rot_t2", [128, 64], F32)
    qr8 = nc.alloc_sbuf_tensor("qr8", [128, 64], BF16)
    ropcs = nc.alloc_sbuf_tensor("ropcs", [128, NT * 64], F32)
    ropss = nc.alloc_sbuf_tensor("ropss", [128, NT * 64], F32)
    mxr = nc.alloc_sbuf_tensor("mxr", [128, 2], F32)
    negc = nc.alloc_sbuf_tensor("negc", [128, 1], F32)
    ptile = [nc.alloc_sbuf_tensor(f"ptile{i}", [128, 512], BF16) for i in range(2)]
    rs_row = nc.alloc_sbuf_tensor("rs_row", [1, 512], F32)
    bc = nc.alloc_sbuf_tensor("bc", [64, 512], F32)
    attn_sb = [nc.alloc_sbuf_tensor(f"attn{i}", [128, TOK], BF16) for i in range(8)]
    p512a = nc.alloc_psum_tensor("p512a", [128, 512], F32)
    p512b = nc.alloc_psum_tensor("p512b", [128, 512], F32)
    pmix = nc.alloc_psum_tensor("pmix", [128, 512], F32)
    pacc = nc.alloc_psum_tensor("pacc", [128, 512], F32)
    po = nc.alloc_psum_tensor("po", [65, 512], F32)
    pb = nc.alloc_psum_tensor("pb", [64, 512], F32)
    ptpb = nc.alloc_psum_tensor("ptpb", [128, 1024], BF16)[0:128, 0:128]
    ptpf = nc.alloc_psum_tensor("ptpf", [128, 512], F32)[0:128, 0:128]

    S = Script(nc)
    ctx.emit_init(S, (EPS, 1.0))
    for dst, src in ((tri, tri_d), (trit, trit_d), (lowt, lowt_d),
                     (low01, low01_d), (id8, id8_d)):
        S.op("sync", lambda e, dst=dst, src=src: e.dma_start(
            out=dst[:, :], in_=src.ap()), incs=[(ld, 16)])
    S.op("sync", lambda e: e.dma_start(
        out=ropcs[:, :], in_=ropc.ap().rearrange("(t p) c -> p t c", p=128)),
        incs=[(ld, 16)])
    S.op("sync", lambda e: e.dma_start(
        out=ropss[:, :], in_=rops.ap().rearrange("(t p) c -> p t c", p=128)),
        incs=[(ld, 16)])
    S.emit()

    for h in range(H):
        S = Script(nc)
        hw = 64 * (h % 2)
        S.op("sync", lambda e, h=h: e.dma_start(
            out=qTh[:, :], in_=qTd[h * 64:(h + 1) * 64, :]),
            waits=[(pe, None)], incs=[(ld, 16)])
        S.op("sync", lambda e, h=h: e.dma_start(
            out=khat[0:64, :], in_=krtf[h * 64:(h + 1) * 64, :]),
            waits=[(pe, None)], incs=[(ld, 16)])
        S.op("vector", lambda e: e.memset(khat[64:65, :], 1.0), incs=[(dv, 1)])
        if h == 0:
            S.op("vector", lambda e: e.memset(
                vhat[:, :].rearrange("p (kt c) -> p kt c", c=65)[:, :, 64:65],
                1.0),
                incs=[(dv, 1)])
        S.op("sync", lambda e, h=h: e.dma_start(
            out=vhat[:, :].rearrange("p (kt c) -> p kt c", c=65)[:, :, 0:64],
            in_=vf.ap().rearrange("(kt p) d -> p kt d", p=128)[:, :, h * 64:(h + 1) * 64]),
            waits=[(pe, None), (dv, None)], incs=[(ld, 16)])
        S.op("sync", lambda e, h=h: e.dma_start(
            out=kkh[:, :], in_=kkd.ap().rearrange("(t p) d -> p t d", p=128)[:, :, h * 64:(h + 1) * 64]),
            waits=[(pe, None)], incs=[(ld, 16)])
        S.op("sync", lambda e, h=h: e.dma_start(
            out=vkh[:, :], in_=vkd.ap().rearrange("(t p) d -> p t d", p=128)[:, :, h * 64:(h + 1) * 64]),
            waits=[(pe, None)], incs=[(ld, 16)])
        S.op("sync", lambda e, h=h: e.dma_start(
            out=pcg[:, :], in_=pctg.ap().rearrange("p (gc hh c) -> p gc hh c", hh=H, c=64)[:, :, h, :]),
            waits=[(dv, None)], incs=[(ld, 16)])
        for t in range(NT):
            for (srcb, dstb) in ((kkh, kkTh), (vkh, vkTh)):
                S.op("tensor", lambda e, t=t, srcb=srcb: e.transpose(
                    ptpb[0:64, :], srcb[:, t * 64:(t + 1) * 64], ctx.idb[:, :]),
                    waits=[(ld, None), (dv, None)], incs=[(pe, 1)])
                S.op("vector", lambda e, t=t, dstb=dstb: e.tensor_copy(
                    dstb[:, t * 128:(t + 1) * 128], ptpb[0:64, :]),
                    waits=[(pe, None)], incs=[(dv, 1)])
        S.op("vector", lambda e: e.tensor_copy(run[:, :], id8[:, :]),
             waits=[(ld, None)], incs=[(dv, 1)])
        for gc in range(NKT):
            if gc in chunks:
                t = chunks.index(gc)
                S.op("vector", lambda e, t=t: e.tensor_copy(
                    sct[:, t * 64:(t + 1) * 64], run[:, :]),
                    waits=[(dv, None)], incs=[(dv, 1)])
            if gc < NKT - 1:
                S.op("vector", lambda e, gc=gc: e.tensor_tensor(
                    run[:, :], run[:, :], pcg[:, gc * 64:(gc + 1) * 64], ALU.add),
                    waits=[(dv, None)], incs=[(dv, 1)])
        for t in range(NT):
            S.op("vector", lambda e, t=t: e.tensor_copy(
                sctb[:, t * 64:(t + 1) * 64], sct[:, t * 64:(t + 1) * 64]),
                waits=[(dv, None)], incs=[(dv, 1)])
            S.op("scalar", lambda e, t=t: e.activation(
                sq64[:, :], sct[:, t * 64:(t + 1) * 64], AF.Square),
                waits=[(dv, None), (ac, None)], incs=[(ac, 1)])
            S.op("vector", lambda e, t=t: e.reduce_sum(
                ssqcols[:, t:t + 1], sq64[:, :], axis=AX.X),
                waits=[(ac, None)], incs=[(dv, 1)])
        S.op("tensor", lambda e: e.matmul(
            pacc[0:1, 0:NT], ctx.constcol(1.0)[0:64, :], ssqcols[:, :],
            start=True, stop=True),
            waits=[(dv, None)], incs=[(pe, 1)])
        S.op("vector", lambda e: e.tensor_copy(ssqrow[:, :], pacc[0:1, 0:NT]),
             waits=[(pe, None)], incs=[(dv, 1)])
        S.op("tensor", lambda e: e.matmul(
            pacc[0:128, 64:64 + NT],
            ctx.ones_r[0:1, :], ssqrow[:, :], start=True, stop=True),
            waits=[(dv, None)], incs=[(pe, 1)])
        S.op("vector", lambda e: e.tensor_copy(ssqb[:, :], pacc[:, 64:64 + NT]),
             waits=[(pe, None)], incs=[(dv, 1)])
        S.emit()

        S = Script(nc)
        for t in range(NT):
            tc = slice(t * 128, (t + 1) * 128)
            hc = slice(t * 64, (t + 1) * 64)
            S.op("tensor", lambda e, t=t: e.matmul(
                pmix[:, 0:128], vkTh[:, t * 128:(t + 1) * 128],
                qTh[:, t * 128:(t + 1) * 128], start=True, stop=True),
                waits=[(dv, None)], incs=[(pe, 1)])
            S.op("vector", lambda e: e.tensor_tensor(
                ptile[0][:, 0:128], pmix[:, 0:128], lowt[:, :], ALU.mult),
                waits=[(pe, None)], incs=[(dv, 1)])
            S.op("tensor", lambda e, t=t: e.matmul(
                pacc[:, 0:64], ptile[0][:, 0:128], kkh[:, t * 64:(t + 1) * 64],
                start=True, stop=False),
                waits=[(dv, None)], incs=[(pe, 1)])
            S.op("tensor", lambda e, t=t: e.matmul(
                pacc[:, 0:64], qTh[:, t * 128:(t + 1) * 128],
                sctb[:, t * 64:(t + 1) * 64], start=False, stop=True),
                incs=[(pe, 1)])
            S.op("tensor", lambda e, t=t: e.matmul(
                pacc[:, 64:128], vkTh[:, t * 128:(t + 1) * 128],
                sctb[:, t * 64:(t + 1) * 64], start=True, stop=True),
                incs=[(pe, 1)])
            S.op("vector", lambda e, t=t: e.tensor_tensor(
                uprod[:, :], pacc[:, 64:128], kkh[:, t * 64:(t + 1) * 64], ALU.mult),
                waits=[(pe, None)], incs=[(dv, 1)])
            S.op("vector", lambda e: e.reduce_sum(ucol[:, 0:1], uprod[:, :], axis=AX.X),
                 waits=[(dv, None)], incs=[(dv, 1)])
            S.op("tensor", lambda e, t=t: e.matmul(
                pmix[:, 128:256], kkTh[:, t * 128:(t + 1) * 128],
                kkTh[:, t * 128:(t + 1) * 128], start=True, stop=True),
                waits=[(dv, None)], incs=[(pe, 1)])
            S.op("tensor", lambda e, t=t: e.matmul(
                pmix[:, 256:384], vkTh[:, t * 128:(t + 1) * 128],
                vkTh[:, t * 128:(t + 1) * 128], start=True, stop=True),
                incs=[(pe, 1)])
            S.op("scalar", lambda e: e.activation(g1[:, :], pmix[:, 128:256], AF.Copy),
                 waits=[(pe, None), (dv, None)], incs=[(ac, 1)])
            S.op("vector", lambda e: e.tensor_tensor(
                gp[:, :], pmix[:, 256:384], g1[:, :], ALU.mult),
                waits=[(ac, None)], incs=[(dv, 1)])
            S.op("vector", lambda e: e.tensor_tensor(
                gp[:, :], gp[:, :], low01[:, :], ALU.mult),
                waits=[(dv, None)], incs=[(dv, 1)])
            S.op("vector", lambda e: e.reduce_sum(ucol[:, 1:2], gp[:, :], axis=AX.X),
                 waits=[(dv, None)], incs=[(dv, 1)])
            S.op("scalar", lambda e, t=t: e.activation(
                uprod[:, :], kkh[:, t * 64:(t + 1) * 64], AF.Square),
                waits=[(dv, None)], incs=[(ac, 1)])
            S.op("vector", lambda e: e.reduce_sum(ucol[:, 2:3], uprod[:, :], axis=AX.X),
                 waits=[(ac, None)], incs=[(dv, 1)])
            S.op("scalar", lambda e, t=t: e.activation(
                uprod[:, :], vkh[:, t * 64:(t + 1) * 64], AF.Square),
                waits=[(dv, None)], incs=[(ac, 1)])
            S.op("vector", lambda e: e.reduce_sum(ucol[:, 3:4], uprod[:, :], axis=AX.X),
                 waits=[(ac, None)], incs=[(dv, 1)])
            S.op("vector", lambda e: e.tensor_tensor(
                ucol[:, 4:5], ucol[:, 2:3], ucol[:, 3:4], ALU.mult),
                waits=[(dv, None)], incs=[(dv, 1)])
            S.op("vector", lambda e: e.tensor_tensor(
                ucol[:, 5:6], ucol[:, 0:1], ucol[:, 1:2], ALU.add),
                waits=[(dv, None)], incs=[(dv, 1)])
            S.op("vector", lambda e: e.tensor_scalar(
                ucol[:, 5:6], ucol[:, 5:6], 2.0, None, ALU.mult),
                waits=[(dv, None)], incs=[(dv, 1)])
            S.op("vector", lambda e: e.tensor_sub(ucol[:, 5:6], ucol[:, 5:6],
                                                  ucol[:, 4:5]),
                 waits=[(dv, None)], incs=[(dv, 1)])
            S.op("tensor", lambda e: e.matmul(
                pacc[:, 128:129], lowt[:, :], ucol[:, 5:6], start=True, stop=True),
                waits=[(dv, None)], incs=[(pe, 1)])
            S.op("scalar", lambda e, t=t: e.activation(
                stdc[:, 0:1], pacc[:, 128:129], AF.Sqrt, bias=ssqb[:, t:t + 1]),
                waits=[(pe, None)], incs=[(ac, 1)])
            S.op("vector", lambda e: e.tensor_scalar(
                stdc[:, 1:2], stdc[:, 0:1], 1e-8, None, ALU.add),
                waits=[(ac, None)], incs=[(dv, 1)])
            S.op("vector", lambda e: e.reciprocal(stdc[:, 2:3], stdc[:, 1:2]),
                 waits=[(dv, None)], incs=[(dv, 1)])
            S.op("vector", lambda e: e.tensor_scalar_mul(
                qn32[:, :], pacc[:, 0:64], stdc[:, 2:3]),
                waits=[(dv, None)], incs=[(dv, 1)])
            emit_rotary(ctx, S, qn32[:, :], ropcs[:, t * 64:(t + 1) * 64],
                        ropss[:, t * 64:(t + 1) * 64], rot_t[:, :], rot_t2[:, :],
                        qr8[:, :], waits=[(dv, None)])
            S.op("tensor", lambda e, t=t: e.transpose(
                ptpb[0:64, 0:128], qr8[:, :], ctx.idb[:, :]),
                waits=[(dv, None)], incs=[(pe, 1)])
            S.op("vector", lambda e, t=t: e.tensor_copy(
                qhat[0:64, t * 128:(t + 1) * 128], ptpb[0:64, :]),
                waits=[(pe, None)], incs=[(dv, 1)])
        S.emit()

        S = Script(nc)
        for t in range(NT):
            g = chunks[t]
            nkb = (g + 4) // 4
            for kb in range(nkb):
                n = min(512, (g + 1) * 128 - kb * 512)
                pbuf = p512a if kb % 2 == 0 else p512b
                S.op("tensor", lambda e, t=t, kb=kb, n=n, pbuf=pbuf: e.matmul(
                    pbuf[:, 0:n], qhat[0:64, t * 128:(t + 1) * 128],
                    khat[0:64, kb * 512:kb * 512 + n], start=True, stop=True),
                    waits=[(dv, None)], incs=[(pe, 1)])
                if kb == nkb - 1:
                    S.op("vector", lambda e, n=n, pbuf=pbuf: e.tensor_tensor(
                        pbuf[:, n - 128:n], pbuf[:, n - 128:n], tri[:, :], ALU.add),
                        waits=[(pe, None)], incs=[(dv, 1)])
                S.op("vector", lambda e, kb=kb, n=n, pbuf=pbuf: e.reduce_max(
                    mxr[:, 1:2] if kb > 0 else mxr[:, 0:1], pbuf[:, 0:n], axis=AX.X),
                    waits=[(pe, None), (dv, None)], incs=[(dv, 1)])
                if kb > 0:
                    S.op("vector", lambda e: e.tensor_max(
                        mxr[:, 0:1], mxr[:, 0:1], mxr[:, 1:2]),
                        waits=[(dv, None)], incs=[(dv, 1)])
            S.op("vector", lambda e: e.tensor_scalar_mul(negc[:, :], mxr[:, 0:1], -1.0),
                 waits=[(dv, None)], incs=[(dv, 1)])
            S.op("tensor", lambda e: e.transpose(ptpf[0:1, 0:128],
                                                 negc[:, :], ctx.idf[:, :]),
                 waits=[(dv, None)], incs=[(pe, 1)])
            S.op("vector", lambda e, t=t: e.tensor_copy(
                qhat[64:65, t * 128:(t + 1) * 128], ptpf[0:1, 0:128]),
                waits=[(pe, None)], incs=[(dv, 1)])
        S.emit()

        S = Script(nc)
        for qb in range(4):
            gs0 = chunks[4 * qb]
            kmax = gs0 + 3
            for kt in range(kmax + 1):
                pbuf = p512a if kt % 2 == 0 else p512b
                S.op("tensor", lambda e, qb=qb, kt=kt, pbuf=pbuf: e.matmul(
                    pbuf[:, :], khat[0:65, kt * 128:(kt + 1) * 128],
                    qhat[0:65, qb * 512:(qb + 1) * 512], start=True, stop=True),
                    waits=[(ac, None)], incs=[(pe, 1)])
                rel = kt - gs0
                if rel >= 0:
                    for j in range(4):
                        if j < rel:
                            S.op("vector", lambda e, j=j, pbuf=pbuf: e.tensor_scalar(
                                pbuf[:, j * 128:(j + 1) * 128],
                                pbuf[:, j * 128:(j + 1) * 128], NEG, None, ALU.add),
                                waits=[(pe, None)], incs=[(dv, 1)])
                        elif j == rel:
                            S.op("vector", lambda e, j=j, pbuf=pbuf: e.tensor_tensor(
                                pbuf[:, j * 128:(j + 1) * 128],
                                pbuf[:, j * 128:(j + 1) * 128], trit[:, :], ALU.add),
                                waits=[(pe, None)], incs=[(dv, 1)])
                S.op("scalar", lambda e, kt=kt, pbuf=pbuf: e.activation(
                    ptile[kt % 2][:, :], pbuf[:, :], AF.Exp),
                    waits=[(pe, None), (dv, None)], incs=[(ac, 1)])
                S.op("tensor", lambda e, kt=kt, kmax=kmax: e.matmul(
                    po[:, :], vhat[:, kt * 65:(kt + 1) * 65], ptile[kt % 2][:, :],
                    start=(kt == 0), stop=(kt == kmax)),
                    waits=[(ac, None)] + ([(dv, None)] if kt == 0 else []),
                    incs=[(pe, 1)])
            S.op("vector", lambda e: e.reciprocal(rs_row[:, :], po[64:65, :]),
                 waits=[(pe, None)], incs=[(dv, 1)])
            S.op("tensor", lambda e: e.matmul(
                pb[:, :], ctx.ones_r[0:1, 0:64], rs_row[:, :], start=True, stop=True),
                waits=[(dv, None)], incs=[(pe, 1)])
            S.op("vector", lambda e: e.tensor_copy(bc[:, :], pb[:, :]),
                 waits=[(pe, None)], incs=[(dv, 1)])
            S.op("vector", lambda e, h=h, qb=qb, hw=hw: e.tensor_tensor(
                attn_sb[h // 2][hw:hw + 64, qb * 512:(qb + 1) * 512],
                po[0:64, :], bc[:, :], ALU.mult),
                waits=[(dv, None)], incs=[(dv, 1)])
        S.emit()
    hd = dict(
        o_x1=o_x1, o_qt2=o_qt2, o_kt2=o_kt2, o_v2=o_v2, attn_sb=attn_sb,
        xs=xs, woutd=woutd, boutd=boutd, wq2d=wq2d, wk2d=wk2d, wv2d=wv2d,
        bq2d=bq2d, bk2d=bk2d, bv2d=bv2d,
        p512a=p512a, p512b=p512b, ptpb=ptpb)
    prog2_tail(nc, ctx, hd)
    return nc, ctx, hd


def prog2_tail(nc, ctx, hd):
    ld, ldg, st, pe, dv, ac = ctx.ld, ctx.ldg, ctx.st, ctx.pe, ctx.dv, ctx.ac
    attn_sb = hd["attn_sb"]
    xs, woutd, boutd = hd["xs"], hd["woutd"], hd["boutd"]
    wq2d, wk2d, wv2d = hd["wq2d"], hd["wk2d"], hd["wv2d"]
    bq2d, bk2d, bv2d = hd["bq2d"], hd["bk2d"], hd["bv2d"]
    o_x1, o_qt2, o_kt2, o_v2 = hd["o_x1"], hd["o_qt2"], hd["o_kt2"], hd["o_v2"]

    wout = [nc.alloc_sbuf_tensor(f"wout{c}", [128, D], BF16) for c in range(8)]
    bout = nc.alloc_sbuf_tensor("bout", [1, D], BF16)
    bv2 = nc.alloc_sbuf_tensor("bv2s", [1, D], BF16)
    bqc = nc.alloc_sbuf_tensor("bqc", [128, 64], F32)
    bkc = nc.alloc_sbuf_tensor("bkc", [128, 64], F32)
    ytile = nc.alloc_sbuf_tensor("ytile", [128, D], F32)
    x1t = nc.alloc_sbuf_tensor("x1t", [128, D], F32)
    sxt = [nc.alloc_sbuf_tensor(f"sxt{i}", [128, D], F32) for i in range(2)]
    sq2 = nc.alloc_sbuf_tensor("sq2", [128, D], F32)
    cst2 = {k: nc.alloc_sbuf_tensor(f"c2_{k}", [128, 1], F32) for k in
            ("c0", "c1", "c2", "c3")}
    sbf = nc.alloc_sbuf_tensor("sbf", [128, D], BF16)
    slt = [nc.alloc_sbuf_tensor(f"slt{c}", [128, TOK], BF16) for c in range(8)]
    wfm2 = [nc.alloc_sbuf_tensor(f"wfm2_{i}", [128, 8 * 128], BF16) for i in range(2)]
    wtm2 = [nc.alloc_sbuf_tensor(f"wtm2_{i}", [128, 8 * 512], BF16) for i in range(2)]
    stage2 = [nc.alloc_sbuf_tensor(f"stg2_{i}", [128, 512], BF16) for i in range(4)]
    p512a, p512b, ptpb = hd["p512a"], hd["p512b"], hd["ptpb"]

    S = Script(nc)
    for c in range(8):
        S.op("gpsimd", lambda e, c=c: e.dma_start(
            out=wout[c][:, :], in_=woutd[c * 128:(c + 1) * 128, :]),
            incs=[(ldg, 16)])
    S.op("gpsimd", lambda e: e.dma_start(out=bout[:, :], in_=boutd[:, :]),
         incs=[(ldg, 16)])
    S.op("gpsimd", lambda e: e.dma_start(out=bv2[:, :], in_=bv2d[:, :]),
         incs=[(ldg, 16)])
    for c in range(8):
        S.op("sync", lambda e, c=c: e.dma_start(
            out=bqc[:, 8 * c:8 * c + 1], in_=bq2d[c * 128:(c + 1) * 128, :]),
            incs=[(ld, 16)])
        S.op("sync", lambda e, c=c: e.dma_start(
            out=bkc[:, 8 * c:8 * c + 1], in_=bk2d[c * 128:(c + 1) * 128, :]),
            incs=[(ld, 16)])
    scr = dict(sq=sq2[:, :], c0=cst2["c0"][:, :], c1=cst2["c1"][:, :],
               c2=cst2["c2"][:, :], c3=cst2["c3"][:, :])
    for t in range(NT):
        tb = t % 2
        S.op("sync", lambda e, t=t, tb=tb: e.dma_start(
            out=sxt[tb][:, :], in_=xs[t * 128:(t + 1) * 128, :]),
            waits=[(dv, None)], incs=[(ld, 16)])
        for g in range(2):
            pbuf = p512a if g == 0 else p512b
            pairs = [(attn_sb[c][:, t * 128:(t + 1) * 128],
                      wout[c][:, g * 512:(g + 1) * 512]) for c in range(8)]
            mm_group(S, ctx, pbuf[:, :], pairs,
                     waits_first=[(ldg, None), (ac, None)], stop_last=False)
            S.op("tensor", lambda e, g=g, pbuf=pbuf: e.matmul(
                pbuf[:, :], ctx.ones_rb[:, :], bout[:, g * 512:(g + 1) * 512],
                start=False, stop=True),
                incs=[(pe, 1)])
            S.op("scalar", lambda e, g=g, pbuf=pbuf: e.activation(
                ytile[:, g * 512:(g + 1) * 512], pbuf[:, :], AF.Copy),
                waits=[(pe, None)], incs=[(ac, 1)])
        emit_ln(ctx, S, ytile[:, :], ytile[:, :], scr, x_waits=[(ac, None)])
        S.op("vector", lambda e, tb=tb: e.tensor_tensor(
            x1t[:, :], sxt[tb][:, :], ytile[:, :], ALU.add),
            waits=[(ld, None), (dv, None), (st, None)], incs=[(dv, 1)])
        S.op("gpsimd", lambda e, t=t: e.dma_start(
            out=o_x1[t * 128:(t + 1) * 128, :], in_=x1t[:, :]),
            waits=[(dv, None)], incs=[(st, 16)])
        emit_ln(ctx, S, x1t[:, :], sbf[:, :], scr, x_waits=[(dv, None)])
        for c in range(8):
            emit_tp(ctx, S, sbf[:, c * 128:(c + 1) * 128],
                    slt[c][:, t * 128:(t + 1) * 128], ptpb[:, :])
    S.emit()

    S = Script(nc)
    for (wdram, biascols, odram) in ((wq2d, bqc, o_qt2), (wk2d, bkc, o_kt2)):
        for oc in range(8):
            b = oc % 2
            for kc in range(8):
                S.op("gpsimd", lambda e, kc=kc, oc=oc, b=b, wdram=wdram: e.dma_start(
                    out=wfm2[b][:, kc * 128:(kc + 1) * 128],
                    in_=wdram[kc * 128:(kc + 1) * 128, oc * 128:oc * 128 + 128]),
                    waits=([(pe, None)] if kc == 0 else []), incs=[(ldg, 16)])
            for tb in range(4):
                pbuf = p512a if tb % 2 == 0 else p512b
                pairs = [(wfm2[b][:, kc * 128:(kc + 1) * 128],
                          slt[kc][:, tb * 512:(tb + 1) * 512]) for kc in range(8)]
                mm_group(S, ctx, pbuf[:, :], pairs,
                         waits_first=[(ldg, None), (ac, None), (dv, None)])
                S.op("vector", lambda e, pbuf=pbuf, tb=tb, oc=oc, biascols=biascols:
                     e.tensor_scalar(stage2[tb][:, :], pbuf[:, :],
                                     biascols[:, 8 * oc:8 * oc + 1], None,
                                     ALU.add),
                     waits=[(pe, None), (st, None), (ld, None)], incs=[(dv, 1)])
                S.op("gpsimd", lambda e, oc=oc, tb=tb, odram=odram: e.dma_start(
                    out=odram[oc * 128:(oc + 1) * 128, tb * 512:(tb + 1) * 512],
                    in_=stage2[tb][:, :]),
                    waits=[(dv, None)], incs=[(st, 16)])
    for g in range(2):
        wb = wtm2[g % 2]
        for kc in range(8):
            S.op("gpsimd", lambda e, kc=kc, g=g, wb=wb: e.dma_start(
                out=wb[:, kc * 512:(kc + 1) * 512],
                in_=wv2d[kc * 128:(kc + 1) * 128, g * 512:g * 512 + 512]),
                waits=([(pe, None)] if kc == 0 else []), incs=[(ldg, 16)])
        for t in range(NT):
            pbuf = p512a if t % 2 == 0 else p512b
            pairs = [(slt[kc][:, t * 128:(t + 1) * 128],
                      wb[:, kc * 512:(kc + 1) * 512]) for kc in range(8)]
            mm_group(S, ctx, pbuf[:, :], pairs,
                     waits_first=[(ldg, None), (ac, None)], stop_last=False)
            S.op("tensor", lambda e, g=g, pbuf=pbuf: e.matmul(
                pbuf[:, :], ctx.ones_rb[:, :], bv2[:, g * 512:(g + 1) * 512],
                start=False, stop=True),
                incs=[(pe, 1)])
            S.op("scalar", lambda e, t=t, pbuf=pbuf: e.activation(
                stage2[2 + t % 2][:, :], pbuf[:, :], AF.Copy),
                waits=[(pe, None), (st, None)], incs=[(ac, 1)])
            S.op("gpsimd", lambda e, t=t, g=g: e.dma_start(
                out=o_v2[t * 128:(t + 1) * 128, g * 512:(g + 1) * 512],
                in_=stage2[2 + t % 2][:, :]),
                waits=[(ac, None)], incs=[(st, 16)])
    S.op("gpsimd", None, waits=[(st, None), (ldg, None)])
    S.emit()



def build_prog3():
    nc = bass.Bass("TRN2", target_bir_lowering=False)
    ctx = Ctx(nc)
    ld, ldg, st, pe, dv, ac = ctx.ld, ctx.ldg, ctx.st, ctx.pe, ctx.dv, ctx.ac

    x1d = nc.dram_tensor("x1", [TOK, D], F32, kind="ExternalInput")
    qT2d = nc.dram_tensor("qT2", [D, TOK], BF16, kind="ExternalInput")
    kT2f = nc.dram_tensor("kT2f", [D, NSEQ], BF16, kind="ExternalInput")
    v2f = nc.dram_tensor("V2f", [NSEQ, D], BF16, kind="ExternalInput")
    wod = nc.dram_tensor("wo2", [D, D], F32, kind="ExternalInput")
    bod = nc.dram_tensor("bo2", [1, D], F32, kind="ExternalInput")
    gwd = None and nc.dram_tensor("gW", [D, 8], F32, kind="ExternalInput")
    gbd = None and nc.dram_tensor("gb", [1, 8], F32, kind="ExternalInput")
    w1d = None and nc.dram_tensor("W1", [8, D, 2048], F32, kind="ExternalInput")
    b1td = None and nc.dram_tensor("b1T", [2048, 8], F32, kind="ExternalInput")
    w2d = None and nc.dram_tensor("W2", [8, 2048, D], F32, kind="ExternalInput")
    b2d = None and nc.dram_tensor("b2", [8, D], F32, kind="ExternalInput")
    ln1gd = nc.dram_tensor("ln1g", [1, D], F32, kind="ExternalInput")
    ln1bd = nc.dram_tensor("ln1b", [1, D], F32, kind="ExternalInput")
    ln2gd = None and nc.dram_tensor("ln2g", [1, D], F32, kind="ExternalInput")
    ln2bd = None and nc.dram_tensor("ln2b", [1, D], F32, kind="ExternalInput")
    wq3d = None and nc.dram_tensor("wq3", [D, D], F32, kind="ExternalInput")
    wk3d = None and nc.dram_tensor("wk3", [D, D], F32, kind="ExternalInput")
    wv3d = None and nc.dram_tensor("wv3", [D, D], F32, kind="ExternalInput")
    bq3d = None and nc.dram_tensor("bq3c", [D, 1], F32, kind="ExternalInput")
    bk3d = None and nc.dram_tensor("bk3c", [D, 1], F32, kind="ExternalInput")
    bv3d = None and nc.dram_tensor("bv3", [1, D], F32, kind="ExternalInput")
    o_x2 = None and nc.dram_tensor("x2", [TOK, D], F32, kind="ExternalOutput")
    o_qt3 = None and nc.dram_tensor("qT3", [D, TOK], BF16, kind="ExternalOutput")
    o_kt3 = None and nc.dram_tensor("kT3", [D, TOK], BF16, kind="ExternalOutput")
    o_v3 = None and nc.dram_tensor("V3", [TOK, D], BF16, kind="ExternalOutput")
    o_qnn = None and nc.dram_tensor("qnn", [H, TOK], F32, kind="ExternalOutput")
    o_knn = None and nc.dram_tensor("knn", [H, TOK], F32, kind="ExternalOutput")
    o_x1a = nc.dram_tensor("x1a", [TOK, D], F32, kind="ExternalOutput")
    o_xaT = nc.dram_tensor("xaT", [D, TOK], BF16, kind="ExternalOutput")

    qhat = nc.alloc_sbuf_tensor("qhat", [128, TOK], BF16)
    qstg = nc.alloc_sbuf_tensor("qstg", [64, TOK], BF16)
    khat = nc.alloc_sbuf_tensor("khat", [128, NSEQ], BF16)
    vhat = nc.alloc_sbuf_tensor("vhat", [128, NKT * 65], BF16)
    mxr = nc.alloc_sbuf_tensor("mxr", [128, 2], F32)
    negc = nc.alloc_sbuf_tensor("negc", [128, 1], F32)
    ptile = [nc.alloc_sbuf_tensor(f"ptile{i}", [128, 512], BF16) for i in range(2)]
    rs_row = nc.alloc_sbuf_tensor("rs_row", [1, 512], F32)
    bc = nc.alloc_sbuf_tensor("bc", [64, 512], F32)
    attn_sb = [nc.alloc_sbuf_tensor(f"attn{i}", [128, TOK], BF16) for i in range(8)]
    p512a = nc.alloc_psum_tensor("p512a", [128, 512], F32)
    p512b = nc.alloc_psum_tensor("p512b", [128, 512], F32)
    p512c = nc.alloc_psum_tensor("p512c", [128, 512], F32)
    p512d = nc.alloc_psum_tensor("p512d", [128, 512], F32)
    po = nc.alloc_psum_tensor("po", [65, 512], F32)
    pb = nc.alloc_psum_tensor("pb", [64, 512], F32)
    ptpb = nc.alloc_psum_tensor("ptpb", [128, 1024], BF16)[0:128, 0:128]
    ptpf = nc.alloc_psum_tensor("ptpf", [128, 512], F32)[0:128, 0:128]

    S = Script(nc)
    ctx.emit_init(S, (EPS, 1.0, -0.5))
    S.emit()

    for h in range(H):
        S = Script(nc)
        hw = 64 * (h % 2)
        S.op("sync", lambda e, h=h: e.dma_start(
            out=qstg[:, :], in_=qT2d[h * 64:(h + 1) * 64, :]),
            waits=[(pe, None), (dv, None)], incs=[(ld, 16)])
        S.op("sync", lambda e, h=h: e.dma_start(
            out=khat[0:64, :], in_=kT2f[h * 64:(h + 1) * 64, :]),
            incs=[(ld, 16)])
        if h == 0:
            S.op("vector", lambda e: e.memset(
                vhat[:, :].rearrange("p (kt c) -> p kt c", c=65)[:, :, 64:65],
                1.0),
                incs=[(dv, 1)])
        S.op("sync", lambda e, h=h: e.dma_start(
            out=vhat[:, :].rearrange("p (kt c) -> p kt c", c=65)[:, :, 0:64],
            in_=v2f.ap().rearrange("(kt p) d -> p kt d", p=128)[:, :, h * 64:(h + 1) * 64]),
            waits=[(dv, None)], incs=[(ld, 16)])
        S.op("vector", lambda e: e.tensor_scalar_mul(qhat[0:64, :], qstg[:, :], 0.125),
             waits=[(ld, None)], incs=[(dv, 1)])
        S.op("vector", lambda e: e.memset(khat[64:65, :], 1.0), incs=[(dv, 1)])
        for t in range(NT):
            for kb in range(8):
                pbuf = p512a if kb % 2 == 0 else p512b
                S.op("tensor", lambda e, t=t, kb=kb, pbuf=pbuf: e.matmul(
                    pbuf[:, :], qhat[0:64, t * 128:(t + 1) * 128],
                    khat[0:64, kb * 512:(kb + 1) * 512], start=True, stop=True),
                    waits=[(dv, None), (ld, None)], incs=[(pe, 1)])
                S.op("vector", lambda e, kb=kb, pbuf=pbuf: e.reduce_max(
                    mxr[:, 1:2] if kb > 0 else mxr[:, 0:1], pbuf[:, :], axis=AX.X),
                    waits=[(pe, None)], incs=[(dv, 1)])
                if kb > 0:
                    S.op("vector", lambda e: e.tensor_max(
                        mxr[:, 0:1], mxr[:, 0:1], mxr[:, 1:2]),
                        waits=[(dv, None)], incs=[(dv, 1)])
            S.op("vector", lambda e: e.tensor_scalar_mul(negc[:, :], mxr[:, 0:1], -1.0),
                 waits=[(dv, None)], incs=[(dv, 1)])
            S.op("tensor", lambda e: e.transpose(ptpf[0:1, 0:128], negc[:, :],
                                                 ctx.idf[:, :]),
                 waits=[(dv, None)], incs=[(pe, 1)])
            S.op("vector", lambda e, t=t: e.tensor_copy(
                qhat[64:65, t * 128:(t + 1) * 128], ptpf[0:1, 0:128]),
                waits=[(pe, None)], incs=[(dv, 1)])
        for qb in range(4):
            for kt in range(NKT):
                pbuf = p512a if kt % 2 == 0 else p512b
                S.op("tensor", lambda e, qb=qb, kt=kt, pbuf=pbuf: e.matmul(
                    pbuf[:, :], khat[0:65, kt * 128:(kt + 1) * 128],
                    qhat[0:65, qb * 512:(qb + 1) * 512], start=True, stop=True),
                    waits=[(ac, None)], incs=[(pe, 1)])
                S.op("scalar", lambda e, kt=kt, pbuf=pbuf: e.activation(
                    ptile[kt % 2][:, :], pbuf[:, :], AF.Exp),
                    waits=[(pe, None)], incs=[(ac, 1)])
                S.op("tensor", lambda e, kt=kt: e.matmul(
                    po[:, :], vhat[:, kt * 65:(kt + 1) * 65], ptile[kt % 2][:, :],
                    start=(kt == 0), stop=(kt == NKT - 1)),
                    waits=[(ac, None)] + ([(dv, None)] if kt == 0 else []),
                    incs=[(pe, 1)])
            S.op("vector", lambda e: e.reciprocal(rs_row[:, :], po[64:65, :]),
                 waits=[(pe, None)], incs=[(dv, 1)])
            S.op("tensor", lambda e: e.matmul(
                pb[:, :], ctx.ones_r[0:1, 0:64], rs_row[:, :], start=True, stop=True),
                waits=[(dv, None)], incs=[(pe, 1)])
            S.op("vector", lambda e: e.tensor_copy(bc[:, :], pb[:, :]),
                 waits=[(pe, None)], incs=[(dv, 1)])
            S.op("vector", lambda e, h=h, qb=qb, hw=hw: e.tensor_tensor(
                attn_sb[h // 2][hw:hw + 64, qb * 512:(qb + 1) * 512],
                po[0:64, :], bc[:, :], ALU.mult),
                waits=[(dv, None)], incs=[(dv, 1)])
        S.emit()

    wout = [nc.alloc_sbuf_tensor(f"wout{c}", [128, D], BF16) for c in range(8)]
    bout = nc.alloc_sbuf_tensor("bout", [1, D], BF16)
    ytile = nc.alloc_sbuf_tensor("ytile", [128, D], F32)
    xat = nc.alloc_sbuf_tensor("xat", [128, D], F32)
    sxt = [nc.alloc_sbuf_tensor(f"sxt{i}", [128, D], F32) for i in range(2)]
    sq2 = nc.alloc_sbuf_tensor("sq2", [128, D], F32)
    cst2 = {k: nc.alloc_sbuf_tensor(f"c2_{k}", [128, 1], F32) for k in
            ("c0", "c1", "c2", "c3")}
    gb_t = {}
    grow = nc.alloc_sbuf_tensor("grow", [1, 2 * D], F32)
    for i, nm in enumerate(("ln1g", "ln1b")):
        gb_t[nm] = nc.alloc_sbuf_tensor(f"gbt_{nm}", [128, D], F32)
    xaT = [nc.alloc_sbuf_tensor(f"xaT{c}", [128, TOK], BF16) for c in range(8)]
    sbf = nc.alloc_sbuf_tensor("sbf", [128, D], BF16)
    stile = nc.alloc_sbuf_tensor("stile", [128, D], F32)

    S = Script(nc)
    for c in range(8):
        S.op("gpsimd", lambda e, c=c: e.dma_start(
            out=wout[c][:, :], in_=wod[c * 128:(c + 1) * 128, :]), incs=[(ldg, 16)])
    S.op("gpsimd", lambda e: e.dma_start(out=bout[:, :], in_=bod[:, :]),
         incs=[(ldg, 16)])
    for i, (nm, dd) in enumerate((("ln1g", ln1gd), ("ln1b", ln1bd))):
        S.op("sync", lambda e, i=i, dd=dd: e.dma_start(
            out=grow[:, i * D:(i + 1) * D], in_=dd[:, :]), incs=[(ld, 16)])
    for i, nm in enumerate(("ln1g", "ln1b")):
        for g in range(2):
            S.op("tensor", lambda e, i=i, g=g: e.matmul(
                pb[:, :] if False else p512c[:, :], ctx.ones_r[0:1, :],
                grow[:, i * D + g * 512: i * D + (g + 1) * 512],
                start=True, stop=True),
                waits=[(ld, None), (dv, None)], incs=[(pe, 1)])
            S.op("vector", lambda e, nm=nm, g=g: e.tensor_copy(
                gb_t[nm][:, g * 512:(g + 1) * 512], p512c[:, :]),
                waits=[(pe, None)], incs=[(dv, 1)])
    scr = dict(sq=sq2[:, :], c0=cst2["c0"][:, :], c1=cst2["c1"][:, :],
               c2=cst2["c2"][:, :], c3=cst2["c3"][:, :])
    for t in range(NT):
        tb = t % 2
        S.op("sync", lambda e, t=t, tb=tb: e.dma_start(
            out=sxt[tb][:, :], in_=x1d[t * 128:(t + 1) * 128, :]),
            waits=[(dv, None)], incs=[(ld, 16)])
        for g in range(2):
            pbuf = p512a if g == 0 else p512b
            pairs = [(attn_sb[c][:, t * 128:(t + 1) * 128],
                      wout[c][:, g * 512:(g + 1) * 512]) for c in range(8)]
            mm_group(S, ctx, pbuf[:, :], pairs,
                     waits_first=[(ldg, None), (ac, None)], stop_last=False)
            S.op("tensor", lambda e, g=g, pbuf=pbuf: e.matmul(
                pbuf[:, :], ctx.ones_rb[:, :], bout[:, g * 512:(g + 1) * 512],
                start=False, stop=True), incs=[(pe, 1)])
            S.op("scalar", lambda e, g=g, pbuf=pbuf: e.activation(
                ytile[:, g * 512:(g + 1) * 512], pbuf[:, :], AF.Copy),
                waits=[(pe, None)], incs=[(ac, 1)])
        emit_ln(ctx, S, sxt[tb][:, :], stile[:, :], scr,
                x_waits=[(ld, None), (dv, None)])
        S.op("vector", lambda e: e.tensor_tensor(
            ytile[:, :], stile[:, :], ytile[:, :], ALU.add),
            waits=[(ac, None), (dv, None)], incs=[(dv, 1)])
        S.op("vector", None, waits=[(st, None)])
        emit_ln(ctx, S, ytile[:, :], xat[:, :], scr, x_waits=[(dv, None)],
                gain_bias=(gb_t["ln1g"][:, :], gb_t["ln1b"][:, :]))
        S.op("gpsimd", lambda e, t=t: e.dma_start(
            out=o_x1a[t * 128:(t + 1) * 128, :], in_=xat[:, :]),
            waits=[(dv, None)], incs=[(st, 16)])
        S.op("vector", lambda e: e.tensor_copy(sbf[:, :], xat[:, :]),
             waits=[(dv, None)], incs=[(dv, 1)])
        for c in range(8):
            emit_tp(ctx, S, sbf[:, c * 128:(c + 1) * 128],
                    xaT[c][:, t * 128:(t + 1) * 128], ptpb[:, :])
    for c in range(8):
        S.op("gpsimd", lambda e, c=c: e.dma_start(
            out=o_xaT[c * 128:(c + 1) * 128, :], in_=xaT[c][:, :]),
            waits=[(dv, None)], incs=[(st, 16)])
    S.op("gpsimd", None, waits=[(st, None), (ldg, None)])
    S.emit()
    return nc, ctx




def prog3_moe(nc, ctx, hd):
    ld, ldg, st, pe, dv, ac = ctx.ld, ctx.ldg, ctx.st, ctx.pe, ctx.dv, ctx.ac
    xaT, gb_t, scr = hd["xaT"], hd["gb_t"], hd["scr"]
    p512a, p512b, p512c, p512d = hd["p512a"], hd["p512b"], hd["p512c"], hd["p512d"]
    po, pb, ptpb, ptpf = hd["po"], hd["pb"], hd["ptpb"], hd["ptpf"]
    sxt, ytile, xat, sbf = hd["sxt"], hd["ytile"], hd["xat"], hd["sbf"]

    gw_sb = nc.alloc_sbuf_tensor("gw_sb", [128, 8 * 16], BF16)
    gb_sb = nc.alloc_sbuf_tensor("gb_sb", [1, 8], BF16)
    b1c = nc.alloc_sbuf_tensor("b1c", [128, 16 * 8], F32)
    b2_sb = nc.alloc_sbuf_tensor("b2_sb", [8, D], F32)
    gcol = nc.alloc_sbuf_tensor("gcol", [128, 8], F32)
    gtmp = nc.alloc_sbuf_tensor("gtmp", [128, 8], F32)
    gred = nc.alloc_sbuf_tensor("gred", [128, 4], F32)
    w8 = nc.alloc_sbuf_tensor("w8", [128, NT * 8], F32)
    w8tt = nc.alloc_sbuf_tensor("w8tt", [8, 128], F32)
    w1e = nc.alloc_sbuf_tensor("w1e", [128, 8 * 2048], BF16)
    w2e = nc.alloc_sbuf_tensor("w2e", [128, 8 * 1024], BF16)
    hts = nc.alloc_sbuf_tensor("hts", [128, 16 * 1024], BF16)
    accs = nc.alloc_sbuf_tensor("accs", [128, 8 * 1024], F32)

    S = Script(nc)
    for c in range(8):
        S.op("gpsimd", lambda e, c=c: e.dma_start(
            out=gw_sb[:, c * 16:c * 16 + 8],
            in_=hd["gwd"][c * 128:(c + 1) * 128, :]), incs=[(ldg, 16)])
    S.op("gpsimd", lambda e: e.dma_start(out=gb_sb[:, :], in_=hd["gbd"][:, :]),
         incs=[(ldg, 16)])
    S.op("sync", lambda e: e.dma_start(
        out=b1c[:, :], in_=hd["b1td"].ap().rearrange("(f p) e -> p f e", p=128)),
        incs=[(ld, 16)])
    S.op("sync", lambda e: e.dma_start(out=b2_sb[:, :], in_=hd["b2d"][:, :]),
         incs=[(ld, 16)])
    for t in range(NT):
        pairs = [(xaT[kc][:, t * 128:(t + 1) * 128], gw_sb[:, kc * 16:kc * 16 + 8])
                 for kc in range(8)]
        mm_group(S, ctx, p512c[:, 0:8], pairs,
                 waits_first=[(ldg, None), (dv, None), (ld, None)], stop_last=False)
        S.op("tensor", lambda e: e.matmul(
            p512c[:, 0:8], ctx.ones_rb[:, :], gb_sb[:, :], start=False, stop=True),
            incs=[(pe, 1)])
        S.op("vector", lambda e: e.reduce_max(gred[:, 0:1], p512c[:, 0:8],
                                              axis=AX.X, negate=True),
             waits=[(pe, None)], incs=[(dv, 1)])
        S.op("scalar", lambda e: e.activation(
            gcol[:, :], p512c[:, 0:8], AF.Exp, bias=gred[:, 0:1],
            accum_out=gred[:, 1:2]),
            waits=[(dv, None)], incs=[(ac, 1)])
        S.op("vector", lambda e: e.reciprocal(gred[:, 2:3], gred[:, 1:2]),
             waits=[(ac, None)], incs=[(dv, 1)])
        S.op("vector", lambda e: e.tensor_scalar_mul(gcol[:, :], gcol[:, :],
                                                     gred[:, 2:3]),
             waits=[(dv, None)], incs=[(dv, 1)])
        S.op("vector", lambda e: e.reduce_max(gred[:, 2:3], gcol[:, :], axis=AX.X),
             waits=[(dv, None)], incs=[(dv, 1)])
        S.op("vector", lambda e: e.tensor_scalar(gtmp[:, :], gcol[:, :],
                                                 gred[:, 2:3], None, ALU.is_ge),
             waits=[(dv, None)], incs=[(dv, 1)])
        S.op("vector", lambda e: e.scalar_tensor_tensor(
            gtmp[:, :], gtmp[:, :], -1e30, gcol[:, :], ALU.mult, ALU.add),
            waits=[(dv, None)], incs=[(dv, 1)])
        S.op("vector", lambda e: e.reduce_max(gred[:, 3:4], gtmp[:, :], axis=AX.X),
             waits=[(dv, None)], incs=[(dv, 1)])
        S.op("vector", lambda e: e.tensor_scalar(gtmp[:, :], gcol[:, :],
                                                 gred[:, 3:4], None, ALU.is_ge),
             waits=[(dv, None)], incs=[(dv, 1)])
        S.op("vector", lambda e, t=t: e.tensor_tensor(
            w8[:, t * 8:(t + 1) * 8], gcol[:, :], gtmp[:, :], ALU.mult),
            waits=[(dv, None)], incs=[(dv, 1)])
    S.emit()

    for th in range(2):
        S = Script(nc)
        t0 = th * 8
        for tt in range(8):
            S.op("tensor", lambda e, tt=tt: e.transpose(
                ptpf[0:8, 0:128], w8[:, (t0 + tt) * 8:(t0 + tt + 1) * 8],
                ctx.idf[:, :]),
                waits=[(dv, None), (ac, None)], incs=[(pe, 1)])
            S.op("vector", lambda e: e.tensor_copy(w8tt[:, :], ptpf[0:8, 0:128]),
                 waits=[(pe, None)], incs=[(dv, 1)])
            S.op("tensor", lambda e, tt=tt: e.matmul(
                p512c[:, :], w8tt[:, :],
                b2_sb[:, 0:512], start=True, stop=True),
                waits=[(dv, None), (ac, None), (ld, None)], incs=[(pe, 1)])
            S.op("tensor", lambda e, tt=tt: e.matmul(
                p512d[:, :], w8tt[:, :],
                b2_sb[:, 512:1024], start=True, stop=True),
                incs=[(pe, 1)])
            S.op("scalar", lambda e, tt=tt: e.activation(
                accs[:, tt * 1024: tt * 1024 + 512], p512c[:, :], AF.Copy),
                waits=[(pe, None)], incs=[(ac, 1)])
            S.op("scalar", lambda e, tt=tt: e.activation(
                accs[:, tt * 1024 + 512:(tt + 1) * 1024], p512d[:, :], AF.Copy),
                waits=[(pe, None)], incs=[(ac, 1)])
        for e_i in range(8):
            for kc in range(8):
                S.op("gpsimd", lambda e, e_i=e_i, kc=kc: e.dma_start(
                    out=w1e[:, kc * 2048:(kc + 1) * 2048],
                    in_=hd["w1d"][e_i, kc * 128:(kc + 1) * 128, :]),
                    waits=([(pe, None)] if kc == 0 else []), incs=[(ldg, 16)])

            for ft in range(16):
                for tb2 in range(2):
                    pbuf = p512a if tb2 == 0 else p512b
                    pairs = [(w1e[:, kc * 2048 + ft * 128: kc * 2048 + (ft + 1) * 128],
                              xaT[kc][:, (t0 + tb2 * 4) * 128:(t0 + (tb2 + 1) * 4) * 128])
                             for kc in range(8)]
                    mm_group(S, ctx, pbuf[:, :], pairs,
                             waits_first=[(ldg, None), (ac, None)])
                    S.op("scalar", lambda e, ft=ft, tb2=tb2, e_i=e_i, pbuf=pbuf:
                         e.activation(hts[:, ft * 1024 + tb2 * 512:
                                          ft * 1024 + (tb2 + 1) * 512],
                                      pbuf[:, :], AF.Relu,
                                      bias=b1c[:, ft * 8 + e_i: ft * 8 + e_i + 1]),
                         waits=[(pe, None), (ld, None)], incs=[(ac, 1)])
            for fcg in range(2):
                for fc8 in range(8):
                    fc = fcg * 8 + fc8
                    S.op("gpsimd", lambda e, e_i=e_i, fc=fc, fc8=fc8: e.dma_start(
                        out=w2e[:, fc8 * 1024:(fc8 + 1) * 1024],
                        in_=hd["w2d"][e_i, fc * 128:(fc + 1) * 128, :]),
                        waits=([(pe, None)] if fc8 == 0 else []), incs=[(ldg, 16)])
                for tt in range(8):
                    for g in range(2):
                        pbuf = p512c if g == 0 else p512d
                        pairs = [(hts[:, (fcg * 8 + fc8) * 1024 + tt * 128:
                                      (fcg * 8 + fc8) * 1024 + (tt + 1) * 128],
                                  w2e[:, fc8 * 1024 + g * 512:
                                      fc8 * 1024 + (g + 1) * 512])
                                 for fc8 in range(8)]
                        mm_group(S, ctx, pbuf[:, :], pairs,
                                 waits_first=[(ldg, None), (ac, None), (dv, None)])
                        S.op("vector", lambda e, tt=tt, g=g, e_i=e_i, pbuf=pbuf:
                             e.scalar_tensor_tensor(
                                 accs[:, tt * 1024 + g * 512:
                                      tt * 1024 + (g + 1) * 512],
                                 pbuf[:, :], w8[:, (t0 + tt) * 8 + e_i:
                                                (t0 + tt) * 8 + e_i + 1],
                                 accs[:, tt * 1024 + g * 512:
                                      tt * 1024 + (g + 1) * 512],
                                 ALU.mult, ALU.add),
                             waits=[(pe, None)], incs=[(dv, 1)])
        S.emit()

        S = Script(nc)
        for tt in range(8):
            t = t0 + tt
            tb = t % 2
            S.op("sync", lambda e, t=t: e.dma_start(
                out=xat[:, :], in_=hd["x1a"][t * 128:(t + 1) * 128, :]),
                waits=[(dv, None), (ac, None), (st, None)], incs=[(ld, 16)])
            S.op("sync", lambda e, t=t, tb=tb: e.dma_start(
                out=sxt[tb][:, :], in_=hd["x1d"][t * 128:(t + 1) * 128, :]),
                waits=[(dv, None)], incs=[(ld, 16)])
            S.op("vector", lambda e, tt=tt: e.tensor_tensor(
                ytile[:, :], xat[:, :], accs[:, tt * 1024:(tt + 1) * 1024], ALU.add),
                waits=[(ld, None), (dv, None)], incs=[(dv, 1)])
            emit_ln(ctx, S, ytile[:, :], ytile[:, :], scr, x_waits=[(dv, None)],
                    gain_bias=(gb_t["ln2g"][:, :], gb_t["ln2b"][:, :]))
            emit_ln(ctx, S, ytile[:, :], ytile[:, :], scr, x_waits=[(dv, None)])
            S.op("vector", lambda e, tb=tb: e.tensor_tensor(
                xat[:, :], sxt[tb][:, :], ytile[:, :], ALU.add),
                waits=[(dv, None), (st, None)], incs=[(dv, 1)])
            S.op("gpsimd", lambda e, t=t: e.dma_start(
                out=hd["o_x2"][t * 128:(t + 1) * 128, :], in_=xat[:, :]),
                waits=[(dv, None)], incs=[(st, 16)])
            emit_ln(ctx, S, xat[:, :], sbf[:, :], scr, x_waits=[(dv, None)])
            for c in range(8):
                emit_tp(ctx, S, sbf[:, c * 128:(c + 1) * 128],
                        xaT[c][:, t * 128:(t + 1) * 128], ptpb[:, :])
        S.emit()

    wfm3 = [nc.alloc_sbuf_tensor(f"wfm3_{i}", [128, 8 * 128], BF16) for i in range(2)]
    wtm3 = [nc.alloc_sbuf_tensor("wtm3_0", [128, 8 * 512], BF16)] * 2
    stage3 = [nc.alloc_sbuf_tensor(f"stg3_{i}", [128, 512], BF16) for i in range(4)]
    sqf3 = nc.alloc_sbuf_tensor("sqf3", [128, 512], F32)
    bq3c = nc.alloc_sbuf_tensor("bq3cs", [128, 64], F32)
    bk3c = nc.alloc_sbuf_tensor("bk3cs", [128, 64], F32)
    bv3s = nc.alloc_sbuf_tensor("bv3s", [1, D], BF16)
    nrow512 = nc.alloc_sbuf_tensor("nrow512", [1, 512], F32)

    S = Script(nc)
    for c in range(8):
        S.op("sync", lambda e, c=c: e.dma_start(
            out=bq3c[:, 8 * c:8 * c + 1], in_=hd["bq3d"][c * 128:(c + 1) * 128, :]),
            incs=[(ld, 16)])
        S.op("sync", lambda e, c=c: e.dma_start(
            out=bk3c[:, 8 * c:8 * c + 1], in_=hd["bk3d"][c * 128:(c + 1) * 128, :]),
            incs=[(ld, 16)])
    S.op("gpsimd", lambda e: e.dma_start(out=bv3s[:, :], in_=hd["bv3d"][:, :]),
         incs=[(ldg, 16)])
    for (wdram, biascols, odram, nr) in ((hd["wq3d"], bq3c, hd["o_qt3"], hd["o_qnn"]),
                                         (hd["wk3d"], bk3c, hd["o_kt3"], hd["o_knn"])):
        for oc in range(8):
            b = oc % 2
            for kc in range(8):
                S.op("gpsimd", lambda e, kc=kc, oc=oc, b=b, wdram=wdram: e.dma_start(
                    out=wfm3[b][:, kc * 128:(kc + 1) * 128],
                    in_=wdram[kc * 128:(kc + 1) * 128, oc * 128:oc * 128 + 128]),
                    waits=([(pe, None)] if kc == 0 else []), incs=[(ldg, 16)])
            for tb in range(4):
                pbuf = p512a if tb % 2 == 0 else p512b
                pairs = [(wfm3[b][:, kc * 128:(kc + 1) * 128],
                          xaT[kc][:, tb * 512:(tb + 1) * 512]) for kc in range(8)]
                mm_group(S, ctx, pbuf[:, :], pairs,
                         waits_first=[(ldg, None), (ac, None), (dv, None)])
                S.op("vector", lambda e, pbuf=pbuf, tb=tb, oc=oc, biascols=biascols:
                     e.tensor_scalar(stage3[tb][:, :], pbuf[:, :],
                                     biascols[:, 8 * oc:8 * oc + 1], None,
                                     ALU.add),
                     waits=[(pe, None), (st, None), (ac, None), (ld, None)],
                     incs=[(dv, 1)])
                S.op("gpsimd", lambda e, oc=oc, tb=tb, odram=odram: e.dma_start(
                    out=odram[oc * 128:(oc + 1) * 128, tb * 512:(tb + 1) * 512],
                    in_=stage3[tb][:, :]),
                    waits=[(dv, None)], incs=[(st, 16)])
                S.op("scalar", lambda e, tb=tb: e.activation(
                    sqf3[:, :], stage3[tb][:, :], AF.Square),
                    waits=[(dv, None)], incs=[(ac, 1)])
                for hh in range(2):
                    head = 2 * oc + hh
                    S.op("tensor", lambda e, hh=hh: e.matmul(
                        pb[0:1, :], ctx.constcol(-0.5)[hh * 64:hh * 64 + 64, :],
                        sqf3[hh * 64:hh * 64 + 64, :], start=True, stop=True),
                        waits=[(ac, None), (dv, None), (st, None)], incs=[(pe, 1)])
                    S.op("vector", lambda e: e.tensor_copy(
                        nrow512[:, :], pb[0:1, :]),
                        waits=[(pe, None)], incs=[(dv, 1)])
                    S.op("gpsimd", lambda e, head=head, tb=tb, nr=nr: e.dma_start(
                        out=nr[head:head + 1, tb * 512:(tb + 1) * 512],
                        in_=nrow512[:, :]),
                        waits=[(dv, None)], incs=[(st, 16)])
    for g in range(2):
        wb = wtm3[g % 2]
        for kc in range(8):
            S.op("gpsimd", lambda e, kc=kc, g=g, wb=wb: e.dma_start(
                out=wb[:, kc * 512:(kc + 1) * 512],
                in_=hd["wv3d"][kc * 128:(kc + 1) * 128, g * 512:g * 512 + 512]),
                waits=([(pe, None), (ac, None)] if kc == 0 else []),
                incs=[(ldg, 16)])
        for t in range(NT):
            pbuf = p512a if t % 2 == 0 else p512b
            pairs = [(xaT[kc][:, t * 128:(t + 1) * 128],
                      wb[:, kc * 512:(kc + 1) * 512]) for kc in range(8)]
            mm_group(S, ctx, pbuf[:, :], pairs,
                     waits_first=[(ldg, None), (ac, None), (dv, None)],
                     stop_last=False)
            S.op("tensor", lambda e, g=g, pbuf=pbuf: e.matmul(
                pbuf[:, :], ctx.ones_rb[:, :], bv3s[:, g * 512:(g + 1) * 512],
                start=False, stop=True), incs=[(pe, 1)])
            S.op("scalar", lambda e, t=t, pbuf=pbuf: e.activation(
                stage3[2 + t % 2][:, :], pbuf[:, :], AF.Copy),
                waits=[(pe, None), (st, None)], incs=[(ac, 1)])
            S.op("gpsimd", lambda e, t=t, g=g: e.dma_start(
                out=hd["o_v3"][t * 128:(t + 1) * 128, g * 512:(g + 1) * 512],
                in_=stage3[2 + t % 2][:, :]),
                waits=[(ac, None)], incs=[(st, 16)])
    S.op("gpsimd", None, waits=[(st, None), (ldg, None)])
    S.emit()


def build_prog3b():
    nc = bass.Bass("TRN2", target_bir_lowering=False)
    ctx = Ctx(nc)
    ld, ldg, st, pe, dv, ac = ctx.ld, ctx.ldg, ctx.st, ctx.pe, ctx.dv, ctx.ac
    x1d = nc.dram_tensor("x1", [TOK, D], F32, kind="ExternalInput")
    x1ad = nc.dram_tensor("x1a", [TOK, D], F32, kind="ExternalInput")
    xaTd = nc.dram_tensor("xaT", [D, TOK], BF16, kind="ExternalInput")
    gwd = nc.dram_tensor("gW", [D, 8], F32, kind="ExternalInput")
    gbd = nc.dram_tensor("gb", [1, 8], F32, kind="ExternalInput")
    w1d = nc.dram_tensor("W1", [8, D, 2048], F32, kind="ExternalInput")
    b1td = nc.dram_tensor("b1T", [2048, 8], F32, kind="ExternalInput")
    w2d = nc.dram_tensor("W2", [8, 2048, D], F32, kind="ExternalInput")
    b2d = nc.dram_tensor("b2", [8, D], F32, kind="ExternalInput")
    ln2gd = nc.dram_tensor("ln2g", [1, D], F32, kind="ExternalInput")
    ln2bd = nc.dram_tensor("ln2b", [1, D], F32, kind="ExternalInput")
    wq3d = nc.dram_tensor("wq3", [D, D], F32, kind="ExternalInput")
    wk3d = nc.dram_tensor("wk3", [D, D], F32, kind="ExternalInput")
    wv3d = nc.dram_tensor("wv3", [D, D], F32, kind="ExternalInput")
    bq3d = nc.dram_tensor("bq3c", [D, 1], F32, kind="ExternalInput")
    bk3d = nc.dram_tensor("bk3c", [D, 1], F32, kind="ExternalInput")
    bv3d = nc.dram_tensor("bv3", [1, D], F32, kind="ExternalInput")
    o_x2 = nc.dram_tensor("x2", [TOK, D], F32, kind="ExternalOutput")
    o_qt3 = nc.dram_tensor("qT3", [D, TOK], BF16, kind="ExternalOutput")
    o_kt3 = nc.dram_tensor("kT3", [D, TOK], BF16, kind="ExternalOutput")
    o_v3 = nc.dram_tensor("V3", [TOK, D], BF16, kind="ExternalOutput")
    o_qnn = nc.dram_tensor("qnn", [H, TOK], F32, kind="ExternalOutput")
    o_knn = nc.dram_tensor("knn", [H, TOK], F32, kind="ExternalOutput")

    xaT = [nc.alloc_sbuf_tensor(f"xaT{c}", [128, TOK], BF16) for c in range(8)]
    ytile = nc.alloc_sbuf_tensor("ytile", [128, D], F32)
    xat = nc.alloc_sbuf_tensor("xat", [128, D], F32)
    sxt = [nc.alloc_sbuf_tensor(f"sxt{i}", [128, D], F32) for i in range(2)]
    sq2 = nc.alloc_sbuf_tensor("sq2", [128, D], F32)
    cst2 = {k: nc.alloc_sbuf_tensor(f"c2_{k}", [128, 1], F32) for k in
            ("c0", "c1", "c2", "c3")}
    sbf = nc.alloc_sbuf_tensor("sbf", [128, D], BF16)
    grow = nc.alloc_sbuf_tensor("grow", [1, 512], F32)
    gb_t = {nm: nc.alloc_sbuf_tensor(f"gbt_{nm}", [128, D], F32)
            for nm in ("ln2g", "ln2b")}
    p512a = nc.alloc_psum_tensor("p512a", [128, 512], F32)
    p512b = nc.alloc_psum_tensor("p512b", [128, 512], F32)
    p512c = nc.alloc_psum_tensor("p512c", [128, 512], F32)
    p512d = nc.alloc_psum_tensor("p512d", [128, 512], F32)
    po = nc.alloc_psum_tensor("po", [65, 512], F32)
    pb = nc.alloc_psum_tensor("pb", [64, 512], F32)
    ptpb = nc.alloc_psum_tensor("ptpb", [128, 1024], BF16)[0:128, 0:128]
    ptpf = nc.alloc_psum_tensor("ptpf", [128, 512], F32)[0:128, 0:128]
    scr = dict(sq=sq2[:, :], c0=cst2["c0"][:, :], c1=cst2["c1"][:, :],
               c2=cst2["c2"][:, :], c3=cst2["c3"][:, :])

    S = Script(nc)
    ctx.emit_init(S, (EPS, 1.0, -0.5))
    for c in range(8):
        S.op("sync", lambda e, c=c: e.dma_start(
            out=xaT[c][:, :], in_=xaTd[c * 128:(c + 1) * 128, :]),
            incs=[(ld, 16)])
    for nm, dd in (("ln2g", ln2gd), ("ln2b", ln2bd)):
        for g in range(2):
            S.op("sync", lambda e, dd=dd, g=g: e.dma_start(
                out=grow[:, :], in_=dd[:, g * 512:(g + 1) * 512]),
                waits=[(pe, None)], incs=[(ld, 16)])
            S.op("tensor", lambda e, g=g: e.matmul(
                p512c[:, :], ctx.ones_r[0:1, :], grow[:, :],
                start=True, stop=True),
                waits=[(ld, None), (dv, None)], incs=[(pe, 1)])
            S.op("vector", lambda e, nm=nm, g=g: e.tensor_copy(
                gb_t[nm][:, g * 512:(g + 1) * 512], p512c[:, :]),
                waits=[(pe, None)], incs=[(dv, 1)])
    S.emit()
    hd = dict(o_x2=o_x2, o_qt3=o_qt3, o_kt3=o_kt3, o_v3=o_v3, o_qnn=o_qnn,
              o_knn=o_knn, x1a=x1ad, x1d=x1d, xaT=xaT, gb_t=gb_t, scr=scr,
              gwd=gwd, gbd=gbd, w1d=w1d, b1td=b1td, w2d=w2d, b2d=b2d,
              wq3d=wq3d, wk3d=wk3d, wv3d=wv3d, bq3d=bq3d, bk3d=bk3d, bv3d=bv3d,
              p512a=p512a, p512b=p512b, p512c=p512c, p512d=p512d, po=po, pb=pb,
              ptpb=ptpb, ptpf=ptpf, sxt=sxt, ytile=ytile, xat=xat, sbf=sbf)
    prog3_moe(nc, ctx, hd)
    return nc, ctx



LNBASE = float(np.log(np.sqrt(64.0) / np.log1p(64.0)))


def build_prog4():
    nc = bass.Bass("TRN2", target_bir_lowering=False)
    ctx = Ctx(nc)
    ld, ldg, st, pe, dv, ac = ctx.ld, ctx.ldg, ctx.st, ctx.pe, ctx.dv, ctx.ac

    x2d = nc.dram_tensor("x2", [TOK, D], F32, kind="ExternalInput")
    qT3d = nc.dram_tensor("qT3", [D, TOK], BF16, kind="ExternalInput")
    qnnd = nc.dram_tensor("qnn", [H, TOK], F32, kind="ExternalInput")
    kT3f = nc.dram_tensor("kT3f", [D, NSEQ], BF16, kind="ExternalInput")
    knnf = nc.dram_tensor("knnf", [H, NSEQ], F32, kind="ExternalInput")
    v3f = nc.dram_tensor("V3f", [NSEQ, D], BF16, kind="ExternalInput")
    wo4d = nc.dram_tensor("wo4", [D, D], F32, kind="ExternalInput")
    bo4d = nc.dram_tensor("bo4", [1, D], F32, kind="ExternalInput")
    alphad = nc.dram_tensor("alpha", [1, 1], F32, kind="ExternalInput")
    wstd = nc.dram_tensor("WsT", [D, D], F32, kind="ExternalInput")
    wetd = nc.dram_tensor("WeT", [D, D], F32, kind="ExternalInput")
    wgd = nc.dram_tensor("Wg", [D, 7], F32, kind="ExternalInput")
    bias4d = nc.dram_tensor("bias4", [1, D], F32, kind="ExternalInput")
    o_x = nc.dram_tensor("xout", [TOK, D], F32, kind="ExternalOutput")
    import os as _os
    _p4dbg = _os.environ.get("P4_DBG", "0") == "1"
    if _p4dbg:
        x3_scr = nc.dram_tensor("x3_scr", [TOK, D], F32, kind="ExternalOutput")
        o_attn = nc.dram_tensor("o_attn", [D, TOK], BF16, kind="ExternalOutput")
        o_tst = nc.dram_tensor("o_tst", [128, NKT * 512], F32, kind="ExternalOutput")
        o_crow = nc.dram_tensor("o_crow", [64, 512], F32, kind="ExternalOutput")
        o_po = nc.dram_tensor("o_po", [64 * 65, 512], F32, kind="ExternalOutput")
        postg = nc.alloc_sbuf_tensor("postg", [65, 512], F32)
    else:
        x3_scr = nc.dram_tensor("x3_scr", [TOK, D], F32, kind="Internal")

    qhat2 = nc.alloc_sbuf_tensor("qhat2", [128, TOK], BF16)
    khat2 = nc.alloc_sbuf_tensor("khat2", [128, NSEQ], BF16)
    vhat = nc.alloc_sbuf_tensor("vhat", [128, NKT * 65], BF16)
    arena = nc.alloc_sbuf_tensor("arena4", [128, 65536], mybir.dt.uint8)
    tst = arena[:, 0:65536].bitcast(F32)
    Mrun = nc.alloc_sbuf_tensor("Mrun", [128, 512], F32)
    lbuf = nc.alloc_sbuf_tensor("lbuf", [128, 512], F32)
    rsd = [nc.alloc_sbuf_tensor(f"rsd{i}", [128, 512], F32) for i in range(2)]
    ubuf2 = [nc.alloc_sbuf_tensor(f"ubuf{i}", [128, 512], F32) for i in range(2)]
    fence = nc.alloc_sbuf_tensor("fence", [1, 1], F32)
    crow = nc.alloc_sbuf_tensor("crow", [1, 512], F32)
    ccol = nc.alloc_sbuf_tensor("ccol", [128, 1], F32)
    Csb = nc.alloc_sbuf_tensor("Csb", [128, 512], F32)
    hls = nc.alloc_sbuf_tensor("hls", [128, 1], F32)
    asb = nc.alloc_sbuf_tensor("asb", [1, 1], F32)
    ptile = [nc.alloc_sbuf_tensor(f"ptile{i}", [128, 512], BF16) for i in range(2)]
    rs_row = nc.alloc_sbuf_tensor("rs_row", [1, 512], F32)
    bcb = nc.alloc_sbuf_tensor("bcb", [64, 512], F32)
    attn_sb = [nc.alloc_sbuf_tensor(f"attn{i}", [128, TOK], BF16) for i in range(8)]
    p512a = nc.alloc_psum_tensor("p512a", [128, 512], F32)
    p512b = nc.alloc_psum_tensor("p512b", [128, 512], F32)
    p512c = nc.alloc_psum_tensor("p512c", [128, 512], F32)
    p512d = nc.alloc_psum_tensor("p512d", [128, 512], F32)
    po = nc.alloc_psum_tensor("po", [65, 512], F32)
    pC = nc.alloc_psum_tensor("pC", [128, 512], F32)
    ptpb = nc.alloc_psum_tensor("ptpb", [128, 1024], BF16)[0:128, 0:128]
    ptpf = nc.alloc_psum_tensor("ptpf", [128, 512], F32)[0:128, 0:128]

    S = Script(nc)
    ctx.emit_init(S, (EPS, 1.0, -0.5))
    S.op("sync", lambda e: e.dma_start(out=asb[:, :], in_=alphad[:, :]),
         incs=[(ld, 16)])
    S.op("vector", lambda e: e.tensor_scalar_mul(asb[:, :], asb[:, :],
                                                 0.5 * LNBASE),
         waits=[(ld, None)], incs=[(dv, 1)])
    S.op("tensor", lambda e: e.matmul(pC[:, 0:1], ctx.ones_r[0:1, :], asb[:, :],
                                      start=True, stop=True),
         waits=[(dv, None)], incs=[(pe, 1)])
    S.op("vector", lambda e: e.tensor_copy(hls[:, :], pC[:, 0:1]),
         waits=[(pe, None)], incs=[(dv, 1)])
    S.emit()

    for h in range(H):
        S = Script(nc)
        hw = 64 * (h % 2)
        S.op("sync", lambda e, h=h: e.dma_start(
            out=qhat2[0:64, :], in_=qT3d[h * 64:(h + 1) * 64, :]),
            waits=[(pe, None), (ac, None), (dv, None)], incs=[(ld, 16)])
        S.op("sync", lambda e, h=h: e.dma_start(
            out=khat2[0:64, :], in_=kT3f[h * 64:(h + 1) * 64, :]),
            incs=[(ld, 16)])
        if h == 0:
            S.op("vector", lambda e: e.memset(
                vhat[:, :].rearrange("p (kt c) -> p kt c", c=65)[:, :, 64:65],
                1.0),
                incs=[(dv, 1)])
        S.op("sync", lambda e, h=h: e.dma_start(
            out=vhat[:, :].rearrange("p (kt c) -> p kt c", c=65)[:, :, 0:64],
            in_=v3f.ap().rearrange("(kt p) d -> p kt d", p=128)[:, :, h * 64:(h + 1) * 64]),
            waits=[(dv, None)], incs=[(ld, 16)])
        S.op("vector", lambda e: e.memset(qhat2[64:66, :], 1.0), incs=[(dv, 1)])
        S.op("vector", lambda e: e.memset(khat2[64:66, :], 1.0), incs=[(dv, 1)])
        S.op("gpsimd", lambda e, h=h: e.dma_start(
            out=qhat2[64:65, :], in_=qnnd[h:h + 1, :]),
            waits=[(dv, None)], incs=[(ldg, 16)])
        S.op("gpsimd", lambda e, h=h: e.dma_start(
            out=khat2[65:66, :], in_=knnf[h:h + 1, :]),
            waits=[(dv, None)], incs=[(ldg, 16)])
        def pe_fence():
            pass

        for qb in range(4):
            qsl = slice(qb * 512, (qb + 1) * 512)
            def p1_mms(kt):
                pd = p512a if kt % 2 == 0 else p512b
                pn = p512c if kt % 2 == 0 else p512d
                S.op("tensor", lambda e, kt=kt, pd=pd, qsl=qsl: e.matmul(
                    pd[:, :], khat2[0:64, kt * 128:(kt + 1) * 128],
                    qhat2[0:64, qsl], start=True, stop=True),
                    waits=[(ld, None), (ldg, None), (dv, None)], incs=[(pe, 1)])
                S.op("tensor", lambda e, kt=kt, pn=pn, qsl=qsl: e.matmul(
                    pn[:, :], khat2[0:66, kt * 128:(kt + 1) * 128],
                    qhat2[0:66, qsl], start=True, stop=True),
                    incs=[(pe, 1)])

            def p1_ew(kt):
                pd = p512a if kt % 2 == 0 else p512b
                pn = p512c if kt % 2 == 0 else p512d
                S.op("scalar", lambda e, pn=pn: e.activation(
                    lbuf[:, :], pn[:, :], AF.Ln, bias=ctx.constcol(EPS),
                    scale=-2.0),
                    waits=[(pe, None), (dv, None)], incs=[(ac, 1)])
                S.op("scalar", lambda e, kt=kt: e.activation(
                    rsd[kt % 2][:, :], lbuf[:, :], AF.Exp, bias=hls[:, :],
                    scale=-0.5),
                    waits=[(ac, None), (dv, None)], incs=[(ac, 1)])
                S.op("vector", lambda e, kt=kt, pd=pd: e.tensor_tensor(
                    ubuf2[kt % 2][:, :], pd[:, :], rsd[kt % 2][:, :], ALU.mult),
                    waits=[(ac, None)], incs=[(dv, 1)])
                S.op("vector", lambda e: e.memset(fence[:, :], 0.0),
                     incs=[(dv, 1)])
                S.op("scalar", lambda e, kt=kt: e.activation(
                    tst[:, kt * 512:(kt + 1) * 512], ubuf2[kt % 2][:, :],
                    AF.Square),
                    waits=[(dv, None)], incs=[(ac, 1)])
                S.op("scalar", lambda e: e.activation(fence[:, :], fence[:, :],
                                                      AF.Square),
                     waits=[(ac, None)], incs=[(ac, 1)])
                if kt == 0:
                    S.op("vector", lambda e: e.tensor_copy(
                        Mrun[:, :], tst[:, 0:512]),
                        waits=[(ac, None), (pe, None)], incs=[(dv, 1)])
                else:
                    S.op("vector", lambda e, kt=kt: e.tensor_max(
                        Mrun[:, :], Mrun[:, :], tst[:, kt * 512:(kt + 1) * 512]),
                        waits=[(ac, None)], incs=[(dv, 1)])

            p1_mms(0)
            for kt in range(1, NKT):
                p1_mms(kt)
                p1_ew(kt - 1)
            pe_fence()
            p1_ew(NKT - 1)
            for j in range(4):
                S.op("tensor", lambda e, j=j: e.transpose(
                    ptpf[:, :], Mrun[:, j * 128:(j + 1) * 128], ctx.idf[:, :]),
                    waits=[(dv, None)], incs=[(pe, 1)])
                S.op("vector", lambda e: e.reduce_max(
                    ccol[:, :], ptpf[:, :], axis=AX.X),
                    waits=[(pe, None)], incs=[(dv, 1)])
                S.op("tensor", lambda e: e.transpose(
                    ptpf[0:1, 0:128], ccol[:, :], ctx.idf[:, :]),
                    waits=[(dv, None)], incs=[(pe, 1)])
                S.op("vector", lambda e, j=j: e.tensor_copy(
                    crow[:, j * 128:(j + 1) * 128], ptpf[0:1, 0:128]),
                    waits=[(pe, None)], incs=[(dv, 1)])
            if _p4dbg:
                S.op("gpsimd", lambda e, h=h, qb=qb: e.dma_start(
                    out=o_crow[h * 4 + qb:h * 4 + qb + 1, :], in_=crow[:, :]),
                    waits=[(dv, None)], incs=[(st, 16)])
                if h == 0 and qb == 0:
                    S.op("gpsimd", lambda e: e.dma_start(
                        out=o_tst[:, :], in_=tst[:, 0:NKT * 512]),
                        waits=[(dv, None), (ac, None)], incs=[(st, 16)])
            S.op("tensor", lambda e: e.matmul(
                pC[:, :], ctx.ones_r[0:1, :], crow[:, :], start=True, stop=True),
                waits=[(dv, None)] + ([(st, None)] if _p4dbg else []),
                incs=[(pe, 1)])
            def p2_sub(kt):
                S.op("vector", lambda e, kt=kt: e.tensor_sub(
                    tst[:, kt * 512:(kt + 1) * 512],
                    tst[:, kt * 512:(kt + 1) * 512], pC[:, :]),
                    waits=[(dv, None), (ac, None), (pe, None), (ld, None),
                           (ldg, None), (st, None)],
                    incs=[(dv, 1)])

            def p2_exp_pv(kt):
                S.op("scalar", lambda e, kt=kt: e.activation(
                    ptile[kt % 2][:, :], tst[:, kt * 512:(kt + 1) * 512], AF.Exp),
                    waits=[(dv, None), (pe, None)], incs=[(ac, 1)])
                S.op("tensor", lambda e, kt=kt: e.matmul(
                    po[:, :], vhat[:, kt * 65:(kt + 1) * 65], ptile[kt % 2][:, :],
                    start=(kt == 0), stop=(kt == NKT - 1)),
                    waits=[(ac, None)] + ([(dv, None)] if kt == 0 else []),
                    incs=[(pe, 1)])

            def p2_exp(kt):
                S.op("scalar", lambda e, kt=kt: e.activation(
                    ptile[kt % 2][:, :], tst[:, kt * 512:(kt + 1) * 512], AF.Exp),
                    waits=[(dv, None), (pe, None), (ac, None)], incs=[(ac, 1)])

            def p2_pv(kt):
                S.op("tensor", lambda e, kt=kt: e.matmul(
                    po[:, :], vhat[:, kt * 65:(kt + 1) * 65], ptile[kt % 2][:, :],
                    start=(kt == 0), stop=(kt == NKT - 1)),
                    waits=[(ac, None), (dv, None)],
                    incs=[(pe, 1)])

            for kt in range(NKT):
                p2_sub(kt)
                p2_exp(kt)
                p2_pv(kt)
            if _p4dbg:
                S.op("vector", lambda e: e.tensor_copy(postg[:, :], po[:, :]),
                     waits=[(pe, None), (st, None)], incs=[(dv, 1)])
                S.op("gpsimd", lambda e, h=h, qb=qb: e.dma_start(
                    out=o_po[(h * 4 + qb) * 65:(h * 4 + qb + 1) * 65, :],
                    in_=postg[:, :]),
                    waits=[(dv, None)], incs=[(st, 16)])
            S.op("vector", lambda e: e.reciprocal(rs_row[:, :], po[64:65, :]),
                 waits=[(pe, None)], incs=[(dv, 1)])
            S.op("tensor", lambda e: e.matmul(
                pC[0:64, :], ctx.ones_r[0:1, 0:64], rs_row[:, :],
                start=True, stop=True),
                waits=[(dv, None)], incs=[(pe, 1)])
            S.op("vector", lambda e: e.tensor_copy(bcb[:, :], pC[0:64, :]),
                 waits=[(pe, None)], incs=[(dv, 1)])
            S.op("vector", lambda e, h=h, qb=qb, hw=hw: e.tensor_tensor(
                attn_sb[h // 2][hw:hw + 64, qb * 512:(qb + 1) * 512],
                po[0:64, :], bcb[:, :], ALU.mult),
                waits=[(dv, None)], incs=[(dv, 1)])
        S.emit()

    wout = [arena[:, c * 2048:(c + 1) * 2048].bitcast(BF16) for c in range(8)]
    bout = nc.alloc_sbuf_tensor("bout", [1, D], BF16)
    wst = [arena[:, 16384 + c * 2048:16384 + (c + 1) * 2048].bitcast(BF16)
           for c in range(8)]
    wet = [arena[:, 32768 + c * 2048:32768 + (c + 1) * 2048].bitcast(BF16)
           for c in range(8)]
    wg_sb = arena[:, 49152:49152 + 256].bitcast(BF16)
    b4 = arena[0:1, 49536:49536 + 2048].bitcast(BF16)
    ytile = nc.alloc_sbuf_tensor("ytile", [128, D], F32)
    x3t = nc.alloc_sbuf_tensor("x3t", [128, D], F32)
    sxt = [nc.alloc_sbuf_tensor(f"sxt{i}", [128, D], F32) for i in range(2)]
    sq2 = nc.alloc_sbuf_tensor("sq2", [128, D], F32)
    cst2 = {k: nc.alloc_sbuf_tensor(f"c2_{k}", [128, 1], F32) for k in
            ("c0", "c1", "c2", "c3")}
    sbf = nc.alloc_sbuf_tensor("sbf", [128, D], BF16)
    s4t = [nc.alloc_sbuf_tensor(f"s4t{c}", [128, TOK], BF16) for c in range(8)]
    g7 = nc.alloc_sbuf_tensor("g7", [128, 7], F32)
    g7t = nc.alloc_sbuf_tensor("g7t", [128, 7], F32)
    gred = nc.alloc_sbuf_tensor("gred", [128, 6], F32)
    wscol = nc.alloc_sbuf_tensor("wscol", [128, NT], F32)
    shtmp = nc.alloc_sbuf_tensor("shtmp", [128, 512], F32)
    scr = dict(sq=sq2[:, :], c0=cst2["c0"][:, :], c1=cst2["c1"][:, :],
               c2=cst2["c2"][:, :], c3=cst2["c3"][:, :])

    S = Script(nc)
    if _p4dbg:
        for c in range(8):
            S.op("gpsimd", lambda e, c=c: e.dma_start(
                out=o_attn[c * 128:(c + 1) * 128, :], in_=attn_sb[c][:, :]),
                incs=[(st, 16)])
    for c in range(8):
        S.op("gpsimd", lambda e, c=c: e.dma_start(
            out=wout[c][:, :], in_=wo4d[c * 128:(c + 1) * 128, :]), incs=[(ldg, 16)])
    S.op("gpsimd", lambda e: e.dma_start(out=bout[:, :], in_=bo4d[:, :]),
         incs=[(ldg, 16)])
    for t in range(NT):
        tb = t % 2
        S.op("sync", lambda e, t=t, tb=tb: e.dma_start(
            out=sxt[tb][:, :], in_=x2d[t * 128:(t + 1) * 128, :]),
            waits=[(dv, None)], incs=[(ld, 16)])
        for g in range(2):
            pbuf = p512a if g == 0 else p512b
            pairs = [(attn_sb[c][:, t * 128:(t + 1) * 128],
                      wout[c][:, g * 512:(g + 1) * 512]) for c in range(8)]
            mm_group(S, ctx, pbuf[:, :], pairs,
                     waits_first=[(ldg, None), (ac, None)], stop_last=False)
            S.op("tensor", lambda e, g=g, pbuf=pbuf: e.matmul(
                pbuf[:, :], ctx.ones_rb[:, :], bout[:, g * 512:(g + 1) * 512],
                start=False, stop=True), incs=[(pe, 1)])
            S.op("scalar", lambda e, g=g, pbuf=pbuf: e.activation(
                ytile[:, g * 512:(g + 1) * 512], pbuf[:, :], AF.Copy),
                waits=[(pe, None)], incs=[(ac, 1)])
        emit_ln(ctx, S, ytile[:, :], ytile[:, :], scr, x_waits=[(ac, None)])
        S.op("vector", lambda e, tb=tb: e.tensor_tensor(
            x3t[:, :], sxt[tb][:, :], ytile[:, :], ALU.add),
            waits=[(ld, None), (dv, None), (st, None)], incs=[(dv, 1)])
        S.op("gpsimd", lambda e, t=t: e.dma_start(
            out=x3_scr[t * 128:(t + 1) * 128, :], in_=x3t[:, :]),
            waits=[(dv, None)], incs=[(st, 16)])
        emit_ln(ctx, S, x3t[:, :], sbf[:, :], scr, x_waits=[(dv, None)])
        for c in range(8):
            emit_tp(ctx, S, sbf[:, c * 128:(c + 1) * 128],
                    s4t[c][:, t * 128:(t + 1) * 128], ptpb[:, :])
    S.emit()

    S = Script(nc)
    for c in range(8):
        S.op("gpsimd", lambda e, c=c: e.dma_start(
            out=wst[c][:, :], in_=wstd[c * 128:(c + 1) * 128, :]), incs=[(ldg, 16)])
        S.op("gpsimd", lambda e, c=c: e.dma_start(
            out=wet[c][:, :], in_=wetd[c * 128:(c + 1) * 128, :]), incs=[(ldg, 16)])
        S.op("gpsimd", lambda e, c=c: e.dma_start(
            out=wg_sb[:, c * 16:c * 16 + 7],
            in_=wgd[c * 128:(c + 1) * 128, :]), incs=[(ldg, 16)])
    S.op("gpsimd", lambda e: e.dma_start(out=b4[:, :], in_=bias4d[:, :]),
         incs=[(ldg, 16)])
    for t in range(NT):
        pairs = [(s4t[kc][:, t * 128:(t + 1) * 128], wg_sb[:, kc * 16:kc * 16 + 7])
                 for kc in range(8)]
        mm_group(S, ctx, p512c[:, 0:7], pairs,
                 waits_first=[(ldg, None), (dv, None)])
        S.op("vector", lambda e: e.reduce_max(gred[:, 0:1], p512c[:, 0:7],
                                              axis=AX.X, negate=True),
             waits=[(pe, None)], incs=[(dv, 1)])
        S.op("scalar", lambda e: e.activation(
            g7[:, :], p512c[:, 0:7], AF.Exp, bias=gred[:, 0:1],
            accum_out=gred[:, 1:2]),
            waits=[(dv, None)], incs=[(ac, 1)])
        S.op("vector", lambda e: e.reciprocal(gred[:, 2:3], gred[:, 1:2]),
             waits=[(ac, None)], incs=[(dv, 1)])
        S.op("vector", lambda e: e.tensor_scalar_mul(g7[:, :], g7[:, :],
                                                     gred[:, 2:3]),
             waits=[(dv, None)], incs=[(dv, 1)])
        S.op("vector", lambda e: e.reduce_max(gred[:, 2:3], g7[:, :], axis=AX.X),
             waits=[(dv, None)], incs=[(dv, 1)])
        S.op("vector", lambda e: e.tensor_scalar(g7t[:, :], g7[:, :],
                                                 gred[:, 2:3], None, ALU.is_equal),
             waits=[(dv, None)], incs=[(dv, 1)])
        S.op("vector", lambda e: e.scalar_tensor_tensor(
            g7t[:, :], g7t[:, :], -1e30, g7[:, :], ALU.mult, ALU.add),
            waits=[(dv, None)], incs=[(dv, 1)])
        S.op("vector", lambda e: e.reduce_max(gred[:, 3:4], g7t[:, :], axis=AX.X),
             waits=[(dv, None)], incs=[(dv, 1)])
        S.op("vector", lambda e: e.tensor_tensor(gred[:, 4:5], gred[:, 2:3],
                                                 gred[:, 3:4], ALU.add),
             waits=[(dv, None)], incs=[(dv, 1)])
        S.op("vector", lambda e: e.reciprocal(gred[:, 5:6], gred[:, 4:5]),
             waits=[(dv, None)], incs=[(dv, 1)])
        S.op("vector", lambda e: e.tensor_tensor(gred[:, 4:5], gred[:, 2:3],
                                                 gred[:, 5:6], ALU.mult),
             waits=[(dv, None)], incs=[(dv, 1)])
        S.op("vector", lambda e: e.tensor_tensor(gred[:, 5:6], gred[:, 3:4],
                                                 gred[:, 5:6], ALU.mult),
             waits=[(dv, None)], incs=[(dv, 1)])
        S.op("vector", lambda e, t=t: e.tensor_tensor(
            wscol[:, t:t + 1], gred[:, 4:5], gred[:, 5:6], ALU.add),
            waits=[(dv, None)], incs=[(dv, 1)])
    for t in range(NT):
        tb = t % 2
        S.op("sync", lambda e, t=t, tb=tb: e.dma_start(
            out=sxt[tb][:, :], in_=x3_scr[t * 128:(t + 1) * 128, :]),
            waits=[(dv, None)], incs=[(ld, 16)])
        for g in range(2):
            pe_buf = p512a if g == 0 else p512b
            ps_buf = p512c if g == 0 else p512d
            pairs_e = [(s4t[kc][:, t * 128:(t + 1) * 128],
                        wet[kc][:, g * 512:(g + 1) * 512]) for kc in range(8)]
            mm_group(S, ctx, pe_buf[:, :], pairs_e,
                     waits_first=[(ldg, None), (dv, None)])
            pairs_s = [(s4t[kc][:, t * 128:(t + 1) * 128],
                        wst[kc][:, g * 512:(g + 1) * 512]) for kc in range(8)]
            mm_group(S, ctx, ps_buf[:, :], pairs_s,
                     waits_first=[(ac, None)], stop_last=False)
            S.op("tensor", lambda e, g=g, ps_buf=ps_buf: e.matmul(
                ps_buf[:, :], ctx.ones_rb[:, :], b4[:, g * 512:(g + 1) * 512],
                start=False, stop=True), incs=[(pe, 1)])
            S.op("scalar", lambda e, g=g, ps_buf=ps_buf: e.activation(
                shtmp[:, :], ps_buf[:, :], AF.Copy),
                waits=[(pe, None), (dv, None)], incs=[(ac, 1)])
            S.op("vector", lambda e, g=g, t=t, pe_buf=pe_buf: e.scalar_tensor_tensor(
                ytile[:, g * 512:(g + 1) * 512], pe_buf[:, :],
                wscol[:, t:t + 1], shtmp[:, :], ALU.mult, ALU.add),
                waits=[(ac, None)], incs=[(dv, 1)])
        emit_ln(ctx, S, ytile[:, :], ytile[:, :], scr, x_waits=[(dv, None)])
        S.op("vector", lambda e, tb=tb: e.tensor_tensor(
            x3t[:, :], sxt[tb][:, :], ytile[:, :], ALU.add),
            waits=[(ld, None), (dv, None), (st, None)], incs=[(dv, 1)])
        S.op("gpsimd", lambda e, t=t: e.dma_start(
            out=o_x[t * 128:(t + 1) * 128, :], in_=x3t[:, :]),
            waits=[(dv, None)], incs=[(st, 16)])
    S.op("gpsimd", None, waits=[(st, None), (ldg, None)])
    S.emit()
    return nc, ctx

RG = [[0, 1], [2, 3], [4, 5], [6, 7]]
NH = 8
HC = NH * HD


class Arena:
    def __init__(self, nc, nbytes=184 * 1024):
        self.t = nc.alloc_sbuf_tensor("ar_back", [128, nbytes], mybir.dt.uint8)
        self.nb = nbytes
        self.off = 0

    def alloc(self, cols, dtype):
        sz = cols * mybir.dt.size(dtype)
        sz32 = (sz + 31) & ~31
        assert self.off + sz32 <= self.nb, (self.off, sz32, self.nb)
        ap = self.t[:, self.off:self.off + sz].bitcast(dtype)
        self.off += sz32
        return ap

    def at(self, off, cols, dtype):
        return self.t[:, off:off + cols * mybir.dt.size(dtype)].bitcast(dtype)

    def reset(self):
        self.off = 0


def build_fused():
    nc = bass.Bass("TRN2", target_bir_lowering=False)
    ctx = Ctx(nc)
    cc = Sem(nc, "s_cc")
    ld, ldg, st, pe, dv, ac = ctx.ld, ctx.ldg, ctx.st, ctx.pe, ctx.dv, ctx.ac
    ar = Arena(nc)

    EI = lambda n, s, d=F32: nc.dram_tensor(n, s, d, kind="ExternalInput")
    xs = EI("xs", [TOK, D])
    w_qkv = EI("w_qkv", [D, 3 * D])
    w_kv = EI("w_kv", [D, 2 * D])
    ropec = EI("ropec", [TOK, D])
    ropes = EI("ropes", [TOK, D])
    ropecq = EI("ropecq", [NSEQ, HD])
    ropesq = EI("ropesq", [NSEQ, HD])
    woutd = EI("w_out", [D, D]); boutd = EI("b_out", [1, D])
    wq2d = EI("wq2", [D, D]); wk2d = EI("wk2", [D, D]); wv2d = EI("wv2", [D, D])
    bq2d = EI("bq2c", [D, 1]); bk2d = EI("bk2c", [D, 1]); bv2d = EI("bv2", [1, D])
    wo2d = EI("wo2", [D, D]); bo2d = EI("bo2", [1, D])
    gwd = EI("gW", [D, 8]); gbd = EI("gb", [1, 8])
    w1d = EI("W1", [8, D, 2048]); b1td = EI("b1T", [2048, 8])
    w2d = EI("W2", [8, 2048, D]); b2d = EI("b2", [8, D])
    ln1gd = EI("ln1g", [1, D]); ln1bd = EI("ln1b", [1, D])
    ln2gd = EI("ln2g", [1, D]); ln2bd = EI("ln2b", [1, D])
    wq3d = EI("wq3", [D, D]); wk3d = EI("wk3", [D, D]); wv3d = EI("wv3", [D, D])
    bq3d = EI("bq3c", [D, 1]); bk3d = EI("bk3c", [D, 1]); bv3d = EI("bv3", [1, D])
    wo4d = EI("wo4", [D, D]); bo4d = EI("bo4", [1, D])
    alphad = EI("alpha", [1, 1])
    wstd = EI("WsT", [D, D]); wetd = EI("WeT", [D, D])
    wgd = EI("Wg", [D, 7]); bias4d = EI("bias4", [1, D])
    flagsd = EI("flags", [128, 2])
    o_x = nc.dram_tensor("xout", [TOK, D], F32, kind="ExternalOutput")

    IN = lambda n, s, d: nc.dram_tensor(n, s, d, kind="Internal")
    d_qT_c = IN("d_qT_c", [D, TOK], BF16)
    d_kr_c = IN("d_kr_c", [D, TOK], BF16)
    d_kk_c = IN("d_kk_c", [TOK, D], BF16)
    d_vk_c = IN("d_vk_c", [TOK, D], BF16)
    d_v_c = IN("d_v_c", [TOK, D], BF16)
    d_qT = IN("d_qT", [2 * D, TOK], BF16)
    d_kr = IN("d_kr", [2 * D, TOK], BF16)
    d_kk = IN("d_kk", [NSEQ, D], BF16)
    d_vk = IN("d_vk", [NSEQ, D], BF16)
    d_v = IN("d_v", [NSEQ, D], BF16)
    d_at_c = IN("d_at_c", [HC, NSEQ], BF16)
    d_at = IN("d_at", [2 * HC, NSEQ], BF16)
    d_kt2_c = IN("d_kt2_c", [D, TOK], BF16)
    d_v2_c = IN("d_v2_c", [TOK, D], BF16)
    d_kt2 = IN("d_kt2", [2 * D, TOK], BF16)
    d_v2 = IN("d_v2", [NSEQ, D], BF16)
    d_kt3_c = IN("d_kt3_c", [D, TOK], BF16)
    d_v3_c = IN("d_v3_c", [TOK, D], BF16)
    d_knn_c = IN("d_knn_c", [H, TOK], F32)
    d_kt3 = IN("d_kt3", [2 * D, TOK], BF16)
    d_v3 = IN("d_v3", [NSEQ, D], BF16)
    d_knn = IN("d_knn", [2 * H, TOK], F32)
    d_x1 = IN("d_x1", [TOK, D], F32)
    d_qT2 = IN("d_qT2", [D, TOK], BF16)
    d_x1a = IN("d_x1a", [TOK, D], F32)
    d_xaT = IN("d_xaT", [D, TOK], BF16)
    d_x2 = IN("d_x2", [TOK, D], F32)
    d_qT3 = IN("d_qT3", [D, TOK], BF16)
    d_qnn = IN("d_qnn", [H, TOK], F32)
    d_x3 = IN("d_x3", [TOK, D], F32)

    tri_a, trit_a, lowt_a, low01_a, id8_a = mask_arrays()
    tri_d = nc.inline_tensor(tri_a, name="tri_d")
    trit_d = nc.inline_tensor(trit_a, name="trit_d")
    lowt_d = nc.inline_tensor(lowt_a, name="lowt_d")
    low01_d = nc.inline_tensor(low01_a, name="low01_d")
    id8_d = nc.inline_tensor(id8_a, name="id8_d")
    tri = nc.alloc_sbuf_tensor("tri", [128, 128], F32)
    trit = nc.alloc_sbuf_tensor("trit", [128, 128], F32)
    lowt = nc.alloc_sbuf_tensor("lowt", [128, 128], F32)
    low01 = nc.alloc_sbuf_tensor("low01", [128, 128], F32)
    id8 = nc.alloc_sbuf_tensor("id8", [64, 64], F32)
    lnsq = nc.alloc_sbuf_tensor("lnsq", [128, D], F32)
    cst = {k: nc.alloc_sbuf_tensor(f"cg_{k}", [128, 1], F32) for k in
           ("c0", "c1", "c2", "c3")}
    scr = dict(sq=lnsq[:, :], c0=cst["c0"][:, :], c1=cst["c1"][:, :],
               c2=cst["c2"][:, :], c3=cst["c3"][:, :])
    p512a = nc.alloc_psum_tensor("p512a", [128, 512], F32)
    p512b = nc.alloc_psum_tensor("p512b", [128, 512], F32)
    p512c = nc.alloc_psum_tensor("p512c", [128, 512], F32)
    p512d = nc.alloc_psum_tensor("p512d", [128, 512], F32)
    po = nc.alloc_psum_tensor("po", [65, 512], F32)
    pb = nc.alloc_psum_tensor("pbx", [128, 512], F32)
    ptpb = nc.alloc_psum_tensor("ptpb", [128, 1024], BF16)[0:128, 0:128]
    ptpf = nc.alloc_psum_tensor("ptpf", [128, 512], F32)[0:128, 0:128]

    flg = nc.alloc_sbuf_tensor("flg", [128, 2], F32)
    S = Script(nc)
    ctx.emit_init(S, (EPS, 1.0, -0.5))
    for dst, srcd in ((tri, tri_d), (trit, trit_d), (lowt, lowt_d),
                      (low01, low01_d), (id8, id8_d)):
        S.op("sync", lambda e, dst=dst, srcd=srcd: e.dma_start(
            out=dst[:, :], in_=srcd.ap()), incs=[(ld, 16)])
    S.op("sync", lambda e: e.dma_start(out=flg[:, :], in_=flagsd[:, :]),
         incs=[(ld, 16)])
    S.emit()

    sx = [ar.alloc(D, F32) for _ in range(2)]
    xln = ar.alloc(D, BF16)
    xlt = [ar.alloc(TOK, BF16) for _ in range(8)]
    wfm = [ar.alloc(8 * 128, BF16) for _ in range(2)]
    wtm = [ar.alloc(8 * 512, BF16) for _ in range(2)]
    ropci = [ar.alloc(512, F32) for _ in range(2)]
    ropsi = [ar.alloc(512, F32) for _ in range(2)]
    kf32 = ar.alloc(512, F32)
    tmp = ar.alloc(512, F32)
    tmp2 = ar.alloc(512, F32)
    stage = [ar.alloc(512, BF16) for _ in range(4)]

    S = Script(nc)
    for t in range(NT):
        b = t % 2
        S.op("sync", lambda e, t=t, b=b: e.dma_start(
            out=sx[b][:, :], in_=xs[t * 128:(t + 1) * 128, :]),
            waits=[(dv, None)], incs=[(ld, 16)])
        emit_ln(ctx, S, sx[b][:, :], xln[:, :], scr, x_waits=[(ld, None)])
        for c in range(8):
            emit_tp(ctx, S, xln[:, c * 128:(c + 1) * 128],
                    xlt[c][:, t * 128:(t + 1) * 128], ptpb)
    S.emit()

    S = Script(nc)

    def fm_proj(wdram, colbase, dst_dram, bias_cols=None, scale=None):
        for oc in range(8):
            b = oc % 2
            for kc in range(8):
                S.op("gpsimd", lambda e, kc=kc, oc=oc, b=b, wdram=wdram,
                     colbase=colbase: e.dma_start(
                         out=wfm[b][:, kc * 128:(kc + 1) * 128],
                         in_=wdram[kc * 128:(kc + 1) * 128,
                                   colbase + oc * 128:colbase + oc * 128 + 128]),
                     waits=([(pe, None)] if kc == 0 else []), incs=[(ldg, 16)])
            for tb in range(4):
                pbuf = p512a if tb % 2 == 0 else p512b
                pairs = [(wfm[b][:, kc * 128:(kc + 1) * 128],
                          xlt[kc][:, tb * 512:(tb + 1) * 512]) for kc in range(8)]
                mm_group(S, ctx, pbuf[:, :], pairs,
                         waits_first=[(ldg, None), (ac, None), (dv, None)])
                if bias_cols is not None:
                    S.op("vector", lambda e, pbuf=pbuf, tb=tb, oc=oc: e.tensor_scalar(
                        stage[tb][:, :], pbuf[:, :], bias_cols[:, 8 * oc:8 * oc + 1],
                        None, ALU.add),
                        waits=[(pe, None), (st, None), (ld, None)], incs=[(dv, 1)])
                elif scale is not None:
                    S.op("vector", lambda e, pbuf=pbuf, tb=tb: e.tensor_scalar(
                        stage[tb][:, :], pbuf[:, :], scale, None, ALU.mult),
                        waits=[(pe, None), (st, None)], incs=[(dv, 1)])
                else:
                    S.op("vector", lambda e, pbuf=pbuf, tb=tb: e.tensor_copy(
                        stage[tb][:, :], pbuf[:, :]),
                        waits=[(pe, None), (st, None)], incs=[(dv, 1)])
                S.op("gpsimd", lambda e, oc=oc, tb=tb, dst_dram=dst_dram:
                     e.dma_start(
                         out=dst_dram[oc * 128:(oc + 1) * 128,
                                      tb * 512:(tb + 1) * 512],
                         in_=stage[tb][:, :]),
                     waits=[(dv, None)], incs=[(st, 16)])

    fm_proj(w_qkv, 0, d_qT_c)
    S.emit()

    S = Script(nc)
    ntp = [0]
    krt_half = [ar.alloc(TOK, BF16) for _ in range(8)]

    def tm_pass(wdram, col_base, evac):
        for g in range(2):
            wb = wtm[g % 2]
            for kc in range(8):
                S.op("gpsimd", lambda e, kc=kc, wdram=wdram, col0=col_base + g * 512,
                     wb=wb: e.dma_start(
                         out=wb[:, kc * 512:(kc + 1) * 512],
                         in_=wdram[kc * 128:(kc + 1) * 128, col0:col0 + 512]),
                     waits=([(pe, None), (ac, None)] if kc == 0 else []),
                     incs=[(ldg, 16)])
            for t in range(NT):
                pbuf = p512a if t % 2 == 0 else p512b
                pairs = [(xlt[kc][:, t * 128:(t + 1) * 128],
                          wb[:, kc * 512:(kc + 1) * 512]) for kc in range(8)]
                mm_group(S, ctx, pbuf[:, :], pairs,
                         waits_first=[(ldg, None), (ac, None), (dv, None)])
                evac(t, g, pbuf)

    def evac_k(t, g, psum):
        S.op("sync", lambda e, t=t, g=g: e.dma_start(
            out=ropci[t % 2][:, :], in_=ropec[t * 128:(t + 1) * 128,
                                             g * 512:(g + 1) * 512]),
            waits=[(dv, None)], incs=[(ld, 16)])
        S.op("sync", lambda e, t=t, g=g: e.dma_start(
            out=ropsi[t % 2][:, :], in_=ropes[t * 128:(t + 1) * 128,
                                             g * 512:(g + 1) * 512]),
            incs=[(ld, 16)])
        S.op("scalar", lambda e, psum=psum: e.activation(kf32[:, :], psum[:, :],
                                                         AF.Copy),
             waits=[(pe, None), (dv, None)], incs=[(ac, 1)])
        emit_rotary(ctx, S, kf32[:, :], ropci[t % 2][:, :], ropsi[t % 2][:, :],
                    tmp[:, :], tmp2[:, :], stage[0][:, :],
                    waits=[(ac, None), (ld, None)])
        for j in range(4):
            emit_tp(ctx, S, stage[0][:, j * 128:(j + 1) * 128],
                    krt_half[g * 4 + j][:, t * 128:(t + 1) * 128], ptpb)

    def mk_evac_tm(dst_dram):
        def evac(t, g, psum):
            sb = stage[2 + (t % 2)]
            S.op("scalar", lambda e, psum=psum, sb=sb: e.activation(
                sb[:, :], psum[:, :], AF.Copy),
                waits=[(pe, None), (st, None)], incs=[(ac, 1)])
            S.op("gpsimd", lambda e, t=t, g=g, sb=sb, dst_dram=dst_dram:
                 e.dma_start(
                     out=dst_dram[t * 128:(t + 1) * 128,
                                  g * 512:(g + 1) * 512],
                     in_=sb[:, :]),
                 waits=[(ac, None)], incs=[(st, 16)])
        return evac

    for g2 in range(2):
        wb = wtm[g2 % 2]
        for kc in range(8):
            S.op("gpsimd", lambda e, kc=kc, g2=g2, wb=wb: e.dma_start(
                out=wb[:, kc * 512:(kc + 1) * 512],
                in_=w_qkv[kc * 128:(kc + 1) * 128,
                          D + g2 * 512:D + g2 * 512 + 512]),
                waits=([(pe, None), (ac, None)] if kc == 0 else []),
                incs=[(ldg, 16)])
        for t in range(NT):
            pbuf = p512a if t % 2 == 0 else p512b
            pairs = [(xlt[kc][:, t * 128:(t + 1) * 128],
                      wb[:, kc * 512:(kc + 1) * 512]) for kc in range(8)]
            mm_group(S, ctx, pbuf[:, :], pairs,
                     waits_first=[(ldg, None), (ac, None), (dv, None)])
            evac_k(t, g2, pbuf)
        for j in range(4):
            S.op("gpsimd", lambda e, j=j, g2=g2: e.dma_start(
                out=d_kr_c[g2 * HC + j * 128:g2 * HC + (j + 1) * 128, :],
                in_=krt_half[g2 * 4 + j][:, :]),
                waits=[(dv, None)], incs=[(st, 16)])
    tm_pass(w_qkv, 2 * D, mk_evac_tm(d_v_c))
    tm_pass(w_kv, 0, mk_evac_tm(d_kk_c))
    tm_pass(w_kv, D, mk_evac_tm(d_vk_c))
    for (i_ap, o_ap) in ((d_qT_c, d_qT), (d_kr_c, d_kr), (d_kk_c, d_kk),
                         (d_vk_c, d_vk), (d_v_c, d_v)):
        S.op("gpsimd", lambda e, i_ap=i_ap, o_ap=o_ap: e.collective_compute(
            "AllGather", ALU.bypass, replica_groups=RG,
            ins=[i_ap.ap()], outs=[o_ap.ap()]),
            waits=[(st, None), (ldg, None)], incs=[(cc, 1)])
    S.emit()
    return nc, ctx, dict(ar=ar, cc=cc, scr=scr, S=None, flg=flg,
                         tri=tri, trit=trit, lowt=lowt, low01=low01, id8=id8,
                         psum=(p512a, p512b, p512c, p512d, po, pb, ptpb, ptpf),
                         dram=dict(
                             xs=xs, ropecq=ropecq, ropesq=ropesq, woutd=woutd,
                             boutd=boutd, wq2d=wq2d, wk2d=wk2d, wv2d=wv2d,
                             bq2d=bq2d, bk2d=bk2d, bv2d=bv2d, wo2d=wo2d,
                             bo2d=bo2d, gwd=gwd, gbd=gbd, w1d=w1d, b1td=b1td,
                             w2d=w2d, b2d=b2d, ln1gd=ln1gd, ln1bd=ln1bd,
                             ln2gd=ln2gd, ln2bd=ln2bd, wq3d=wq3d, wk3d=wk3d,
                             wv3d=wv3d, bq3d=bq3d, bk3d=bk3d, bv3d=bv3d,
                             wo4d=wo4d, bo4d=bo4d, alphad=alphad, wstd=wstd,
                             wetd=wetd, wgd=wgd, bias4d=bias4d, o_x=o_x,
                             d_qT=d_qT, d_kr=d_kr, d_kk=d_kk, d_vk=d_vk,
                             d_v=d_v, d_at_c=d_at_c, d_at=d_at,
                             d_kt2_c=d_kt2_c, d_v2_c=d_v2_c, d_kt2=d_kt2,
                             d_v2=d_v2, d_kt3_c=d_kt3_c, d_v3_c=d_v3_c,
                             d_knn_c=d_knn_c, d_kt3=d_kt3, d_v3=d_v3,
                             d_knn=d_knn, d_x1=d_x1, d_qT2=d_qT2, d_x1a=d_x1a,
                             d_xaT=d_xaT, d_x2=d_x2, d_qT3=d_qT3, d_qnn=d_qnn,
                             d_x3=d_x3))


def phase1_l0a(nc, ctx, cc, ar, dd, masks, psum, flg):
    ld, ldg, st, pe, dv, ac = ctx.ld, ctx.ldg, ctx.st, ctx.pe, ctx.dv, ctx.ac
    (tri, trit, lowt, low01, id8) = masks
    (p512a, p512b, p512c, p512d, po, pb, ptpb, ptpf) = psum
    d_qT, d_kr, d_kk, d_vk, d_v = dd["d_qT"], dd["d_kr"], dd["d_kk"], dd["d_vk"], dd["d_v"]
    ropecq, ropesq = dd["ropecq"], dd["ropesq"]

    ar.reset()
    qTh = ar.alloc(NSEQ, BF16)[0:64, :]
    qhat = ar.alloc(NSEQ, BF16)
    khat = ar.alloc(NSEQ, BF16)
    vhat = ar.alloc(NKT * 65, BF16)
    kkh = ar.alloc(NKT * 64, BF16)
    vkh = ar.alloc(NKT * 64, BF16)
    kkTh = ar.alloc(NSEQ, BF16)[0:64, :]
    vkTh = ar.alloc(NSEQ, BF16)[0:64, :]
    sct = ar.alloc(NKT * 64, F32)[0:64, :]
    sctb = ar.alloc(NKT * 64, BF16)[0:64, :]
    run = ar.alloc(64, F32)[0:64, :]
    ssqcols = ar.alloc(NKT, F32)[0:64, :]
    ssqrow = ar.alloc(NKT, F32)[0:1, :]
    ssqb = ar.alloc(NKT, F32)
    sq64 = ar.alloc(64, F32)[0:64, :]
    sqst = ar.alloc(NKT * 64, F32)[0:64, :]
    g1 = ar.alloc(128, F32)
    gp = ar.alloc(128, F32)
    ucol = ar.alloc(8, F32)
    uprod = ar.alloc(64, F32)
    stdc = ar.alloc(3, F32)
    qpair = ar.alloc(128, F32)
    rot_t = ar.alloc(128, F32)
    rot_t2 = ar.alloc(128, F32)
    qr8 = ar.alloc(128, BF16)
    ropcs = [ar.alloc(128, F32) for _ in range(2)]
    ropss = [ar.alloc(128, F32) for _ in range(2)]
    mxr = ar.alloc(9, F32)
    negc = ar.alloc(1, F32)
    ptile = [ar.alloc(512, BF16) for _ in range(2)]
    atb = ar.alloc(128, BF16)
    sqall = ar.alloc(2048, F32)
    ukk = ar.alloc(32, F32)
    uvk = ar.alloc(32, F32)
    rs_row = ar.alloc(512, F32)[0:1, :]
    bc = ar.alloc(512, F32)[0:64, :]
    attn_sb = [ar.alloc(NSEQ, BF16) for _ in range(4)]
    stgp = [(ar.alloc(TOK, BF16), ar.alloc(TOK, BF16)) for _ in range(2)]
    i128 = np.arange(128)
    trit_blk = np.where(i128[:, None] <= i128[None, :], 0.0, NEG
                        ).astype(np.float32)
    m4_d = []
    for r in range(4):
        blocks = []
        for j in range(4):
            if j < r:
                blocks.append(np.full((128, 128), NEG, np.float32))
            elif j == r:
                blocks.append(trit_blk)
            else:
                blocks.append(np.zeros((128, 128), np.float32))
        m4_d.append(nc.inline_tensor(np.concatenate(blocks, 1),
                                     name=f"m4_{r}"))
    m4 = [ar.alloc(512, F32) for _ in range(4)]

    for i in range(NH):
        S = Script(nc)
        hw = 64 * (i % 2)
        if i == 0:
            S.op("sync", None, waits=[(cc, None)])
            for r in range(4):
                S.op("sync", lambda e, r=r: e.dma_start(
                    out=m4[r][:, :], in_=m4_d[r].ap()), incs=[(ld, 16)])
            S.op("vector", lambda e: e.memset(
                vhat[:, :].rearrange("p (kt c) -> p kt c", c=65)[:, :, 64:65],
                1.0), incs=[(dv, 1)])
            S.op("vector", lambda e: e.memset(khat[64:65, :], 1.0),
                 incs=[(dv, 1)])
        np_ = [0]

        def ld_pair(ops, blends):
            qi = np_[0] % 2
            eng, sem = ("sync", ld) if qi == 0 else ("gpsimd", ldg)
            sA, sB = stgp[qi]
            np_[0] += 1
            for k, mk in enumerate(ops):
                S.op(eng, mk(sA if k == 0 else sB),
                     waits=([(dv, None)] if k == 0 else []), incs=[(sem, 16)])
            dst, sAv, sBv, P = blends(sA, sB)
            S.op("vector", lambda e, dst=dst, sAv=sAv, P=P: e.tensor_scalar_mul(
                dst, sAv, flg[0:P, 0:1]),
                waits=[(sem, None), (pe, None), (dv, None)], incs=[(dv, 1)])
            S.op("vector", lambda e, dst=dst, sBv=sBv, P=P: e.scalar_tensor_tensor(
                dst, sBv, flg[0:P, 1:2], dst, ALU.mult, ALU.add),
                waits=[(dv, None)], incs=[(dv, 1)])

        for blk_ in range(2):
            for (dst, dsrc) in ((qTh[:, blk_ * TOK:(blk_ + 1) * TOK], d_qT),
                                (khat[0:64, blk_ * TOK:(blk_ + 1) * TOK], d_kr)):
                def mk(off):
                    def m(s_):
                        return lambda e, s_=s_, dsrc=dsrc, blk_=blk_, i=i, off=off: \
                            e.dma_start(out=s_[0:64, :],
                                        in_=dsrc[blk_ * D + off + i * 64:
                                                 blk_ * D + off + i * 64 + 64, :])
                    return m
                ld_pair([mk(0), mk(HC)],
                        lambda sA, sB, dst=dst: (dst, sA[0:64, :], sB[0:64, :], 64))
        for (dstv, dsrc) in (
                (kkh[:, :].rearrange("p (kt c) -> p kt c", c=64), d_kk),
                (vkh[:, :].rearrange("p (kt c) -> p kt c", c=64), d_vk),
                (vhat[:, :].rearrange("p (kt c) -> p kt c", c=65)[:, :, 0:64],
                 d_v)):
            def mk(off):
                def m(s_):
                    return lambda e, s_=s_, dsrc=dsrc, off=off, i=i: e.dma_start(
                        out=s_[:, :].rearrange("p (kt c) -> p kt c", c=64),
                        in_=dsrc.ap().rearrange("(kt p) c -> p kt c", p=128)
                        [:, :, off + i * 64:off + (i + 1) * 64])
                return m
            ld_pair([mk(0), mk(HC)],
                    lambda sA, sB, dstv=dstv: (
                        dstv, sA[:, :].rearrange("p (kt c) -> p kt c", c=64),
                        sB[:, :].rearrange("p (kt c) -> p kt c", c=64), 128))
        S.op("vector", lambda e: e.tensor_copy(run[:, :], id8[:, :]),
             waits=[(ld, None)], incs=[(dv, 1)])
        for gc in range(NKT):
            S.op("vector", lambda e, gc=gc: e.tensor_copy(
                sct[:, gc * 64:(gc + 1) * 64], run[:, :]),
                waits=[(dv, None)], incs=[(dv, 1)])
            if gc < NKT - 1:
                S.op("tensor", lambda e, gc=gc: e.matmul(
                    po[0:64, 0:64], vkh[:, gc * 64:(gc + 1) * 64],
                    kkh[:, gc * 64:(gc + 1) * 64], start=True, stop=True),
                    waits=[(dv, None), (ld, None)], incs=[(pe, 1)])
                S.op("vector", lambda e: e.tensor_tensor(
                    run[:, :], run[:, :], po[0:64, 0:64], ALU.add),
                    waits=[(pe, None), (dv, None)], incs=[(dv, 1)])
        S.op("vector", lambda e: e.tensor_copy(sctb[:, :], sct[:, :]),
             waits=[(dv, None)], incs=[(dv, 1)])
        S.op("scalar", lambda e: e.activation(sqst[:, :], sct[:, :], AF.Square),
             waits=[(dv, None), (ac, None)], incs=[(ac, 1)])
        for gc in range(NKT):
            S.op("vector", lambda e, gc=gc: e.reduce_sum(
                ssqcols[:, gc:gc + 1], sqst[:, gc * 64:(gc + 1) * 64],
                axis=AX.X),
                waits=[(ac, None)], incs=[(dv, 1)])
        S.op("tensor", lambda e: e.matmul(
            pb[0:1, 0:NKT], ctx.constcol(1.0)[0:64, :], ssqcols[:, :],
            start=True, stop=True),
            waits=[(dv, None)], incs=[(pe, 1)])
        S.op("vector", lambda e: e.tensor_copy(ssqrow[:, :], pb[0:1, 0:NKT]),
             waits=[(pe, None)], incs=[(dv, 1)])
        S.op("tensor", lambda e: e.matmul(
            pb[0:128, 64:64 + NKT], ctx.ones_r[0:1, :], ssqrow[:, :],
            start=True, stop=True),
            waits=[(dv, None)], incs=[(pe, 1)])
        S.op("vector", lambda e: e.tensor_copy(ssqb[:, :], pb[:, 64:64 + NKT]),
             waits=[(pe, None)], incs=[(dv, 1)])
        for gc in range(0, NKT, 2):
            for (srcb, dstb) in ((kkh, kkTh), (vkh, vkTh)):
                S.op("tensor", lambda e, gc=gc, srcb=srcb: e.transpose(
                    ptpb[0:128, 0:128], srcb[:, gc * 64:(gc + 2) * 64],
                    ctx.idb[:, :]),
                    waits=[(ld, None), (dv, None)], incs=[(pe, 1)])
                S.op("vector", lambda e, gc=gc, dstb=dstb: e.tensor_copy(
                    dstb[:, gc * 128:(gc + 1) * 128], ptpb[0:64, 0:128]),
                    waits=[(pe, None)], incs=[(dv, 1)])
                S.op("vector", lambda e, gc=gc, dstb=dstb: e.tensor_copy(
                    dstb[:, (gc + 1) * 128:(gc + 2) * 128], ptpb[64:128, 0:128]),
                    waits=[(dv, None)], incs=[(dv, 1)])
        S.emit()

        S = Script(nc)
        for (srcb, ucols) in ((kkh, ukk), (vkh, uvk)):
            S.op("scalar", lambda e, srcb=srcb: e.activation(
                sqall[:, :], srcb[:, :], AF.Square),
                waits=[(ac, None), (dv, None)], incs=[(ac, 1)])
            for gc in range(NKT):
                S.op("vector", lambda e, gc=gc, ucols=ucols: e.reduce_sum(
                    ucols[:, gc:gc + 1], sqall[:, gc * 64:(gc + 1) * 64],
                    axis=AX.X),
                    waits=[(ac, None)], incs=[(dv, 1)])
        for t in range(NKT):
            if t % 2 == 0:
                pb2 = (t // 2) % 2
                for (dst, srcd) in ((ropcs[pb2], ropecq), (ropss[pb2], ropesq)):
                    for j in range(2):
                        S.op("sync", lambda e, dst=dst, srcd=srcd, j=j, t=t:
                             e.dma_start(
                                 out=dst[:, j * 64:(j + 1) * 64],
                                 in_=srcd[(t + j) * 128:(t + j + 1) * 128, :]),
                             waits=([(dv, None)] if dst is ropcs[pb2] and j == 0
                                    else []),
                             incs=[(ld, 16)])
            S.op("tensor", lambda e, t=t: e.matmul(
                p512c[:, 0:128], vkTh[:, t * 128:(t + 1) * 128],
                qTh[:, t * 128:(t + 1) * 128], start=True, stop=True),
                waits=[(dv, None)], incs=[(pe, 1)])
            S.op("vector", lambda e: e.tensor_tensor(
                atb[:, :], p512c[:, 0:128], lowt[:, :], ALU.mult),
                waits=[(pe, None)], incs=[(dv, 1)])
            S.op("tensor", lambda e, t=t: e.matmul(
                p512d[:, 0:64], atb[:, :], kkh[:, t * 64:(t + 1) * 64],
                start=True, stop=False),
                waits=[(dv, None)], incs=[(pe, 1)])
            S.op("tensor", lambda e, t=t: e.matmul(
                p512d[:, 0:64], qTh[:, t * 128:(t + 1) * 128],
                sctb[:, t * 64:(t + 1) * 64], start=False, stop=True),
                incs=[(pe, 1)])
            S.op("tensor", lambda e, t=t: e.matmul(
                p512d[:, 64:128], vkTh[:, t * 128:(t + 1) * 128],
                sctb[:, t * 64:(t + 1) * 64], start=True, stop=True),
                incs=[(pe, 1)])
            S.op("vector", lambda e, t=t: e.tensor_tensor(
                uprod[:, :], p512d[:, 64:128], kkh[:, t * 64:(t + 1) * 64],
                ALU.mult),
                waits=[(pe, None)], incs=[(dv, 1)])
            S.op("vector", lambda e: e.reduce_sum(ucol[:, 0:1], uprod[:, :],
                                                  axis=AX.X),
                 waits=[(dv, None)], incs=[(dv, 1)])
            S.op("tensor", lambda e, t=t: e.matmul(
                p512c[:, 128:256], kkTh[:, t * 128:(t + 1) * 128],
                kkTh[:, t * 128:(t + 1) * 128], start=True, stop=True),
                waits=[(dv, None)], incs=[(pe, 1)])
            S.op("tensor", lambda e, t=t: e.matmul(
                p512c[:, 256:384], vkTh[:, t * 128:(t + 1) * 128],
                vkTh[:, t * 128:(t + 1) * 128], start=True, stop=True),
                incs=[(pe, 1)])
            S.op("scalar", lambda e: e.activation(g1[:, :], p512c[:, 128:256],
                                                  AF.Copy),
                 waits=[(pe, None), (dv, None)], incs=[(ac, 1)])
            S.op("vector", lambda e: e.tensor_tensor(
                gp[:, :], p512c[:, 256:384], g1[:, :], ALU.mult),
                waits=[(ac, None)], incs=[(dv, 1)])
            S.op("vector", lambda e: e.tensor_tensor(
                gp[:, :], gp[:, :], low01[:, :], ALU.mult),
                waits=[(dv, None)], incs=[(dv, 1)])
            S.op("vector", lambda e: e.reduce_sum(ucol[:, 1:2], gp[:, :],
                                                  axis=AX.X),
                 waits=[(dv, None)], incs=[(dv, 1)])
            S.op("vector", lambda e, t=t: e.tensor_tensor(
                ucol[:, 4:5], ukk[:, t:t + 1], uvk[:, t:t + 1], ALU.mult),
                waits=[(dv, None)], incs=[(dv, 1)])
            S.op("vector", lambda e: e.tensor_tensor(
                ucol[:, 5:6], ucol[:, 0:1], ucol[:, 1:2], ALU.add),
                waits=[(dv, None)], incs=[(dv, 1)])
            S.op("vector", lambda e: e.tensor_scalar(
                ucol[:, 5:6], ucol[:, 5:6], 2.0, None, ALU.mult),
                waits=[(dv, None)], incs=[(dv, 1)])
            S.op("vector", lambda e: e.tensor_sub(ucol[:, 5:6], ucol[:, 5:6],
                                                  ucol[:, 4:5]),
                 waits=[(dv, None)], incs=[(dv, 1)])
            S.op("tensor", lambda e: e.matmul(
                pb[:, 256:257], lowt[:, :], ucol[:, 5:6], start=True, stop=True),
                waits=[(dv, None)], incs=[(pe, 1)])
            S.op("scalar", lambda e, t=t: e.activation(
                stdc[:, 0:1], pb[:, 256:257], AF.Sqrt, bias=ssqb[:, t:t + 1]),
                waits=[(pe, None)], incs=[(ac, 1)])
            S.op("vector", lambda e: e.tensor_scalar(
                stdc[:, 1:2], stdc[:, 0:1], 1e-8, None, ALU.add),
                waits=[(ac, None)], incs=[(dv, 1)])
            S.op("vector", lambda e: e.reciprocal(stdc[:, 2:3], stdc[:, 1:2]),
                 waits=[(dv, None)], incs=[(dv, 1)])
            S.op("vector", lambda e, t=t: e.tensor_scalar_mul(
                qpair[:, (t % 2) * 64:(t % 2) * 64 + 64], p512d[:, 0:64],
                stdc[:, 2:3]),
                waits=[(dv, None)], incs=[(dv, 1)])
            if t % 2 == 1:
                pb2 = (t // 2) % 2
                emit_rotary(ctx, S, qpair[:, :], ropcs[pb2][:, :],
                            ropss[pb2][:, :], rot_t[:, :], rot_t2[:, :],
                            qr8[:, :], waits=[(dv, None), (ld, None)])
                S.op("tensor", lambda e, t=t: e.transpose(
                    ptpb[0:128, 0:128], qr8[:, :], ctx.idb[:, :]),
                    waits=[(dv, None)], incs=[(pe, 1)])
                S.op("vector", lambda e, t=t: e.tensor_copy(
                    qhat[0:64, (t - 1) * 128:t * 128], ptpb[0:64, 0:128]),
                    waits=[(pe, None)], incs=[(dv, 1)])
                S.op("vector", lambda e, t=t: e.tensor_copy(
                    qhat[0:64, t * 128:(t + 1) * 128], ptpb[64:128, 0:128]),
                    waits=[(dv, None)], incs=[(dv, 1)])
        S.emit()

        S = Script(nc)
        for g in range(NKT):
            nkb = (g + 4) // 4
            for kb in range(nkb):
                n = min(512, (g + 1) * 128 - kb * 512)
                pbuf = p512a if kb % 2 == 0 else p512b
                S.op("tensor", lambda e, g=g, kb=kb, n=n, pbuf=pbuf: e.matmul(
                    pbuf[:, 0:n], qhat[0:64, g * 128:(g + 1) * 128],
                    khat[0:64, kb * 512:kb * 512 + n], start=True, stop=True),
                    waits=[(dv, None)], incs=[(pe, 1)])
                if kb == nkb - 1:
                    S.op("vector", lambda e, n=n, pbuf=pbuf: e.tensor_tensor(
                        pbuf[:, n - 128:n], pbuf[:, n - 128:n], tri[:, :],
                        ALU.add),
                        waits=[(pe, None)], incs=[(dv, 1)])
                S.op("vector", lambda e, kb=kb, n=n, pbuf=pbuf: e.reduce_max(
                    mxr[:, kb:kb + 1], pbuf[:, 0:n], axis=AX.X),
                    waits=[(pe, None), (dv, None)], incs=[(dv, 1)])
            if nkb > 1:
                S.op("vector", lambda e, nkb=nkb: e.reduce_max(
                    mxr[:, 8:9], mxr[:, 0:nkb], axis=AX.X),
                    waits=[(dv, None)], incs=[(dv, 1)])
            mc = 8 if nkb > 1 else 0
            S.op("vector", lambda e, mc=mc: e.tensor_scalar_mul(
                negc[:, :], mxr[:, mc:mc + 1], -1.0),
                waits=[(dv, None)], incs=[(dv, 1)])
            S.op("tensor", lambda e: e.transpose(ptpf[0:1, 0:128], negc[:, :],
                                                 ctx.idf[:, :]),
                 waits=[(dv, None)], incs=[(pe, 1)])
            S.op("vector", lambda e, g=g: e.tensor_copy(
                qhat[64:65, g * 128:(g + 1) * 128], ptpf[0:1, 0:128]),
                waits=[(pe, None)], incs=[(dv, 1)])
        S.emit()

        S = Script(nc)
        for qb in range(8):
            gs0 = 4 * qb
            kmax = gs0 + 3
            for kt in range(kmax + 1):
                pbuf = p512a if kt % 2 == 0 else p512b
                S.op("tensor", lambda e, qb=qb, kt=kt, pbuf=pbuf: e.matmul(
                    pbuf[:, :], khat[0:65, kt * 128:(kt + 1) * 128],
                    qhat[0:65, qb * 512:(qb + 1) * 512], start=True, stop=True),
                    waits=[(ac, None)], incs=[(pe, 1)])
                rel = kt - gs0
                if rel >= 0:
                    S.op("vector", lambda e, rel=rel, pbuf=pbuf: e.tensor_tensor(
                        pbuf[:, :], pbuf[:, :], m4[rel][:, :], ALU.add),
                        waits=[(pe, None), (ld, None)], incs=[(dv, 1)])
                S.op("scalar", lambda e, kt=kt, pbuf=pbuf: e.activation(
                    ptile[kt % 2][:, :], pbuf[:, :], AF.Exp),
                    waits=[(pe, None), (dv, None)], incs=[(ac, 1)])
                S.op("tensor", lambda e, kt=kt, kmax=kmax: e.matmul(
                    po[:, :], vhat[:, kt * 65:(kt + 1) * 65],
                    ptile[kt % 2][:, :],
                    start=(kt == 0), stop=(kt == kmax)),
                    waits=[(ac, None)] + ([(dv, None)] if kt == 0 else []),
                    incs=[(pe, 1)])
            S.op("vector", lambda e: e.reciprocal(rs_row[:, :], po[64:65, :]),
                 waits=[(pe, None)], incs=[(dv, 1)])
            S.op("tensor", lambda e: e.matmul(
                pb[0:64, 0:512], ctx.ones_r[0:1, 0:64], rs_row[:, :],
                start=True, stop=True),
                waits=[(dv, None)], incs=[(pe, 1)])
            S.op("vector", lambda e: e.tensor_copy(bc[:, :], pb[0:64, 0:512]),
                 waits=[(pe, None)], incs=[(dv, 1)])
            S.op("vector", lambda e, i=i, qb=qb, hw=hw: e.tensor_tensor(
                attn_sb[i // 2][hw:hw + 64, qb * 512:(qb + 1) * 512],
                po[0:64, :], bc[:, :], ALU.mult),
                waits=[(dv, None)], incs=[(dv, 1)])
        S.emit()

    S = Script(nc)
    for ti in range(4):
        S.op("gpsimd", lambda e, ti=ti: e.dma_start(
            out=dd["d_at_c"][ti * 128:(ti + 1) * 128, :],
            in_=attn_sb[ti][:, :]),
            incs=[(st, 16)])
    S.op("gpsimd", lambda e: e.collective_compute(
        "AllGather", ALU.bypass, replica_groups=RG,
        ins=[dd["d_at_c"].ap()], outs=[dd["d_at"].ap()]),
        waits=[(st, None)], incs=[(cc, 1)])
    S.emit()


def phase2_outproj(nc, ctx, cc, ar, dd, psum, scr, flg):
    ld, ldg, st, pe, dv, ac = ctx.ld, ctx.ldg, ctx.st, ctx.pe, ctx.dv, ctx.ac
    (p512a, p512b, p512c, p512d, po, pbx, ptpb, ptpf) = psum
    d_at, xs = dd["d_at"], dd["xs"]

    ar.reset()
    atn = [ar.alloc(TOK, BF16) for _ in range(8)]
    stg_at = [ar.alloc(TOK, BF16) for _ in range(2)]
    wout = [ar.alloc(D, BF16) for _ in range(8)]
    bout = ar.alloc(D, BF16)[0:1, :]
    bv2 = ar.alloc(D, BF16)[0:1, :]
    bqc = ar.alloc(64, F32)
    bkc = ar.alloc(64, F32)
    ytile = ar.alloc(D, F32)
    x1t = ar.alloc(D, F32)
    sxt = [ar.alloc(D, F32) for _ in range(2)]
    sbf = ar.alloc(D, BF16)
    slt = [ar.alloc(TOK, BF16) for _ in range(8)]
    wfm2 = [ar.alloc(8 * 128, BF16) for _ in range(2)]
    wtm2 = [ar.alloc(8 * 512, BF16) for _ in range(2)]
    stage2 = [ar.alloc(512, BF16) for _ in range(4)]

    S = Script(nc)
    S.op("sync", None, waits=[(cc, None), (st, None)])
    S.op("gpsimd", None, waits=[(cc, None), (st, None)])
    stg_at2 = [ar.alloc(TOK, BF16) for _ in range(2)]
    for c in range(8):
        qi = c % 2
        eng, sem = ("sync", ld) if qi == 0 else ("gpsimd", ldg)
        sA, sB = (stg_at[0], stg_at[1]) if qi == 0 else (stg_at2[0], stg_at2[1])
        S.op(eng, lambda e, c=c, sA=sA: e.dma_start(
            out=sA[:, :], in_=d_at[c * 128:(c + 1) * 128, 0:TOK]),
            waits=[(dv, None)], incs=[(sem, 16)])
        S.op(eng, lambda e, c=c, sB=sB: e.dma_start(
            out=sB[:, :], in_=d_at[c * 128:(c + 1) * 128, TOK:NSEQ]),
            incs=[(sem, 16)])
        S.op("vector", lambda e, c=c, sA=sA: e.tensor_scalar_mul(
            atn[c][:, :], sA[:, :], flg[:, 0:1]),
            waits=[(sem, None), (dv, None)], incs=[(dv, 1)])
        S.op("vector", lambda e, c=c, sB=sB: e.scalar_tensor_tensor(
            atn[c][:, :], sB[:, :], flg[:, 1:2], atn[c][:, :],
            ALU.mult, ALU.add),
            waits=[(dv, None)], incs=[(dv, 1)])
    for c in range(8):
        S.op("gpsimd", lambda e, c=c: e.dma_start(
            out=wout[c][:, :], in_=dd["woutd"][c * 128:(c + 1) * 128, :]),
            waits=([(dv, None)] if c == 0 else []), incs=[(ldg, 16)])
    S.op("gpsimd", lambda e: e.dma_start(out=bout[:, :], in_=dd["boutd"][:, :]),
         incs=[(ldg, 16)])
    S.op("gpsimd", lambda e: e.dma_start(out=bv2[:, :], in_=dd["bv2d"][:, :]),
         incs=[(ldg, 16)])
    for c in range(8):
        S.op("sync", lambda e, c=c: e.dma_start(
            out=bqc[:, 8 * c:8 * c + 1], in_=dd["bq2d"][c * 128:(c + 1) * 128, :]),
            waits=([(dv, None)] if c == 0 else []), incs=[(ld, 16)])
        S.op("sync", lambda e, c=c: e.dma_start(
            out=bkc[:, 8 * c:8 * c + 1], in_=dd["bk2d"][c * 128:(c + 1) * 128, :]),
            incs=[(ld, 16)])
    for t in range(NT):
        tb = t % 2
        S.op("sync", lambda e, t=t, tb=tb: e.dma_start(
            out=sxt[tb][:, :], in_=xs[t * 128:(t + 1) * 128, :]),
            waits=[(dv, None)], incs=[(ld, 16)])
        for g in range(2):
            pbuf = p512a if g == 0 else p512b
            pairs = [(atn[c][:, t * 128:(t + 1) * 128],
                      wout[c][:, g * 512:(g + 1) * 512]) for c in range(8)]
            mm_group(S, ctx, pbuf[:, :], pairs,
                     waits_first=[(ldg, None), (ld, None), (dv, None),
                                  (ac, None)],
                     stop_last=False)
            S.op("tensor", lambda e, g=g, pbuf=pbuf: e.matmul(
                pbuf[:, :], ctx.ones_rb[:, :], bout[:, g * 512:(g + 1) * 512],
                start=False, stop=True),
                incs=[(pe, 1)])
            S.op("scalar", lambda e, g=g, pbuf=pbuf: e.activation(
                ytile[:, g * 512:(g + 1) * 512], pbuf[:, :], AF.Copy),
                waits=[(pe, None)], incs=[(ac, 1)])
        emit_ln(ctx, S, ytile[:, :], ytile[:, :], scr, x_waits=[(ac, None)])
        S.op("vector", lambda e, tb=tb: e.tensor_tensor(
            x1t[:, :], sxt[tb][:, :], ytile[:, :], ALU.add),
            waits=[(ld, None), (dv, None), (st, None)], incs=[(dv, 1)])
        S.op("gpsimd", lambda e, t=t: e.dma_start(
            out=dd["d_x1"][t * 128:(t + 1) * 128, :], in_=x1t[:, :]),
            waits=[(dv, None)], incs=[(st, 16)])
        emit_ln(ctx, S, x1t[:, :], sbf[:, :], scr, x_waits=[(dv, None)])
        for c in range(8):
            emit_tp(ctx, S, sbf[:, c * 128:(c + 1) * 128],
                    slt[c][:, t * 128:(t + 1) * 128], ptpb)
    S.emit()

    S = Script(nc)
    for (wdram, biascols, odram) in ((dd["wq2d"], bqc, dd["d_qT2"]),
                                     (dd["wk2d"], bkc, dd["d_kt2_c"])):
        for oc in range(8):
            b = oc % 2
            for kc in range(8):
                S.op("gpsimd", lambda e, kc=kc, oc=oc, b=b, wdram=wdram: e.dma_start(
                    out=wfm2[b][:, kc * 128:(kc + 1) * 128],
                    in_=wdram[kc * 128:(kc + 1) * 128, oc * 128:oc * 128 + 128]),
                    waits=([(pe, None)] if kc == 0 else []), incs=[(ldg, 16)])
            for tb in range(4):
                pbuf = p512a if tb % 2 == 0 else p512b
                pairs = [(wfm2[b][:, kc * 128:(kc + 1) * 128],
                          slt[kc][:, tb * 512:(tb + 1) * 512]) for kc in range(8)]
                mm_group(S, ctx, pbuf[:, :], pairs,
                         waits_first=[(ldg, None), (ac, None), (dv, None)])
                S.op("vector", lambda e, pbuf=pbuf, tb=tb, oc=oc, biascols=biascols:
                     e.tensor_scalar(stage2[tb][:, :], pbuf[:, :],
                                     biascols[:, 8 * oc:8 * oc + 1], None,
                                     ALU.add),
                     waits=[(pe, None), (st, None), (ld, None)], incs=[(dv, 1)])
                S.op("gpsimd", lambda e, oc=oc, tb=tb, odram=odram: e.dma_start(
                    out=odram[oc * 128:(oc + 1) * 128, tb * 512:(tb + 1) * 512],
                    in_=stage2[tb][:, :]),
                    waits=[(dv, None)], incs=[(st, 16)])
    for g in range(2):
        wb = wtm2[g % 2]
        for kc in range(8):
            S.op("gpsimd", lambda e, kc=kc, g=g, wb=wb: e.dma_start(
                out=wb[:, kc * 512:(kc + 1) * 512],
                in_=dd["wv2d"][kc * 128:(kc + 1) * 128, g * 512:g * 512 + 512]),
                waits=([(pe, None)] if kc == 0 else []), incs=[(ldg, 16)])
        for t in range(NT):
            pbuf = p512a if t % 2 == 0 else p512b
            pairs = [(slt[kc][:, t * 128:(t + 1) * 128],
                      wb[:, kc * 512:(kc + 1) * 512]) for kc in range(8)]
            mm_group(S, ctx, pbuf[:, :], pairs,
                     waits_first=[(ldg, None), (ac, None)], stop_last=False)
            S.op("tensor", lambda e, g=g, pbuf=pbuf: e.matmul(
                pbuf[:, :], ctx.ones_rb[:, :], bv2[:, g * 512:(g + 1) * 512],
                start=False, stop=True),
                incs=[(pe, 1)])
            S.op("scalar", lambda e, t=t, pbuf=pbuf: e.activation(
                stage2[2 + t % 2][:, :], pbuf[:, :], AF.Copy),
                waits=[(pe, None), (st, None)], incs=[(ac, 1)])
            S.op("gpsimd", lambda e, t=t, g=g: e.dma_start(
                out=dd["d_v2_c"][t * 128:(t + 1) * 128, g * 512:(g + 1) * 512],
                in_=stage2[2 + t % 2][:, :]),
                waits=[(ac, None)], incs=[(st, 16)])
    for (ci, gi) in ((dd["d_kt2_c"], dd["d_kt2"]), (dd["d_v2_c"], dd["d_v2"])):
        S.op("gpsimd", lambda e, ci=ci, gi=gi: e.collective_compute(
            "AllGather", ALU.bypass, replica_groups=RG,
            ins=[ci.ap()], outs=[gi.ap()]),
            waits=[(st, None), (ldg, None)], incs=[(cc, 1)])
    S.emit()


def phase3_l0f(nc, ctx, cc, ar, dd, psum, scr):
    ld, ldg, st, pe, dv, ac = ctx.ld, ctx.ldg, ctx.st, ctx.pe, ctx.dv, ctx.ac
    (p512a, p512b, p512c, p512d, po, pbx, ptpb, ptpf) = psum
    pb = pbx[0:64, :]
    d_qT2, d_kt2, d_v2 = dd["d_qT2"], dd["d_kt2"], dd["d_v2"]

    ar.reset()
    qhat = ar.alloc(TOK, BF16)
    qstg = ar.alloc(TOK, BF16)[0:64, :]
    khat = ar.alloc(NSEQ, BF16)
    vhat = ar.alloc(NKT * 65, BF16)
    mxr = ar.alloc(9, F32)
    negc = ar.alloc(1, F32)
    ptile = [ar.alloc(512, BF16) for _ in range(2)]
    rs_row = ar.alloc(512, F32)[0:1, :]
    bc = ar.alloc(512, F32)[0:64, :]
    attn_sb = [ar.alloc(TOK, BF16) for _ in range(8)]
    wout = [ar.alloc(D, BF16) for _ in range(8)]
    bout = ar.alloc(D, BF16)[0:1, :]
    ytile = ar.alloc(D, F32)
    xat = ar.alloc(D, F32)
    sxt = [ar.alloc(D, F32) for _ in range(2)]
    grow = ar.alloc(2 * D, F32)[0:1, :]
    gb_t = {nm: ar.alloc(D, F32) for nm in ("ln1g", "ln1b")}
    xaT = [ar.alloc(TOK, BF16) for _ in range(8)]
    sbf = ar.alloc(D, BF16)
    stile = ar.alloc(D, F32)

    first = [True]
    for h in range(H):
        S = Script(nc)
        hw = 64 * (h % 2)
        if first[0]:
            S.op("sync", None, waits=[(cc, None), (st, None)])
            S.op("gpsimd", None, waits=[(cc, None), (st, None)])
            first[0] = False
        S.op("sync", lambda e, h=h: e.dma_start(
            out=qstg[:, :], in_=d_qT2[h * 64:(h + 1) * 64, :]),
            waits=[(pe, None), (dv, None)], incs=[(ld, 16)])
        for blk_ in range(2):
            S.op("sync", lambda e, h=h, blk_=blk_: e.dma_start(
                out=khat[0:64, blk_ * TOK:(blk_ + 1) * TOK],
                in_=d_kt2[blk_ * D + h * 64:blk_ * D + (h + 1) * 64, :]),
                incs=[(ld, 16)])
        if h == 0:
            S.op("vector", lambda e: e.memset(
                vhat[:, :].rearrange("p (kt c) -> p kt c", c=65)[:, :, 64:65],
                1.0),
                incs=[(dv, 1)])
        S.op("sync", lambda e, h=h: e.dma_start(
            out=vhat[:, :].rearrange("p (kt c) -> p kt c", c=65)[:, :, 0:64],
            in_=d_v2.ap().rearrange("(kt p) d -> p kt d", p=128)[:, :, h * 64:(h + 1) * 64]),
            waits=[(dv, None)], incs=[(ld, 16)])
        S.op("vector", lambda e: e.tensor_scalar_mul(qhat[0:64, :], qstg[:, :], 0.125),
             waits=[(ld, None)], incs=[(dv, 1)])
        S.op("vector", lambda e: e.memset(khat[64:65, :], 1.0), incs=[(dv, 1)])
        for t in range(NT):
            for kb in range(8):
                pbuf = p512a if kb % 2 == 0 else p512b
                S.op("tensor", lambda e, t=t, kb=kb, pbuf=pbuf: e.matmul(
                    pbuf[:, :], qhat[0:64, t * 128:(t + 1) * 128],
                    khat[0:64, kb * 512:(kb + 1) * 512], start=True, stop=True),
                    waits=[(dv, None), (ld, None)], incs=[(pe, 1)])
                S.op("vector", lambda e, kb=kb, pbuf=pbuf: e.reduce_max(
                    mxr[:, kb:kb + 1], pbuf[:, :], axis=AX.X),
                    waits=[(pe, None)], incs=[(dv, 1)])
            S.op("vector", lambda e: e.reduce_max(
                mxr[:, 8:9], mxr[:, 0:8], axis=AX.X),
                waits=[(dv, None)], incs=[(dv, 1)])
            S.op("vector", lambda e: e.tensor_scalar_mul(negc[:, :], mxr[:, 8:9], -1.0),
                 waits=[(dv, None)], incs=[(dv, 1)])
            S.op("tensor", lambda e: e.transpose(ptpf[0:1, 0:128], negc[:, :],
                                                 ctx.idf[:, :]),
                 waits=[(dv, None)], incs=[(pe, 1)])
            S.op("vector", lambda e, t=t: e.tensor_copy(
                qhat[64:65, t * 128:(t + 1) * 128], ptpf[0:1, 0:128]),
                waits=[(pe, None)], incs=[(dv, 1)])
        for qb in range(4):
            for kt in range(NKT):
                pbuf = p512a if kt % 2 == 0 else p512b
                S.op("tensor", lambda e, qb=qb, kt=kt, pbuf=pbuf: e.matmul(
                    pbuf[:, :], khat[0:65, kt * 128:(kt + 1) * 128],
                    qhat[0:65, qb * 512:(qb + 1) * 512], start=True, stop=True),
                    waits=[(ac, None)], incs=[(pe, 1)])
                S.op("scalar", lambda e, kt=kt, pbuf=pbuf: e.activation(
                    ptile[kt % 2][:, :], pbuf[:, :], AF.Exp),
                    waits=[(pe, None)], incs=[(ac, 1)])
                S.op("tensor", lambda e, kt=kt: e.matmul(
                    po[:, :], vhat[:, kt * 65:(kt + 1) * 65], ptile[kt % 2][:, :],
                    start=(kt == 0), stop=(kt == NKT - 1)),
                    waits=[(ac, None)] + ([(dv, None)] if kt == 0 else []),
                    incs=[(pe, 1)])
            S.op("vector", lambda e: e.reciprocal(rs_row[:, :], po[64:65, :]),
                 waits=[(pe, None)], incs=[(dv, 1)])
            S.op("tensor", lambda e: e.matmul(
                pb[:, :], ctx.ones_r[0:1, 0:64], rs_row[:, :], start=True, stop=True),
                waits=[(dv, None)], incs=[(pe, 1)])
            S.op("vector", lambda e: e.tensor_copy(bc[:, :], pb[:, :]),
                 waits=[(pe, None)], incs=[(dv, 1)])
            S.op("vector", lambda e, h=h, qb=qb, hw=hw: e.tensor_tensor(
                attn_sb[h // 2][hw:hw + 64, qb * 512:(qb + 1) * 512],
                po[0:64, :], bc[:, :], ALU.mult),
                waits=[(dv, None)], incs=[(dv, 1)])
        S.emit()

    S = Script(nc)
    for c in range(8):
        S.op("gpsimd", lambda e, c=c: e.dma_start(
            out=wout[c][:, :], in_=dd["wo2d"][c * 128:(c + 1) * 128, :]),
            incs=[(ldg, 16)])
    S.op("gpsimd", lambda e: e.dma_start(out=bout[:, :], in_=dd["bo2d"][:, :]),
         incs=[(ldg, 16)])
    for i, (nm, d_) in enumerate((("ln1g", dd["ln1gd"]), ("ln1b", dd["ln1bd"]))):
        S.op("sync", lambda e, i=i, d_=d_: e.dma_start(
            out=grow[:, i * D:(i + 1) * D], in_=d_[:, :]),
            waits=([(dv, None)] if i == 0 else []), incs=[(ld, 16)])
    for i, nm in enumerate(("ln1g", "ln1b")):
        for g in range(2):
            S.op("tensor", lambda e, i=i, g=g: e.matmul(
                p512c[:, :], ctx.ones_r[0:1, :],
                grow[:, i * D + g * 512: i * D + (g + 1) * 512],
                start=True, stop=True),
                waits=[(ld, None), (dv, None)], incs=[(pe, 1)])
            S.op("vector", lambda e, nm=nm, g=g: e.tensor_copy(
                gb_t[nm][:, g * 512:(g + 1) * 512], p512c[:, :]),
                waits=[(pe, None)], incs=[(dv, 1)])
    for t in range(NT):
        tb = t % 2
        S.op("sync", lambda e, t=t, tb=tb: e.dma_start(
            out=sxt[tb][:, :], in_=dd["d_x1"][t * 128:(t + 1) * 128, :]),
            waits=[(dv, None)], incs=[(ld, 16)])
        for g in range(2):
            pbuf = p512a if g == 0 else p512b
            pairs = [(attn_sb[c][:, t * 128:(t + 1) * 128],
                      wout[c][:, g * 512:(g + 1) * 512]) for c in range(8)]
            mm_group(S, ctx, pbuf[:, :], pairs,
                     waits_first=[(ldg, None), (ac, None)], stop_last=False)
            S.op("tensor", lambda e, g=g, pbuf=pbuf: e.matmul(
                pbuf[:, :], ctx.ones_rb[:, :], bout[:, g * 512:(g + 1) * 512],
                start=False, stop=True), incs=[(pe, 1)])
            S.op("scalar", lambda e, g=g, pbuf=pbuf: e.activation(
                ytile[:, g * 512:(g + 1) * 512], pbuf[:, :], AF.Copy),
                waits=[(pe, None)], incs=[(ac, 1)])
        emit_ln(ctx, S, sxt[tb][:, :], stile[:, :], scr,
                x_waits=[(ld, None), (dv, None)])
        S.op("vector", lambda e: e.tensor_tensor(
            ytile[:, :], stile[:, :], ytile[:, :], ALU.add),
            waits=[(ac, None), (dv, None)], incs=[(dv, 1)])
        S.op("vector", None, waits=[(st, None)])
        emit_ln(ctx, S, ytile[:, :], xat[:, :], scr, x_waits=[(dv, None)],
                gain_bias=(gb_t["ln1g"][:, :], gb_t["ln1b"][:, :]))
        S.op("gpsimd", lambda e, t=t: e.dma_start(
            out=dd["d_x1a"][t * 128:(t + 1) * 128, :], in_=xat[:, :]),
            waits=[(dv, None)], incs=[(st, 16)])
        S.op("vector", lambda e: e.tensor_copy(sbf[:, :], xat[:, :]),
             waits=[(dv, None)], incs=[(dv, 1)])
        for c in range(8):
            emit_tp(ctx, S, sbf[:, c * 128:(c + 1) * 128],
                    xaT[c][:, t * 128:(t + 1) * 128], ptpb)
    for c in range(8):
        S.op("gpsimd", lambda e, c=c: e.dma_start(
            out=dd["d_xaT"][c * 128:(c + 1) * 128, :], in_=xaT[c][:, :]),
            waits=[(dv, None)], incs=[(st, 16)])
    S.emit()


def phase4_moe(nc, ctx, cc, ar, dd, psum, scr):
    ld, ldg, st, pe, dv, ac = ctx.ld, ctx.ldg, ctx.st, ctx.pe, ctx.dv, ctx.ac
    (p512a, p512b, p512c, p512d, po, pbx, ptpb, ptpf) = psum

    ar.reset()
    xaT = [ar.alloc(TOK, BF16) for _ in range(8)]
    ytile = ar.alloc(D, F32)
    xat = ar.alloc(D, F32)
    sxt = [ar.alloc(D, F32) for _ in range(2)]
    sbf = ar.alloc(D, BF16)
    grow = ar.alloc(512, F32)[0:1, :]
    gb_t = {nm: ar.alloc(D, F32) for nm in ("ln2g", "ln2b")}
    gw_sb = ar.alloc(8 * 16, BF16)
    gb_sb = ar.alloc(8, BF16)[0:1, :]
    b1c = ar.alloc(16 * 8, F32)
    b2_sb = ar.alloc(D, F32)[0:8, :]
    gcol = ar.alloc(8, F32)
    gtmp = ar.alloc(8, F32)
    gred = ar.alloc(4, F32)
    w8 = ar.alloc(NT * 8, F32)
    w8tt = ar.alloc(128, F32)[0:8, :]
    w1eb = [ar.alloc(8 * 512, BF16) for _ in range(2)]
    w2eb = [ar.alloc(4 * 1024, BF16) for _ in range(2)]
    hts = ar.alloc(8 * 1024, BF16)
    accs = ar.alloc(8 * 1024, F32)
    wfm3 = [ar.alloc(8 * 128, BF16) for _ in range(2)]
    wtm3_ = ar.alloc(8 * 512, BF16)
    wtm3 = [wtm3_, wtm3_]
    stage3 = [ar.alloc(512, BF16) for _ in range(4)]
    sqf3 = ar.alloc(512, F32)
    bq3c = ar.alloc(64, F32)
    bk3c = ar.alloc(64, F32)
    bv3s = ar.alloc(D, BF16)[0:1, :]
    nrow512 = ar.alloc(512, F32)[0:1, :]

    S = Script(nc)
    S.op("sync", None, waits=[(cc, None), (st, None)])
    S.op("gpsimd", None, waits=[(cc, None), (st, None)])
    for c in range(8):
        S.op("sync", lambda e, c=c: e.dma_start(
            out=xaT[c][:, :], in_=dd["d_xaT"][c * 128:(c + 1) * 128, :]),
            incs=[(ld, 16)])
    for nm, d_ in (("ln2g", dd["ln2gd"]), ("ln2b", dd["ln2bd"])):
        for g in range(2):
            S.op("sync", lambda e, d_=d_, g=g: e.dma_start(
                out=grow[:, :], in_=d_[:, g * 512:(g + 1) * 512]),
                waits=[(pe, None)], incs=[(ld, 16)])
            S.op("tensor", lambda e, g=g: e.matmul(
                p512c[:, :], ctx.ones_r[0:1, :], grow[:, :],
                start=True, stop=True),
                waits=[(ld, None), (dv, None)], incs=[(pe, 1)])
            S.op("vector", lambda e, nm=nm, g=g: e.tensor_copy(
                gb_t[nm][:, g * 512:(g + 1) * 512], p512c[:, :]),
                waits=[(pe, None)], incs=[(dv, 1)])
    for c in range(8):
        S.op("gpsimd", lambda e, c=c: e.dma_start(
            out=gw_sb[:, c * 16:c * 16 + 8],
            in_=dd["gwd"][c * 128:(c + 1) * 128, :]), incs=[(ldg, 16)])
    S.op("gpsimd", lambda e: e.dma_start(out=gb_sb[:, :], in_=dd["gbd"][:, :]),
         incs=[(ldg, 16)])
    S.op("sync", lambda e: e.dma_start(
        out=b1c[:, :], in_=dd["b1td"].ap().rearrange("(f p) e -> p f e", p=128)),
        waits=[(dv, None)], incs=[(ld, 16)])
    S.op("sync", lambda e: e.dma_start(out=b2_sb[:, :], in_=dd["b2d"][:, :]),
         incs=[(ld, 16)])
    for t in range(NT):
        pairs = [(xaT[kc][:, t * 128:(t + 1) * 128], gw_sb[:, kc * 16:kc * 16 + 8])
                 for kc in range(8)]
        mm_group(S, ctx, p512c[:, 0:8], pairs,
                 waits_first=[(ldg, None), (dv, None), (ld, None)], stop_last=False)
        S.op("tensor", lambda e: e.matmul(
            p512c[:, 0:8], ctx.ones_rb[:, :], gb_sb[:, :], start=False, stop=True),
            incs=[(pe, 1)])
        S.op("vector", lambda e: e.reduce_max(gred[:, 0:1], p512c[:, 0:8],
                                              axis=AX.X, negate=True),
             waits=[(pe, None)], incs=[(dv, 1)])
        S.op("scalar", lambda e: e.activation(
            gcol[:, :], p512c[:, 0:8], AF.Exp, bias=gred[:, 0:1],
            accum_out=gred[:, 1:2]),
            waits=[(dv, None)], incs=[(ac, 1)])
        S.op("vector", lambda e: e.reciprocal(gred[:, 2:3], gred[:, 1:2]),
             waits=[(ac, None)], incs=[(dv, 1)])
        S.op("vector", lambda e: e.tensor_scalar_mul(gcol[:, :], gcol[:, :],
                                                     gred[:, 2:3]),
             waits=[(dv, None)], incs=[(dv, 1)])
        S.op("vector", lambda e: e.reduce_max(gred[:, 2:3], gcol[:, :], axis=AX.X),
             waits=[(dv, None)], incs=[(dv, 1)])
        S.op("vector", lambda e: e.tensor_scalar(gtmp[:, :], gcol[:, :],
                                                 gred[:, 2:3], None, ALU.is_ge),
             waits=[(dv, None)], incs=[(dv, 1)])
        S.op("vector", lambda e: e.scalar_tensor_tensor(
            gtmp[:, :], gtmp[:, :], -1e30, gcol[:, :], ALU.mult, ALU.add),
            waits=[(dv, None)], incs=[(dv, 1)])
        S.op("vector", lambda e: e.reduce_max(gred[:, 3:4], gtmp[:, :], axis=AX.X),
             waits=[(dv, None)], incs=[(dv, 1)])
        S.op("vector", lambda e: e.tensor_scalar(gtmp[:, :], gcol[:, :],
                                                 gred[:, 3:4], None, ALU.is_ge),
             waits=[(dv, None)], incs=[(dv, 1)])
        S.op("vector", lambda e, t=t: e.tensor_tensor(
            w8[:, t * 8:(t + 1) * 8], gcol[:, :], gtmp[:, :], ALU.mult),
            waits=[(dv, None)], incs=[(dv, 1)])
    S.emit()

    for th in range(2):
        S = Script(nc)
        t0 = th * 8
        for tt in range(8):
            S.op("tensor", lambda e, tt=tt: e.transpose(
                ptpf[0:8, 0:128], w8[:, (t0 + tt) * 8:(t0 + tt + 1) * 8],
                ctx.idf[:, :]),
                waits=[(dv, None), (ac, None)], incs=[(pe, 1)])
            S.op("vector", lambda e: e.tensor_copy(w8tt[:, :], ptpf[0:8, 0:128]),
                 waits=[(pe, None)], incs=[(dv, 1)])
            S.op("tensor", lambda e, tt=tt: e.matmul(
                p512c[:, :], w8tt[:, :],
                b2_sb[:, 0:512], start=True, stop=True),
                waits=[(dv, None), (ac, None), (ld, None)], incs=[(pe, 1)])
            S.op("tensor", lambda e, tt=tt: e.matmul(
                p512d[:, :], w8tt[:, :],
                b2_sb[:, 512:1024], start=True, stop=True),
                incs=[(pe, 1)])
            S.op("scalar", lambda e, tt=tt: e.activation(
                accs[:, tt * 1024: tt * 1024 + 512], p512c[:, :], AF.Copy),
                waits=[(pe, None)], incs=[(ac, 1)])
            S.op("scalar", lambda e, tt=tt: e.activation(
                accs[:, tt * 1024 + 512:(tt + 1) * 1024], p512d[:, :], AF.Copy),
                waits=[(pe, None)], incs=[(ac, 1)])
        w1_guard = [pe.v, pe.v]
        w2_guard = [pe.v, pe.v]
        lsem = (ldg, ctx.lo)
        w2sem = (Sem(nc, f"s_w2a{th}"), Sem(nc, f"s_w2b{th}"))
        w1_ldg = [ldg.v, ctx.lo.v]
        w2_ldg = [0, 0]
        for e_i in range(8):
            for hh2 in range(2):
                for fq2 in range(2):
                    fq = hh2 * 2 + fq2
                    bi = fq % 2
                    buf = w1eb[bi]
                    for kc in range(8):
                        S.op("gpsimd", lambda e, e_i=e_i, kc=kc, fq=fq, buf=buf:
                             e.dma_start(
                                 out=buf[:, kc * 512:(kc + 1) * 512],
                                 in_=dd["w1d"][e_i, kc * 128:(kc + 1) * 128,
                                               fq * 512:(fq + 1) * 512]),
                             waits=([(pe, w1_guard[bi])] if kc == 0 else []),
                             incs=[(lsem[bi], 16)])
                    w1_ldg[bi] = lsem[bi].v
                    for ft4 in range(4):
                        ftl = fq2 * 4 + ft4
                        ftg = hh2 * 8 + ftl
                        for tb2 in range(2):
                            pbuf = p512a if tb2 == 0 else p512b
                            pairs = [(buf[:, kc * 512 + ft4 * 128:
                                          kc * 512 + (ft4 + 1) * 128],
                                      xaT[kc][:, (t0 + tb2 * 4) * 128:(t0 + (tb2 + 1) * 4) * 128])
                                     for kc in range(8)]
                            mm_group(S, ctx, pbuf[:, :], pairs,
                                     waits_first=[(lsem[bi], w1_ldg[bi]),
                                                  (ac, None), (dv, None)])
                            S.op("scalar", lambda e, ftl=ftl, ftg=ftg, tb2=tb2,
                                 e_i=e_i, pbuf=pbuf:
                                 e.activation(hts[:, ftl * 1024 + tb2 * 512:
                                                  ftl * 1024 + (tb2 + 1) * 512],
                                              pbuf[:, :], AF.Relu,
                                              bias=b1c[:, ftg * 8 + e_i:
                                                       ftg * 8 + e_i + 1]),
                                 waits=[(pe, None), (ld, None)], incs=[(ac, 1)])
                    w1_guard[bi] = pe.v
                for w2q in range(2):
                    bi = (hh2 * 2 + w2q) % 2
                    buf = w2eb[bi]
                    for fc4 in range(4):
                        fc = hh2 * 8 + w2q * 4 + fc4
                        S.op("gpsimd", lambda e, e_i=e_i, fc=fc, fc4=fc4,
                             buf=buf: e.dma_start(
                                 out=buf[:, fc4 * 1024:(fc4 + 1) * 1024],
                                 in_=dd["w2d"][e_i, fc * 128:(fc + 1) * 128, :]),
                             waits=([(pe, w2_guard[bi])] if fc4 == 0 else []),
                             incs=[(w2sem[bi], 16)])
                    w2_ldg[bi] = w2sem[bi].v
                    for tt in range(8):
                        for g in range(2):
                            pbuf = p512c if g == 0 else p512d
                            pairs = [(hts[:, (w2q * 4 + fc4) * 1024 + tt * 128:
                                          (w2q * 4 + fc4) * 1024 + (tt + 1) * 128],
                                      buf[:, fc4 * 1024 + g * 512:
                                          fc4 * 1024 + (g + 1) * 512])
                                     for fc4 in range(4)]
                            mm_group(S, ctx, pbuf[:, :], pairs,
                                     waits_first=[(w2sem[bi], w2_ldg[bi]),
                                                  (ac, None), (dv, None)])
                            S.op("vector", lambda e, tt=tt, g=g, e_i=e_i,
                                 pbuf=pbuf: e.scalar_tensor_tensor(
                                     accs[:, tt * 1024 + g * 512:
                                          tt * 1024 + (g + 1) * 512],
                                     pbuf[:, :], w8[:, (t0 + tt) * 8 + e_i:
                                                    (t0 + tt) * 8 + e_i + 1],
                                     accs[:, tt * 1024 + g * 512:
                                          tt * 1024 + (g + 1) * 512],
                                     ALU.mult, ALU.add),
                                 waits=[(pe, None)], incs=[(dv, 1)])
                    w2_guard[bi] = pe.v
        S.emit()

        S = Script(nc)
        for tt in range(8):
            t = t0 + tt
            tb = t % 2
            S.op("sync", lambda e, t=t: e.dma_start(
                out=xat[:, :], in_=dd["d_x1a"][t * 128:(t + 1) * 128, :]),
                waits=[(dv, None), (ac, None), (st, None)], incs=[(ld, 16)])
            S.op("sync", lambda e, t=t, tb=tb: e.dma_start(
                out=sxt[tb][:, :], in_=dd["d_x1"][t * 128:(t + 1) * 128, :]),
                waits=[(dv, None)], incs=[(ld, 16)])
            S.op("vector", lambda e, tt=tt: e.tensor_tensor(
                ytile[:, :], xat[:, :], accs[:, tt * 1024:(tt + 1) * 1024], ALU.add),
                waits=[(ld, None), (dv, None)], incs=[(dv, 1)])
            emit_ln(ctx, S, ytile[:, :], ytile[:, :], scr, x_waits=[(dv, None)],
                    gain_bias=(gb_t["ln2g"][:, :], gb_t["ln2b"][:, :]))
            emit_ln(ctx, S, ytile[:, :], ytile[:, :], scr, x_waits=[(dv, None)])
            S.op("vector", lambda e, tb=tb: e.tensor_tensor(
                xat[:, :], sxt[tb][:, :], ytile[:, :], ALU.add),
                waits=[(dv, None), (st, None)], incs=[(dv, 1)])
            S.op("gpsimd", lambda e, t=t: e.dma_start(
                out=dd["d_x2"][t * 128:(t + 1) * 128, :], in_=xat[:, :]),
                waits=[(dv, None)], incs=[(st, 16)])
            emit_ln(ctx, S, xat[:, :], sbf[:, :], scr, x_waits=[(dv, None)])
            for c in range(8):
                emit_tp(ctx, S, sbf[:, c * 128:(c + 1) * 128],
                        xaT[c][:, t * 128:(t + 1) * 128], ptpb)
        S.emit()

    S = Script(nc)
    for c in range(8):
        S.op("sync", lambda e, c=c: e.dma_start(
            out=bq3c[:, 8 * c:8 * c + 1], in_=dd["bq3d"][c * 128:(c + 1) * 128, :]),
            waits=([(dv, None)] if c == 0 else []), incs=[(ld, 16)])
        S.op("sync", lambda e, c=c: e.dma_start(
            out=bk3c[:, 8 * c:8 * c + 1], in_=dd["bk3d"][c * 128:(c + 1) * 128, :]),
            incs=[(ld, 16)])
    S.op("gpsimd", lambda e: e.dma_start(out=bv3s[:, :], in_=dd["bv3d"][:, :]),
         incs=[(ldg, 16)])
    for (wdram, biascols, odram, nr) in (
            (dd["wq3d"], bq3c, dd["d_qT3"], dd["d_qnn"]),
            (dd["wk3d"], bk3c, dd["d_kt3_c"], dd["d_knn_c"])):
        for oc in range(8):
            b = oc % 2
            for kc in range(8):
                S.op("gpsimd", lambda e, kc=kc, oc=oc, b=b, wdram=wdram: e.dma_start(
                    out=wfm3[b][:, kc * 128:(kc + 1) * 128],
                    in_=wdram[kc * 128:(kc + 1) * 128, oc * 128:oc * 128 + 128]),
                    waits=([(pe, None)] if kc == 0 else []), incs=[(ldg, 16)])
            for tb in range(4):
                pbuf = p512a if tb % 2 == 0 else p512b
                pairs = [(wfm3[b][:, kc * 128:(kc + 1) * 128],
                          xaT[kc][:, tb * 512:(tb + 1) * 512]) for kc in range(8)]
                mm_group(S, ctx, pbuf[:, :], pairs,
                         waits_first=[(ldg, None), (ac, None), (dv, None)])
                S.op("vector", lambda e, pbuf=pbuf, tb=tb, oc=oc, biascols=biascols:
                     e.tensor_scalar(stage3[tb][:, :], pbuf[:, :],
                                     biascols[:, 8 * oc:8 * oc + 1], None,
                                     ALU.add),
                     waits=[(pe, None), (st, None), (ac, None), (ld, None)],
                     incs=[(dv, 1)])
                S.op("gpsimd", lambda e, oc=oc, tb=tb, odram=odram: e.dma_start(
                    out=odram[oc * 128:(oc + 1) * 128, tb * 512:(tb + 1) * 512],
                    in_=stage3[tb][:, :]),
                    waits=[(dv, None)], incs=[(st, 16)])
                S.op("scalar", lambda e, tb=tb: e.activation(
                    sqf3[:, :], stage3[tb][:, :], AF.Square),
                    waits=[(dv, None)], incs=[(ac, 1)])
                for hh in range(2):
                    head = 2 * oc + hh
                    S.op("tensor", lambda e, hh=hh: e.matmul(
                        pbx[0:1, :], ctx.constcol(-0.5)[hh * 64:hh * 64 + 64, :],
                        sqf3[hh * 64:hh * 64 + 64, :], start=True, stop=True),
                        waits=[(ac, None), (dv, None), (st, None)], incs=[(pe, 1)])
                    S.op("vector", lambda e: e.tensor_copy(
                        nrow512[:, :], pbx[0:1, :]),
                        waits=[(pe, None)], incs=[(dv, 1)])
                    S.op("gpsimd", lambda e, head=head, tb=tb, nr=nr: e.dma_start(
                        out=nr[head:head + 1, tb * 512:(tb + 1) * 512],
                        in_=nrow512[:, :]),
                        waits=[(dv, None)], incs=[(st, 16)])
    for g in range(2):
        wb = wtm3[g % 2]
        for kc in range(8):
            S.op("gpsimd", lambda e, kc=kc, g=g, wb=wb: e.dma_start(
                out=wb[:, kc * 512:(kc + 1) * 512],
                in_=dd["wv3d"][kc * 128:(kc + 1) * 128, g * 512:g * 512 + 512]),
                waits=([(pe, None), (ac, None)] if kc == 0 else []),
                incs=[(ldg, 16)])
        for t in range(NT):
            pbuf = p512a if t % 2 == 0 else p512b
            pairs = [(xaT[kc][:, t * 128:(t + 1) * 128],
                      wb[:, kc * 512:(kc + 1) * 512]) for kc in range(8)]
            mm_group(S, ctx, pbuf[:, :], pairs,
                     waits_first=[(ldg, None), (ac, None), (dv, None)],
                     stop_last=False)
            S.op("tensor", lambda e, g=g, pbuf=pbuf: e.matmul(
                pbuf[:, :], ctx.ones_rb[:, :], bv3s[:, g * 512:(g + 1) * 512],
                start=False, stop=True), incs=[(pe, 1)])
            S.op("scalar", lambda e, t=t, pbuf=pbuf: e.activation(
                stage3[2 + t % 2][:, :], pbuf[:, :], AF.Copy),
                waits=[(pe, None), (st, None)], incs=[(ac, 1)])
            S.op("gpsimd", lambda e, t=t, g=g: e.dma_start(
                out=dd["d_v3_c"][t * 128:(t + 1) * 128, g * 512:(g + 1) * 512],
                in_=stage3[2 + t % 2][:, :]),
                waits=[(ac, None)], incs=[(st, 16)])
    for (ci, gi) in ((dd["d_kt3_c"], dd["d_kt3"]), (dd["d_v3_c"], dd["d_v3"]),
                     (dd["d_knn_c"], dd["d_knn"])):
        S.op("gpsimd", lambda e, ci=ci, gi=gi: e.collective_compute(
            "AllGather", ALU.bypass, replica_groups=RG,
            ins=[ci.ap()], outs=[gi.ap()]),
            waits=[(st, None), (ldg, None)], incs=[(cc, 1)])
    S.emit()


def phase5_l1(nc, ctx, cc, ar, dd, psum, scr):
    ld, ldg, st, pe, dv, ac = ctx.ld, ctx.ldg, ctx.st, ctx.pe, ctx.dv, ctx.ac
    (p512a, p512b, p512c, p512d, po, pbx, ptpb, ptpf) = psum
    pC = pbx
    d_qT3, d_kt3, d_v3 = dd["d_qT3"], dd["d_kt3"], dd["d_v3"]
    d_qnn, d_knn, d_x2, d_x3 = dd["d_qnn"], dd["d_knn"], dd["d_x2"], dd["d_x3"]
    o_x = dd["o_x"]

    ar.reset()
    qhat2 = ar.alloc(TOK, BF16)
    khat2 = ar.alloc(NSEQ, BF16)
    vhat = ar.alloc(NKT * 65, BF16)
    Mrun = ar.alloc(512, F32)
    lbuf = ar.alloc(512, F32)
    rsd = [ar.alloc(512, F32) for _ in range(2)]
    ubuf2 = [ar.alloc(512, F32) for _ in range(2)]
    ptile = [ar.alloc(512, BF16) for _ in range(2)]
    bcb = ar.alloc(512, F32)[0:64, :]
    rs_row = ar.alloc(512, F32)[0:1, :]
    crow = ar.alloc(512, F32)[0:1, :]
    fence = ar.alloc(1, F32)[0:1, :]
    ccol = ar.alloc(1, F32)
    hls = ar.alloc(1, F32)
    asb = ar.alloc(1, F32)[0:1, :]
    tst_off = ar.off
    tst = ar.alloc(16384, F32)
    attn_sb = [ar.alloc(TOK, BF16) for _ in range(8)]
    bout = ar.alloc(D, BF16)[0:1, :]
    ytile = ar.alloc(D, F32)
    x3t = ar.alloc(D, F32)
    sxt = [ar.alloc(D, F32) for _ in range(2)]
    sbf = ar.alloc(D, BF16)
    g7 = ar.alloc(7, F32)
    g7t = ar.alloc(7, F32)
    gred = ar.alloc(6, F32)
    wscol = ar.alloc(NT, F32)
    shtmp = ar.alloc(512, F32)
    s4t = [ar.at(j * 4096, TOK, BF16) for j in range(8)]
    wout = [ar.at(tst_off + c * 2048, D, BF16) for c in range(8)]
    wst = [ar.at(tst_off + 16384 + c * 2048, D, BF16) for c in range(8)]
    wet = [ar.at(tst_off + 32768 + c * 2048, D, BF16) for c in range(8)]
    wg_sb = ar.at(tst_off + 49152, 128, BF16)
    b4 = ar.at(tst_off + 49536, D, BF16)[0:1, :]

    S = Script(nc)
    S.op("sync", None, waits=[(cc, None), (st, None)])
    S.op("gpsimd", None, waits=[(cc, None), (st, None)])
    S.op("sync", lambda e: e.dma_start(out=asb[:, :], in_=dd["alphad"][:, :]),
         incs=[(ld, 16)])
    S.op("vector", lambda e: e.tensor_scalar_mul(asb[:, :], asb[:, :],
                                                 0.5 * LNBASE),
         waits=[(ld, None)], incs=[(dv, 1)])
    S.op("tensor", lambda e: e.matmul(pC[:, 0:1], ctx.ones_r[0:1, :], asb[:, :],
                                      start=True, stop=True),
         waits=[(dv, None)], incs=[(pe, 1)])
    S.op("vector", lambda e: e.tensor_copy(hls[:, :], pC[:, 0:1]),
         waits=[(pe, None)], incs=[(dv, 1)])
    S.op("vector", lambda e: e.memset(fence[:, :], 0.0), incs=[(dv, 1)])
    S.emit()

    for h in range(H):
        S = Script(nc)
        hw = 64 * (h % 2)
        S.op("sync", lambda e, h=h: e.dma_start(
            out=qhat2[0:64, :], in_=d_qT3[h * 64:(h + 1) * 64, :]),
            waits=[(pe, None), (ac, None), (dv, None)], incs=[(ld, 16)])
        for blk_ in range(2):
            S.op("sync", lambda e, h=h, blk_=blk_: e.dma_start(
                out=khat2[0:64, blk_ * TOK:(blk_ + 1) * TOK],
                in_=d_kt3[blk_ * D + h * 64:blk_ * D + (h + 1) * 64, :]),
                incs=[(ld, 16)])
        if h == 0:
            S.op("vector", lambda e: e.memset(
                vhat[:, :].rearrange("p (kt c) -> p kt c", c=65)[:, :, 64:65],
                1.0),
                incs=[(dv, 1)])
        S.op("sync", lambda e, h=h: e.dma_start(
            out=vhat[:, :].rearrange("p (kt c) -> p kt c", c=65)[:, :, 0:64],
            in_=d_v3.ap().rearrange("(kt p) d -> p kt d", p=128)[:, :, h * 64:(h + 1) * 64]),
            waits=[(dv, None)], incs=[(ld, 16)])
        S.op("vector", lambda e: e.memset(qhat2[64:66, :], 1.0), incs=[(dv, 1)])
        S.op("vector", lambda e: e.memset(khat2[64:66, :], 1.0), incs=[(dv, 1)])
        S.op("gpsimd", lambda e, h=h: e.dma_start(
            out=qhat2[64:65, :], in_=d_qnn[h:h + 1, :]),
            waits=[(dv, None)], incs=[(ldg, 16)])
        for blk_ in range(2):
            S.op("gpsimd", lambda e, h=h, blk_=blk_: e.dma_start(
                out=khat2[65:66, blk_ * TOK:(blk_ + 1) * TOK],
                in_=d_knn[blk_ * H + h:blk_ * H + h + 1, :]),
                waits=([(dv, None)] if blk_ == 0 else []), incs=[(ldg, 16)])

        for qb in range(4):
            qsl = slice(qb * 512, (qb + 1) * 512)

            def p1_mms(kt):
                pd = p512a if kt % 2 == 0 else p512b
                pn = p512c if kt % 2 == 0 else p512d
                S.op("tensor", lambda e, kt=kt, pd=pd, qsl=qsl: e.matmul(
                    pd[:, :], khat2[0:64, kt * 128:(kt + 1) * 128],
                    qhat2[0:64, qsl], start=True, stop=True),
                    waits=[(ld, None), (ldg, None), (dv, None)], incs=[(pe, 1)])
                S.op("tensor", lambda e, kt=kt, pn=pn, qsl=qsl: e.matmul(
                    pn[:, :], khat2[0:66, kt * 128:(kt + 1) * 128],
                    qhat2[0:66, qsl], start=True, stop=True),
                    incs=[(pe, 1)])

            mrun_guard = pe.v

            def mrun_upd(kt, certify):
                if kt == 0:
                    S.op("vector", lambda e: e.tensor_copy(
                        Mrun[:, :], tst[:, 0:512]),
                        waits=[(ac, certify), (pe, mrun_guard)],
                        incs=[(dv, 1)])
                else:
                    S.op("vector", lambda e, kt=kt: e.tensor_max(
                        Mrun[:, :], Mrun[:, :], tst[:, kt * 512:(kt + 1) * 512]),
                        waits=[(ac, certify)], incs=[(dv, 1)])

            def p1_ew(kt):
                pd = p512a if kt % 2 == 0 else p512b
                pn = p512c if kt % 2 == 0 else p512d
                S.op("scalar", lambda e, pn=pn: e.activation(
                    lbuf[:, :], pn[:, :], AF.Ln, bias=ctx.constcol(EPS),
                    scale=-2.0),
                    waits=[(pe, None), (dv, None)], incs=[(ac, 1)])
                if kt > 0:
                    mrun_upd(kt - 1, ac.v)
                S.op("scalar", lambda e, kt=kt: e.activation(
                    rsd[kt % 2][:, :], lbuf[:, :], AF.Exp, bias=hls[:, :],
                    scale=-0.5),
                    waits=[(ac, None), (dv, None)], incs=[(ac, 1)])
                S.op("vector", lambda e, kt=kt, pd=pd: e.tensor_tensor(
                    ubuf2[kt % 2][:, :], pd[:, :], rsd[kt % 2][:, :], ALU.mult),
                    waits=[(ac, None)], incs=[(dv, 1)])
                S.op("scalar", lambda e, kt=kt: e.activation(
                    tst[:, kt * 512:(kt + 1) * 512], ubuf2[kt % 2][:, :],
                    AF.Square),
                    waits=[(dv, None)], incs=[(ac, 1)])

            p1_mms(0)
            for kt in range(1, NKT):
                p1_mms(kt)
                p1_ew(kt - 1)
            p1_ew(NKT - 1)
            S.op("scalar", lambda e: e.activation(fence[:, :], fence[:, :],
                                                  AF.Square),
                 waits=[(ac, None)], incs=[(ac, 1)])
            mrun_upd(NKT - 1, ac.v)
            for j in range(4):
                S.op("tensor", lambda e, j=j: e.transpose(
                    ptpf[:, :], Mrun[:, j * 128:(j + 1) * 128], ctx.idf[:, :]),
                    waits=[(dv, None)], incs=[(pe, 1)])
                S.op("vector", lambda e: e.reduce_max(
                    ccol[:, :], ptpf[:, :], axis=AX.X),
                    waits=[(pe, None)], incs=[(dv, 1)])
                S.op("tensor", lambda e: e.transpose(
                    ptpf[0:1, 0:128], ccol[:, :], ctx.idf[:, :]),
                    waits=[(dv, None)], incs=[(pe, 1)])
                S.op("vector", lambda e, j=j: e.tensor_copy(
                    crow[:, j * 128:(j + 1) * 128], ptpf[0:1, 0:128]),
                    waits=[(pe, None)], incs=[(dv, 1)])
            S.op("tensor", lambda e: e.matmul(
                pC[:, :], ctx.ones_r[0:1, :], crow[:, :], start=True, stop=True),
                waits=[(dv, None)], incs=[(pe, 1)])
            pC_cnt = pe.v
            p1_ac = ac.v
            qb_guard = dv.v

            subcnt, expcnt, pvcnt = {}, {}, {}

            def p2_sub(kt):
                S.op("vector", lambda e, kt=kt: e.tensor_sub(
                    tst[:, kt * 512:(kt + 1) * 512],
                    tst[:, kt * 512:(kt + 1) * 512], pC[:, :]),
                    waits=[(ac, p1_ac), (pe, pC_cnt)],
                    incs=[(dv, 1)])
                subcnt[kt] = dv.v

            def p2_fence_dv():
                S.op("vector", lambda e: e.memset(fence[:, :], 0.0),
                     incs=[(dv, 1)])
                return dv.v

            def p2_exp(kt, certify):
                w = [(dv, certify)]
                if kt - 2 >= 0:
                    w.append((pe, pvcnt[kt - 2]))
                S.op("scalar", lambda e, kt=kt: e.activation(
                    ptile[kt % 2][:, :], tst[:, kt * 512:(kt + 1) * 512], AF.Exp),
                    waits=w, incs=[(ac, 1)])
                expcnt[kt] = ac.v

            def p2_fence_ac():
                S.op("scalar", lambda e: e.activation(fence[:, :], fence[:, :],
                                                      AF.Square),
                     incs=[(ac, 1)])
                return ac.v

            def p2_pv(kt, certify):
                w = [(ac, certify)]
                if kt == 0:
                    w.append((dv, qb_guard))
                S.op("tensor", lambda e, kt=kt: e.matmul(
                    po[:, :], vhat[:, kt * 65:(kt + 1) * 65], ptile[kt % 2][:, :],
                    start=(kt == 0), stop=(kt == NKT - 1)),
                    waits=w, incs=[(pe, 1)])
                pvcnt[kt] = pe.v

            p2_sub(0)
            p2_sub(1)
            p2_exp(0, subcnt[1])
            for kt in range(2, NKT):
                p2_sub(kt)
                p2_exp(kt - 1, subcnt[kt])
                p2_pv(kt - 2, expcnt[kt - 1])
            fdv = p2_fence_dv()
            p2_exp(NKT - 1, fdv)
            p2_pv(NKT - 2, expcnt[NKT - 1])
            fac = p2_fence_ac()
            p2_pv(NKT - 1, fac)
            S.op("vector", lambda e: e.reciprocal(rs_row[:, :], po[64:65, :]),
                 waits=[(pe, None)], incs=[(dv, 1)])
            S.op("tensor", lambda e: e.matmul(
                pC[0:64, :], ctx.ones_r[0:1, 0:64], rs_row[:, :],
                start=True, stop=True),
                waits=[(dv, None)], incs=[(pe, 1)])
            S.op("vector", lambda e: e.tensor_copy(bcb[:, :], pC[0:64, :]),
                 waits=[(pe, None)], incs=[(dv, 1)])
            S.op("vector", lambda e, h=h, qb=qb, hw=hw: e.tensor_tensor(
                attn_sb[h // 2][hw:hw + 64, qb * 512:(qb + 1) * 512],
                po[0:64, :], bcb[:, :], ALU.mult),
                waits=[(dv, None)], incs=[(dv, 1)])
        S.emit()

    S = Script(nc)
    for c in range(8):
        S.op("gpsimd", lambda e, c=c: e.dma_start(
            out=wout[c][:, :], in_=dd["wo4d"][c * 128:(c + 1) * 128, :]),
            waits=([(dv, None)] if c == 0 else []), incs=[(ldg, 16)])
    S.op("gpsimd", lambda e: e.dma_start(out=bout[:, :], in_=dd["bo4d"][:, :]),
         incs=[(ldg, 16)])
    for t in range(NT):
        tb = t % 2
        S.op("sync", lambda e, t=t, tb=tb: e.dma_start(
            out=sxt[tb][:, :], in_=d_x2[t * 128:(t + 1) * 128, :]),
            waits=[(dv, None)], incs=[(ld, 16)])
        for g in range(2):
            pbuf = p512a if g == 0 else p512b
            pairs = [(attn_sb[c][:, t * 128:(t + 1) * 128],
                      wout[c][:, g * 512:(g + 1) * 512]) for c in range(8)]
            mm_group(S, ctx, pbuf[:, :], pairs,
                     waits_first=[(ldg, None), (ac, None)], stop_last=False)
            S.op("tensor", lambda e, g=g, pbuf=pbuf: e.matmul(
                pbuf[:, :], ctx.ones_rb[:, :], bout[:, g * 512:(g + 1) * 512],
                start=False, stop=True), incs=[(pe, 1)])
            S.op("scalar", lambda e, g=g, pbuf=pbuf: e.activation(
                ytile[:, g * 512:(g + 1) * 512], pbuf[:, :], AF.Copy),
                waits=[(pe, None)], incs=[(ac, 1)])
        emit_ln(ctx, S, ytile[:, :], ytile[:, :], scr, x_waits=[(ac, None)])
        S.op("vector", lambda e, tb=tb: e.tensor_tensor(
            x3t[:, :], sxt[tb][:, :], ytile[:, :], ALU.add),
            waits=[(ld, None), (dv, None), (st, None)], incs=[(dv, 1)])
        S.op("gpsimd", lambda e, t=t: e.dma_start(
            out=d_x3[t * 128:(t + 1) * 128, :], in_=x3t[:, :]),
            waits=[(dv, None)], incs=[(st, 16)])
        emit_ln(ctx, S, x3t[:, :], sbf[:, :], scr, x_waits=[(dv, None)])
        for c in range(8):
            emit_tp(ctx, S, sbf[:, c * 128:(c + 1) * 128],
                    s4t[c][:, t * 128:(t + 1) * 128], ptpb)
    S.emit()

    S = Script(nc)
    for c in range(8):
        S.op("gpsimd", lambda e, c=c: e.dma_start(
            out=wst[c][:, :], in_=dd["wstd"][c * 128:(c + 1) * 128, :]),
            incs=[(ldg, 16)])
        S.op("gpsimd", lambda e, c=c: e.dma_start(
            out=wet[c][:, :], in_=dd["wetd"][c * 128:(c + 1) * 128, :]),
            incs=[(ldg, 16)])
        S.op("gpsimd", lambda e, c=c: e.dma_start(
            out=wg_sb[:, c * 16:c * 16 + 7],
            in_=dd["wgd"][c * 128:(c + 1) * 128, :]), incs=[(ldg, 16)])
    S.op("gpsimd", lambda e: e.dma_start(out=b4[:, :], in_=dd["bias4d"][:, :]),
         incs=[(ldg, 16)])
    for t in range(NT):
        pairs = [(s4t[kc][:, t * 128:(t + 1) * 128], wg_sb[:, kc * 16:kc * 16 + 7])
                 for kc in range(8)]
        mm_group(S, ctx, p512c[:, 0:7], pairs,
                 waits_first=[(ldg, None), (dv, None)])
        S.op("vector", lambda e: e.reduce_max(gred[:, 0:1], p512c[:, 0:7],
                                              axis=AX.X, negate=True),
             waits=[(pe, None)], incs=[(dv, 1)])
        S.op("scalar", lambda e: e.activation(
            g7[:, :], p512c[:, 0:7], AF.Exp, bias=gred[:, 0:1],
            accum_out=gred[:, 1:2]),
            waits=[(dv, None)], incs=[(ac, 1)])
        S.op("vector", lambda e: e.reciprocal(gred[:, 2:3], gred[:, 1:2]),
             waits=[(ac, None)], incs=[(dv, 1)])
        S.op("vector", lambda e: e.tensor_scalar_mul(g7[:, :], g7[:, :],
                                                     gred[:, 2:3]),
             waits=[(dv, None)], incs=[(dv, 1)])
        S.op("vector", lambda e: e.reduce_max(gred[:, 2:3], g7[:, :], axis=AX.X),
             waits=[(dv, None)], incs=[(dv, 1)])
        S.op("vector", lambda e: e.tensor_scalar(g7t[:, :], g7[:, :],
                                                 gred[:, 2:3], None, ALU.is_equal),
             waits=[(dv, None)], incs=[(dv, 1)])
        S.op("vector", lambda e: e.scalar_tensor_tensor(
            g7t[:, :], g7t[:, :], -1e30, g7[:, :], ALU.mult, ALU.add),
            waits=[(dv, None)], incs=[(dv, 1)])
        S.op("vector", lambda e: e.reduce_max(gred[:, 3:4], g7t[:, :], axis=AX.X),
             waits=[(dv, None)], incs=[(dv, 1)])
        S.op("vector", lambda e: e.tensor_tensor(gred[:, 4:5], gred[:, 2:3],
                                                 gred[:, 3:4], ALU.add),
             waits=[(dv, None)], incs=[(dv, 1)])
        S.op("vector", lambda e: e.reciprocal(gred[:, 5:6], gred[:, 4:5]),
             waits=[(dv, None)], incs=[(dv, 1)])
        S.op("vector", lambda e: e.tensor_tensor(gred[:, 4:5], gred[:, 2:3],
                                                 gred[:, 5:6], ALU.mult),
             waits=[(dv, None)], incs=[(dv, 1)])
        S.op("vector", lambda e: e.tensor_tensor(gred[:, 5:6], gred[:, 3:4],
                                                 gred[:, 5:6], ALU.mult),
             waits=[(dv, None)], incs=[(dv, 1)])
        S.op("vector", lambda e, t=t: e.tensor_tensor(
            wscol[:, t:t + 1], gred[:, 4:5], gred[:, 5:6], ALU.add),
            waits=[(dv, None)], incs=[(dv, 1)])
    S.op("sync", None, waits=[(st, None)])
    for t in range(NT):
        tb = t % 2
        S.op("sync", lambda e, t=t, tb=tb: e.dma_start(
            out=sxt[tb][:, :], in_=d_x3[t * 128:(t + 1) * 128, :]),
            waits=[(dv, None)], incs=[(ld, 16)])
        for g in range(2):
            pe_buf = p512a if g == 0 else p512b
            ps_buf = p512c if g == 0 else p512d
            pairs_e = [(s4t[kc][:, t * 128:(t + 1) * 128],
                        wet[kc][:, g * 512:(g + 1) * 512]) for kc in range(8)]
            mm_group(S, ctx, pe_buf[:, :], pairs_e,
                     waits_first=[(ldg, None), (dv, None)])
            pairs_s = [(s4t[kc][:, t * 128:(t + 1) * 128],
                        wst[kc][:, g * 512:(g + 1) * 512]) for kc in range(8)]
            mm_group(S, ctx, ps_buf[:, :], pairs_s,
                     waits_first=[(ac, None)], stop_last=False)
            S.op("tensor", lambda e, g=g, ps_buf=ps_buf: e.matmul(
                ps_buf[:, :], ctx.ones_rb[:, :], b4[:, g * 512:(g + 1) * 512],
                start=False, stop=True), incs=[(pe, 1)])
            S.op("scalar", lambda e, g=g, ps_buf=ps_buf: e.activation(
                shtmp[:, :], ps_buf[:, :], AF.Copy),
                waits=[(pe, None), (dv, None)], incs=[(ac, 1)])
            S.op("vector", lambda e, g=g, t=t, pe_buf=pe_buf: e.scalar_tensor_tensor(
                ytile[:, g * 512:(g + 1) * 512], pe_buf[:, :],
                wscol[:, t:t + 1], shtmp[:, :], ALU.mult, ALU.add),
                waits=[(ac, None)], incs=[(dv, 1)])
        emit_ln(ctx, S, ytile[:, :], ytile[:, :], scr, x_waits=[(dv, None)])
        S.op("vector", lambda e, tb=tb: e.tensor_tensor(
            x3t[:, :], sxt[tb][:, :], ytile[:, :], ALU.add),
            waits=[(ld, None), (dv, None), (st, None)], incs=[(dv, 1)])
        S.op("gpsimd", lambda e, t=t: e.dma_start(
            out=o_x[t * 128:(t + 1) * 128, :], in_=x3t[:, :]),
            waits=[(dv, None)], incs=[(st, 16)])
    S.op("gpsimd", None, waits=[(st, None), (ldg, None), (ld, None)])
    S.emit()


def build_full():
    nc, ctx, hd = build_fused()
    dd = hd["dram"]
    masks = (hd["tri"], hd["trit"], hd["lowt"], hd["low01"], hd["id8"])
    phase1_l0a(nc, ctx, hd["cc"], hd["ar"], dd, masks, hd["psum"], hd["flg"])
    phase2_outproj(nc, ctx, hd["cc"], hd["ar"], dd, hd["psum"], hd["scr"],
                   hd["flg"])
    phase3_l0f(nc, ctx, hd["cc"], hd["ar"], dd, hd["psum"], hd["scr"])
    phase4_moe(nc, ctx, hd["cc"], hd["ar"], dd, hd["psum"], hd["scr"])
    phase5_l1(nc, ctx, hd["cc"], hd["ar"], dd, hd["psum"], hd["scr"])
    return nc, ctx


def make_in_maps(kw, cores):
    f32 = np.float32
    X = np.asarray(kw["L0_attn_X"], f32)

    def row(v):
        return np.ascontiguousarray(np.asarray(v, f32).reshape(1, -1))

    def col(v):
        return np.ascontiguousarray(np.asarray(v, f32).reshape(-1, 1))

    w_kv = np.ascontiguousarray(np.asarray(kw["L0_attn_w_kvkv"], f32)[:, :2 * D])
    ropeQc, ropeQs = rope_tables(np.arange(NSEQ), 0.125)
    ropeQc = np.ascontiguousarray(ropeQc[:, :HD])
    ropeQs = np.ascontiguousarray(ropeQs[:, :HD])
    ropeK = [rope_tables(np.arange(h * TOK, (h + 1) * TOK), 1.0)
             for h in range(2)]
    shared = dict(
        w_qkv=np.asarray(kw["L0_attn_w_qkv"], f32), w_kv=w_kv,
        ropecq=ropeQc, ropesq=ropeQs,
        w_out=np.asarray(kw["L0_attn_w_out"], f32),
        b_out=row(kw["L0_attn_b_out"]),
        wq2=np.asarray(kw["L0_ffn_Wq"], f32), wk2=np.asarray(kw["L0_ffn_Wk"], f32),
        wv2=np.asarray(kw["L0_ffn_Wv"], f32),
        bq2c=col(kw["L0_ffn_bq"]), bk2c=col(kw["L0_ffn_bk"]),
        bv2=row(kw["L0_ffn_bv"]),
        wo2=np.asarray(kw["L0_ffn_Wo"], f32), bo2=row(kw["L0_ffn_bo"]),
        gW=np.asarray(kw["L0_ffn_gW"], f32), gb=row(kw["L0_ffn_gb"]),
        W1=np.asarray(kw["L0_ffn_W1"], f32),
        b1T=np.ascontiguousarray(np.asarray(kw["L0_ffn_b1"], f32).T),
        W2=np.asarray(kw["L0_ffn_W2"], f32), b2=np.asarray(kw["L0_ffn_b2"], f32),
        ln1g=row(kw["L0_ffn_ln1_g"]), ln1b=row(kw["L0_ffn_ln1_b"]),
        ln2g=row(kw["L0_ffn_ln2_g"]), ln2b=row(kw["L0_ffn_ln2_b"]),
        wq3=np.asarray(kw["L1_attn_wq"], f32), wk3=np.asarray(kw["L1_attn_wk"], f32),
        wv3=np.asarray(kw["L1_attn_wv"], f32),
        bq3c=col(kw["L1_attn_bq"]), bk3c=col(kw["L1_attn_bk"]),
        bv3=row(kw["L1_attn_bv"]),
        wo4=np.asarray(kw["L1_attn_wo"], f32), bo4=row(kw["L1_attn_bo"]),
        alpha=np.asarray(kw["L1_attn_alpha"], f32).reshape(1, 1),
        WsT=np.ascontiguousarray(np.asarray(kw["L1_ffn_W_shared"], f32).T),
        WeT=np.ascontiguousarray(np.asarray(kw["L1_ffn_W_expert"], f32).T),
        Wg=np.asarray(kw["L1_ffn_W_gate"], f32),
        bias4=row(kw["L1_ffn_bias"]))
    flags_h = []
    for h in range(2):
        fl = np.zeros((128, 2), f32)
        fl[:, h] = 1.0
        flags_h.append(fl)
    in_maps = []
    for c in cores:
        b, h = c // 2, c % 2
        m = dict(shared)
        m["xs"] = np.ascontiguousarray(X[b, h * TOK:(h + 1) * TOK])
        m["ropec"] = ropeK[h][0]
        m["ropes"] = ropeK[h][1]
        m["flags"] = flags_h[h]
        in_maps.append(m)
    return in_maps


from concourse.bass_utils import run_bass_kernel_spmd

_FPROG = {}


def kernel(L0_attn_X, L0_attn_w_qkv, L0_attn_w_kvkv, L0_attn_w_out,
           L0_attn_b_out, L0_ffn_Wq, L0_ffn_bq, L0_ffn_Wk, L0_ffn_bk,
           L0_ffn_Wv, L0_ffn_bv, L0_ffn_Wo, L0_ffn_bo, L0_ffn_gW, L0_ffn_gb,
           L0_ffn_W1, L0_ffn_b1, L0_ffn_W2, L0_ffn_b2, L0_ffn_ln1_g,
           L0_ffn_ln1_b, L0_ffn_ln2_g, L0_ffn_ln2_b, L1_attn_wq, L1_attn_bq,
           L1_attn_wk, L1_attn_bk, L1_attn_wv, L1_attn_bv, L1_attn_wo,
           L1_attn_bo, L1_attn_alpha, L1_ffn_cond, L1_ffn_mask,
           L1_ffn_W_shared, L1_ffn_W_expert, L1_ffn_W_gate, L1_ffn_bias):
    kw = dict(
        L0_attn_X=L0_attn_X, L0_attn_w_qkv=L0_attn_w_qkv,
        L0_attn_w_kvkv=L0_attn_w_kvkv, L0_attn_w_out=L0_attn_w_out,
        L0_attn_b_out=L0_attn_b_out, L0_ffn_Wq=L0_ffn_Wq, L0_ffn_bq=L0_ffn_bq,
        L0_ffn_Wk=L0_ffn_Wk, L0_ffn_bk=L0_ffn_bk, L0_ffn_Wv=L0_ffn_Wv,
        L0_ffn_bv=L0_ffn_bv, L0_ffn_Wo=L0_ffn_Wo, L0_ffn_bo=L0_ffn_bo,
        L0_ffn_gW=L0_ffn_gW, L0_ffn_gb=L0_ffn_gb, L0_ffn_W1=L0_ffn_W1,
        L0_ffn_b1=L0_ffn_b1, L0_ffn_W2=L0_ffn_W2, L0_ffn_b2=L0_ffn_b2,
        L0_ffn_ln1_g=L0_ffn_ln1_g, L0_ffn_ln1_b=L0_ffn_ln1_b,
        L0_ffn_ln2_g=L0_ffn_ln2_g, L0_ffn_ln2_b=L0_ffn_ln2_b,
        L1_attn_wq=L1_attn_wq, L1_attn_bq=L1_attn_bq, L1_attn_wk=L1_attn_wk,
        L1_attn_bk=L1_attn_bk, L1_attn_wv=L1_attn_wv, L1_attn_bv=L1_attn_bv,
        L1_attn_wo=L1_attn_wo, L1_attn_bo=L1_attn_bo,
        L1_attn_alpha=L1_attn_alpha,
        L1_ffn_W_shared=L1_ffn_W_shared, L1_ffn_W_expert=L1_ffn_W_expert,
        L1_ffn_W_gate=L1_ffn_W_gate, L1_ffn_bias=L1_ffn_bias)
    if "full" not in _FPROG:
        _FPROG["full"] = build_full()
    nc, ctx = _FPROG["full"]
    cores = list(range(8))
    in_maps = make_in_maps(kw, cores)
    res = run_bass_kernel_spmd(nc, in_maps, core_ids=cores).results
    B = np.asarray(L0_attn_X).shape[0]
    out = np.zeros((B, NSEQ, D), np.float32)
    for c in cores:
        b, h = c // 2, c % 2
        out[b, h * TOK:(h + 1) * TOK] = res[c]["xout"]
    return out


# version 08 (final)
# speedup vs baseline: 1.0056x; 1.0056x over previous
import numpy as np
import ml_dtypes
import concourse.bass as bass
import concourse.mybir as mybir

F32 = mybir.dt.float32
BF16 = mybir.dt.bfloat16
AF = mybir.ActivationFunctionType
ALU = mybir.AluOpType
AX = mybir.AxisListType
BF = ml_dtypes.bfloat16

D = 1024
H = 16
HD = 64
TOK = 2048
NT = TOK // 128
NSEQ = 4096
NKT = NSEQ // 128
NEG = -30000.0
EPS = 1e-5


def own_rows(half):
    return np.r_[0:1024, 3072:4096] if half == 0 else np.r_[1024:3072]


def own_chunks(half):
    return (list(range(0, 8)) + list(range(24, 32))) if half == 0 else list(range(8, 24))


class Sem:
    def __init__(self, nc, name):
        self.h = nc.alloc_semaphore(name)
        self.v = 0
        nc._sem_objs = getattr(nc, "_sem_objs", [])
        nc._sem_objs.append(self)

    def bump(self, n):
        self.v += n
        return self.v


class Script:
    ENGINES = ("sync", "scalar", "vector", "tensor", "gpsimd")

    def __init__(self, nc):
        self.nc = nc
        self.items = []

    def op(self, engine, fn, waits=(), incs=()):
        rw = [(s, s.v if v is None else v) for (s, v) in waits]
        ri = []
        for (s, n) in incs:
            s.bump(n)
            ri.append((s, n))
        self.items.append((engine, fn, rw, ri))

    OWN_SEM = {"vector": "s_dv", "scalar": "s_ac", "tensor": "s_pe"}
    COMPUTE_SEMS = ("s_dv", "s_ac", "s_pe")

    def emit(self):
        nc = self.nc
        tr = getattr(nc, "_wait_track", None)
        if tr is None:
            tr = nc._wait_track = {e: {} for e in self.ENGINES}
        with nc.Block() as blk:
            for eng_name in self.ENGINES:
                mine = [it for it in self.items if it[0] == eng_name]
                if not mine:
                    continue

                def body(eng, mine=mine, eng_name=eng_name):
                    dedup = eng_name in self.OWN_SEM
                    known = tr[eng_name]
                    for (_, fn, waits, incs) in mine:
                        for (s, v) in waits:
                            if v <= 0:
                                continue
                            nm = s.h.name
                            if dedup and known.get(nm, 0) >= v:
                                continue
                            eng.wait_ge(s.h, v)
                            known[nm] = max(known.get(nm, 0), v)
                        if fn is None:
                            assert not incs
                            continue
                        inst = fn(eng)
                        for (s, n) in incs:
                            inst.then_inc(s.h, n)
                getattr(blk, eng_name)(body)
        sems = getattr(nc, "_sem_objs", [])
        cross = {}
        for e in self.ENGINES:
            for nm, v in tr[e].items():
                cross[nm] = max(cross.get(nm, 0), v)
        for s in sems:
            if s.h.name in self.COMPUTE_SEMS:
                cross[s.h.name] = max(cross.get(s.h.name, 0), s.v)
        for e in self.ENGINES:
            tr[e].update(cross)
        self.items = []


class Ctx:
    def __init__(self, nc):
        self.nc = nc
        self.ld = Sem(nc, "s_ld")
        self.ldg = Sem(nc, "s_ldg")
        self.lo = Sem(nc, "s_lo")
        self.st = Sem(nc, "s_st")
        self.pe = Sem(nc, "s_pe")
        self.dv = Sem(nc, "s_dv")
        self.ac = Sem(nc, "s_ac")
        self._consts = {}
        self._ctile = nc.alloc_sbuf_tensor("constcols", [128, 24], F32)
        self._cn = 0
        self.idb = nc.alloc_sbuf_tensor("identb", [128, 128], BF16)
        self.idf = nc.alloc_sbuf_tensor("identf", [128, 128], F32)
        self.ones_r = nc.alloc_sbuf_tensor("ones_r", [1, 128], F32)
        self.ones_rb = nc.alloc_sbuf_tensor("ones_rb", [1, 128], BF16)
        self._id_src = nc.inline_tensor(np.eye(128, dtype=BF).view(np.uint16),
                                        name="idb_src")
        self._idf_src = nc.inline_tensor(np.eye(128, dtype=np.float32),
                                         name="idf_src")

    def constcol(self, value):
        value = float(value)
        if value not in self._consts:
            assert self._cn < 24, "const pool full"
            self._consts[value] = self._cn
            self._cn += 1
        return self._ctile[:, self._consts[value]:self._consts[value] + 1]

    def emit_init(self, S, const_values=()):
        for v in const_values:
            self.constcol(v)
        for v, i in list(self._consts.items()):
            S.op("vector", lambda e, i=i, v=v: e.memset(self._ctile[:, i:i + 1], v),
                 incs=[(self.dv, 1)])
        S.op("vector", lambda e: e.memset(self.ones_r[:, :], 1.0), incs=[(self.dv, 1)])
        S.op("vector", lambda e: e.memset(self.ones_rb[:, :], 1.0), incs=[(self.dv, 1)])
        S.op("sync", lambda e: e.dma_start(out=self.idb[:, :],
                                           in_=self._id_src.ap().bitcast(BF16)),
             incs=[(self.ld, 16)])
        S.op("sync", lambda e: e.dma_start(out=self.idf[:, :], in_=self._idf_src.ap()),
             incs=[(self.ld, 16)])


def mm_group(S, ctx, psum_ap, pairs, waits_first=(), stop_last=True):
    n = len(pairs)
    for i, (l, r) in enumerate(pairs):
        stop = (i == n - 1) and stop_last
        S.op("tensor",
             lambda e, l=l, r=r, i=i, stop=stop: e.matmul(
                 psum_ap, l, r, start=(i == 0), stop=stop),
             waits=(list(waits_first) if i == 0 else []),
             incs=([(ctx.pe, 1)] if i == n - 1 else []))


def emit_ln(ctx, S, x_ap, out_ap, scr, x_waits, gain_bias=None):
    dv, ac = ctx.dv, ctx.ac
    sq, c0, c1, c2, c3 = scr["sq"], scr["c0"], scr["c1"], scr["c2"], scr["c3"]
    S.op("scalar", lambda e: e.activation(sq, x_ap, AF.Square, accum_out=c1),
         waits=list(x_waits) + [(dv, None)], incs=[(ac, 1)])
    S.op("vector", lambda e: e.reduce_sum(c0, x_ap, axis=AX.X),
         waits=list(x_waits) + [(dv, None)], incs=[(dv, 1)])
    S.op("vector", None, waits=[(ac, None)])
    S.op("vector", lambda e: e.tensor_scalar_mul(c0, c0, -1.0 / D),
         waits=[(dv, None)], incs=[(dv, 1)])
    S.op("vector", lambda e: e.tensor_scalar_mul(c1, c1, 1.0 / D),
         waits=[(dv, None)], incs=[(dv, 1)])
    S.op("vector", lambda e: e.tensor_tensor(c2, c0, c0, ALU.mult),
         waits=[(dv, None)], incs=[(dv, 1)])
    S.op("vector", lambda e: e.tensor_sub(c1, c1, c2),
         waits=[(dv, None)], incs=[(dv, 1)])
    S.op("scalar", lambda e: e.activation(c2, c1, AF.Sqrt, bias=ctx.constcol(EPS)),
         waits=[(dv, None)], incs=[(ac, 1)])
    S.op("vector", lambda e: e.reciprocal(c3, c2),
         waits=[(ac, None)], incs=[(dv, 1)])
    S.op("vector", lambda e: e.tensor_scalar(out_ap, x_ap, c0, c3, ALU.add, ALU.mult),
         waits=[(dv, None)], incs=[(dv, 1)])
    if gain_bias is not None:
        G, B = gain_bias
        S.op("vector", lambda e: e.tensor_tensor(out_ap, out_ap, G, ALU.mult),
             waits=[(dv, None)], incs=[(dv, 1)])
        S.op("vector", lambda e: e.tensor_tensor(out_ap, out_ap, B, ALU.add),
             waits=[(dv, None)], incs=[(dv, 1)])


def emit_tp(ctx, S, src_ap, dst_ap, psum_ap, waits=(), f32=False):
    ident = ctx.idf if f32 else ctx.idb
    S.op("tensor", lambda e: e.transpose(psum_ap, src_ap, ident[:, :]),
         waits=list(waits) + [(ctx.dv, None)], incs=[(ctx.pe, 1)])
    S.op("vector", lambda e: e.tensor_copy(dst_ap, psum_ap),
         waits=[(ctx.pe, None)], incs=[(ctx.dv, 1)])


def emit_rotary(ctx, S, x_ap, cos_ap, sin_ap, tmp_ap, tmp2_ap, out_ap, waits):
    dv = ctx.dv
    W = x_ap.shape[-1]
    S.op("vector", lambda e: e.tensor_tensor(tmp_ap, x_ap, cos_ap, ALU.mult),
         waits=list(waits) + [(dv, None)], incs=[(dv, 1)])
    S.op("vector", lambda e: e.tensor_tensor(tmp2_ap[:, 0:W:2], x_ap[:, 1:W:2],
                                             sin_ap[:, 0:W:2], ALU.mult),
         waits=[(dv, None)], incs=[(dv, 1)])
    S.op("vector", lambda e: e.tensor_sub(tmp_ap[:, 0:W:2], tmp_ap[:, 0:W:2],
                                          tmp2_ap[:, 0:W:2]),
         waits=[(dv, None)], incs=[(dv, 1)])
    S.op("vector", lambda e: e.tensor_tensor(tmp2_ap[:, 1:W:2], x_ap[:, 0:W:2],
                                             sin_ap[:, 1:W:2], ALU.mult),
         waits=[(dv, None)], incs=[(dv, 1)])
    S.op("vector", lambda e: e.tensor_tensor(tmp_ap[:, 1:W:2], tmp_ap[:, 1:W:2],
                                             tmp2_ap[:, 1:W:2], ALU.add),
         waits=[(dv, None)], incs=[(dv, 1)])
    S.op("vector", lambda e: e.tensor_copy(out_ap, tmp_ap),
         waits=[(dv, None)], incs=[(dv, 1)])



def build_prog1(stop_after=None):
    nc = bass.Bass("TRN2", target_bir_lowering=False)
    ctx = Ctx(nc)
    xs = nc.dram_tensor("xs", [TOK, D], F32, kind="ExternalInput")
    w_qkv = nc.dram_tensor("w_qkv", [D, 3 * D], F32, kind="ExternalInput")
    w_kv = nc.dram_tensor("w_kv", [D, 2 * D], F32, kind="ExternalInput")
    ropec = nc.dram_tensor("ropec", [TOK, D], F32, kind="ExternalInput")
    ropes = nc.dram_tensor("ropes", [TOK, D], F32, kind="ExternalInput")
    o_krt = nc.dram_tensor("KrT", [D, TOK], BF16, kind="ExternalOutput")
    o_v = nc.dram_tensor("V", [TOK, D], BF16, kind="ExternalOutput")
    o_qt = nc.dram_tensor("qT", [D, TOK], BF16, kind="ExternalOutput")
    o_kk = nc.dram_tensor("kk", [TOK, D], BF16, kind="ExternalOutput")
    o_vk = nc.dram_tensor("vk", [TOK, D], BF16, kind="ExternalOutput")
    o_pct = nc.dram_tensor("PcT", [64, NT * H * 64], F32, kind="ExternalOutput")

    sx = [nc.alloc_sbuf_tensor(f"sx{i}", [128, D], F32) for i in range(2)]
    sq = nc.alloc_sbuf_tensor("sq", [128, D], F32)
    cst = {k: nc.alloc_sbuf_tensor(f"c_{k}", [128, 1], F32) for k in
           ("c0", "c1", "c2", "c3")}
    xln = nc.alloc_sbuf_tensor("xln", [128, D], BF16)
    xlt = [nc.alloc_sbuf_tensor(f"xlt{c}", [128, TOK], BF16) for c in range(8)]
    wfm = [nc.alloc_sbuf_tensor(f"wfm{i}", [128, 8 * 128], BF16) for i in range(2)]
    wtm = [nc.alloc_sbuf_tensor(f"wtm{i}", [128, 8 * 512], BF16) for i in range(2)]
    ropci = [nc.alloc_sbuf_tensor(f"ropc{i}", [128, 512], F32) for i in range(2)]
    ropsi = [nc.alloc_sbuf_tensor(f"rops{i}", [128, 512], F32) for i in range(2)]
    kf32 = nc.alloc_sbuf_tensor("kf32", [128, 512], F32)
    tmp = nc.alloc_sbuf_tensor("tmpD", [128, 512], F32)
    tmp2 = nc.alloc_sbuf_tensor("tmpD2", [128, 512], F32)
    krt_sb = [nc.alloc_sbuf_tensor(f"krt{c}", [128, TOK], BF16) for c in range(8)]
    kkfull = nc.alloc_sbuf_tensor("kkfull", [128, NT * D], BF16)
    vkfull = nc.alloc_sbuf_tensor("vkfull", [128, NT * D], BF16)
    stage = [nc.alloc_sbuf_tensor(f"stg{i}", [128, 512], BF16) for i in range(4)]
    pcstage = nc.alloc_sbuf_tensor("pcstage", [64, H * 64], F32)
    pmm = [nc.alloc_psum_tensor(f"pmm{i}", [128, 512], F32) for i in range(2)]
    ptp = [nc.alloc_psum_tensor(f"ptp{i}", [128, 1024], BF16)[0:128, 0:128]
           for i in range(2)]
    ppc = [nc.alloc_psum_tensor(f"ppc{i}", [128, 512], F32)[0:64, 0:64]
           for i in range(2)]

    S = Script(nc)
    ctx.emit_init(S, (EPS,))
    ld, ldg, st, pe, dv, ac = ctx.ld, ctx.ldg, ctx.st, ctx.pe, ctx.dv, ctx.ac

    scr = dict(sq=sq[:, :], c0=cst["c0"][:, :], c1=cst["c1"][:, :],
               c2=cst["c2"][:, :], c3=cst["c3"][:, :])
    ntp = 0
    for t in range(NT):
        b = t % 2
        S.op("sync", lambda e, t=t, b=b: e.dma_start(
            out=sx[b][:, :], in_=xs[t * 128:(t + 1) * 128, :]),
            waits=[(dv, None)], incs=[(ld, 16)])
        emit_ln(ctx, S, sx[b][:, :], xln[:, :], scr, x_waits=[(ld, None)])
        for c in range(8):
            emit_tp(ctx, S, xln[:, c * 128:(c + 1) * 128],
                    xlt[c][:, t * 128:(t + 1) * 128], ptp[ntp % 2][:, :])
            ntp += 1
    S.emit()
    if stop_after == "A":
        return nc, ctx

    S = Script(nc)
    for oc in range(8):
        b = oc % 2
        for kc in range(8):
            S.op("gpsimd", lambda e, kc=kc, oc=oc, b=b: e.dma_start(
                out=wfm[b][:, kc * 128:(kc + 1) * 128],
                in_=w_qkv[kc * 128:(kc + 1) * 128, oc * 128:oc * 128 + 128]),
                waits=([(pe, None)] if kc == 0 else []), incs=[(ldg, 16)])
        for tb in range(4):
            pb = tb % 2
            pairs = [(wfm[b][:, kc * 128:(kc + 1) * 128],
                      xlt[kc][:, tb * 512:(tb + 1) * 512]) for kc in range(8)]
            mm_group(S, ctx, pmm[pb][:, :], pairs,
                     waits_first=[(ldg, None), (ac, None)])
            S.op("scalar", lambda e, pb=pb, tb=tb: e.activation(
                stage[tb][:, :], pmm[pb][:, :], AF.Copy),
                waits=[(pe, None), (st, None)], incs=[(ac, 1)])
            S.op("gpsimd", lambda e, oc=oc, tb=tb: e.dma_start(
                out=o_qt[oc * 128:(oc + 1) * 128, tb * 512:(tb + 1) * 512],
                in_=stage[tb][:, :]),
                waits=[(ac, None)], incs=[(st, 16)])
    S.emit()
    if stop_after == "B":
        return nc, ctx

    S = Script(nc)

    def load_wtm(wdram, col0, buf):
        for kc in range(8):
            S.op("gpsimd", lambda e, kc=kc, wdram=wdram, col0=col0, buf=buf:
                 e.dma_start(out=buf[:, kc * 512:(kc + 1) * 512],
                             in_=wdram[kc * 128:(kc + 1) * 128, col0:col0 + 512]),
                 waits=([(pe, None)] if kc == 0 else []), incs=[(ldg, 16)])

    def tm_pass(wdram, col_base, evac):
        for g in range(2):
            wb = wtm[g % 2]
            load_wtm(wdram, col_base + g * 512, wb)
            for t in range(NT):
                pb = t % 2
                pairs = [(xlt[kc][:, t * 128:(t + 1) * 128],
                          wb[:, kc * 512:(kc + 1) * 512]) for kc in range(8)]
                mm_group(S, ctx, pmm[pb][:, :], pairs,
                         waits_first=[(ldg, None), (ac, None), (dv, None)])
                evac(t, g, pmm[pb][:, :])

    ntp = 0

    def evac_k(t, g, psum):
        nonlocal ntp
        S.op("sync", lambda e: e.dma_start(
            out=ropci[t % 2][:, :], in_=ropec[t * 128:(t + 1) * 128,
                                             g * 512:(g + 1) * 512]),
            waits=[(dv, None)], incs=[(ld, 16)])
        S.op("sync", lambda e: e.dma_start(
            out=ropsi[t % 2][:, :], in_=ropes[t * 128:(t + 1) * 128,
                                             g * 512:(g + 1) * 512]),
            incs=[(ld, 16)])
        S.op("scalar", lambda e: e.activation(kf32[:, :], psum, AF.Copy),
             waits=[(pe, None), (dv, None)], incs=[(ac, 1)])
        emit_rotary(ctx, S, kf32[:, :], ropci[t % 2][:, :], ropsi[t % 2][:, :],
                    tmp[:, :], tmp2[:, :], stage[0][:, :],
                    waits=[(ac, None), (ld, None)])
        for j in range(4):
            c = g * 4 + j
            emit_tp(ctx, S, stage[0][:, j * 128:(j + 1) * 128],
                    krt_sb[c][:, t * 128:(t + 1) * 128], ptp[ntp % 2][:, :])
            ntp += 1

    tm_pass(w_qkv, D, evac_k)

    def evac_v(t, g, psum):
        sb = stage[2 + (t % 2)]
        S.op("scalar", lambda e: e.activation(sb[:, :], psum, AF.Copy),
             waits=[(pe, None), (st, None)], incs=[(ac, 1)])
        S.op("gpsimd", lambda e: e.dma_start(
            out=o_v[t * 128:(t + 1) * 128, g * 512:(g + 1) * 512], in_=sb[:, :]),
            waits=[(ac, None)], incs=[(st, 16)])

    tm_pass(w_qkv, 2 * D, evac_v)

    def mk_evac_full(full, odram):
        def evac(t, g, psum):
            dst = full[:, t * D + g * 512: t * D + (g + 1) * 512]
            S.op("scalar", lambda e: e.activation(dst, psum, AF.Copy),
                 waits=[(pe, None)], incs=[(ac, 1)])
            S.op("gpsimd", lambda e: e.dma_start(
                out=odram[t * 128:(t + 1) * 128, g * 512:(g + 1) * 512], in_=dst),
                waits=[(ac, None)], incs=[(st, 16)])
        return evac

    tm_pass(w_kv, 0, mk_evac_full(kkfull, o_kk))
    tm_pass(w_kv, D, mk_evac_full(vkfull, o_vk))
    S.emit()
    if stop_after == "C":
        return nc, ctx

    import os as _os
    _dpart = _os.environ.get("D_PART", "all")
    S = Script(nc)
    if _dpart in ("all", "krt"):
        for c in range(8):
            S.op("gpsimd", lambda e, c=c: e.dma_start(
                out=o_krt[c * 128:(c + 1) * 128, :], in_=krt_sb[c][:, :]),
                incs=[(st, 16)])
    for t in (range(NT) if _dpart in ("all", "pct") else []):
        for h in range(H):
            pb = (t * H + h) % 2
            S.op("tensor", lambda e, t=t, h=h, pb=pb: e.matmul(
                ppc[pb][:, :],
                vkfull[:, t * D + h * 64: t * D + (h + 1) * 64],
                kkfull[:, t * D + h * 64: t * D + (h + 1) * 64],
                start=True, stop=True),
                waits=[(ac, None)], incs=[(ctx.pe, 1)])
            S.op("scalar", lambda e, h=h, pb=pb: e.activation(
                pcstage[:, h * 64:(h + 1) * 64], ppc[pb][0:64, :], AF.Copy),
                waits=[(pe, None)] + ([(st, None)] if h == 0 else []),
                incs=[(ac, 1)])
        S.op("gpsimd", lambda e, t=t: e.dma_start(
            out=o_pct[:, t * H * 64:(t + 1) * H * 64], in_=pcstage[:, :]),
            waits=[(ac, None)], incs=[(st, 16)])
    S.emit()
    if stop_after == "D":
        return nc, ctx
    S = Script(nc)
    S.op("gpsimd", None, waits=[(ctx.st, None), (ctx.ldg, None)])
    S.op("sync", None, waits=[(ctx.st, None)])
    S.emit()
    return nc, ctx



def rope_tables(positions, scale=1.0):
    inv = 1.0 / (10000.0 ** (np.arange(0, HD, 2, dtype=np.float32) / HD))
    ang = positions[:, None].astype(np.float32) * inv[None, :]
    ang = np.repeat(ang, 2, axis=-1)
    cos = (np.cos(ang) * scale).astype(np.float32)
    sin = (np.sin(ang) * scale).astype(np.float32)
    return np.tile(cos, (1, H)), np.tile(sin, (1, H))



def mask_arrays():
    i = np.arange(128)
    tri = np.where(i[None, :] <= i[:, None], 0.0, NEG).astype(np.float32)
    trit = np.where(i[:, None] <= i[None, :], 0.0, NEG).astype(np.float32)
    lowt = (i[:, None] <= i[None, :]).astype(np.float32)
    low01 = (i[None, :] <= i[:, None]).astype(np.float32)
    id8 = (np.eye(64) * 8.0).astype(np.float32)
    return tri, trit, lowt, low01, id8



def build_prog2(half):
    nc = bass.Bass("TRN2", target_bir_lowering=False)
    ctx = Ctx(nc)
    ld, ldg, st, pe, dv, ac = ctx.ld, ctx.ldg, ctx.st, ctx.pe, ctx.dv, ctx.ac
    chunks = own_chunks(half)

    xs = nc.dram_tensor("xs", [TOK, D], F32, kind="ExternalInput")
    qTd = nc.dram_tensor("qT", [D, TOK], BF16, kind="ExternalInput")
    kkd = nc.dram_tensor("kk", [TOK, D], BF16, kind="ExternalInput")
    vkd = nc.dram_tensor("vk", [TOK, D], BF16, kind="ExternalInput")
    krtf = nc.dram_tensor("KrTf", [D, NSEQ], BF16, kind="ExternalInput")
    vf = nc.dram_tensor("Vf", [NSEQ, D], BF16, kind="ExternalInput")
    pctg = nc.dram_tensor("PcTg", [64, NKT * H * 64], F32, kind="ExternalInput")
    ropc = nc.dram_tensor("ropecq64", [TOK, HD], F32, kind="ExternalInput")
    rops = nc.dram_tensor("ropesq64", [TOK, HD], F32, kind="ExternalInput")
    woutd = nc.dram_tensor("w_out", [D, D], F32, kind="ExternalInput")
    boutd = nc.dram_tensor("b_out", [1, D], F32, kind="ExternalInput")
    wq2d = nc.dram_tensor("wq2", [D, D], F32, kind="ExternalInput")
    wk2d = nc.dram_tensor("wk2", [D, D], F32, kind="ExternalInput")
    wv2d = nc.dram_tensor("wv2", [D, D], F32, kind="ExternalInput")
    bq2d = nc.dram_tensor("bq2c", [D, 1], F32, kind="ExternalInput")
    bk2d = nc.dram_tensor("bk2c", [D, 1], F32, kind="ExternalInput")
    bv2d = nc.dram_tensor("bv2", [1, D], F32, kind="ExternalInput")
    o_x1 = nc.dram_tensor("x1", [TOK, D], F32, kind="ExternalOutput")
    o_qt2 = nc.dram_tensor("qT2", [D, TOK], BF16, kind="ExternalOutput")
    o_kt2 = nc.dram_tensor("kT2", [D, TOK], BF16, kind="ExternalOutput")
    o_v2 = nc.dram_tensor("V2", [TOK, D], BF16, kind="ExternalOutput")

    tri_a, trit_a, lowt_a, low01_a, id8_a = mask_arrays()
    tri_d = nc.inline_tensor(tri_a, name="tri_d")
    trit_d = nc.inline_tensor(trit_a, name="trit_d")
    lowt_d = nc.inline_tensor(lowt_a, name="lowt_d")
    low01_d = nc.inline_tensor(low01_a, name="low01_d")
    id8_d = nc.inline_tensor(id8_a, name="id8_d")

    tri = nc.alloc_sbuf_tensor("tri", [128, 128], F32)
    trit = nc.alloc_sbuf_tensor("trit", [128, 128], F32)
    lowt = nc.alloc_sbuf_tensor("lowt", [128, 128], F32)
    low01 = nc.alloc_sbuf_tensor("low01", [128, 128], F32)
    id8 = nc.alloc_sbuf_tensor("id8", [64, 64], F32)
    qhat = nc.alloc_sbuf_tensor("qhat", [128, TOK], BF16)
    khat = nc.alloc_sbuf_tensor("khat", [128, NSEQ], BF16)
    vhat = nc.alloc_sbuf_tensor("vhat", [128, NKT * 65], BF16)
    qTh = nc.alloc_sbuf_tensor("qTh", [64, TOK], BF16)
    kkh = nc.alloc_sbuf_tensor("kkh", [128, NT * 64], BF16)
    vkh = nc.alloc_sbuf_tensor("vkh", [128, NT * 64], BF16)
    kkTh = nc.alloc_sbuf_tensor("kkTh", [64, TOK], BF16)
    vkTh = nc.alloc_sbuf_tensor("vkTh", [64, TOK], BF16)
    sct = nc.alloc_sbuf_tensor("sct", [64, NT * 64], F32)
    sctb = nc.alloc_sbuf_tensor("sctb", [64, NT * 64], BF16)
    pcg = nc.alloc_sbuf_tensor("pcg", [64, NKT * 64], F32)
    run = nc.alloc_sbuf_tensor("run", [64, 64], F32)
    ssqcols = nc.alloc_sbuf_tensor("ssqcols", [64, NT], F32)
    ssqrow = nc.alloc_sbuf_tensor("ssqrow", [1, NT], F32)
    ssqb = nc.alloc_sbuf_tensor("ssqb", [128, NT], F32)
    sq64 = nc.alloc_sbuf_tensor("sq64", [64, 64], F32)
    c64 = nc.alloc_sbuf_tensor("c64", [64, 1], F32)
    g1 = nc.alloc_sbuf_tensor("g1", [128, 128], F32)
    gp = nc.alloc_sbuf_tensor("gp", [128, 128], F32)
    ucol = nc.alloc_sbuf_tensor("ucol", [128, 6], F32)
    uprod = nc.alloc_sbuf_tensor("uprod", [128, 64], F32)
    stdc = nc.alloc_sbuf_tensor("stdc", [128, 3], F32)
    qn32 = nc.alloc_sbuf_tensor("qn32", [128, 64], F32)
    rot_t = nc.alloc_sbuf_tensor("rot_t", [128, 64], F32)
    rot_t2 = nc.alloc_sbuf_tensor("rot_t2", [128, 64], F32)
    qr8 = nc.alloc_sbuf_tensor("qr8", [128, 64], BF16)
    ropcs = nc.alloc_sbuf_tensor("ropcs", [128, NT * 64], F32)
    ropss = nc.alloc_sbuf_tensor("ropss", [128, NT * 64], F32)
    mxr = nc.alloc_sbuf_tensor("mxr", [128, 2], F32)
    negc = nc.alloc_sbuf_tensor("negc", [128, 1], F32)
    ptile = [nc.alloc_sbuf_tensor(f"ptile{i}", [128, 512], BF16) for i in range(2)]
    rs_row = nc.alloc_sbuf_tensor("rs_row", [1, 512], F32)
    bc = nc.alloc_sbuf_tensor("bc", [64, 512], F32)
    attn_sb = [nc.alloc_sbuf_tensor(f"attn{i}", [128, TOK], BF16) for i in range(8)]
    p512a = nc.alloc_psum_tensor("p512a", [128, 512], F32)
    p512b = nc.alloc_psum_tensor("p512b", [128, 512], F32)
    pmix = nc.alloc_psum_tensor("pmix", [128, 512], F32)
    pacc = nc.alloc_psum_tensor("pacc", [128, 512], F32)
    po = nc.alloc_psum_tensor("po", [65, 512], F32)
    pb = nc.alloc_psum_tensor("pb", [64, 512], F32)
    ptpb = nc.alloc_psum_tensor("ptpb", [128, 1024], BF16)[0:128, 0:128]
    ptpf = nc.alloc_psum_tensor("ptpf", [128, 512], F32)[0:128, 0:128]

    S = Script(nc)
    ctx.emit_init(S, (EPS, 1.0))
    for dst, src in ((tri, tri_d), (trit, trit_d), (lowt, lowt_d),
                     (low01, low01_d), (id8, id8_d)):
        S.op("sync", lambda e, dst=dst, src=src: e.dma_start(
            out=dst[:, :], in_=src.ap()), incs=[(ld, 16)])
    S.op("sync", lambda e: e.dma_start(
        out=ropcs[:, :], in_=ropc.ap().rearrange("(t p) c -> p t c", p=128)),
        incs=[(ld, 16)])
    S.op("sync", lambda e: e.dma_start(
        out=ropss[:, :], in_=rops.ap().rearrange("(t p) c -> p t c", p=128)),
        incs=[(ld, 16)])
    S.emit()

    for h in range(H):
        S = Script(nc)
        hw = 64 * (h % 2)
        S.op("sync", lambda e, h=h: e.dma_start(
            out=qTh[:, :], in_=qTd[h * 64:(h + 1) * 64, :]),
            waits=[(pe, None)], incs=[(ld, 16)])
        S.op("sync", lambda e, h=h: e.dma_start(
            out=khat[0:64, :], in_=krtf[h * 64:(h + 1) * 64, :]),
            waits=[(pe, None)], incs=[(ld, 16)])
        S.op("vector", lambda e: e.memset(khat[64:65, :], 1.0), incs=[(dv, 1)])
        if h == 0:
            S.op("vector", lambda e: e.memset(
                vhat[:, :].rearrange("p (kt c) -> p kt c", c=65)[:, :, 64:65],
                1.0),
                incs=[(dv, 1)])
        S.op("sync", lambda e, h=h: e.dma_start(
            out=vhat[:, :].rearrange("p (kt c) -> p kt c", c=65)[:, :, 0:64],
            in_=vf.ap().rearrange("(kt p) d -> p kt d", p=128)[:, :, h * 64:(h + 1) * 64]),
            waits=[(pe, None), (dv, None)], incs=[(ld, 16)])
        S.op("sync", lambda e, h=h: e.dma_start(
            out=kkh[:, :], in_=kkd.ap().rearrange("(t p) d -> p t d", p=128)[:, :, h * 64:(h + 1) * 64]),
            waits=[(pe, None)], incs=[(ld, 16)])
        S.op("sync", lambda e, h=h: e.dma_start(
            out=vkh[:, :], in_=vkd.ap().rearrange("(t p) d -> p t d", p=128)[:, :, h * 64:(h + 1) * 64]),
            waits=[(pe, None)], incs=[(ld, 16)])
        S.op("sync", lambda e, h=h: e.dma_start(
            out=pcg[:, :], in_=pctg.ap().rearrange("p (gc hh c) -> p gc hh c", hh=H, c=64)[:, :, h, :]),
            waits=[(dv, None)], incs=[(ld, 16)])
        for t in range(NT):
            for (srcb, dstb) in ((kkh, kkTh), (vkh, vkTh)):
                S.op("tensor", lambda e, t=t, srcb=srcb: e.transpose(
                    ptpb[0:64, :], srcb[:, t * 64:(t + 1) * 64], ctx.idb[:, :]),
                    waits=[(ld, None), (dv, None)], incs=[(pe, 1)])
                S.op("vector", lambda e, t=t, dstb=dstb: e.tensor_copy(
                    dstb[:, t * 128:(t + 1) * 128], ptpb[0:64, :]),
                    waits=[(pe, None)], incs=[(dv, 1)])
        S.op("vector", lambda e: e.tensor_copy(run[:, :], id8[:, :]),
             waits=[(ld, None)], incs=[(dv, 1)])
        for gc in range(NKT):
            if gc in chunks:
                t = chunks.index(gc)
                S.op("vector", lambda e, t=t: e.tensor_copy(
                    sct[:, t * 64:(t + 1) * 64], run[:, :]),
                    waits=[(dv, None)], incs=[(dv, 1)])
            if gc < NKT - 1:
                S.op("vector", lambda e, gc=gc: e.tensor_tensor(
                    run[:, :], run[:, :], pcg[:, gc * 64:(gc + 1) * 64], ALU.add),
                    waits=[(dv, None)], incs=[(dv, 1)])
        for t in range(NT):
            S.op("vector", lambda e, t=t: e.tensor_copy(
                sctb[:, t * 64:(t + 1) * 64], sct[:, t * 64:(t + 1) * 64]),
                waits=[(dv, None)], incs=[(dv, 1)])
            S.op("scalar", lambda e, t=t: e.activation(
                sq64[:, :], sct[:, t * 64:(t + 1) * 64], AF.Square),
                waits=[(dv, None), (ac, None)], incs=[(ac, 1)])
            S.op("vector", lambda e, t=t: e.reduce_sum(
                ssqcols[:, t:t + 1], sq64[:, :], axis=AX.X),
                waits=[(ac, None)], incs=[(dv, 1)])
        S.op("tensor", lambda e: e.matmul(
            pacc[0:1, 0:NT], ctx.constcol(1.0)[0:64, :], ssqcols[:, :],
            start=True, stop=True),
            waits=[(dv, None)], incs=[(pe, 1)])
        S.op("vector", lambda e: e.tensor_copy(ssqrow[:, :], pacc[0:1, 0:NT]),
             waits=[(pe, None)], incs=[(dv, 1)])
        S.op("tensor", lambda e: e.matmul(
            pacc[0:128, 64:64 + NT],
            ctx.ones_r[0:1, :], ssqrow[:, :], start=True, stop=True),
            waits=[(dv, None)], incs=[(pe, 1)])
        S.op("vector", lambda e: e.tensor_copy(ssqb[:, :], pacc[:, 64:64 + NT]),
             waits=[(pe, None)], incs=[(dv, 1)])
        S.emit()

        S = Script(nc)
        for t in range(NT):
            tc = slice(t * 128, (t + 1) * 128)
            hc = slice(t * 64, (t + 1) * 64)
            S.op("tensor", lambda e, t=t: e.matmul(
                pmix[:, 0:128], vkTh[:, t * 128:(t + 1) * 128],
                qTh[:, t * 128:(t + 1) * 128], start=True, stop=True),
                waits=[(dv, None)], incs=[(pe, 1)])
            S.op("vector", lambda e: e.tensor_tensor(
                ptile[0][:, 0:128], pmix[:, 0:128], lowt[:, :], ALU.mult),
                waits=[(pe, None)], incs=[(dv, 1)])
            S.op("tensor", lambda e, t=t: e.matmul(
                pacc[:, 0:64], ptile[0][:, 0:128], kkh[:, t * 64:(t + 1) * 64],
                start=True, stop=False),
                waits=[(dv, None)], incs=[(pe, 1)])
            S.op("tensor", lambda e, t=t: e.matmul(
                pacc[:, 0:64], qTh[:, t * 128:(t + 1) * 128],
                sctb[:, t * 64:(t + 1) * 64], start=False, stop=True),
                incs=[(pe, 1)])
            S.op("tensor", lambda e, t=t: e.matmul(
                pacc[:, 64:128], vkTh[:, t * 128:(t + 1) * 128],
                sctb[:, t * 64:(t + 1) * 64], start=True, stop=True),
                incs=[(pe, 1)])
            S.op("vector", lambda e, t=t: e.tensor_tensor(
                uprod[:, :], pacc[:, 64:128], kkh[:, t * 64:(t + 1) * 64], ALU.mult),
                waits=[(pe, None)], incs=[(dv, 1)])
            S.op("vector", lambda e: e.reduce_sum(ucol[:, 0:1], uprod[:, :], axis=AX.X),
                 waits=[(dv, None)], incs=[(dv, 1)])
            S.op("tensor", lambda e, t=t: e.matmul(
                pmix[:, 128:256], kkTh[:, t * 128:(t + 1) * 128],
                kkTh[:, t * 128:(t + 1) * 128], start=True, stop=True),
                waits=[(dv, None)], incs=[(pe, 1)])
            S.op("tensor", lambda e, t=t: e.matmul(
                pmix[:, 256:384], vkTh[:, t * 128:(t + 1) * 128],
                vkTh[:, t * 128:(t + 1) * 128], start=True, stop=True),
                incs=[(pe, 1)])
            S.op("scalar", lambda e: e.activation(g1[:, :], pmix[:, 128:256], AF.Copy),
                 waits=[(pe, None), (dv, None)], incs=[(ac, 1)])
            S.op("vector", lambda e: e.tensor_tensor(
                gp[:, :], pmix[:, 256:384], g1[:, :], ALU.mult),
                waits=[(ac, None)], incs=[(dv, 1)])
            S.op("vector", lambda e: e.tensor_tensor(
                gp[:, :], gp[:, :], low01[:, :], ALU.mult),
                waits=[(dv, None)], incs=[(dv, 1)])
            S.op("vector", lambda e: e.reduce_sum(ucol[:, 1:2], gp[:, :], axis=AX.X),
                 waits=[(dv, None)], incs=[(dv, 1)])
            S.op("scalar", lambda e, t=t: e.activation(
                uprod[:, :], kkh[:, t * 64:(t + 1) * 64], AF.Square),
                waits=[(dv, None)], incs=[(ac, 1)])
            S.op("vector", lambda e: e.reduce_sum(ucol[:, 2:3], uprod[:, :], axis=AX.X),
                 waits=[(ac, None)], incs=[(dv, 1)])
            S.op("scalar", lambda e, t=t: e.activation(
                uprod[:, :], vkh[:, t * 64:(t + 1) * 64], AF.Square),
                waits=[(dv, None)], incs=[(ac, 1)])
            S.op("vector", lambda e: e.reduce_sum(ucol[:, 3:4], uprod[:, :], axis=AX.X),
                 waits=[(ac, None)], incs=[(dv, 1)])
            S.op("vector", lambda e: e.tensor_tensor(
                ucol[:, 4:5], ucol[:, 2:3], ucol[:, 3:4], ALU.mult),
                waits=[(dv, None)], incs=[(dv, 1)])
            S.op("vector", lambda e: e.tensor_tensor(
                ucol[:, 5:6], ucol[:, 0:1], ucol[:, 1:2], ALU.add),
                waits=[(dv, None)], incs=[(dv, 1)])
            S.op("vector", lambda e: e.tensor_scalar(
                ucol[:, 5:6], ucol[:, 5:6], 2.0, None, ALU.mult),
                waits=[(dv, None)], incs=[(dv, 1)])
            S.op("vector", lambda e: e.tensor_sub(ucol[:, 5:6], ucol[:, 5:6],
                                                  ucol[:, 4:5]),
                 waits=[(dv, None)], incs=[(dv, 1)])
            S.op("tensor", lambda e: e.matmul(
                pacc[:, 128:129], lowt[:, :], ucol[:, 5:6], start=True, stop=True),
                waits=[(dv, None)], incs=[(pe, 1)])
            S.op("scalar", lambda e, t=t: e.activation(
                stdc[:, 0:1], pacc[:, 128:129], AF.Sqrt, bias=ssqb[:, t:t + 1]),
                waits=[(pe, None)], incs=[(ac, 1)])
            S.op("vector", lambda e: e.tensor_scalar(
                stdc[:, 1:2], stdc[:, 0:1], 1e-8, None, ALU.add),
                waits=[(ac, None)], incs=[(dv, 1)])
            S.op("vector", lambda e: e.reciprocal(stdc[:, 2:3], stdc[:, 1:2]),
                 waits=[(dv, None)], incs=[(dv, 1)])
            S.op("vector", lambda e: e.tensor_scalar_mul(
                qn32[:, :], pacc[:, 0:64], stdc[:, 2:3]),
                waits=[(dv, None)], incs=[(dv, 1)])
            emit_rotary(ctx, S, qn32[:, :], ropcs[:, t * 64:(t + 1) * 64],
                        ropss[:, t * 64:(t + 1) * 64], rot_t[:, :], rot_t2[:, :],
                        qr8[:, :], waits=[(dv, None)])
            S.op("tensor", lambda e, t=t: e.transpose(
                ptpb[0:64, 0:128], qr8[:, :], ctx.idb[:, :]),
                waits=[(dv, None)], incs=[(pe, 1)])
            S.op("vector", lambda e, t=t: e.tensor_copy(
                qhat[0:64, t * 128:(t + 1) * 128], ptpb[0:64, :]),
                waits=[(pe, None)], incs=[(dv, 1)])
        S.emit()

        S = Script(nc)
        for t in range(NT):
            g = chunks[t]
            nkb = (g + 4) // 4
            for kb in range(nkb):
                n = min(512, (g + 1) * 128 - kb * 512)
                pbuf = p512a if kb % 2 == 0 else p512b
                S.op("tensor", lambda e, t=t, kb=kb, n=n, pbuf=pbuf: e.matmul(
                    pbuf[:, 0:n], qhat[0:64, t * 128:(t + 1) * 128],
                    khat[0:64, kb * 512:kb * 512 + n], start=True, stop=True),
                    waits=[(dv, None)], incs=[(pe, 1)])
                if kb == nkb - 1:
                    S.op("vector", lambda e, n=n, pbuf=pbuf: e.tensor_tensor(
                        pbuf[:, n - 128:n], pbuf[:, n - 128:n], tri[:, :], ALU.add),
                        waits=[(pe, None)], incs=[(dv, 1)])
                S.op("vector", lambda e, kb=kb, n=n, pbuf=pbuf: e.reduce_max(
                    mxr[:, 1:2] if kb > 0 else mxr[:, 0:1], pbuf[:, 0:n], axis=AX.X),
                    waits=[(pe, None), (dv, None)], incs=[(dv, 1)])
                if kb > 0:
                    S.op("vector", lambda e: e.tensor_max(
                        mxr[:, 0:1], mxr[:, 0:1], mxr[:, 1:2]),
                        waits=[(dv, None)], incs=[(dv, 1)])
            S.op("vector", lambda e: e.tensor_scalar_mul(negc[:, :], mxr[:, 0:1], -1.0),
                 waits=[(dv, None)], incs=[(dv, 1)])
            S.op("tensor", lambda e: e.transpose(ptpf[0:1, 0:128],
                                                 negc[:, :], ctx.idf[:, :]),
                 waits=[(dv, None)], incs=[(pe, 1)])
            S.op("vector", lambda e, t=t: e.tensor_copy(
                qhat[64:65, t * 128:(t + 1) * 128], ptpf[0:1, 0:128]),
                waits=[(pe, None)], incs=[(dv, 1)])
        S.emit()

        S = Script(nc)
        for qb in range(4):
            gs0 = chunks[4 * qb]
            kmax = gs0 + 3
            for kt in range(kmax + 1):
                pbuf = p512a if kt % 2 == 0 else p512b
                S.op("tensor", lambda e, qb=qb, kt=kt, pbuf=pbuf: e.matmul(
                    pbuf[:, :], khat[0:65, kt * 128:(kt + 1) * 128],
                    qhat[0:65, qb * 512:(qb + 1) * 512], start=True, stop=True),
                    waits=[(ac, None)], incs=[(pe, 1)])
                rel = kt - gs0
                if rel >= 0:
                    for j in range(4):
                        if j < rel:
                            S.op("vector", lambda e, j=j, pbuf=pbuf: e.tensor_scalar(
                                pbuf[:, j * 128:(j + 1) * 128],
                                pbuf[:, j * 128:(j + 1) * 128], NEG, None, ALU.add),
                                waits=[(pe, None)], incs=[(dv, 1)])
                        elif j == rel:
                            S.op("vector", lambda e, j=j, pbuf=pbuf: e.tensor_tensor(
                                pbuf[:, j * 128:(j + 1) * 128],
                                pbuf[:, j * 128:(j + 1) * 128], trit[:, :], ALU.add),
                                waits=[(pe, None)], incs=[(dv, 1)])
                S.op("scalar", lambda e, kt=kt, pbuf=pbuf: e.activation(
                    ptile[kt % 2][:, :], pbuf[:, :], AF.Exp),
                    waits=[(pe, None), (dv, None)], incs=[(ac, 1)])
                S.op("tensor", lambda e, kt=kt, kmax=kmax: e.matmul(
                    po[:, :], vhat[:, kt * 65:(kt + 1) * 65], ptile[kt % 2][:, :],
                    start=(kt == 0), stop=(kt == kmax)),
                    waits=[(ac, None)] + ([(dv, None)] if kt == 0 else []),
                    incs=[(pe, 1)])
            S.op("vector", lambda e: e.reciprocal(rs_row[:, :], po[64:65, :]),
                 waits=[(pe, None)], incs=[(dv, 1)])
            S.op("tensor", lambda e: e.matmul(
                pb[:, :], ctx.ones_r[0:1, 0:64], rs_row[:, :], start=True, stop=True),
                waits=[(dv, None)], incs=[(pe, 1)])
            S.op("vector", lambda e: e.tensor_copy(bc[:, :], pb[:, :]),
                 waits=[(pe, None)], incs=[(dv, 1)])
            S.op("vector", lambda e, h=h, qb=qb, hw=hw: e.tensor_tensor(
                attn_sb[h // 2][hw:hw + 64, qb * 512:(qb + 1) * 512],
                po[0:64, :], bc[:, :], ALU.mult),
                waits=[(dv, None)], incs=[(dv, 1)])
        S.emit()
    hd = dict(
        o_x1=o_x1, o_qt2=o_qt2, o_kt2=o_kt2, o_v2=o_v2, attn_sb=attn_sb,
        xs=xs, woutd=woutd, boutd=boutd, wq2d=wq2d, wk2d=wk2d, wv2d=wv2d,
        bq2d=bq2d, bk2d=bk2d, bv2d=bv2d,
        p512a=p512a, p512b=p512b, ptpb=ptpb)
    prog2_tail(nc, ctx, hd)
    return nc, ctx, hd


def prog2_tail(nc, ctx, hd):
    ld, ldg, st, pe, dv, ac = ctx.ld, ctx.ldg, ctx.st, ctx.pe, ctx.dv, ctx.ac
    attn_sb = hd["attn_sb"]
    xs, woutd, boutd = hd["xs"], hd["woutd"], hd["boutd"]
    wq2d, wk2d, wv2d = hd["wq2d"], hd["wk2d"], hd["wv2d"]
    bq2d, bk2d, bv2d = hd["bq2d"], hd["bk2d"], hd["bv2d"]
    o_x1, o_qt2, o_kt2, o_v2 = hd["o_x1"], hd["o_qt2"], hd["o_kt2"], hd["o_v2"]

    wout = [nc.alloc_sbuf_tensor(f"wout{c}", [128, D], BF16) for c in range(8)]
    bout = nc.alloc_sbuf_tensor("bout", [1, D], BF16)
    bv2 = nc.alloc_sbuf_tensor("bv2s", [1, D], BF16)
    bqc = nc.alloc_sbuf_tensor("bqc", [128, 64], F32)
    bkc = nc.alloc_sbuf_tensor("bkc", [128, 64], F32)
    ytile = nc.alloc_sbuf_tensor("ytile", [128, D], F32)
    x1t = nc.alloc_sbuf_tensor("x1t", [128, D], F32)
    sxt = [nc.alloc_sbuf_tensor(f"sxt{i}", [128, D], F32) for i in range(2)]
    sq2 = nc.alloc_sbuf_tensor("sq2", [128, D], F32)
    cst2 = {k: nc.alloc_sbuf_tensor(f"c2_{k}", [128, 1], F32) for k in
            ("c0", "c1", "c2", "c3")}
    sbf = nc.alloc_sbuf_tensor("sbf", [128, D], BF16)
    slt = [nc.alloc_sbuf_tensor(f"slt{c}", [128, TOK], BF16) for c in range(8)]
    wfm2 = [nc.alloc_sbuf_tensor(f"wfm2_{i}", [128, 8 * 128], BF16) for i in range(2)]
    wtm2 = [nc.alloc_sbuf_tensor(f"wtm2_{i}", [128, 8 * 512], BF16) for i in range(2)]
    stage2 = [nc.alloc_sbuf_tensor(f"stg2_{i}", [128, 512], BF16) for i in range(4)]
    p512a, p512b, ptpb = hd["p512a"], hd["p512b"], hd["ptpb"]

    S = Script(nc)
    for c in range(8):
        S.op("gpsimd", lambda e, c=c: e.dma_start(
            out=wout[c][:, :], in_=woutd[c * 128:(c + 1) * 128, :]),
            incs=[(ldg, 16)])
    S.op("gpsimd", lambda e: e.dma_start(out=bout[:, :], in_=boutd[:, :]),
         incs=[(ldg, 16)])
    S.op("gpsimd", lambda e: e.dma_start(out=bv2[:, :], in_=bv2d[:, :]),
         incs=[(ldg, 16)])
    for c in range(8):
        S.op("sync", lambda e, c=c: e.dma_start(
            out=bqc[:, 8 * c:8 * c + 1], in_=bq2d[c * 128:(c + 1) * 128, :]),
            incs=[(ld, 16)])
        S.op("sync", lambda e, c=c: e.dma_start(
            out=bkc[:, 8 * c:8 * c + 1], in_=bk2d[c * 128:(c + 1) * 128, :]),
            incs=[(ld, 16)])
    scr = dict(sq=sq2[:, :], c0=cst2["c0"][:, :], c1=cst2["c1"][:, :],
               c2=cst2["c2"][:, :], c3=cst2["c3"][:, :])
    for t in range(NT):
        tb = t % 2
        S.op("sync", lambda e, t=t, tb=tb: e.dma_start(
            out=sxt[tb][:, :], in_=xs[t * 128:(t + 1) * 128, :]),
            waits=[(dv, None)], incs=[(ld, 16)])
        for g in range(2):
            pbuf = p512a if g == 0 else p512b
            pairs = [(attn_sb[c][:, t * 128:(t + 1) * 128],
                      wout[c][:, g * 512:(g + 1) * 512]) for c in range(8)]
            mm_group(S, ctx, pbuf[:, :], pairs,
                     waits_first=[(ldg, None), (ac, None)], stop_last=False)
            S.op("tensor", lambda e, g=g, pbuf=pbuf: e.matmul(
                pbuf[:, :], ctx.ones_rb[:, :], bout[:, g * 512:(g + 1) * 512],
                start=False, stop=True),
                incs=[(pe, 1)])
            S.op("scalar", lambda e, g=g, pbuf=pbuf: e.activation(
                ytile[:, g * 512:(g + 1) * 512], pbuf[:, :], AF.Copy),
                waits=[(pe, None)], incs=[(ac, 1)])
        emit_ln(ctx, S, ytile[:, :], ytile[:, :], scr, x_waits=[(ac, None)])
        S.op("vector", lambda e, tb=tb: e.tensor_tensor(
            x1t[:, :], sxt[tb][:, :], ytile[:, :], ALU.add),
            waits=[(ld, None), (dv, None), (st, None)], incs=[(dv, 1)])
        S.op("gpsimd", lambda e, t=t: e.dma_start(
            out=o_x1[t * 128:(t + 1) * 128, :], in_=x1t[:, :]),
            waits=[(dv, None)], incs=[(st, 16)])
        emit_ln(ctx, S, x1t[:, :], sbf[:, :], scr, x_waits=[(dv, None)])
        for c in range(8):
            emit_tp(ctx, S, sbf[:, c * 128:(c + 1) * 128],
                    slt[c][:, t * 128:(t + 1) * 128], ptpb[:, :])
    S.emit()

    S = Script(nc)
    for (wdram, biascols, odram) in ((wq2d, bqc, o_qt2), (wk2d, bkc, o_kt2)):
        for oc in range(8):
            b = oc % 2
            for kc in range(8):
                S.op("gpsimd", lambda e, kc=kc, oc=oc, b=b, wdram=wdram: e.dma_start(
                    out=wfm2[b][:, kc * 128:(kc + 1) * 128],
                    in_=wdram[kc * 128:(kc + 1) * 128, oc * 128:oc * 128 + 128]),
                    waits=([(pe, None)] if kc == 0 else []), incs=[(ldg, 16)])
            for tb in range(4):
                pbuf = p512a if tb % 2 == 0 else p512b
                pairs = [(wfm2[b][:, kc * 128:(kc + 1) * 128],
                          slt[kc][:, tb * 512:(tb + 1) * 512]) for kc in range(8)]
                mm_group(S, ctx, pbuf[:, :], pairs,
                         waits_first=[(ldg, None), (ac, None), (dv, None)])
                S.op("vector", lambda e, pbuf=pbuf, tb=tb, oc=oc, biascols=biascols:
                     e.tensor_scalar(stage2[tb][:, :], pbuf[:, :],
                                     biascols[:, 8 * oc:8 * oc + 1], None,
                                     ALU.add),
                     waits=[(pe, None), (st, None), (ld, None)], incs=[(dv, 1)])
                S.op("gpsimd", lambda e, oc=oc, tb=tb, odram=odram: e.dma_start(
                    out=odram[oc * 128:(oc + 1) * 128, tb * 512:(tb + 1) * 512],
                    in_=stage2[tb][:, :]),
                    waits=[(dv, None)], incs=[(st, 16)])
    for g in range(2):
        wb = wtm2[g % 2]
        for kc in range(8):
            S.op("gpsimd", lambda e, kc=kc, g=g, wb=wb: e.dma_start(
                out=wb[:, kc * 512:(kc + 1) * 512],
                in_=wv2d[kc * 128:(kc + 1) * 128, g * 512:g * 512 + 512]),
                waits=([(pe, None)] if kc == 0 else []), incs=[(ldg, 16)])
        for t in range(NT):
            pbuf = p512a if t % 2 == 0 else p512b
            pairs = [(slt[kc][:, t * 128:(t + 1) * 128],
                      wb[:, kc * 512:(kc + 1) * 512]) for kc in range(8)]
            mm_group(S, ctx, pbuf[:, :], pairs,
                     waits_first=[(ldg, None), (ac, None)], stop_last=False)
            S.op("tensor", lambda e, g=g, pbuf=pbuf: e.matmul(
                pbuf[:, :], ctx.ones_rb[:, :], bv2[:, g * 512:(g + 1) * 512],
                start=False, stop=True),
                incs=[(pe, 1)])
            S.op("scalar", lambda e, t=t, pbuf=pbuf: e.activation(
                stage2[2 + t % 2][:, :], pbuf[:, :], AF.Copy),
                waits=[(pe, None), (st, None)], incs=[(ac, 1)])
            S.op("gpsimd", lambda e, t=t, g=g: e.dma_start(
                out=o_v2[t * 128:(t + 1) * 128, g * 512:(g + 1) * 512],
                in_=stage2[2 + t % 2][:, :]),
                waits=[(ac, None)], incs=[(st, 16)])
    S.op("gpsimd", None, waits=[(st, None), (ldg, None)])
    S.emit()



def build_prog3():
    nc = bass.Bass("TRN2", target_bir_lowering=False)
    ctx = Ctx(nc)
    ld, ldg, st, pe, dv, ac = ctx.ld, ctx.ldg, ctx.st, ctx.pe, ctx.dv, ctx.ac

    x1d = nc.dram_tensor("x1", [TOK, D], F32, kind="ExternalInput")
    qT2d = nc.dram_tensor("qT2", [D, TOK], BF16, kind="ExternalInput")
    kT2f = nc.dram_tensor("kT2f", [D, NSEQ], BF16, kind="ExternalInput")
    v2f = nc.dram_tensor("V2f", [NSEQ, D], BF16, kind="ExternalInput")
    wod = nc.dram_tensor("wo2", [D, D], F32, kind="ExternalInput")
    bod = nc.dram_tensor("bo2", [1, D], F32, kind="ExternalInput")
    gwd = None and nc.dram_tensor("gW", [D, 8], F32, kind="ExternalInput")
    gbd = None and nc.dram_tensor("gb", [1, 8], F32, kind="ExternalInput")
    w1d = None and nc.dram_tensor("W1", [8, D, 2048], F32, kind="ExternalInput")
    b1td = None and nc.dram_tensor("b1T", [2048, 8], F32, kind="ExternalInput")
    w2d = None and nc.dram_tensor("W2", [8, 2048, D], F32, kind="ExternalInput")
    b2d = None and nc.dram_tensor("b2", [8, D], F32, kind="ExternalInput")
    ln1gd = nc.dram_tensor("ln1g", [1, D], F32, kind="ExternalInput")
    ln1bd = nc.dram_tensor("ln1b", [1, D], F32, kind="ExternalInput")
    ln2gd = None and nc.dram_tensor("ln2g", [1, D], F32, kind="ExternalInput")
    ln2bd = None and nc.dram_tensor("ln2b", [1, D], F32, kind="ExternalInput")
    wq3d = None and nc.dram_tensor("wq3", [D, D], F32, kind="ExternalInput")
    wk3d = None and nc.dram_tensor("wk3", [D, D], F32, kind="ExternalInput")
    wv3d = None and nc.dram_tensor("wv3", [D, D], F32, kind="ExternalInput")
    bq3d = None and nc.dram_tensor("bq3c", [D, 1], F32, kind="ExternalInput")
    bk3d = None and nc.dram_tensor("bk3c", [D, 1], F32, kind="ExternalInput")
    bv3d = None and nc.dram_tensor("bv3", [1, D], F32, kind="ExternalInput")
    o_x2 = None and nc.dram_tensor("x2", [TOK, D], F32, kind="ExternalOutput")
    o_qt3 = None and nc.dram_tensor("qT3", [D, TOK], BF16, kind="ExternalOutput")
    o_kt3 = None and nc.dram_tensor("kT3", [D, TOK], BF16, kind="ExternalOutput")
    o_v3 = None and nc.dram_tensor("V3", [TOK, D], BF16, kind="ExternalOutput")
    o_qnn = None and nc.dram_tensor("qnn", [H, TOK], F32, kind="ExternalOutput")
    o_knn = None and nc.dram_tensor("knn", [H, TOK], F32, kind="ExternalOutput")
    o_x1a = nc.dram_tensor("x1a", [TOK, D], F32, kind="ExternalOutput")
    o_xaT = nc.dram_tensor("xaT", [D, TOK], BF16, kind="ExternalOutput")

    qhat = nc.alloc_sbuf_tensor("qhat", [128, TOK], BF16)
    qstg = nc.alloc_sbuf_tensor("qstg", [64, TOK], BF16)
    khat = nc.alloc_sbuf_tensor("khat", [128, NSEQ], BF16)
    vhat = nc.alloc_sbuf_tensor("vhat", [128, NKT * 65], BF16)
    mxr = nc.alloc_sbuf_tensor("mxr", [128, 2], F32)
    negc = nc.alloc_sbuf_tensor("negc", [128, 1], F32)
    ptile = [nc.alloc_sbuf_tensor(f"ptile{i}", [128, 512], BF16) for i in range(2)]
    rs_row = nc.alloc_sbuf_tensor("rs_row", [1, 512], F32)
    bc = nc.alloc_sbuf_tensor("bc", [64, 512], F32)
    attn_sb = [nc.alloc_sbuf_tensor(f"attn{i}", [128, TOK], BF16) for i in range(8)]
    p512a = nc.alloc_psum_tensor("p512a", [128, 512], F32)
    p512b = nc.alloc_psum_tensor("p512b", [128, 512], F32)
    p512c = nc.alloc_psum_tensor("p512c", [128, 512], F32)
    p512d = nc.alloc_psum_tensor("p512d", [128, 512], F32)
    po = nc.alloc_psum_tensor("po", [65, 512], F32)
    pb = nc.alloc_psum_tensor("pb", [64, 512], F32)
    ptpb = nc.alloc_psum_tensor("ptpb", [128, 1024], BF16)[0:128, 0:128]
    ptpf = nc.alloc_psum_tensor("ptpf", [128, 512], F32)[0:128, 0:128]

    S = Script(nc)
    ctx.emit_init(S, (EPS, 1.0, -0.5))
    S.emit()

    for h in range(H):
        S = Script(nc)
        hw = 64 * (h % 2)
        S.op("sync", lambda e, h=h: e.dma_start(
            out=qstg[:, :], in_=qT2d[h * 64:(h + 1) * 64, :]),
            waits=[(pe, None), (dv, None)], incs=[(ld, 16)])
        S.op("sync", lambda e, h=h: e.dma_start(
            out=khat[0:64, :], in_=kT2f[h * 64:(h + 1) * 64, :]),
            incs=[(ld, 16)])
        if h == 0:
            S.op("vector", lambda e: e.memset(
                vhat[:, :].rearrange("p (kt c) -> p kt c", c=65)[:, :, 64:65],
                1.0),
                incs=[(dv, 1)])
        S.op("sync", lambda e, h=h: e.dma_start(
            out=vhat[:, :].rearrange("p (kt c) -> p kt c", c=65)[:, :, 0:64],
            in_=v2f.ap().rearrange("(kt p) d -> p kt d", p=128)[:, :, h * 64:(h + 1) * 64]),
            waits=[(dv, None)], incs=[(ld, 16)])
        S.op("vector", lambda e: e.tensor_scalar_mul(qhat[0:64, :], qstg[:, :], 0.125),
             waits=[(ld, None)], incs=[(dv, 1)])
        S.op("vector", lambda e: e.memset(khat[64:65, :], 1.0), incs=[(dv, 1)])
        for t in range(NT):
            for kb in range(8):
                pbuf = p512a if kb % 2 == 0 else p512b
                S.op("tensor", lambda e, t=t, kb=kb, pbuf=pbuf: e.matmul(
                    pbuf[:, :], qhat[0:64, t * 128:(t + 1) * 128],
                    khat[0:64, kb * 512:(kb + 1) * 512], start=True, stop=True),
                    waits=[(dv, None), (ld, None)], incs=[(pe, 1)])
                S.op("vector", lambda e, kb=kb, pbuf=pbuf: e.reduce_max(
                    mxr[:, 1:2] if kb > 0 else mxr[:, 0:1], pbuf[:, :], axis=AX.X),
                    waits=[(pe, None)], incs=[(dv, 1)])
                if kb > 0:
                    S.op("vector", lambda e: e.tensor_max(
                        mxr[:, 0:1], mxr[:, 0:1], mxr[:, 1:2]),
                        waits=[(dv, None)], incs=[(dv, 1)])
            S.op("vector", lambda e: e.tensor_scalar_mul(negc[:, :], mxr[:, 0:1], -1.0),
                 waits=[(dv, None)], incs=[(dv, 1)])
            S.op("tensor", lambda e: e.transpose(ptpf[0:1, 0:128], negc[:, :],
                                                 ctx.idf[:, :]),
                 waits=[(dv, None)], incs=[(pe, 1)])
            S.op("vector", lambda e, t=t: e.tensor_copy(
                qhat[64:65, t * 128:(t + 1) * 128], ptpf[0:1, 0:128]),
                waits=[(pe, None)], incs=[(dv, 1)])
        for qb in range(4):
            for kt in range(NKT):
                pbuf = p512a if kt % 2 == 0 else p512b
                S.op("tensor", lambda e, qb=qb, kt=kt, pbuf=pbuf: e.matmul(
                    pbuf[:, :], khat[0:65, kt * 128:(kt + 1) * 128],
                    qhat[0:65, qb * 512:(qb + 1) * 512], start=True, stop=True),
                    waits=[(ac, None)], incs=[(pe, 1)])
                S.op("scalar", lambda e, kt=kt, pbuf=pbuf: e.activation(
                    ptile[kt % 2][:, :], pbuf[:, :], AF.Exp),
                    waits=[(pe, None)], incs=[(ac, 1)])
                S.op("tensor", lambda e, kt=kt: e.matmul(
                    po[:, :], vhat[:, kt * 65:(kt + 1) * 65], ptile[kt % 2][:, :],
                    start=(kt == 0), stop=(kt == NKT - 1)),
                    waits=[(ac, None)] + ([(dv, None)] if kt == 0 else []),
                    incs=[(pe, 1)])
            S.op("vector", lambda e: e.reciprocal(rs_row[:, :], po[64:65, :]),
                 waits=[(pe, None)], incs=[(dv, 1)])
            S.op("tensor", lambda e: e.matmul(
                pb[:, :], ctx.ones_r[0:1, 0:64], rs_row[:, :], start=True, stop=True),
                waits=[(dv, None)], incs=[(pe, 1)])
            S.op("vector", lambda e: e.tensor_copy(bc[:, :], pb[:, :]),
                 waits=[(pe, None)], incs=[(dv, 1)])
            S.op("vector", lambda e, h=h, qb=qb, hw=hw: e.tensor_tensor(
                attn_sb[h // 2][hw:hw + 64, qb * 512:(qb + 1) * 512],
                po[0:64, :], bc[:, :], ALU.mult),
                waits=[(dv, None)], incs=[(dv, 1)])
        S.emit()

    wout = [nc.alloc_sbuf_tensor(f"wout{c}", [128, D], BF16) for c in range(8)]
    bout = nc.alloc_sbuf_tensor("bout", [1, D], BF16)
    ytile = nc.alloc_sbuf_tensor("ytile", [128, D], F32)
    xat = nc.alloc_sbuf_tensor("xat", [128, D], F32)
    sxt = [nc.alloc_sbuf_tensor(f"sxt{i}", [128, D], F32) for i in range(2)]
    sq2 = nc.alloc_sbuf_tensor("sq2", [128, D], F32)
    cst2 = {k: nc.alloc_sbuf_tensor(f"c2_{k}", [128, 1], F32) for k in
            ("c0", "c1", "c2", "c3")}
    gb_t = {}
    grow = nc.alloc_sbuf_tensor("grow", [1, 2 * D], F32)
    for i, nm in enumerate(("ln1g", "ln1b")):
        gb_t[nm] = nc.alloc_sbuf_tensor(f"gbt_{nm}", [128, D], F32)
    xaT = [nc.alloc_sbuf_tensor(f"xaT{c}", [128, TOK], BF16) for c in range(8)]
    sbf = nc.alloc_sbuf_tensor("sbf", [128, D], BF16)
    stile = nc.alloc_sbuf_tensor("stile", [128, D], F32)

    S = Script(nc)
    for c in range(8):
        S.op("gpsimd", lambda e, c=c: e.dma_start(
            out=wout[c][:, :], in_=wod[c * 128:(c + 1) * 128, :]), incs=[(ldg, 16)])
    S.op("gpsimd", lambda e: e.dma_start(out=bout[:, :], in_=bod[:, :]),
         incs=[(ldg, 16)])
    for i, (nm, dd) in enumerate((("ln1g", ln1gd), ("ln1b", ln1bd))):
        S.op("sync", lambda e, i=i, dd=dd: e.dma_start(
            out=grow[:, i * D:(i + 1) * D], in_=dd[:, :]), incs=[(ld, 16)])
    for i, nm in enumerate(("ln1g", "ln1b")):
        for g in range(2):
            S.op("tensor", lambda e, i=i, g=g: e.matmul(
                pb[:, :] if False else p512c[:, :], ctx.ones_r[0:1, :],
                grow[:, i * D + g * 512: i * D + (g + 1) * 512],
                start=True, stop=True),
                waits=[(ld, None), (dv, None)], incs=[(pe, 1)])
            S.op("vector", lambda e, nm=nm, g=g: e.tensor_copy(
                gb_t[nm][:, g * 512:(g + 1) * 512], p512c[:, :]),
                waits=[(pe, None)], incs=[(dv, 1)])
    scr = dict(sq=sq2[:, :], c0=cst2["c0"][:, :], c1=cst2["c1"][:, :],
               c2=cst2["c2"][:, :], c3=cst2["c3"][:, :])
    for t in range(NT):
        tb = t % 2
        S.op("sync", lambda e, t=t, tb=tb: e.dma_start(
            out=sxt[tb][:, :], in_=x1d[t * 128:(t + 1) * 128, :]),
            waits=[(dv, None)], incs=[(ld, 16)])
        for g in range(2):
            pbuf = p512a if g == 0 else p512b
            pairs = [(attn_sb[c][:, t * 128:(t + 1) * 128],
                      wout[c][:, g * 512:(g + 1) * 512]) for c in range(8)]
            mm_group(S, ctx, pbuf[:, :], pairs,
                     waits_first=[(ldg, None), (ac, None)], stop_last=False)
            S.op("tensor", lambda e, g=g, pbuf=pbuf: e.matmul(
                pbuf[:, :], ctx.ones_rb[:, :], bout[:, g * 512:(g + 1) * 512],
                start=False, stop=True), incs=[(pe, 1)])
            S.op("scalar", lambda e, g=g, pbuf=pbuf: e.activation(
                ytile[:, g * 512:(g + 1) * 512], pbuf[:, :], AF.Copy),
                waits=[(pe, None)], incs=[(ac, 1)])
        emit_ln(ctx, S, sxt[tb][:, :], stile[:, :], scr,
                x_waits=[(ld, None), (dv, None)])
        S.op("vector", lambda e: e.tensor_tensor(
            ytile[:, :], stile[:, :], ytile[:, :], ALU.add),
            waits=[(ac, None), (dv, None)], incs=[(dv, 1)])
        S.op("vector", None, waits=[(st, None)])
        emit_ln(ctx, S, ytile[:, :], xat[:, :], scr, x_waits=[(dv, None)],
                gain_bias=(gb_t["ln1g"][:, :], gb_t["ln1b"][:, :]))
        S.op("gpsimd", lambda e, t=t: e.dma_start(
            out=o_x1a[t * 128:(t + 1) * 128, :], in_=xat[:, :]),
            waits=[(dv, None)], incs=[(st, 16)])
        S.op("vector", lambda e: e.tensor_copy(sbf[:, :], xat[:, :]),
             waits=[(dv, None)], incs=[(dv, 1)])
        for c in range(8):
            emit_tp(ctx, S, sbf[:, c * 128:(c + 1) * 128],
                    xaT[c][:, t * 128:(t + 1) * 128], ptpb[:, :])
    for c in range(8):
        S.op("gpsimd", lambda e, c=c: e.dma_start(
            out=o_xaT[c * 128:(c + 1) * 128, :], in_=xaT[c][:, :]),
            waits=[(dv, None)], incs=[(st, 16)])
    S.op("gpsimd", None, waits=[(st, None), (ldg, None)])
    S.emit()
    return nc, ctx




def prog3_moe(nc, ctx, hd):
    ld, ldg, st, pe, dv, ac = ctx.ld, ctx.ldg, ctx.st, ctx.pe, ctx.dv, ctx.ac
    xaT, gb_t, scr = hd["xaT"], hd["gb_t"], hd["scr"]
    p512a, p512b, p512c, p512d = hd["p512a"], hd["p512b"], hd["p512c"], hd["p512d"]
    po, pb, ptpb, ptpf = hd["po"], hd["pb"], hd["ptpb"], hd["ptpf"]
    sxt, ytile, xat, sbf = hd["sxt"], hd["ytile"], hd["xat"], hd["sbf"]

    gw_sb = nc.alloc_sbuf_tensor("gw_sb", [128, 8 * 16], BF16)
    gb_sb = nc.alloc_sbuf_tensor("gb_sb", [1, 8], BF16)
    b1c = nc.alloc_sbuf_tensor("b1c", [128, 16 * 8], F32)
    b2_sb = nc.alloc_sbuf_tensor("b2_sb", [8, D], F32)
    gcol = nc.alloc_sbuf_tensor("gcol", [128, 8], F32)
    gtmp = nc.alloc_sbuf_tensor("gtmp", [128, 8], F32)
    gred = nc.alloc_sbuf_tensor("gred", [128, 4], F32)
    w8 = nc.alloc_sbuf_tensor("w8", [128, NT * 8], F32)
    w8tt = nc.alloc_sbuf_tensor("w8tt", [8, 128], F32)
    w1e = nc.alloc_sbuf_tensor("w1e", [128, 8 * 2048], BF16)
    w2e = nc.alloc_sbuf_tensor("w2e", [128, 8 * 1024], BF16)
    hts = nc.alloc_sbuf_tensor("hts", [128, 16 * 1024], BF16)
    accs = nc.alloc_sbuf_tensor("accs", [128, 8 * 1024], F32)

    S = Script(nc)
    for c in range(8):
        S.op("gpsimd", lambda e, c=c: e.dma_start(
            out=gw_sb[:, c * 16:c * 16 + 8],
            in_=hd["gwd"][c * 128:(c + 1) * 128, :]), incs=[(ldg, 16)])
    S.op("gpsimd", lambda e: e.dma_start(out=gb_sb[:, :], in_=hd["gbd"][:, :]),
         incs=[(ldg, 16)])
    S.op("sync", lambda e: e.dma_start(
        out=b1c[:, :], in_=hd["b1td"].ap().rearrange("(f p) e -> p f e", p=128)),
        incs=[(ld, 16)])
    S.op("sync", lambda e: e.dma_start(out=b2_sb[:, :], in_=hd["b2d"][:, :]),
         incs=[(ld, 16)])
    for t in range(NT):
        pairs = [(xaT[kc][:, t * 128:(t + 1) * 128], gw_sb[:, kc * 16:kc * 16 + 8])
                 for kc in range(8)]
        mm_group(S, ctx, p512c[:, 0:8], pairs,
                 waits_first=[(ldg, None), (dv, None), (ld, None)], stop_last=False)
        S.op("tensor", lambda e: e.matmul(
            p512c[:, 0:8], ctx.ones_rb[:, :], gb_sb[:, :], start=False, stop=True),
            incs=[(pe, 1)])
        S.op("vector", lambda e: e.reduce_max(gred[:, 0:1], p512c[:, 0:8],
                                              axis=AX.X, negate=True),
             waits=[(pe, None)], incs=[(dv, 1)])
        S.op("scalar", lambda e: e.activation(
            gcol[:, :], p512c[:, 0:8], AF.Exp, bias=gred[:, 0:1],
            accum_out=gred[:, 1:2]),
            waits=[(dv, None)], incs=[(ac, 1)])
        S.op("vector", lambda e: e.reciprocal(gred[:, 2:3], gred[:, 1:2]),
             waits=[(ac, None)], incs=[(dv, 1)])
        S.op("vector", lambda e: e.tensor_scalar_mul(gcol[:, :], gcol[:, :],
                                                     gred[:, 2:3]),
             waits=[(dv, None)], incs=[(dv, 1)])
        S.op("vector", lambda e: e.reduce_max(gred[:, 2:3], gcol[:, :], axis=AX.X),
             waits=[(dv, None)], incs=[(dv, 1)])
        S.op("vector", lambda e: e.tensor_scalar(gtmp[:, :], gcol[:, :],
                                                 gred[:, 2:3], None, ALU.is_ge),
             waits=[(dv, None)], incs=[(dv, 1)])
        S.op("vector", lambda e: e.scalar_tensor_tensor(
            gtmp[:, :], gtmp[:, :], -1e30, gcol[:, :], ALU.mult, ALU.add),
            waits=[(dv, None)], incs=[(dv, 1)])
        S.op("vector", lambda e: e.reduce_max(gred[:, 3:4], gtmp[:, :], axis=AX.X),
             waits=[(dv, None)], incs=[(dv, 1)])
        S.op("vector", lambda e: e.tensor_scalar(gtmp[:, :], gcol[:, :],
                                                 gred[:, 3:4], None, ALU.is_ge),
             waits=[(dv, None)], incs=[(dv, 1)])
        S.op("vector", lambda e, t=t: e.tensor_tensor(
            w8[:, t * 8:(t + 1) * 8], gcol[:, :], gtmp[:, :], ALU.mult),
            waits=[(dv, None)], incs=[(dv, 1)])
    S.emit()

    for th in range(2):
        S = Script(nc)
        t0 = th * 8
        for tt in range(8):
            S.op("tensor", lambda e, tt=tt: e.transpose(
                ptpf[0:8, 0:128], w8[:, (t0 + tt) * 8:(t0 + tt + 1) * 8],
                ctx.idf[:, :]),
                waits=[(dv, None), (ac, None)], incs=[(pe, 1)])
            S.op("vector", lambda e: e.tensor_copy(w8tt[:, :], ptpf[0:8, 0:128]),
                 waits=[(pe, None)], incs=[(dv, 1)])
            S.op("tensor", lambda e, tt=tt: e.matmul(
                p512c[:, :], w8tt[:, :],
                b2_sb[:, 0:512], start=True, stop=True),
                waits=[(dv, None), (ac, None), (ld, None)], incs=[(pe, 1)])
            S.op("tensor", lambda e, tt=tt: e.matmul(
                p512d[:, :], w8tt[:, :],
                b2_sb[:, 512:1024], start=True, stop=True),
                incs=[(pe, 1)])
            S.op("scalar", lambda e, tt=tt: e.activation(
                accs[:, tt * 1024: tt * 1024 + 512], p512c[:, :], AF.Copy),
                waits=[(pe, None)], incs=[(ac, 1)])
            S.op("scalar", lambda e, tt=tt: e.activation(
                accs[:, tt * 1024 + 512:(tt + 1) * 1024], p512d[:, :], AF.Copy),
                waits=[(pe, None)], incs=[(ac, 1)])
        for e_i in range(8):
            for kc in range(8):
                S.op("gpsimd", lambda e, e_i=e_i, kc=kc: e.dma_start(
                    out=w1e[:, kc * 2048:(kc + 1) * 2048],
                    in_=hd["w1d"][e_i, kc * 128:(kc + 1) * 128, :]),
                    waits=([(pe, None)] if kc == 0 else []), incs=[(ldg, 16)])

            for ft in range(16):
                for tb2 in range(2):
                    pbuf = p512a if tb2 == 0 else p512b
                    pairs = [(w1e[:, kc * 2048 + ft * 128: kc * 2048 + (ft + 1) * 128],
                              xaT[kc][:, (t0 + tb2 * 4) * 128:(t0 + (tb2 + 1) * 4) * 128])
                             for kc in range(8)]
                    mm_group(S, ctx, pbuf[:, :], pairs,
                             waits_first=[(ldg, None), (ac, None)])
                    S.op("scalar", lambda e, ft=ft, tb2=tb2, e_i=e_i, pbuf=pbuf:
                         e.activation(hts[:, ft * 1024 + tb2 * 512:
                                          ft * 1024 + (tb2 + 1) * 512],
                                      pbuf[:, :], AF.Relu,
                                      bias=b1c[:, ft * 8 + e_i: ft * 8 + e_i + 1]),
                         waits=[(pe, None), (ld, None)], incs=[(ac, 1)])
            for fcg in range(2):
                for fc8 in range(8):
                    fc = fcg * 8 + fc8
                    S.op("gpsimd", lambda e, e_i=e_i, fc=fc, fc8=fc8: e.dma_start(
                        out=w2e[:, fc8 * 1024:(fc8 + 1) * 1024],
                        in_=hd["w2d"][e_i, fc * 128:(fc + 1) * 128, :]),
                        waits=([(pe, None)] if fc8 == 0 else []), incs=[(ldg, 16)])
                for tt in range(8):
                    for g in range(2):
                        pbuf = p512c if g == 0 else p512d
                        pairs = [(hts[:, (fcg * 8 + fc8) * 1024 + tt * 128:
                                      (fcg * 8 + fc8) * 1024 + (tt + 1) * 128],
                                  w2e[:, fc8 * 1024 + g * 512:
                                      fc8 * 1024 + (g + 1) * 512])
                                 for fc8 in range(8)]
                        mm_group(S, ctx, pbuf[:, :], pairs,
                                 waits_first=[(ldg, None), (ac, None), (dv, None)])
                        S.op("vector", lambda e, tt=tt, g=g, e_i=e_i, pbuf=pbuf:
                             e.scalar_tensor_tensor(
                                 accs[:, tt * 1024 + g * 512:
                                      tt * 1024 + (g + 1) * 512],
                                 pbuf[:, :], w8[:, (t0 + tt) * 8 + e_i:
                                                (t0 + tt) * 8 + e_i + 1],
                                 accs[:, tt * 1024 + g * 512:
                                      tt * 1024 + (g + 1) * 512],
                                 ALU.mult, ALU.add),
                             waits=[(pe, None)], incs=[(dv, 1)])
        S.emit()

        S = Script(nc)
        for tt in range(8):
            t = t0 + tt
            tb = t % 2
            S.op("sync", lambda e, t=t: e.dma_start(
                out=xat[:, :], in_=hd["x1a"][t * 128:(t + 1) * 128, :]),
                waits=[(dv, None), (ac, None), (st, None)], incs=[(ld, 16)])
            S.op("sync", lambda e, t=t, tb=tb: e.dma_start(
                out=sxt[tb][:, :], in_=hd["x1d"][t * 128:(t + 1) * 128, :]),
                waits=[(dv, None)], incs=[(ld, 16)])
            S.op("vector", lambda e, tt=tt: e.tensor_tensor(
                ytile[:, :], xat[:, :], accs[:, tt * 1024:(tt + 1) * 1024], ALU.add),
                waits=[(ld, None), (dv, None)], incs=[(dv, 1)])
            emit_ln(ctx, S, ytile[:, :], ytile[:, :], scr, x_waits=[(dv, None)],
                    gain_bias=(gb_t["ln2g"][:, :], gb_t["ln2b"][:, :]))
            emit_ln(ctx, S, ytile[:, :], ytile[:, :], scr, x_waits=[(dv, None)])
            S.op("vector", lambda e, tb=tb: e.tensor_tensor(
                xat[:, :], sxt[tb][:, :], ytile[:, :], ALU.add),
                waits=[(dv, None), (st, None)], incs=[(dv, 1)])
            S.op("gpsimd", lambda e, t=t: e.dma_start(
                out=hd["o_x2"][t * 128:(t + 1) * 128, :], in_=xat[:, :]),
                waits=[(dv, None)], incs=[(st, 16)])
            emit_ln(ctx, S, xat[:, :], sbf[:, :], scr, x_waits=[(dv, None)])
            for c in range(8):
                emit_tp(ctx, S, sbf[:, c * 128:(c + 1) * 128],
                        xaT[c][:, t * 128:(t + 1) * 128], ptpb[:, :])
        S.emit()

    wfm3 = [nc.alloc_sbuf_tensor(f"wfm3_{i}", [128, 8 * 128], BF16) for i in range(2)]
    wtm3 = [nc.alloc_sbuf_tensor("wtm3_0", [128, 8 * 512], BF16)] * 2
    stage3 = [nc.alloc_sbuf_tensor(f"stg3_{i}", [128, 512], BF16) for i in range(4)]
    sqf3 = nc.alloc_sbuf_tensor("sqf3", [128, 512], F32)
    bq3c = nc.alloc_sbuf_tensor("bq3cs", [128, 64], F32)
    bk3c = nc.alloc_sbuf_tensor("bk3cs", [128, 64], F32)
    bv3s = nc.alloc_sbuf_tensor("bv3s", [1, D], BF16)
    nrow512 = nc.alloc_sbuf_tensor("nrow512", [1, 512], F32)

    S = Script(nc)
    for c in range(8):
        S.op("sync", lambda e, c=c: e.dma_start(
            out=bq3c[:, 8 * c:8 * c + 1], in_=hd["bq3d"][c * 128:(c + 1) * 128, :]),
            incs=[(ld, 16)])
        S.op("sync", lambda e, c=c: e.dma_start(
            out=bk3c[:, 8 * c:8 * c + 1], in_=hd["bk3d"][c * 128:(c + 1) * 128, :]),
            incs=[(ld, 16)])
    S.op("gpsimd", lambda e: e.dma_start(out=bv3s[:, :], in_=hd["bv3d"][:, :]),
         incs=[(ldg, 16)])
    for (wdram, biascols, odram, nr) in ((hd["wq3d"], bq3c, hd["o_qt3"], hd["o_qnn"]),
                                         (hd["wk3d"], bk3c, hd["o_kt3"], hd["o_knn"])):
        for oc in range(8):
            b = oc % 2
            for kc in range(8):
                S.op("gpsimd", lambda e, kc=kc, oc=oc, b=b, wdram=wdram: e.dma_start(
                    out=wfm3[b][:, kc * 128:(kc + 1) * 128],
                    in_=wdram[kc * 128:(kc + 1) * 128, oc * 128:oc * 128 + 128]),
                    waits=([(pe, None)] if kc == 0 else []), incs=[(ldg, 16)])
            for tb in range(4):
                pbuf = p512a if tb % 2 == 0 else p512b
                pairs = [(wfm3[b][:, kc * 128:(kc + 1) * 128],
                          xaT[kc][:, tb * 512:(tb + 1) * 512]) for kc in range(8)]
                mm_group(S, ctx, pbuf[:, :], pairs,
                         waits_first=[(ldg, None), (ac, None), (dv, None)])
                S.op("vector", lambda e, pbuf=pbuf, tb=tb, oc=oc, biascols=biascols:
                     e.tensor_scalar(stage3[tb][:, :], pbuf[:, :],
                                     biascols[:, 8 * oc:8 * oc + 1], None,
                                     ALU.add),
                     waits=[(pe, None), (st, None), (ac, None), (ld, None)],
                     incs=[(dv, 1)])
                S.op("gpsimd", lambda e, oc=oc, tb=tb, odram=odram: e.dma_start(
                    out=odram[oc * 128:(oc + 1) * 128, tb * 512:(tb + 1) * 512],
                    in_=stage3[tb][:, :]),
                    waits=[(dv, None)], incs=[(st, 16)])
                S.op("scalar", lambda e, tb=tb: e.activation(
                    sqf3[:, :], stage3[tb][:, :], AF.Square),
                    waits=[(dv, None)], incs=[(ac, 1)])
                for hh in range(2):
                    head = 2 * oc + hh
                    S.op("tensor", lambda e, hh=hh: e.matmul(
                        pb[0:1, :], ctx.constcol(-0.5)[hh * 64:hh * 64 + 64, :],
                        sqf3[hh * 64:hh * 64 + 64, :], start=True, stop=True),
                        waits=[(ac, None), (dv, None), (st, None)], incs=[(pe, 1)])
                    S.op("vector", lambda e: e.tensor_copy(
                        nrow512[:, :], pb[0:1, :]),
                        waits=[(pe, None)], incs=[(dv, 1)])
                    S.op("gpsimd", lambda e, head=head, tb=tb, nr=nr: e.dma_start(
                        out=nr[head:head + 1, tb * 512:(tb + 1) * 512],
                        in_=nrow512[:, :]),
                        waits=[(dv, None)], incs=[(st, 16)])
    for g in range(2):
        wb = wtm3[g % 2]
        for kc in range(8):
            S.op("gpsimd", lambda e, kc=kc, g=g, wb=wb: e.dma_start(
                out=wb[:, kc * 512:(kc + 1) * 512],
                in_=hd["wv3d"][kc * 128:(kc + 1) * 128, g * 512:g * 512 + 512]),
                waits=([(pe, None), (ac, None)] if kc == 0 else []),
                incs=[(ldg, 16)])
        for t in range(NT):
            pbuf = p512a if t % 2 == 0 else p512b
            pairs = [(xaT[kc][:, t * 128:(t + 1) * 128],
                      wb[:, kc * 512:(kc + 1) * 512]) for kc in range(8)]
            mm_group(S, ctx, pbuf[:, :], pairs,
                     waits_first=[(ldg, None), (ac, None), (dv, None)],
                     stop_last=False)
            S.op("tensor", lambda e, g=g, pbuf=pbuf: e.matmul(
                pbuf[:, :], ctx.ones_rb[:, :], bv3s[:, g * 512:(g + 1) * 512],
                start=False, stop=True), incs=[(pe, 1)])
            S.op("scalar", lambda e, t=t, pbuf=pbuf: e.activation(
                stage3[2 + t % 2][:, :], pbuf[:, :], AF.Copy),
                waits=[(pe, None), (st, None)], incs=[(ac, 1)])
            S.op("gpsimd", lambda e, t=t, g=g: e.dma_start(
                out=hd["o_v3"][t * 128:(t + 1) * 128, g * 512:(g + 1) * 512],
                in_=stage3[2 + t % 2][:, :]),
                waits=[(ac, None)], incs=[(st, 16)])
    S.op("gpsimd", None, waits=[(st, None), (ldg, None)])
    S.emit()


def build_prog3b():
    nc = bass.Bass("TRN2", target_bir_lowering=False)
    ctx = Ctx(nc)
    ld, ldg, st, pe, dv, ac = ctx.ld, ctx.ldg, ctx.st, ctx.pe, ctx.dv, ctx.ac
    x1d = nc.dram_tensor("x1", [TOK, D], F32, kind="ExternalInput")
    x1ad = nc.dram_tensor("x1a", [TOK, D], F32, kind="ExternalInput")
    xaTd = nc.dram_tensor("xaT", [D, TOK], BF16, kind="ExternalInput")
    gwd = nc.dram_tensor("gW", [D, 8], F32, kind="ExternalInput")
    gbd = nc.dram_tensor("gb", [1, 8], F32, kind="ExternalInput")
    w1d = nc.dram_tensor("W1", [8, D, 2048], F32, kind="ExternalInput")
    b1td = nc.dram_tensor("b1T", [2048, 8], F32, kind="ExternalInput")
    w2d = nc.dram_tensor("W2", [8, 2048, D], F32, kind="ExternalInput")
    b2d = nc.dram_tensor("b2", [8, D], F32, kind="ExternalInput")
    ln2gd = nc.dram_tensor("ln2g", [1, D], F32, kind="ExternalInput")
    ln2bd = nc.dram_tensor("ln2b", [1, D], F32, kind="ExternalInput")
    wq3d = nc.dram_tensor("wq3", [D, D], F32, kind="ExternalInput")
    wk3d = nc.dram_tensor("wk3", [D, D], F32, kind="ExternalInput")
    wv3d = nc.dram_tensor("wv3", [D, D], F32, kind="ExternalInput")
    bq3d = nc.dram_tensor("bq3c", [D, 1], F32, kind="ExternalInput")
    bk3d = nc.dram_tensor("bk3c", [D, 1], F32, kind="ExternalInput")
    bv3d = nc.dram_tensor("bv3", [1, D], F32, kind="ExternalInput")
    o_x2 = nc.dram_tensor("x2", [TOK, D], F32, kind="ExternalOutput")
    o_qt3 = nc.dram_tensor("qT3", [D, TOK], BF16, kind="ExternalOutput")
    o_kt3 = nc.dram_tensor("kT3", [D, TOK], BF16, kind="ExternalOutput")
    o_v3 = nc.dram_tensor("V3", [TOK, D], BF16, kind="ExternalOutput")
    o_qnn = nc.dram_tensor("qnn", [H, TOK], F32, kind="ExternalOutput")
    o_knn = nc.dram_tensor("knn", [H, TOK], F32, kind="ExternalOutput")

    xaT = [nc.alloc_sbuf_tensor(f"xaT{c}", [128, TOK], BF16) for c in range(8)]
    ytile = nc.alloc_sbuf_tensor("ytile", [128, D], F32)
    xat = nc.alloc_sbuf_tensor("xat", [128, D], F32)
    sxt = [nc.alloc_sbuf_tensor(f"sxt{i}", [128, D], F32) for i in range(2)]
    sq2 = nc.alloc_sbuf_tensor("sq2", [128, D], F32)
    cst2 = {k: nc.alloc_sbuf_tensor(f"c2_{k}", [128, 1], F32) for k in
            ("c0", "c1", "c2", "c3")}
    sbf = nc.alloc_sbuf_tensor("sbf", [128, D], BF16)
    grow = nc.alloc_sbuf_tensor("grow", [1, 512], F32)
    gb_t = {nm: nc.alloc_sbuf_tensor(f"gbt_{nm}", [128, D], F32)
            for nm in ("ln2g", "ln2b")}
    p512a = nc.alloc_psum_tensor("p512a", [128, 512], F32)
    p512b = nc.alloc_psum_tensor("p512b", [128, 512], F32)
    p512c = nc.alloc_psum_tensor("p512c", [128, 512], F32)
    p512d = nc.alloc_psum_tensor("p512d", [128, 512], F32)
    po = nc.alloc_psum_tensor("po", [65, 512], F32)
    pb = nc.alloc_psum_tensor("pb", [64, 512], F32)
    ptpb = nc.alloc_psum_tensor("ptpb", [128, 1024], BF16)[0:128, 0:128]
    ptpf = nc.alloc_psum_tensor("ptpf", [128, 512], F32)[0:128, 0:128]
    scr = dict(sq=sq2[:, :], c0=cst2["c0"][:, :], c1=cst2["c1"][:, :],
               c2=cst2["c2"][:, :], c3=cst2["c3"][:, :])

    S = Script(nc)
    ctx.emit_init(S, (EPS, 1.0, -0.5))
    for c in range(8):
        S.op("sync", lambda e, c=c: e.dma_start(
            out=xaT[c][:, :], in_=xaTd[c * 128:(c + 1) * 128, :]),
            incs=[(ld, 16)])
    for nm, dd in (("ln2g", ln2gd), ("ln2b", ln2bd)):
        for g in range(2):
            S.op("sync", lambda e, dd=dd, g=g: e.dma_start(
                out=grow[:, :], in_=dd[:, g * 512:(g + 1) * 512]),
                waits=[(pe, None)], incs=[(ld, 16)])
            S.op("tensor", lambda e, g=g: e.matmul(
                p512c[:, :], ctx.ones_r[0:1, :], grow[:, :],
                start=True, stop=True),
                waits=[(ld, None), (dv, None)], incs=[(pe, 1)])
            S.op("vector", lambda e, nm=nm, g=g: e.tensor_copy(
                gb_t[nm][:, g * 512:(g + 1) * 512], p512c[:, :]),
                waits=[(pe, None)], incs=[(dv, 1)])
    S.emit()
    hd = dict(o_x2=o_x2, o_qt3=o_qt3, o_kt3=o_kt3, o_v3=o_v3, o_qnn=o_qnn,
              o_knn=o_knn, x1a=x1ad, x1d=x1d, xaT=xaT, gb_t=gb_t, scr=scr,
              gwd=gwd, gbd=gbd, w1d=w1d, b1td=b1td, w2d=w2d, b2d=b2d,
              wq3d=wq3d, wk3d=wk3d, wv3d=wv3d, bq3d=bq3d, bk3d=bk3d, bv3d=bv3d,
              p512a=p512a, p512b=p512b, p512c=p512c, p512d=p512d, po=po, pb=pb,
              ptpb=ptpb, ptpf=ptpf, sxt=sxt, ytile=ytile, xat=xat, sbf=sbf)
    prog3_moe(nc, ctx, hd)
    return nc, ctx



LNBASE = float(np.log(np.sqrt(64.0) / np.log1p(64.0)))


def build_prog4():
    nc = bass.Bass("TRN2", target_bir_lowering=False)
    ctx = Ctx(nc)
    ld, ldg, st, pe, dv, ac = ctx.ld, ctx.ldg, ctx.st, ctx.pe, ctx.dv, ctx.ac

    x2d = nc.dram_tensor("x2", [TOK, D], F32, kind="ExternalInput")
    qT3d = nc.dram_tensor("qT3", [D, TOK], BF16, kind="ExternalInput")
    qnnd = nc.dram_tensor("qnn", [H, TOK], F32, kind="ExternalInput")
    kT3f = nc.dram_tensor("kT3f", [D, NSEQ], BF16, kind="ExternalInput")
    knnf = nc.dram_tensor("knnf", [H, NSEQ], F32, kind="ExternalInput")
    v3f = nc.dram_tensor("V3f", [NSEQ, D], BF16, kind="ExternalInput")
    wo4d = nc.dram_tensor("wo4", [D, D], F32, kind="ExternalInput")
    bo4d = nc.dram_tensor("bo4", [1, D], F32, kind="ExternalInput")
    alphad = nc.dram_tensor("alpha", [1, 1], F32, kind="ExternalInput")
    wstd = nc.dram_tensor("WsT", [D, D], F32, kind="ExternalInput")
    wetd = nc.dram_tensor("WeT", [D, D], F32, kind="ExternalInput")
    wgd = nc.dram_tensor("Wg", [D, 7], F32, kind="ExternalInput")
    bias4d = nc.dram_tensor("bias4", [1, D], F32, kind="ExternalInput")
    o_x = nc.dram_tensor("xout", [TOK, D], F32, kind="ExternalOutput")
    import os as _os
    _p4dbg = _os.environ.get("P4_DBG", "0") == "1"
    if _p4dbg:
        x3_scr = nc.dram_tensor("x3_scr", [TOK, D], F32, kind="ExternalOutput")
        o_attn = nc.dram_tensor("o_attn", [D, TOK], BF16, kind="ExternalOutput")
        o_tst = nc.dram_tensor("o_tst", [128, NKT * 512], F32, kind="ExternalOutput")
        o_crow = nc.dram_tensor("o_crow", [64, 512], F32, kind="ExternalOutput")
        o_po = nc.dram_tensor("o_po", [64 * 65, 512], F32, kind="ExternalOutput")
        postg = nc.alloc_sbuf_tensor("postg", [65, 512], F32)
    else:
        x3_scr = nc.dram_tensor("x3_scr", [TOK, D], F32, kind="Internal")

    qhat2 = nc.alloc_sbuf_tensor("qhat2", [128, TOK], BF16)
    khat2 = nc.alloc_sbuf_tensor("khat2", [128, NSEQ], BF16)
    vhat = nc.alloc_sbuf_tensor("vhat", [128, NKT * 65], BF16)
    arena = nc.alloc_sbuf_tensor("arena4", [128, 65536], mybir.dt.uint8)
    tst = arena[:, 0:65536].bitcast(F32)
    Mrun = nc.alloc_sbuf_tensor("Mrun", [128, 512], F32)
    lbuf = nc.alloc_sbuf_tensor("lbuf", [128, 512], F32)
    rsd = [nc.alloc_sbuf_tensor(f"rsd{i}", [128, 512], F32) for i in range(2)]
    ubuf2 = [nc.alloc_sbuf_tensor(f"ubuf{i}", [128, 512], F32) for i in range(2)]
    fence = nc.alloc_sbuf_tensor("fence", [1, 1], F32)
    crow = nc.alloc_sbuf_tensor("crow", [1, 512], F32)
    ccol = nc.alloc_sbuf_tensor("ccol", [128, 1], F32)
    Csb = nc.alloc_sbuf_tensor("Csb", [128, 512], F32)
    hls = nc.alloc_sbuf_tensor("hls", [128, 1], F32)
    asb = nc.alloc_sbuf_tensor("asb", [1, 1], F32)
    ptile = [nc.alloc_sbuf_tensor(f"ptile{i}", [128, 512], BF16) for i in range(2)]
    rs_row = nc.alloc_sbuf_tensor("rs_row", [1, 512], F32)
    bcb = nc.alloc_sbuf_tensor("bcb", [64, 512], F32)
    attn_sb = [nc.alloc_sbuf_tensor(f"attn{i}", [128, TOK], BF16) for i in range(8)]
    p512a = nc.alloc_psum_tensor("p512a", [128, 512], F32)
    p512b = nc.alloc_psum_tensor("p512b", [128, 512], F32)
    p512c = nc.alloc_psum_tensor("p512c", [128, 512], F32)
    p512d = nc.alloc_psum_tensor("p512d", [128, 512], F32)
    po = nc.alloc_psum_tensor("po", [65, 512], F32)
    pC = nc.alloc_psum_tensor("pC", [128, 512], F32)
    ptpb = nc.alloc_psum_tensor("ptpb", [128, 1024], BF16)[0:128, 0:128]
    ptpf = nc.alloc_psum_tensor("ptpf", [128, 512], F32)[0:128, 0:128]

    S = Script(nc)
    ctx.emit_init(S, (EPS, 1.0, -0.5))
    S.op("sync", lambda e: e.dma_start(out=asb[:, :], in_=alphad[:, :]),
         incs=[(ld, 16)])
    S.op("vector", lambda e: e.tensor_scalar_mul(asb[:, :], asb[:, :],
                                                 0.5 * LNBASE),
         waits=[(ld, None)], incs=[(dv, 1)])
    S.op("tensor", lambda e: e.matmul(pC[:, 0:1], ctx.ones_r[0:1, :], asb[:, :],
                                      start=True, stop=True),
         waits=[(dv, None)], incs=[(pe, 1)])
    S.op("vector", lambda e: e.tensor_copy(hls[:, :], pC[:, 0:1]),
         waits=[(pe, None)], incs=[(dv, 1)])
    S.emit()

    for h in range(H):
        S = Script(nc)
        hw = 64 * (h % 2)
        S.op("sync", lambda e, h=h: e.dma_start(
            out=qhat2[0:64, :], in_=qT3d[h * 64:(h + 1) * 64, :]),
            waits=[(pe, None), (ac, None), (dv, None)], incs=[(ld, 16)])
        S.op("sync", lambda e, h=h: e.dma_start(
            out=khat2[0:64, :], in_=kT3f[h * 64:(h + 1) * 64, :]),
            incs=[(ld, 16)])
        if h == 0:
            S.op("vector", lambda e: e.memset(
                vhat[:, :].rearrange("p (kt c) -> p kt c", c=65)[:, :, 64:65],
                1.0),
                incs=[(dv, 1)])
        S.op("sync", lambda e, h=h: e.dma_start(
            out=vhat[:, :].rearrange("p (kt c) -> p kt c", c=65)[:, :, 0:64],
            in_=v3f.ap().rearrange("(kt p) d -> p kt d", p=128)[:, :, h * 64:(h + 1) * 64]),
            waits=[(dv, None)], incs=[(ld, 16)])
        S.op("vector", lambda e: e.memset(qhat2[64:66, :], 1.0), incs=[(dv, 1)])
        S.op("vector", lambda e: e.memset(khat2[64:66, :], 1.0), incs=[(dv, 1)])
        S.op("gpsimd", lambda e, h=h: e.dma_start(
            out=qhat2[64:65, :], in_=qnnd[h:h + 1, :]),
            waits=[(dv, None)], incs=[(ldg, 16)])
        S.op("gpsimd", lambda e, h=h: e.dma_start(
            out=khat2[65:66, :], in_=knnf[h:h + 1, :]),
            waits=[(dv, None)], incs=[(ldg, 16)])
        def pe_fence():
            pass

        for qb in range(4):
            qsl = slice(qb * 512, (qb + 1) * 512)
            def p1_mms(kt):
                pd = p512a if kt % 2 == 0 else p512b
                pn = p512c if kt % 2 == 0 else p512d
                S.op("tensor", lambda e, kt=kt, pd=pd, qsl=qsl: e.matmul(
                    pd[:, :], khat2[0:64, kt * 128:(kt + 1) * 128],
                    qhat2[0:64, qsl], start=True, stop=True),
                    waits=[(ld, None), (ldg, None), (dv, None)], incs=[(pe, 1)])
                S.op("tensor", lambda e, kt=kt, pn=pn, qsl=qsl: e.matmul(
                    pn[:, :], khat2[0:66, kt * 128:(kt + 1) * 128],
                    qhat2[0:66, qsl], start=True, stop=True),
                    incs=[(pe, 1)])

            def p1_ew(kt):
                pd = p512a if kt % 2 == 0 else p512b
                pn = p512c if kt % 2 == 0 else p512d
                S.op("scalar", lambda e, pn=pn: e.activation(
                    lbuf[:, :], pn[:, :], AF.Ln, bias=ctx.constcol(EPS),
                    scale=-2.0),
                    waits=[(pe, None), (dv, None)], incs=[(ac, 1)])
                S.op("scalar", lambda e, kt=kt: e.activation(
                    rsd[kt % 2][:, :], lbuf[:, :], AF.Exp, bias=hls[:, :],
                    scale=-0.5),
                    waits=[(ac, None), (dv, None)], incs=[(ac, 1)])
                S.op("vector", lambda e, kt=kt, pd=pd: e.tensor_tensor(
                    ubuf2[kt % 2][:, :], pd[:, :], rsd[kt % 2][:, :], ALU.mult),
                    waits=[(ac, None)], incs=[(dv, 1)])
                S.op("vector", lambda e: e.memset(fence[:, :], 0.0),
                     incs=[(dv, 1)])
                S.op("scalar", lambda e, kt=kt: e.activation(
                    tst[:, kt * 512:(kt + 1) * 512], ubuf2[kt % 2][:, :],
                    AF.Square),
                    waits=[(dv, None)], incs=[(ac, 1)])
                S.op("scalar", lambda e: e.activation(fence[:, :], fence[:, :],
                                                      AF.Square),
                     waits=[(ac, None)], incs=[(ac, 1)])
                if kt == 0:
                    S.op("vector", lambda e: e.tensor_copy(
                        Mrun[:, :], tst[:, 0:512]),
                        waits=[(ac, None), (pe, None)], incs=[(dv, 1)])
                else:
                    S.op("vector", lambda e, kt=kt: e.tensor_max(
                        Mrun[:, :], Mrun[:, :], tst[:, kt * 512:(kt + 1) * 512]),
                        waits=[(ac, None)], incs=[(dv, 1)])

            p1_mms(0)
            for kt in range(1, NKT):
                p1_mms(kt)
                p1_ew(kt - 1)
            pe_fence()
            p1_ew(NKT - 1)
            for j in range(4):
                S.op("tensor", lambda e, j=j: e.transpose(
                    ptpf[:, :], Mrun[:, j * 128:(j + 1) * 128], ctx.idf[:, :]),
                    waits=[(dv, None)], incs=[(pe, 1)])
                S.op("vector", lambda e: e.reduce_max(
                    ccol[:, :], ptpf[:, :], axis=AX.X),
                    waits=[(pe, None)], incs=[(dv, 1)])
                S.op("tensor", lambda e: e.transpose(
                    ptpf[0:1, 0:128], ccol[:, :], ctx.idf[:, :]),
                    waits=[(dv, None)], incs=[(pe, 1)])
                S.op("vector", lambda e, j=j: e.tensor_copy(
                    crow[:, j * 128:(j + 1) * 128], ptpf[0:1, 0:128]),
                    waits=[(pe, None)], incs=[(dv, 1)])
            if _p4dbg:
                S.op("gpsimd", lambda e, h=h, qb=qb: e.dma_start(
                    out=o_crow[h * 4 + qb:h * 4 + qb + 1, :], in_=crow[:, :]),
                    waits=[(dv, None)], incs=[(st, 16)])
                if h == 0 and qb == 0:
                    S.op("gpsimd", lambda e: e.dma_start(
                        out=o_tst[:, :], in_=tst[:, 0:NKT * 512]),
                        waits=[(dv, None), (ac, None)], incs=[(st, 16)])
            S.op("tensor", lambda e: e.matmul(
                pC[:, :], ctx.ones_r[0:1, :], crow[:, :], start=True, stop=True),
                waits=[(dv, None)] + ([(st, None)] if _p4dbg else []),
                incs=[(pe, 1)])
            def p2_sub(kt):
                S.op("vector", lambda e, kt=kt: e.tensor_sub(
                    tst[:, kt * 512:(kt + 1) * 512],
                    tst[:, kt * 512:(kt + 1) * 512], pC[:, :]),
                    waits=[(dv, None), (ac, None), (pe, None), (ld, None),
                           (ldg, None), (st, None)],
                    incs=[(dv, 1)])

            def p2_exp_pv(kt):
                S.op("scalar", lambda e, kt=kt: e.activation(
                    ptile[kt % 2][:, :], tst[:, kt * 512:(kt + 1) * 512], AF.Exp),
                    waits=[(dv, None), (pe, None)], incs=[(ac, 1)])
                S.op("tensor", lambda e, kt=kt: e.matmul(
                    po[:, :], vhat[:, kt * 65:(kt + 1) * 65], ptile[kt % 2][:, :],
                    start=(kt == 0), stop=(kt == NKT - 1)),
                    waits=[(ac, None)] + ([(dv, None)] if kt == 0 else []),
                    incs=[(pe, 1)])

            def p2_exp(kt):
                S.op("scalar", lambda e, kt=kt: e.activation(
                    ptile[kt % 2][:, :], tst[:, kt * 512:(kt + 1) * 512], AF.Exp),
                    waits=[(dv, None), (pe, None), (ac, None)], incs=[(ac, 1)])

            def p2_pv(kt):
                S.op("tensor", lambda e, kt=kt: e.matmul(
                    po[:, :], vhat[:, kt * 65:(kt + 1) * 65], ptile[kt % 2][:, :],
                    start=(kt == 0), stop=(kt == NKT - 1)),
                    waits=[(ac, None), (dv, None)],
                    incs=[(pe, 1)])

            for kt in range(NKT):
                p2_sub(kt)
                p2_exp(kt)
                p2_pv(kt)
            if _p4dbg:
                S.op("vector", lambda e: e.tensor_copy(postg[:, :], po[:, :]),
                     waits=[(pe, None), (st, None)], incs=[(dv, 1)])
                S.op("gpsimd", lambda e, h=h, qb=qb: e.dma_start(
                    out=o_po[(h * 4 + qb) * 65:(h * 4 + qb + 1) * 65, :],
                    in_=postg[:, :]),
                    waits=[(dv, None)], incs=[(st, 16)])
            S.op("vector", lambda e: e.reciprocal(rs_row[:, :], po[64:65, :]),
                 waits=[(pe, None)], incs=[(dv, 1)])
            S.op("tensor", lambda e: e.matmul(
                pC[0:64, :], ctx.ones_r[0:1, 0:64], rs_row[:, :],
                start=True, stop=True),
                waits=[(dv, None)], incs=[(pe, 1)])
            S.op("vector", lambda e: e.tensor_copy(bcb[:, :], pC[0:64, :]),
                 waits=[(pe, None)], incs=[(dv, 1)])
            S.op("vector", lambda e, h=h, qb=qb, hw=hw: e.tensor_tensor(
                attn_sb[h // 2][hw:hw + 64, qb * 512:(qb + 1) * 512],
                po[0:64, :], bcb[:, :], ALU.mult),
                waits=[(dv, None)], incs=[(dv, 1)])
        S.emit()

    wout = [arena[:, c * 2048:(c + 1) * 2048].bitcast(BF16) for c in range(8)]
    bout = nc.alloc_sbuf_tensor("bout", [1, D], BF16)
    wst = [arena[:, 16384 + c * 2048:16384 + (c + 1) * 2048].bitcast(BF16)
           for c in range(8)]
    wet = [arena[:, 32768 + c * 2048:32768 + (c + 1) * 2048].bitcast(BF16)
           for c in range(8)]
    wg_sb = arena[:, 49152:49152 + 256].bitcast(BF16)
    b4 = arena[0:1, 49536:49536 + 2048].bitcast(BF16)
    ytile = nc.alloc_sbuf_tensor("ytile", [128, D], F32)
    x3t = nc.alloc_sbuf_tensor("x3t", [128, D], F32)
    sxt = [nc.alloc_sbuf_tensor(f"sxt{i}", [128, D], F32) for i in range(2)]
    sq2 = nc.alloc_sbuf_tensor("sq2", [128, D], F32)
    cst2 = {k: nc.alloc_sbuf_tensor(f"c2_{k}", [128, 1], F32) for k in
            ("c0", "c1", "c2", "c3")}
    sbf = nc.alloc_sbuf_tensor("sbf", [128, D], BF16)
    s4t = [nc.alloc_sbuf_tensor(f"s4t{c}", [128, TOK], BF16) for c in range(8)]
    g7 = nc.alloc_sbuf_tensor("g7", [128, 7], F32)
    g7t = nc.alloc_sbuf_tensor("g7t", [128, 7], F32)
    gred = nc.alloc_sbuf_tensor("gred", [128, 6], F32)
    wscol = nc.alloc_sbuf_tensor("wscol", [128, NT], F32)
    shtmp = nc.alloc_sbuf_tensor("shtmp", [128, 512], F32)
    scr = dict(sq=sq2[:, :], c0=cst2["c0"][:, :], c1=cst2["c1"][:, :],
               c2=cst2["c2"][:, :], c3=cst2["c3"][:, :])

    S = Script(nc)
    if _p4dbg:
        for c in range(8):
            S.op("gpsimd", lambda e, c=c: e.dma_start(
                out=o_attn[c * 128:(c + 1) * 128, :], in_=attn_sb[c][:, :]),
                incs=[(st, 16)])
    for c in range(8):
        S.op("gpsimd", lambda e, c=c: e.dma_start(
            out=wout[c][:, :], in_=wo4d[c * 128:(c + 1) * 128, :]), incs=[(ldg, 16)])
    S.op("gpsimd", lambda e: e.dma_start(out=bout[:, :], in_=bo4d[:, :]),
         incs=[(ldg, 16)])
    for t in range(NT):
        tb = t % 2
        S.op("sync", lambda e, t=t, tb=tb: e.dma_start(
            out=sxt[tb][:, :], in_=x2d[t * 128:(t + 1) * 128, :]),
            waits=[(dv, None)], incs=[(ld, 16)])
        for g in range(2):
            pbuf = p512a if g == 0 else p512b
            pairs = [(attn_sb[c][:, t * 128:(t + 1) * 128],
                      wout[c][:, g * 512:(g + 1) * 512]) for c in range(8)]
            mm_group(S, ctx, pbuf[:, :], pairs,
                     waits_first=[(ldg, None), (ac, None)], stop_last=False)
            S.op("tensor", lambda e, g=g, pbuf=pbuf: e.matmul(
                pbuf[:, :], ctx.ones_rb[:, :], bout[:, g * 512:(g + 1) * 512],
                start=False, stop=True), incs=[(pe, 1)])
            S.op("scalar", lambda e, g=g, pbuf=pbuf: e.activation(
                ytile[:, g * 512:(g + 1) * 512], pbuf[:, :], AF.Copy),
                waits=[(pe, None)], incs=[(ac, 1)])
        emit_ln(ctx, S, ytile[:, :], ytile[:, :], scr, x_waits=[(ac, None)])
        S.op("vector", lambda e, tb=tb: e.tensor_tensor(
            x3t[:, :], sxt[tb][:, :], ytile[:, :], ALU.add),
            waits=[(ld, None), (dv, None), (st, None)], incs=[(dv, 1)])
        S.op("gpsimd", lambda e, t=t: e.dma_start(
            out=x3_scr[t * 128:(t + 1) * 128, :], in_=x3t[:, :]),
            waits=[(dv, None)], incs=[(st, 16)])
        emit_ln(ctx, S, x3t[:, :], sbf[:, :], scr, x_waits=[(dv, None)])
        for c in range(8):
            emit_tp(ctx, S, sbf[:, c * 128:(c + 1) * 128],
                    s4t[c][:, t * 128:(t + 1) * 128], ptpb[:, :])
    S.emit()

    S = Script(nc)
    for c in range(8):
        S.op("gpsimd", lambda e, c=c: e.dma_start(
            out=wst[c][:, :], in_=wstd[c * 128:(c + 1) * 128, :]), incs=[(ldg, 16)])
        S.op("gpsimd", lambda e, c=c: e.dma_start(
            out=wet[c][:, :], in_=wetd[c * 128:(c + 1) * 128, :]), incs=[(ldg, 16)])
        S.op("gpsimd", lambda e, c=c: e.dma_start(
            out=wg_sb[:, c * 16:c * 16 + 7],
            in_=wgd[c * 128:(c + 1) * 128, :]), incs=[(ldg, 16)])
    S.op("gpsimd", lambda e: e.dma_start(out=b4[:, :], in_=bias4d[:, :]),
         incs=[(ldg, 16)])
    for t in range(NT):
        pairs = [(s4t[kc][:, t * 128:(t + 1) * 128], wg_sb[:, kc * 16:kc * 16 + 7])
                 for kc in range(8)]
        mm_group(S, ctx, p512c[:, 0:7], pairs,
                 waits_first=[(ldg, None), (dv, None)])
        S.op("vector", lambda e: e.reduce_max(gred[:, 0:1], p512c[:, 0:7],
                                              axis=AX.X, negate=True),
             waits=[(pe, None)], incs=[(dv, 1)])
        S.op("scalar", lambda e: e.activation(
            g7[:, :], p512c[:, 0:7], AF.Exp, bias=gred[:, 0:1],
            accum_out=gred[:, 1:2]),
            waits=[(dv, None)], incs=[(ac, 1)])
        S.op("vector", lambda e: e.reciprocal(gred[:, 2:3], gred[:, 1:2]),
             waits=[(ac, None)], incs=[(dv, 1)])
        S.op("vector", lambda e: e.tensor_scalar_mul(g7[:, :], g7[:, :],
                                                     gred[:, 2:3]),
             waits=[(dv, None)], incs=[(dv, 1)])
        S.op("vector", lambda e: e.reduce_max(gred[:, 2:3], g7[:, :], axis=AX.X),
             waits=[(dv, None)], incs=[(dv, 1)])
        S.op("vector", lambda e: e.tensor_scalar(g7t[:, :], g7[:, :],
                                                 gred[:, 2:3], None, ALU.is_equal),
             waits=[(dv, None)], incs=[(dv, 1)])
        S.op("vector", lambda e: e.scalar_tensor_tensor(
            g7t[:, :], g7t[:, :], -1e30, g7[:, :], ALU.mult, ALU.add),
            waits=[(dv, None)], incs=[(dv, 1)])
        S.op("vector", lambda e: e.reduce_max(gred[:, 3:4], g7t[:, :], axis=AX.X),
             waits=[(dv, None)], incs=[(dv, 1)])
        S.op("vector", lambda e: e.tensor_tensor(gred[:, 4:5], gred[:, 2:3],
                                                 gred[:, 3:4], ALU.add),
             waits=[(dv, None)], incs=[(dv, 1)])
        S.op("vector", lambda e: e.reciprocal(gred[:, 5:6], gred[:, 4:5]),
             waits=[(dv, None)], incs=[(dv, 1)])
        S.op("vector", lambda e: e.tensor_tensor(gred[:, 4:5], gred[:, 2:3],
                                                 gred[:, 5:6], ALU.mult),
             waits=[(dv, None)], incs=[(dv, 1)])
        S.op("vector", lambda e: e.tensor_tensor(gred[:, 5:6], gred[:, 3:4],
                                                 gred[:, 5:6], ALU.mult),
             waits=[(dv, None)], incs=[(dv, 1)])
        S.op("vector", lambda e, t=t: e.tensor_tensor(
            wscol[:, t:t + 1], gred[:, 4:5], gred[:, 5:6], ALU.add),
            waits=[(dv, None)], incs=[(dv, 1)])
    for t in range(NT):
        tb = t % 2
        S.op("sync", lambda e, t=t, tb=tb: e.dma_start(
            out=sxt[tb][:, :], in_=x3_scr[t * 128:(t + 1) * 128, :]),
            waits=[(dv, None)], incs=[(ld, 16)])
        for g in range(2):
            pe_buf = p512a if g == 0 else p512b
            ps_buf = p512c if g == 0 else p512d
            pairs_e = [(s4t[kc][:, t * 128:(t + 1) * 128],
                        wet[kc][:, g * 512:(g + 1) * 512]) for kc in range(8)]
            mm_group(S, ctx, pe_buf[:, :], pairs_e,
                     waits_first=[(ldg, None), (dv, None)])
            pairs_s = [(s4t[kc][:, t * 128:(t + 1) * 128],
                        wst[kc][:, g * 512:(g + 1) * 512]) for kc in range(8)]
            mm_group(S, ctx, ps_buf[:, :], pairs_s,
                     waits_first=[(ac, None)], stop_last=False)
            S.op("tensor", lambda e, g=g, ps_buf=ps_buf: e.matmul(
                ps_buf[:, :], ctx.ones_rb[:, :], b4[:, g * 512:(g + 1) * 512],
                start=False, stop=True), incs=[(pe, 1)])
            S.op("scalar", lambda e, g=g, ps_buf=ps_buf: e.activation(
                shtmp[:, :], ps_buf[:, :], AF.Copy),
                waits=[(pe, None), (dv, None)], incs=[(ac, 1)])
            S.op("vector", lambda e, g=g, t=t, pe_buf=pe_buf: e.scalar_tensor_tensor(
                ytile[:, g * 512:(g + 1) * 512], pe_buf[:, :],
                wscol[:, t:t + 1], shtmp[:, :], ALU.mult, ALU.add),
                waits=[(ac, None)], incs=[(dv, 1)])
        emit_ln(ctx, S, ytile[:, :], ytile[:, :], scr, x_waits=[(dv, None)])
        S.op("vector", lambda e, tb=tb: e.tensor_tensor(
            x3t[:, :], sxt[tb][:, :], ytile[:, :], ALU.add),
            waits=[(ld, None), (dv, None), (st, None)], incs=[(dv, 1)])
        S.op("gpsimd", lambda e, t=t: e.dma_start(
            out=o_x[t * 128:(t + 1) * 128, :], in_=x3t[:, :]),
            waits=[(dv, None)], incs=[(st, 16)])
    S.op("gpsimd", None, waits=[(st, None), (ldg, None)])
    S.emit()
    return nc, ctx

RG = [[0, 1], [2, 3], [4, 5], [6, 7]]
NH = 8
HC = NH * HD


class Arena:
    def __init__(self, nc, nbytes=184 * 1024):
        self.t = nc.alloc_sbuf_tensor("ar_back", [128, nbytes], mybir.dt.uint8)
        self.nb = nbytes
        self.off = 0

    def alloc(self, cols, dtype):
        sz = cols * mybir.dt.size(dtype)
        sz32 = (sz + 31) & ~31
        assert self.off + sz32 <= self.nb, (self.off, sz32, self.nb)
        ap = self.t[:, self.off:self.off + sz].bitcast(dtype)
        self.off += sz32
        return ap

    def at(self, off, cols, dtype):
        return self.t[:, off:off + cols * mybir.dt.size(dtype)].bitcast(dtype)

    def reset(self):
        self.off = 0


def build_fused():
    nc = bass.Bass("TRN2", target_bir_lowering=False)
    ctx = Ctx(nc)
    cc = Sem(nc, "s_cc")
    ld, ldg, st, pe, dv, ac = ctx.ld, ctx.ldg, ctx.st, ctx.pe, ctx.dv, ctx.ac
    ar = Arena(nc)

    EI = lambda n, s, d=F32: nc.dram_tensor(n, s, d, kind="ExternalInput")
    xs = EI("xs", [TOK, D])
    w_qkv = EI("w_qkv", [D, 3 * D])
    w_kv = EI("w_kv", [D, 2 * D])
    ropec = EI("ropec", [TOK, D])
    ropes = EI("ropes", [TOK, D])
    ropecq = EI("ropecq", [NSEQ, HD])
    ropesq = EI("ropesq", [NSEQ, HD])
    woutd = EI("w_out", [D, D]); boutd = EI("b_out", [1, D])
    wq2d = EI("wq2", [D, D]); wk2d = EI("wk2", [D, D]); wv2d = EI("wv2", [D, D])
    bq2d = EI("bq2c", [D, 1]); bk2d = EI("bk2c", [D, 1]); bv2d = EI("bv2", [1, D])
    wo2d = EI("wo2", [D, D]); bo2d = EI("bo2", [1, D])
    gwd = EI("gW", [D, 8]); gbd = EI("gb", [1, 8])
    w1d = EI("W1", [8, D, 2048]); b1td = EI("b1T", [2048, 8])
    w2d = EI("W2", [8, 2048, D]); b2d = EI("b2", [8, D])
    ln1gd = EI("ln1g", [1, D]); ln1bd = EI("ln1b", [1, D])
    ln2gd = EI("ln2g", [1, D]); ln2bd = EI("ln2b", [1, D])
    wq3d = EI("wq3", [D, D]); wk3d = EI("wk3", [D, D]); wv3d = EI("wv3", [D, D])
    bq3d = EI("bq3c", [D, 1]); bk3d = EI("bk3c", [D, 1]); bv3d = EI("bv3", [1, D])
    wo4d = EI("wo4", [D, D]); bo4d = EI("bo4", [1, D])
    alphad = EI("alpha", [1, 1])
    wstd = EI("WsT", [D, D]); wetd = EI("WeT", [D, D])
    wgd = EI("Wg", [D, 7]); bias4d = EI("bias4", [1, D])
    flagsd = EI("flags", [128, 2])
    o_x = nc.dram_tensor("xout", [TOK, D], F32, kind="ExternalOutput")

    IN = lambda n, s, d: nc.dram_tensor(n, s, d, kind="Internal")
    d_qT_c = IN("d_qT_c", [D, TOK], BF16)
    d_kr_c = IN("d_kr_c", [D, TOK], BF16)
    d_kk_c = IN("d_kk_c", [TOK, D], BF16)
    d_vk_c = IN("d_vk_c", [TOK, D], BF16)
    d_v_c = IN("d_v_c", [TOK, D], BF16)
    d_qT = IN("d_qT", [2 * D, TOK], BF16)
    d_kr = IN("d_kr", [2 * D, TOK], BF16)
    d_kk = IN("d_kk", [NSEQ, D], BF16)
    d_vk = IN("d_vk", [NSEQ, D], BF16)
    d_v = IN("d_v", [NSEQ, D], BF16)
    d_at_c = IN("d_at_c", [HC, NSEQ], BF16)
    d_at = IN("d_at", [2 * HC, NSEQ], BF16)
    d_kt2_c = IN("d_kt2_c", [D, TOK], BF16)
    d_v2_c = IN("d_v2_c", [TOK, D], BF16)
    d_kt2 = IN("d_kt2", [2 * D, TOK], BF16)
    d_v2 = IN("d_v2", [NSEQ, D], BF16)
    d_kt3_c = IN("d_kt3_c", [D, TOK], BF16)
    d_v3_c = IN("d_v3_c", [TOK, D], BF16)
    d_knn_c = IN("d_knn_c", [H, TOK], F32)
    d_kt3 = IN("d_kt3", [2 * D, TOK], BF16)
    d_v3 = IN("d_v3", [NSEQ, D], BF16)
    d_knn = IN("d_knn", [2 * H, TOK], F32)
    d_x1 = IN("d_x1", [TOK, D], F32)
    d_qT2 = IN("d_qT2", [D, TOK], BF16)
    d_x1a = IN("d_x1a", [TOK, D], F32)
    d_xaT = IN("d_xaT", [D, TOK], BF16)
    d_x2 = IN("d_x2", [TOK, D], F32)
    d_qT3 = IN("d_qT3", [D, TOK], BF16)
    d_qnn = IN("d_qnn", [H, TOK], F32)
    d_x3 = IN("d_x3", [TOK, D], F32)

    tri_a, trit_a, lowt_a, low01_a, id8_a = mask_arrays()
    tri_d = nc.inline_tensor(tri_a, name="tri_d")
    trit_d = nc.inline_tensor(trit_a, name="trit_d")
    lowt_d = nc.inline_tensor(lowt_a, name="lowt_d")
    low01_d = nc.inline_tensor(low01_a, name="low01_d")
    id8_d = nc.inline_tensor(id8_a, name="id8_d")
    tri = nc.alloc_sbuf_tensor("tri", [128, 128], F32)
    trit = nc.alloc_sbuf_tensor("trit", [128, 128], F32)
    lowt = nc.alloc_sbuf_tensor("lowt", [128, 128], F32)
    low01 = nc.alloc_sbuf_tensor("low01", [128, 128], F32)
    id8 = nc.alloc_sbuf_tensor("id8", [64, 64], F32)
    lnsq = nc.alloc_sbuf_tensor("lnsq", [128, D], F32)
    cst = {k: nc.alloc_sbuf_tensor(f"cg_{k}", [128, 1], F32) for k in
           ("c0", "c1", "c2", "c3")}
    scr = dict(sq=lnsq[:, :], c0=cst["c0"][:, :], c1=cst["c1"][:, :],
               c2=cst["c2"][:, :], c3=cst["c3"][:, :])
    p512a = nc.alloc_psum_tensor("p512a", [128, 512], F32)
    p512b = nc.alloc_psum_tensor("p512b", [128, 512], F32)
    p512c = nc.alloc_psum_tensor("p512c", [128, 512], F32)
    p512d = nc.alloc_psum_tensor("p512d", [128, 512], F32)
    po = nc.alloc_psum_tensor("po", [65, 512], F32)
    pb = nc.alloc_psum_tensor("pbx", [128, 512], F32)
    ptpb = nc.alloc_psum_tensor("ptpb", [128, 1024], BF16)[0:128, 0:128]
    ptpf = nc.alloc_psum_tensor("ptpf", [128, 512], F32)[0:128, 0:128]

    flg = nc.alloc_sbuf_tensor("flg", [128, 2], F32)
    S = Script(nc)
    ctx.emit_init(S, (EPS, 1.0, -0.5))
    for dst, srcd in ((tri, tri_d), (trit, trit_d), (lowt, lowt_d),
                      (low01, low01_d), (id8, id8_d)):
        S.op("sync", lambda e, dst=dst, srcd=srcd: e.dma_start(
            out=dst[:, :], in_=srcd.ap()), incs=[(ld, 16)])
    S.op("sync", lambda e: e.dma_start(out=flg[:, :], in_=flagsd[:, :]),
         incs=[(ld, 16)])
    S.emit()

    sx = [ar.alloc(D, F32) for _ in range(2)]
    xln = ar.alloc(D, BF16)
    xlt = [ar.alloc(TOK, BF16) for _ in range(8)]
    wfm = [ar.alloc(8 * 128, BF16) for _ in range(2)]
    wtm = [ar.alloc(8 * 512, BF16) for _ in range(2)]
    ropci = [ar.alloc(512, F32) for _ in range(2)]
    ropsi = [ar.alloc(512, F32) for _ in range(2)]
    kf32 = ar.alloc(512, F32)
    tmp = ar.alloc(512, F32)
    tmp2 = ar.alloc(512, F32)
    stage = [ar.alloc(512, BF16) for _ in range(4)]

    S = Script(nc)
    for t in range(NT):
        b = t % 2
        S.op("sync", lambda e, t=t, b=b: e.dma_start(
            out=sx[b][:, :], in_=xs[t * 128:(t + 1) * 128, :]),
            waits=[(dv, None)], incs=[(ld, 16)])
        emit_ln(ctx, S, sx[b][:, :], xln[:, :], scr, x_waits=[(ld, None)])
        for c in range(8):
            emit_tp(ctx, S, xln[:, c * 128:(c + 1) * 128],
                    xlt[c][:, t * 128:(t + 1) * 128], ptpb)
    S.emit()

    S = Script(nc)

    def fm_proj(wdram, colbase, dst_dram, bias_cols=None, scale=None):
        for oc in range(8):
            b = oc % 2
            for kc in range(8):
                S.op("gpsimd", lambda e, kc=kc, oc=oc, b=b, wdram=wdram,
                     colbase=colbase: e.dma_start(
                         out=wfm[b][:, kc * 128:(kc + 1) * 128],
                         in_=wdram[kc * 128:(kc + 1) * 128,
                                   colbase + oc * 128:colbase + oc * 128 + 128]),
                     waits=([(pe, None)] if kc == 0 else []), incs=[(ldg, 16)])
            for tb in range(4):
                pbuf = p512a if tb % 2 == 0 else p512b
                pairs = [(wfm[b][:, kc * 128:(kc + 1) * 128],
                          xlt[kc][:, tb * 512:(tb + 1) * 512]) for kc in range(8)]
                mm_group(S, ctx, pbuf[:, :], pairs,
                         waits_first=[(ldg, None), (ac, None), (dv, None)])
                if bias_cols is not None:
                    S.op("vector", lambda e, pbuf=pbuf, tb=tb, oc=oc: e.tensor_scalar(
                        stage[tb][:, :], pbuf[:, :], bias_cols[:, 8 * oc:8 * oc + 1],
                        None, ALU.add),
                        waits=[(pe, None), (st, None), (ld, None)], incs=[(dv, 1)])
                elif scale is not None:
                    S.op("vector", lambda e, pbuf=pbuf, tb=tb: e.tensor_scalar(
                        stage[tb][:, :], pbuf[:, :], scale, None, ALU.mult),
                        waits=[(pe, None), (st, None)], incs=[(dv, 1)])
                else:
                    S.op("vector", lambda e, pbuf=pbuf, tb=tb: e.tensor_copy(
                        stage[tb][:, :], pbuf[:, :]),
                        waits=[(pe, None), (st, None)], incs=[(dv, 1)])
                S.op("gpsimd", lambda e, oc=oc, tb=tb, dst_dram=dst_dram:
                     e.dma_start(
                         out=dst_dram[oc * 128:(oc + 1) * 128,
                                      tb * 512:(tb + 1) * 512],
                         in_=stage[tb][:, :]),
                     waits=[(dv, None)], incs=[(st, 16)])

    fm_proj(w_qkv, 0, d_qT_c)
    S.emit()

    S = Script(nc)
    ntp = [0]
    krt_half = [ar.alloc(TOK, BF16) for _ in range(8)]

    def tm_pass(wdram, col_base, evac):
        for g in range(2):
            wb = wtm[g % 2]
            for kc in range(8):
                S.op("gpsimd", lambda e, kc=kc, wdram=wdram, col0=col_base + g * 512,
                     wb=wb: e.dma_start(
                         out=wb[:, kc * 512:(kc + 1) * 512],
                         in_=wdram[kc * 128:(kc + 1) * 128, col0:col0 + 512]),
                     waits=([(pe, None), (ac, None)] if kc == 0 else []),
                     incs=[(ldg, 16)])
            for t in range(NT):
                pbuf = p512a if t % 2 == 0 else p512b
                pairs = [(xlt[kc][:, t * 128:(t + 1) * 128],
                          wb[:, kc * 512:(kc + 1) * 512]) for kc in range(8)]
                mm_group(S, ctx, pbuf[:, :], pairs,
                         waits_first=[(ldg, None), (ac, None), (dv, None)])
                evac(t, g, pbuf)

    def evac_k(t, g, psum):
        S.op("sync", lambda e, t=t, g=g: e.dma_start(
            out=ropci[t % 2][:, :], in_=ropec[t * 128:(t + 1) * 128,
                                             g * 512:(g + 1) * 512]),
            waits=[(dv, None)], incs=[(ld, 16)])
        S.op("sync", lambda e, t=t, g=g: e.dma_start(
            out=ropsi[t % 2][:, :], in_=ropes[t * 128:(t + 1) * 128,
                                             g * 512:(g + 1) * 512]),
            incs=[(ld, 16)])
        S.op("scalar", lambda e, psum=psum: e.activation(kf32[:, :], psum[:, :],
                                                         AF.Copy),
             waits=[(pe, None), (dv, None)], incs=[(ac, 1)])
        emit_rotary(ctx, S, kf32[:, :], ropci[t % 2][:, :], ropsi[t % 2][:, :],
                    tmp[:, :], tmp2[:, :], stage[0][:, :],
                    waits=[(ac, None), (ld, None)])
        for j in range(4):
            emit_tp(ctx, S, stage[0][:, j * 128:(j + 1) * 128],
                    krt_half[g * 4 + j][:, t * 128:(t + 1) * 128], ptpb)

    def mk_evac_tm(dst_dram):
        def evac(t, g, psum):
            sb = stage[2 + (t % 2)]
            S.op("scalar", lambda e, psum=psum, sb=sb: e.activation(
                sb[:, :], psum[:, :], AF.Copy),
                waits=[(pe, None), (st, None)], incs=[(ac, 1)])
            S.op("gpsimd", lambda e, t=t, g=g, sb=sb, dst_dram=dst_dram:
                 e.dma_start(
                     out=dst_dram[t * 128:(t + 1) * 128,
                                  g * 512:(g + 1) * 512],
                     in_=sb[:, :]),
                 waits=[(ac, None)], incs=[(st, 16)])
        return evac

    for g2 in range(2):
        wb = wtm[g2 % 2]
        for kc in range(8):
            S.op("gpsimd", lambda e, kc=kc, g2=g2, wb=wb: e.dma_start(
                out=wb[:, kc * 512:(kc + 1) * 512],
                in_=w_qkv[kc * 128:(kc + 1) * 128,
                          D + g2 * 512:D + g2 * 512 + 512]),
                waits=([(pe, None), (ac, None)] if kc == 0 else []),
                incs=[(ldg, 16)])
        for t in range(NT):
            pbuf = p512a if t % 2 == 0 else p512b
            pairs = [(xlt[kc][:, t * 128:(t + 1) * 128],
                      wb[:, kc * 512:(kc + 1) * 512]) for kc in range(8)]
            mm_group(S, ctx, pbuf[:, :], pairs,
                     waits_first=[(ldg, None), (ac, None), (dv, None)])
            evac_k(t, g2, pbuf)
        for j in range(4):
            S.op("gpsimd", lambda e, j=j, g2=g2: e.dma_start(
                out=d_kr_c[g2 * HC + j * 128:g2 * HC + (j + 1) * 128, :],
                in_=krt_half[g2 * 4 + j][:, :]),
                waits=[(dv, None)], incs=[(st, 16)])
    tm_pass(w_qkv, 2 * D, mk_evac_tm(d_v_c))
    tm_pass(w_kv, 0, mk_evac_tm(d_kk_c))
    tm_pass(w_kv, D, mk_evac_tm(d_vk_c))
    for (i_ap, o_ap) in ((d_qT_c, d_qT), (d_kr_c, d_kr), (d_kk_c, d_kk),
                         (d_vk_c, d_vk), (d_v_c, d_v)):
        S.op("gpsimd", lambda e, i_ap=i_ap, o_ap=o_ap: e.collective_compute(
            "AllGather", ALU.bypass, replica_groups=RG,
            ins=[i_ap.ap()], outs=[o_ap.ap()]),
            waits=[(st, None), (ldg, None)], incs=[(cc, 1)])
    S.emit()
    return nc, ctx, dict(ar=ar, cc=cc, scr=scr, S=None, flg=flg,
                         tri=tri, trit=trit, lowt=lowt, low01=low01, id8=id8,
                         psum=(p512a, p512b, p512c, p512d, po, pb, ptpb, ptpf),
                         dram=dict(
                             xs=xs, ropecq=ropecq, ropesq=ropesq, woutd=woutd,
                             boutd=boutd, wq2d=wq2d, wk2d=wk2d, wv2d=wv2d,
                             bq2d=bq2d, bk2d=bk2d, bv2d=bv2d, wo2d=wo2d,
                             bo2d=bo2d, gwd=gwd, gbd=gbd, w1d=w1d, b1td=b1td,
                             w2d=w2d, b2d=b2d, ln1gd=ln1gd, ln1bd=ln1bd,
                             ln2gd=ln2gd, ln2bd=ln2bd, wq3d=wq3d, wk3d=wk3d,
                             wv3d=wv3d, bq3d=bq3d, bk3d=bk3d, bv3d=bv3d,
                             wo4d=wo4d, bo4d=bo4d, alphad=alphad, wstd=wstd,
                             wetd=wetd, wgd=wgd, bias4d=bias4d, o_x=o_x,
                             d_qT=d_qT, d_kr=d_kr, d_kk=d_kk, d_vk=d_vk,
                             d_v=d_v, d_at_c=d_at_c, d_at=d_at,
                             d_kt2_c=d_kt2_c, d_v2_c=d_v2_c, d_kt2=d_kt2,
                             d_v2=d_v2, d_kt3_c=d_kt3_c, d_v3_c=d_v3_c,
                             d_knn_c=d_knn_c, d_kt3=d_kt3, d_v3=d_v3,
                             d_knn=d_knn, d_x1=d_x1, d_qT2=d_qT2, d_x1a=d_x1a,
                             d_xaT=d_xaT, d_x2=d_x2, d_qT3=d_qT3, d_qnn=d_qnn,
                             d_x3=d_x3))


def phase1_l0a(nc, ctx, cc, ar, dd, masks, psum, flg):
    ld, ldg, st, pe, dv, ac = ctx.ld, ctx.ldg, ctx.st, ctx.pe, ctx.dv, ctx.ac
    (tri, trit, lowt, low01, id8) = masks
    (p512a, p512b, p512c, p512d, po, pb, ptpb, ptpf) = psum
    d_qT, d_kr, d_kk, d_vk, d_v = dd["d_qT"], dd["d_kr"], dd["d_kk"], dd["d_vk"], dd["d_v"]
    ropecq, ropesq = dd["ropecq"], dd["ropesq"]

    ar.reset()
    qTh = ar.alloc(NSEQ, BF16)[0:64, :]
    qhat = ar.alloc(NSEQ, BF16)
    khat = ar.alloc(NSEQ, BF16)
    vhat = ar.alloc(NKT * 65, BF16)
    kkh = ar.alloc(NKT * 64, BF16)
    vkh = ar.alloc(NKT * 64, BF16)
    kkTh = ar.alloc(NSEQ, BF16)[0:64, :]
    vkTh = ar.alloc(NSEQ, BF16)[0:64, :]
    sct = ar.alloc(NKT * 64, F32)[0:64, :]
    sctb = ar.alloc(NKT * 64, BF16)[0:64, :]
    run = ar.alloc(64, F32)[0:64, :]
    ssqcols = ar.alloc(NKT, F32)[0:64, :]
    ssqrow = ar.alloc(NKT, F32)[0:1, :]
    ssqb = ar.alloc(NKT, F32)
    sq64 = ar.alloc(64, F32)[0:64, :]
    sqst = ar.alloc(NKT * 64, F32)[0:64, :]
    g1 = ar.alloc(128, F32)
    gp = ar.alloc(128, F32)
    ucol = ar.alloc(8, F32)
    uprod = ar.alloc(64, F32)
    stdc = ar.alloc(3, F32)
    qpair = ar.alloc(128, F32)
    rot_t = ar.alloc(128, F32)
    rot_t2 = ar.alloc(128, F32)
    qr8 = ar.alloc(128, BF16)
    ropcs = [ar.alloc(128, F32) for _ in range(2)]
    ropss = [ar.alloc(128, F32) for _ in range(2)]
    mxr = ar.alloc(9, F32)
    negc = ar.alloc(1, F32)
    ptile = [ar.alloc(512, BF16) for _ in range(2)]
    atb = ar.alloc(128, BF16)
    sqall = ar.alloc(2048, F32)
    ukk = ar.alloc(32, F32)
    uvk = ar.alloc(32, F32)
    rs_row = ar.alloc(512, F32)[0:1, :]
    bc = ar.alloc(512, F32)[0:64, :]
    attn_sb = [ar.alloc(NSEQ, BF16) for _ in range(4)]
    stgp = [(ar.alloc(TOK, BF16), ar.alloc(TOK, BF16)) for _ in range(2)]
    i128 = np.arange(128)
    trit_blk = np.where(i128[:, None] <= i128[None, :], 0.0, NEG
                        ).astype(np.float32)
    m4_d = []
    for r in range(4):
        blocks = []
        for j in range(4):
            if j < r:
                blocks.append(np.full((128, 128), NEG, np.float32))
            elif j == r:
                blocks.append(trit_blk)
            else:
                blocks.append(np.zeros((128, 128), np.float32))
        m4_d.append(nc.inline_tensor(np.concatenate(blocks, 1),
                                     name=f"m4_{r}"))
    m4 = [ar.alloc(512, F32) for _ in range(4)]

    for i in range(NH):
        S = Script(nc)
        hw = 64 * (i % 2)
        if i == 0:
            S.op("sync", None, waits=[(cc, None)])
            for r in range(4):
                S.op("sync", lambda e, r=r: e.dma_start(
                    out=m4[r][:, :], in_=m4_d[r].ap()), incs=[(ld, 16)])
            S.op("vector", lambda e: e.memset(
                vhat[:, :].rearrange("p (kt c) -> p kt c", c=65)[:, :, 64:65],
                1.0), incs=[(dv, 1)])
            S.op("vector", lambda e: e.memset(khat[64:65, :], 1.0),
                 incs=[(dv, 1)])
        np_ = [0]

        def ld_pair(ops, blends):
            qi = np_[0] % 2
            eng, sem = ("sync", ld) if qi == 0 else ("gpsimd", ldg)
            sA, sB = stgp[qi]
            np_[0] += 1
            for k, mk in enumerate(ops):
                S.op(eng, mk(sA if k == 0 else sB),
                     waits=([(dv, None)] if k == 0 else []), incs=[(sem, 16)])
            dst, sAv, sBv, P = blends(sA, sB)
            S.op("vector", lambda e, dst=dst, sAv=sAv, P=P: e.tensor_scalar_mul(
                dst, sAv, flg[0:P, 0:1]),
                waits=[(sem, None), (pe, None), (dv, None)], incs=[(dv, 1)])
            S.op("vector", lambda e, dst=dst, sBv=sBv, P=P: e.scalar_tensor_tensor(
                dst, sBv, flg[0:P, 1:2], dst, ALU.mult, ALU.add),
                waits=[(dv, None)], incs=[(dv, 1)])

        for blk_ in range(2):
            for (dst, dsrc) in ((qTh[:, blk_ * TOK:(blk_ + 1) * TOK], d_qT),
                                (khat[0:64, blk_ * TOK:(blk_ + 1) * TOK], d_kr)):
                def mk(off):
                    def m(s_):
                        return lambda e, s_=s_, dsrc=dsrc, blk_=blk_, i=i, off=off: \
                            e.dma_start(out=s_[0:64, :],
                                        in_=dsrc[blk_ * D + off + i * 64:
                                                 blk_ * D + off + i * 64 + 64, :])
                    return m
                ld_pair([mk(0), mk(HC)],
                        lambda sA, sB, dst=dst: (dst, sA[0:64, :], sB[0:64, :], 64))
        for (dstv, dsrc) in (
                (kkh[:, :].rearrange("p (kt c) -> p kt c", c=64), d_kk),
                (vkh[:, :].rearrange("p (kt c) -> p kt c", c=64), d_vk),
                (vhat[:, :].rearrange("p (kt c) -> p kt c", c=65)[:, :, 0:64],
                 d_v)):
            def mk(off):
                def m(s_):
                    return lambda e, s_=s_, dsrc=dsrc, off=off, i=i: e.dma_start(
                        out=s_[:, :].rearrange("p (kt c) -> p kt c", c=64),
                        in_=dsrc.ap().rearrange("(kt p) c -> p kt c", p=128)
                        [:, :, off + i * 64:off + (i + 1) * 64])
                return m
            ld_pair([mk(0), mk(HC)],
                    lambda sA, sB, dstv=dstv: (
                        dstv, sA[:, :].rearrange("p (kt c) -> p kt c", c=64),
                        sB[:, :].rearrange("p (kt c) -> p kt c", c=64), 128))
        S.op("vector", lambda e: e.tensor_copy(run[:, :], id8[:, :]),
             waits=[(ld, None)], incs=[(dv, 1)])
        for gc in range(NKT):
            S.op("vector", lambda e, gc=gc: e.tensor_copy(
                sct[:, gc * 64:(gc + 1) * 64], run[:, :]),
                waits=[(dv, None)], incs=[(dv, 1)])
            if gc < NKT - 1:
                S.op("tensor", lambda e, gc=gc: e.matmul(
                    po[0:64, 0:64], vkh[:, gc * 64:(gc + 1) * 64],
                    kkh[:, gc * 64:(gc + 1) * 64], start=True, stop=True),
                    waits=[(dv, None), (ld, None)], incs=[(pe, 1)])
                S.op("vector", lambda e: e.tensor_tensor(
                    run[:, :], run[:, :], po[0:64, 0:64], ALU.add),
                    waits=[(pe, None), (dv, None)], incs=[(dv, 1)])
        S.op("vector", lambda e: e.tensor_copy(sctb[:, :], sct[:, :]),
             waits=[(dv, None)], incs=[(dv, 1)])
        S.op("scalar", lambda e: e.activation(sqst[:, :], sct[:, :], AF.Square),
             waits=[(dv, None), (ac, None)], incs=[(ac, 1)])
        for gc in range(NKT):
            S.op("vector", lambda e, gc=gc: e.reduce_sum(
                ssqcols[:, gc:gc + 1], sqst[:, gc * 64:(gc + 1) * 64],
                axis=AX.X),
                waits=[(ac, None)], incs=[(dv, 1)])
        S.op("tensor", lambda e: e.matmul(
            pb[0:1, 0:NKT], ctx.constcol(1.0)[0:64, :], ssqcols[:, :],
            start=True, stop=True),
            waits=[(dv, None)], incs=[(pe, 1)])
        S.op("vector", lambda e: e.tensor_copy(ssqrow[:, :], pb[0:1, 0:NKT]),
             waits=[(pe, None)], incs=[(dv, 1)])
        S.op("tensor", lambda e: e.matmul(
            pb[0:128, 64:64 + NKT], ctx.ones_r[0:1, :], ssqrow[:, :],
            start=True, stop=True),
            waits=[(dv, None)], incs=[(pe, 1)])
        S.op("vector", lambda e: e.tensor_copy(ssqb[:, :], pb[:, 64:64 + NKT]),
             waits=[(pe, None)], incs=[(dv, 1)])
        for gc in range(0, NKT, 2):
            for (srcb, dstb) in ((kkh, kkTh), (vkh, vkTh)):
                S.op("tensor", lambda e, gc=gc, srcb=srcb: e.transpose(
                    ptpb[0:128, 0:128], srcb[:, gc * 64:(gc + 2) * 64],
                    ctx.idb[:, :]),
                    waits=[(ld, None), (dv, None)], incs=[(pe, 1)])
                S.op("vector", lambda e, gc=gc, dstb=dstb: e.tensor_copy(
                    dstb[:, gc * 128:(gc + 1) * 128], ptpb[0:64, 0:128]),
                    waits=[(pe, None)], incs=[(dv, 1)])
                S.op("vector", lambda e, gc=gc, dstb=dstb: e.tensor_copy(
                    dstb[:, (gc + 1) * 128:(gc + 2) * 128], ptpb[64:128, 0:128]),
                    waits=[(dv, None)], incs=[(dv, 1)])
        S.emit()

        S = Script(nc)
        for (srcb, ucols) in ((kkh, ukk), (vkh, uvk)):
            S.op("scalar", lambda e, srcb=srcb: e.activation(
                sqall[:, :], srcb[:, :], AF.Square),
                waits=[(ac, None), (dv, None)], incs=[(ac, 1)])
            for gc in range(NKT):
                S.op("vector", lambda e, gc=gc, ucols=ucols: e.reduce_sum(
                    ucols[:, gc:gc + 1], sqall[:, gc * 64:(gc + 1) * 64],
                    axis=AX.X),
                    waits=[(ac, None)], incs=[(dv, 1)])
        for t in range(NKT):
            if t % 2 == 0:
                pb2 = (t // 2) % 2
                for (dst, srcd) in ((ropcs[pb2], ropecq), (ropss[pb2], ropesq)):
                    for j in range(2):
                        S.op("sync", lambda e, dst=dst, srcd=srcd, j=j, t=t:
                             e.dma_start(
                                 out=dst[:, j * 64:(j + 1) * 64],
                                 in_=srcd[(t + j) * 128:(t + j + 1) * 128, :]),
                             waits=([(dv, None)] if dst is ropcs[pb2] and j == 0
                                    else []),
                             incs=[(ld, 16)])
            S.op("tensor", lambda e, t=t: e.matmul(
                p512c[:, 0:128], vkTh[:, t * 128:(t + 1) * 128],
                qTh[:, t * 128:(t + 1) * 128], start=True, stop=True),
                waits=[(dv, None)], incs=[(pe, 1)])
            S.op("vector", lambda e: e.tensor_tensor(
                atb[:, :], p512c[:, 0:128], lowt[:, :], ALU.mult),
                waits=[(pe, None)], incs=[(dv, 1)])
            S.op("tensor", lambda e, t=t: e.matmul(
                p512d[:, 0:64], atb[:, :], kkh[:, t * 64:(t + 1) * 64],
                start=True, stop=False),
                waits=[(dv, None)], incs=[(pe, 1)])
            S.op("tensor", lambda e, t=t: e.matmul(
                p512d[:, 0:64], qTh[:, t * 128:(t + 1) * 128],
                sctb[:, t * 64:(t + 1) * 64], start=False, stop=True),
                incs=[(pe, 1)])
            S.op("tensor", lambda e, t=t: e.matmul(
                p512d[:, 64:128], vkTh[:, t * 128:(t + 1) * 128],
                sctb[:, t * 64:(t + 1) * 64], start=True, stop=True),
                incs=[(pe, 1)])
            S.op("vector", lambda e, t=t: e.tensor_tensor(
                uprod[:, :], p512d[:, 64:128], kkh[:, t * 64:(t + 1) * 64],
                ALU.mult),
                waits=[(pe, None)], incs=[(dv, 1)])
            S.op("vector", lambda e: e.reduce_sum(ucol[:, 0:1], uprod[:, :],
                                                  axis=AX.X),
                 waits=[(dv, None)], incs=[(dv, 1)])
            S.op("tensor", lambda e, t=t: e.matmul(
                p512c[:, 128:256], kkTh[:, t * 128:(t + 1) * 128],
                kkTh[:, t * 128:(t + 1) * 128], start=True, stop=True),
                waits=[(dv, None)], incs=[(pe, 1)])
            S.op("tensor", lambda e, t=t: e.matmul(
                p512c[:, 256:384], vkTh[:, t * 128:(t + 1) * 128],
                vkTh[:, t * 128:(t + 1) * 128], start=True, stop=True),
                incs=[(pe, 1)])
            S.op("scalar", lambda e: e.activation(g1[:, :], p512c[:, 128:256],
                                                  AF.Copy),
                 waits=[(pe, None), (dv, None)], incs=[(ac, 1)])
            S.op("vector", lambda e: e.tensor_tensor(
                gp[:, :], p512c[:, 256:384], g1[:, :], ALU.mult),
                waits=[(ac, None)], incs=[(dv, 1)])
            S.op("vector", lambda e: e.tensor_tensor(
                gp[:, :], gp[:, :], low01[:, :], ALU.mult),
                waits=[(dv, None)], incs=[(dv, 1)])
            S.op("vector", lambda e: e.reduce_sum(ucol[:, 1:2], gp[:, :],
                                                  axis=AX.X),
                 waits=[(dv, None)], incs=[(dv, 1)])
            S.op("vector", lambda e, t=t: e.tensor_tensor(
                ucol[:, 4:5], ukk[:, t:t + 1], uvk[:, t:t + 1], ALU.mult),
                waits=[(dv, None)], incs=[(dv, 1)])
            S.op("vector", lambda e: e.tensor_tensor(
                ucol[:, 5:6], ucol[:, 0:1], ucol[:, 1:2], ALU.add),
                waits=[(dv, None)], incs=[(dv, 1)])
            S.op("vector", lambda e: e.tensor_scalar(
                ucol[:, 5:6], ucol[:, 5:6], 2.0, None, ALU.mult),
                waits=[(dv, None)], incs=[(dv, 1)])
            S.op("vector", lambda e: e.tensor_sub(ucol[:, 5:6], ucol[:, 5:6],
                                                  ucol[:, 4:5]),
                 waits=[(dv, None)], incs=[(dv, 1)])
            S.op("tensor", lambda e: e.matmul(
                pb[:, 256:257], lowt[:, :], ucol[:, 5:6], start=True, stop=True),
                waits=[(dv, None)], incs=[(pe, 1)])
            S.op("scalar", lambda e, t=t: e.activation(
                stdc[:, 0:1], pb[:, 256:257], AF.Sqrt, bias=ssqb[:, t:t + 1]),
                waits=[(pe, None)], incs=[(ac, 1)])
            S.op("vector", lambda e: e.tensor_scalar(
                stdc[:, 1:2], stdc[:, 0:1], 1e-8, None, ALU.add),
                waits=[(ac, None)], incs=[(dv, 1)])
            S.op("vector", lambda e: e.reciprocal(stdc[:, 2:3], stdc[:, 1:2]),
                 waits=[(dv, None)], incs=[(dv, 1)])
            S.op("vector", lambda e, t=t: e.tensor_scalar_mul(
                qpair[:, (t % 2) * 64:(t % 2) * 64 + 64], p512d[:, 0:64],
                stdc[:, 2:3]),
                waits=[(dv, None)], incs=[(dv, 1)])
            if t % 2 == 1:
                pb2 = (t // 2) % 2
                emit_rotary(ctx, S, qpair[:, :], ropcs[pb2][:, :],
                            ropss[pb2][:, :], rot_t[:, :], rot_t2[:, :],
                            qr8[:, :], waits=[(dv, None), (ld, None)])
                S.op("tensor", lambda e, t=t: e.transpose(
                    ptpb[0:128, 0:128], qr8[:, :], ctx.idb[:, :]),
                    waits=[(dv, None)], incs=[(pe, 1)])
                S.op("vector", lambda e, t=t: e.tensor_copy(
                    qhat[0:64, (t - 1) * 128:t * 128], ptpb[0:64, 0:128]),
                    waits=[(pe, None)], incs=[(dv, 1)])
                S.op("vector", lambda e, t=t: e.tensor_copy(
                    qhat[0:64, t * 128:(t + 1) * 128], ptpb[64:128, 0:128]),
                    waits=[(dv, None)], incs=[(dv, 1)])
        S.emit()

        S = Script(nc)
        for g in range(NKT):
            nkb = (g + 4) // 4
            for kb in range(nkb):
                n = min(512, (g + 1) * 128 - kb * 512)
                pbuf = p512a if kb % 2 == 0 else p512b
                S.op("tensor", lambda e, g=g, kb=kb, n=n, pbuf=pbuf: e.matmul(
                    pbuf[:, 0:n], qhat[0:64, g * 128:(g + 1) * 128],
                    khat[0:64, kb * 512:kb * 512 + n], start=True, stop=True),
                    waits=[(dv, None)], incs=[(pe, 1)])
                if kb == nkb - 1:
                    S.op("vector", lambda e, n=n, pbuf=pbuf: e.tensor_tensor(
                        pbuf[:, n - 128:n], pbuf[:, n - 128:n], tri[:, :],
                        ALU.add),
                        waits=[(pe, None)], incs=[(dv, 1)])
                S.op("vector", lambda e, kb=kb, n=n, pbuf=pbuf: e.reduce_max(
                    mxr[:, kb:kb + 1], pbuf[:, 0:n], axis=AX.X),
                    waits=[(pe, None), (dv, None)], incs=[(dv, 1)])
            if nkb > 1:
                S.op("vector", lambda e, nkb=nkb: e.reduce_max(
                    mxr[:, 8:9], mxr[:, 0:nkb], axis=AX.X),
                    waits=[(dv, None)], incs=[(dv, 1)])
            mc = 8 if nkb > 1 else 0
            S.op("vector", lambda e, mc=mc: e.tensor_scalar_mul(
                negc[:, :], mxr[:, mc:mc + 1], -1.0),
                waits=[(dv, None)], incs=[(dv, 1)])
            S.op("tensor", lambda e: e.transpose(ptpf[0:1, 0:128], negc[:, :],
                                                 ctx.idf[:, :]),
                 waits=[(dv, None)], incs=[(pe, 1)])
            S.op("vector", lambda e, g=g: e.tensor_copy(
                qhat[64:65, g * 128:(g + 1) * 128], ptpf[0:1, 0:128]),
                waits=[(pe, None)], incs=[(dv, 1)])
        S.emit()

        S = Script(nc)
        for qb in range(8):
            gs0 = 4 * qb
            kmax = gs0 + 3
            for kt in range(kmax + 1):
                pbuf = p512a if kt % 2 == 0 else p512b
                S.op("tensor", lambda e, qb=qb, kt=kt, pbuf=pbuf: e.matmul(
                    pbuf[:, :], khat[0:65, kt * 128:(kt + 1) * 128],
                    qhat[0:65, qb * 512:(qb + 1) * 512], start=True, stop=True),
                    waits=[(ac, None)], incs=[(pe, 1)])
                rel = kt - gs0
                if rel >= 0:
                    S.op("vector", lambda e, rel=rel, pbuf=pbuf: e.tensor_tensor(
                        pbuf[:, :], pbuf[:, :], m4[rel][:, :], ALU.add),
                        waits=[(pe, None), (ld, None)], incs=[(dv, 1)])
                S.op("scalar", lambda e, kt=kt, pbuf=pbuf: e.activation(
                    ptile[kt % 2][:, :], pbuf[:, :], AF.Exp),
                    waits=[(pe, None), (dv, None)], incs=[(ac, 1)])
                S.op("tensor", lambda e, kt=kt, kmax=kmax: e.matmul(
                    po[:, :], vhat[:, kt * 65:(kt + 1) * 65],
                    ptile[kt % 2][:, :],
                    start=(kt == 0), stop=(kt == kmax)),
                    waits=[(ac, None)] + ([(dv, None)] if kt == 0 else []),
                    incs=[(pe, 1)])
            S.op("vector", lambda e: e.reciprocal(rs_row[:, :], po[64:65, :]),
                 waits=[(pe, None)], incs=[(dv, 1)])
            S.op("tensor", lambda e: e.matmul(
                pb[0:64, 0:512], ctx.ones_r[0:1, 0:64], rs_row[:, :],
                start=True, stop=True),
                waits=[(dv, None)], incs=[(pe, 1)])
            S.op("vector", lambda e: e.tensor_copy(bc[:, :], pb[0:64, 0:512]),
                 waits=[(pe, None)], incs=[(dv, 1)])
            S.op("vector", lambda e, i=i, qb=qb, hw=hw: e.tensor_tensor(
                attn_sb[i // 2][hw:hw + 64, qb * 512:(qb + 1) * 512],
                po[0:64, :], bc[:, :], ALU.mult),
                waits=[(dv, None)], incs=[(dv, 1)])
        S.emit()

    S = Script(nc)
    for ti in range(4):
        S.op("gpsimd", lambda e, ti=ti: e.dma_start(
            out=dd["d_at_c"][ti * 128:(ti + 1) * 128, :],
            in_=attn_sb[ti][:, :]),
            incs=[(st, 16)])
    S.op("gpsimd", lambda e: e.collective_compute(
        "AllGather", ALU.bypass, replica_groups=RG,
        ins=[dd["d_at_c"].ap()], outs=[dd["d_at"].ap()]),
        waits=[(st, None)], incs=[(cc, 1)])
    S.emit()


def phase2_outproj(nc, ctx, cc, ar, dd, psum, scr, flg):
    ld, ldg, st, pe, dv, ac = ctx.ld, ctx.ldg, ctx.st, ctx.pe, ctx.dv, ctx.ac
    (p512a, p512b, p512c, p512d, po, pbx, ptpb, ptpf) = psum
    d_at, xs = dd["d_at"], dd["xs"]

    ar.reset()
    atn = [ar.alloc(TOK, BF16) for _ in range(8)]
    stg_at = [ar.alloc(TOK, BF16) for _ in range(2)]
    wout = [ar.alloc(D, BF16) for _ in range(8)]
    bout = ar.alloc(D, BF16)[0:1, :]
    bv2 = ar.alloc(D, BF16)[0:1, :]
    bqc = ar.alloc(64, F32)
    bkc = ar.alloc(64, F32)
    ytile = ar.alloc(D, F32)
    x1t = ar.alloc(D, F32)
    sxt = [ar.alloc(D, F32) for _ in range(2)]
    sbf = ar.alloc(D, BF16)
    slt = [ar.alloc(TOK, BF16) for _ in range(8)]
    wfm2 = [ar.alloc(8 * 128, BF16) for _ in range(2)]
    wtm2 = [ar.alloc(8 * 512, BF16) for _ in range(2)]
    stage2 = [ar.alloc(512, BF16) for _ in range(4)]

    S = Script(nc)
    S.op("sync", None, waits=[(cc, None), (st, None)])
    S.op("gpsimd", None, waits=[(cc, None), (st, None)])
    stg_at2 = [ar.alloc(TOK, BF16) for _ in range(2)]
    for c in range(8):
        qi = c % 2
        eng, sem = ("sync", ld) if qi == 0 else ("gpsimd", ldg)
        sA, sB = (stg_at[0], stg_at[1]) if qi == 0 else (stg_at2[0], stg_at2[1])
        S.op(eng, lambda e, c=c, sA=sA: e.dma_start(
            out=sA[:, :], in_=d_at[c * 128:(c + 1) * 128, 0:TOK]),
            waits=[(dv, None)], incs=[(sem, 16)])
        S.op(eng, lambda e, c=c, sB=sB: e.dma_start(
            out=sB[:, :], in_=d_at[c * 128:(c + 1) * 128, TOK:NSEQ]),
            incs=[(sem, 16)])
        S.op("vector", lambda e, c=c, sA=sA: e.tensor_scalar_mul(
            atn[c][:, :], sA[:, :], flg[:, 0:1]),
            waits=[(sem, None), (dv, None)], incs=[(dv, 1)])
        S.op("vector", lambda e, c=c, sB=sB: e.scalar_tensor_tensor(
            atn[c][:, :], sB[:, :], flg[:, 1:2], atn[c][:, :],
            ALU.mult, ALU.add),
            waits=[(dv, None)], incs=[(dv, 1)])
    for c in range(8):
        S.op("gpsimd", lambda e, c=c: e.dma_start(
            out=wout[c][:, :], in_=dd["woutd"][c * 128:(c + 1) * 128, :]),
            waits=([(dv, None)] if c == 0 else []), incs=[(ldg, 16)])
    S.op("gpsimd", lambda e: e.dma_start(out=bout[:, :], in_=dd["boutd"][:, :]),
         incs=[(ldg, 16)])
    S.op("gpsimd", lambda e: e.dma_start(out=bv2[:, :], in_=dd["bv2d"][:, :]),
         incs=[(ldg, 16)])
    for c in range(8):
        S.op("sync", lambda e, c=c: e.dma_start(
            out=bqc[:, 8 * c:8 * c + 1], in_=dd["bq2d"][c * 128:(c + 1) * 128, :]),
            waits=([(dv, None)] if c == 0 else []), incs=[(ld, 16)])
        S.op("sync", lambda e, c=c: e.dma_start(
            out=bkc[:, 8 * c:8 * c + 1], in_=dd["bk2d"][c * 128:(c + 1) * 128, :]),
            incs=[(ld, 16)])
    for t in range(NT):
        tb = t % 2
        S.op("sync", lambda e, t=t, tb=tb: e.dma_start(
            out=sxt[tb][:, :], in_=xs[t * 128:(t + 1) * 128, :]),
            waits=[(dv, None)], incs=[(ld, 16)])
        for g in range(2):
            pbuf = p512a if g == 0 else p512b
            pairs = [(atn[c][:, t * 128:(t + 1) * 128],
                      wout[c][:, g * 512:(g + 1) * 512]) for c in range(8)]
            mm_group(S, ctx, pbuf[:, :], pairs,
                     waits_first=[(ldg, None), (ld, None), (dv, None),
                                  (ac, None)],
                     stop_last=False)
            S.op("tensor", lambda e, g=g, pbuf=pbuf: e.matmul(
                pbuf[:, :], ctx.ones_rb[:, :], bout[:, g * 512:(g + 1) * 512],
                start=False, stop=True),
                incs=[(pe, 1)])
            S.op("scalar", lambda e, g=g, pbuf=pbuf: e.activation(
                ytile[:, g * 512:(g + 1) * 512], pbuf[:, :], AF.Copy),
                waits=[(pe, None)], incs=[(ac, 1)])
        emit_ln(ctx, S, ytile[:, :], ytile[:, :], scr, x_waits=[(ac, None)])
        S.op("vector", lambda e, tb=tb: e.tensor_tensor(
            x1t[:, :], sxt[tb][:, :], ytile[:, :], ALU.add),
            waits=[(ld, None), (dv, None), (st, None)], incs=[(dv, 1)])
        S.op("gpsimd", lambda e, t=t: e.dma_start(
            out=dd["d_x1"][t * 128:(t + 1) * 128, :], in_=x1t[:, :]),
            waits=[(dv, None)], incs=[(st, 16)])
        emit_ln(ctx, S, x1t[:, :], sbf[:, :], scr, x_waits=[(dv, None)])
        for c in range(8):
            emit_tp(ctx, S, sbf[:, c * 128:(c + 1) * 128],
                    slt[c][:, t * 128:(t + 1) * 128], ptpb)
    S.emit()

    S = Script(nc)
    for (wdram, biascols, odram) in ((dd["wq2d"], bqc, dd["d_qT2"]),
                                     (dd["wk2d"], bkc, dd["d_kt2_c"])):
        for oc in range(8):
            b = oc % 2
            for kc in range(8):
                S.op("gpsimd", lambda e, kc=kc, oc=oc, b=b, wdram=wdram: e.dma_start(
                    out=wfm2[b][:, kc * 128:(kc + 1) * 128],
                    in_=wdram[kc * 128:(kc + 1) * 128, oc * 128:oc * 128 + 128]),
                    waits=([(pe, None)] if kc == 0 else []), incs=[(ldg, 16)])
            for tb in range(4):
                pbuf = p512a if tb % 2 == 0 else p512b
                pairs = [(wfm2[b][:, kc * 128:(kc + 1) * 128],
                          slt[kc][:, tb * 512:(tb + 1) * 512]) for kc in range(8)]
                mm_group(S, ctx, pbuf[:, :], pairs,
                         waits_first=[(ldg, None), (ac, None), (dv, None)])
                S.op("vector", lambda e, pbuf=pbuf, tb=tb, oc=oc, biascols=biascols:
                     e.tensor_scalar(stage2[tb][:, :], pbuf[:, :],
                                     biascols[:, 8 * oc:8 * oc + 1], None,
                                     ALU.add),
                     waits=[(pe, None), (st, None), (ld, None)], incs=[(dv, 1)])
                S.op("gpsimd", lambda e, oc=oc, tb=tb, odram=odram: e.dma_start(
                    out=odram[oc * 128:(oc + 1) * 128, tb * 512:(tb + 1) * 512],
                    in_=stage2[tb][:, :]),
                    waits=[(dv, None)], incs=[(st, 16)])
    for g in range(2):
        wb = wtm2[g % 2]
        for kc in range(8):
            S.op("gpsimd", lambda e, kc=kc, g=g, wb=wb: e.dma_start(
                out=wb[:, kc * 512:(kc + 1) * 512],
                in_=dd["wv2d"][kc * 128:(kc + 1) * 128, g * 512:g * 512 + 512]),
                waits=([(pe, None)] if kc == 0 else []), incs=[(ldg, 16)])
        for t in range(NT):
            pbuf = p512a if t % 2 == 0 else p512b
            pairs = [(slt[kc][:, t * 128:(t + 1) * 128],
                      wb[:, kc * 512:(kc + 1) * 512]) for kc in range(8)]
            mm_group(S, ctx, pbuf[:, :], pairs,
                     waits_first=[(ldg, None), (ac, None)], stop_last=False)
            S.op("tensor", lambda e, g=g, pbuf=pbuf: e.matmul(
                pbuf[:, :], ctx.ones_rb[:, :], bv2[:, g * 512:(g + 1) * 512],
                start=False, stop=True),
                incs=[(pe, 1)])
            S.op("scalar", lambda e, t=t, pbuf=pbuf: e.activation(
                stage2[2 + t % 2][:, :], pbuf[:, :], AF.Copy),
                waits=[(pe, None), (st, None)], incs=[(ac, 1)])
            S.op("gpsimd", lambda e, t=t, g=g: e.dma_start(
                out=dd["d_v2_c"][t * 128:(t + 1) * 128, g * 512:(g + 1) * 512],
                in_=stage2[2 + t % 2][:, :]),
                waits=[(ac, None)], incs=[(st, 16)])
    for (ci, gi) in ((dd["d_kt2_c"], dd["d_kt2"]), (dd["d_v2_c"], dd["d_v2"])):
        S.op("gpsimd", lambda e, ci=ci, gi=gi: e.collective_compute(
            "AllGather", ALU.bypass, replica_groups=RG,
            ins=[ci.ap()], outs=[gi.ap()]),
            waits=[(st, None), (ldg, None)], incs=[(cc, 1)])
    S.emit()


def phase3_l0f(nc, ctx, cc, ar, dd, psum, scr):
    ld, ldg, st, pe, dv, ac = ctx.ld, ctx.ldg, ctx.st, ctx.pe, ctx.dv, ctx.ac
    (p512a, p512b, p512c, p512d, po, pbx, ptpb, ptpf) = psum
    pb = pbx[0:64, :]
    d_qT2, d_kt2, d_v2 = dd["d_qT2"], dd["d_kt2"], dd["d_v2"]

    ar.reset()
    qhat = ar.alloc(TOK, BF16)
    qstg = ar.alloc(TOK, BF16)[0:64, :]
    khat = ar.alloc(NSEQ, BF16)
    vhat = ar.alloc(NKT * 65, BF16)
    mxr = ar.alloc(9, F32)
    negc = ar.alloc(1, F32)
    ptile = [ar.alloc(512, BF16) for _ in range(2)]
    rs_row = ar.alloc(512, F32)[0:1, :]
    bc = ar.alloc(512, F32)[0:64, :]
    attn_sb = [ar.alloc(TOK, BF16) for _ in range(8)]
    wout = [ar.alloc(D, BF16) for _ in range(8)]
    bout = ar.alloc(D, BF16)[0:1, :]
    ytile = ar.alloc(D, F32)
    xat = ar.alloc(D, F32)
    sxt = [ar.alloc(D, F32) for _ in range(2)]
    grow = ar.alloc(2 * D, F32)[0:1, :]
    gb_t = {nm: ar.alloc(D, F32) for nm in ("ln1g", "ln1b")}
    xaT = [ar.alloc(TOK, BF16) for _ in range(8)]
    sbf = ar.alloc(D, BF16)
    stile = ar.alloc(D, F32)

    first = [True]
    for h in range(H):
        S = Script(nc)
        hw = 64 * (h % 2)
        if first[0]:
            S.op("sync", None, waits=[(cc, None), (st, None)])
            S.op("gpsimd", None, waits=[(cc, None), (st, None)])
            first[0] = False
        S.op("sync", lambda e, h=h: e.dma_start(
            out=qstg[:, :], in_=d_qT2[h * 64:(h + 1) * 64, :]),
            waits=[(pe, None), (dv, None)], incs=[(ld, 16)])
        for blk_ in range(2):
            S.op("sync", lambda e, h=h, blk_=blk_: e.dma_start(
                out=khat[0:64, blk_ * TOK:(blk_ + 1) * TOK],
                in_=d_kt2[blk_ * D + h * 64:blk_ * D + (h + 1) * 64, :]),
                incs=[(ld, 16)])
        if h == 0:
            S.op("vector", lambda e: e.memset(
                vhat[:, :].rearrange("p (kt c) -> p kt c", c=65)[:, :, 64:65],
                1.0),
                incs=[(dv, 1)])
        S.op("sync", lambda e, h=h: e.dma_start(
            out=vhat[:, :].rearrange("p (kt c) -> p kt c", c=65)[:, :, 0:64],
            in_=d_v2.ap().rearrange("(kt p) d -> p kt d", p=128)[:, :, h * 64:(h + 1) * 64]),
            waits=[(dv, None)], incs=[(ld, 16)])
        S.op("vector", lambda e: e.tensor_scalar_mul(qhat[0:64, :], qstg[:, :], 0.125),
             waits=[(ld, None)], incs=[(dv, 1)])
        S.op("vector", lambda e: e.memset(khat[64:65, :], 1.0), incs=[(dv, 1)])
        for t in range(NT):
            for kb in range(8):
                pbuf = p512a if kb % 2 == 0 else p512b
                S.op("tensor", lambda e, t=t, kb=kb, pbuf=pbuf: e.matmul(
                    pbuf[:, :], qhat[0:64, t * 128:(t + 1) * 128],
                    khat[0:64, kb * 512:(kb + 1) * 512], start=True, stop=True),
                    waits=[(dv, None), (ld, None)], incs=[(pe, 1)])
                S.op("vector", lambda e, kb=kb, pbuf=pbuf: e.reduce_max(
                    mxr[:, kb:kb + 1], pbuf[:, :], axis=AX.X),
                    waits=[(pe, None)], incs=[(dv, 1)])
            S.op("vector", lambda e: e.reduce_max(
                mxr[:, 8:9], mxr[:, 0:8], axis=AX.X),
                waits=[(dv, None)], incs=[(dv, 1)])
            S.op("vector", lambda e: e.tensor_scalar_mul(negc[:, :], mxr[:, 8:9], -1.0),
                 waits=[(dv, None)], incs=[(dv, 1)])
            S.op("tensor", lambda e: e.transpose(ptpf[0:1, 0:128], negc[:, :],
                                                 ctx.idf[:, :]),
                 waits=[(dv, None)], incs=[(pe, 1)])
            S.op("vector", lambda e, t=t: e.tensor_copy(
                qhat[64:65, t * 128:(t + 1) * 128], ptpf[0:1, 0:128]),
                waits=[(pe, None)], incs=[(dv, 1)])
        for qb in range(4):
            for kt in range(NKT):
                pbuf = p512a if kt % 2 == 0 else p512b
                S.op("tensor", lambda e, qb=qb, kt=kt, pbuf=pbuf: e.matmul(
                    pbuf[:, :], khat[0:65, kt * 128:(kt + 1) * 128],
                    qhat[0:65, qb * 512:(qb + 1) * 512], start=True, stop=True),
                    waits=[(ac, None)], incs=[(pe, 1)])
                S.op("scalar", lambda e, kt=kt, pbuf=pbuf: e.activation(
                    ptile[kt % 2][:, :], pbuf[:, :], AF.Exp),
                    waits=[(pe, None)], incs=[(ac, 1)])
                S.op("tensor", lambda e, kt=kt: e.matmul(
                    po[:, :], vhat[:, kt * 65:(kt + 1) * 65], ptile[kt % 2][:, :],
                    start=(kt == 0), stop=(kt == NKT - 1)),
                    waits=[(ac, None)] + ([(dv, None)] if kt == 0 else []),
                    incs=[(pe, 1)])
            S.op("vector", lambda e: e.reciprocal(rs_row[:, :], po[64:65, :]),
                 waits=[(pe, None)], incs=[(dv, 1)])
            S.op("tensor", lambda e: e.matmul(
                pb[:, :], ctx.ones_r[0:1, 0:64], rs_row[:, :], start=True, stop=True),
                waits=[(dv, None)], incs=[(pe, 1)])
            S.op("vector", lambda e: e.tensor_copy(bc[:, :], pb[:, :]),
                 waits=[(pe, None)], incs=[(dv, 1)])
            S.op("vector", lambda e, h=h, qb=qb, hw=hw: e.tensor_tensor(
                attn_sb[h // 2][hw:hw + 64, qb * 512:(qb + 1) * 512],
                po[0:64, :], bc[:, :], ALU.mult),
                waits=[(dv, None)], incs=[(dv, 1)])
        S.emit()

    S = Script(nc)
    for c in range(8):
        S.op("gpsimd", lambda e, c=c: e.dma_start(
            out=wout[c][:, :], in_=dd["wo2d"][c * 128:(c + 1) * 128, :]),
            incs=[(ldg, 16)])
    S.op("gpsimd", lambda e: e.dma_start(out=bout[:, :], in_=dd["bo2d"][:, :]),
         incs=[(ldg, 16)])
    for i, (nm, d_) in enumerate((("ln1g", dd["ln1gd"]), ("ln1b", dd["ln1bd"]))):
        S.op("sync", lambda e, i=i, d_=d_: e.dma_start(
            out=grow[:, i * D:(i + 1) * D], in_=d_[:, :]),
            waits=([(dv, None)] if i == 0 else []), incs=[(ld, 16)])
    for i, nm in enumerate(("ln1g", "ln1b")):
        for g in range(2):
            S.op("tensor", lambda e, i=i, g=g: e.matmul(
                p512c[:, :], ctx.ones_r[0:1, :],
                grow[:, i * D + g * 512: i * D + (g + 1) * 512],
                start=True, stop=True),
                waits=[(ld, None), (dv, None)], incs=[(pe, 1)])
            S.op("vector", lambda e, nm=nm, g=g: e.tensor_copy(
                gb_t[nm][:, g * 512:(g + 1) * 512], p512c[:, :]),
                waits=[(pe, None)], incs=[(dv, 1)])
    for t in range(NT):
        tb = t % 2
        S.op("sync", lambda e, t=t, tb=tb: e.dma_start(
            out=sxt[tb][:, :], in_=dd["d_x1"][t * 128:(t + 1) * 128, :]),
            waits=[(dv, None)], incs=[(ld, 16)])
        for g in range(2):
            pbuf = p512a if g == 0 else p512b
            pairs = [(attn_sb[c][:, t * 128:(t + 1) * 128],
                      wout[c][:, g * 512:(g + 1) * 512]) for c in range(8)]
            mm_group(S, ctx, pbuf[:, :], pairs,
                     waits_first=[(ldg, None), (ac, None)], stop_last=False)
            S.op("tensor", lambda e, g=g, pbuf=pbuf: e.matmul(
                pbuf[:, :], ctx.ones_rb[:, :], bout[:, g * 512:(g + 1) * 512],
                start=False, stop=True), incs=[(pe, 1)])
            S.op("scalar", lambda e, g=g, pbuf=pbuf: e.activation(
                ytile[:, g * 512:(g + 1) * 512], pbuf[:, :], AF.Copy),
                waits=[(pe, None)], incs=[(ac, 1)])
        emit_ln(ctx, S, sxt[tb][:, :], stile[:, :], scr,
                x_waits=[(ld, None), (dv, None)])
        S.op("vector", lambda e: e.tensor_tensor(
            ytile[:, :], stile[:, :], ytile[:, :], ALU.add),
            waits=[(ac, None), (dv, None)], incs=[(dv, 1)])
        S.op("vector", None, waits=[(st, None)])
        emit_ln(ctx, S, ytile[:, :], xat[:, :], scr, x_waits=[(dv, None)],
                gain_bias=(gb_t["ln1g"][:, :], gb_t["ln1b"][:, :]))
        S.op("gpsimd", lambda e, t=t: e.dma_start(
            out=dd["d_x1a"][t * 128:(t + 1) * 128, :], in_=xat[:, :]),
            waits=[(dv, None)], incs=[(st, 16)])
        S.op("vector", lambda e: e.tensor_copy(sbf[:, :], xat[:, :]),
             waits=[(dv, None)], incs=[(dv, 1)])
        for c in range(8):
            emit_tp(ctx, S, sbf[:, c * 128:(c + 1) * 128],
                    xaT[c][:, t * 128:(t + 1) * 128], ptpb)
    for c in range(8):
        S.op("gpsimd", lambda e, c=c: e.dma_start(
            out=dd["d_xaT"][c * 128:(c + 1) * 128, :], in_=xaT[c][:, :]),
            waits=[(dv, None)], incs=[(st, 16)])
    S.emit()


def phase4_moe(nc, ctx, cc, ar, dd, psum, scr):
    ld, ldg, st, pe, dv, ac = ctx.ld, ctx.ldg, ctx.st, ctx.pe, ctx.dv, ctx.ac
    (p512a, p512b, p512c, p512d, po, pbx, ptpb, ptpf) = psum

    ar.reset()
    xaT = [ar.alloc(TOK, BF16) for _ in range(8)]
    ytile = ar.alloc(D, F32)
    xat = ar.alloc(D, F32)
    sxt = [ar.alloc(D, F32) for _ in range(2)]
    sbf = ar.alloc(D, BF16)
    grow = ar.alloc(512, F32)[0:1, :]
    gb_t = {nm: ar.alloc(D, F32) for nm in ("ln2g", "ln2b")}
    gw_sb = ar.alloc(8 * 16, BF16)
    gb_sb = ar.alloc(8, BF16)[0:1, :]
    b1c = ar.alloc(16 * 8, F32)
    b2_sb = ar.alloc(D, F32)[0:8, :]
    gcol = ar.alloc(8, F32)
    gtmp = ar.alloc(8, F32)
    gred = ar.alloc(4, F32)
    w8 = ar.alloc(NT * 8, F32)
    w8tt = ar.alloc(128, F32)[0:8, :]
    w1eb = [ar.alloc(8 * 512, BF16) for _ in range(2)]
    w2eb = [ar.alloc(4 * 1024, BF16) for _ in range(2)]
    hts = ar.alloc(8 * 1024, BF16)
    accs = ar.alloc(8 * 1024, F32)
    wfm3 = [ar.alloc(8 * 128, BF16) for _ in range(2)]
    wtm3_ = ar.alloc(8 * 512, BF16)
    wtm3 = [wtm3_, wtm3_]
    stage3 = [ar.alloc(512, BF16) for _ in range(4)]
    sqf3 = ar.alloc(512, F32)
    bq3c = ar.alloc(64, F32)
    bk3c = ar.alloc(64, F32)
    bv3s = ar.alloc(D, BF16)[0:1, :]
    nrow512 = ar.alloc(512, F32)[0:1, :]

    S = Script(nc)
    S.op("sync", None, waits=[(cc, None), (st, None)])
    S.op("gpsimd", None, waits=[(cc, None), (st, None)])
    for c in range(8):
        S.op("sync", lambda e, c=c: e.dma_start(
            out=xaT[c][:, :], in_=dd["d_xaT"][c * 128:(c + 1) * 128, :]),
            incs=[(ld, 16)])
    for nm, d_ in (("ln2g", dd["ln2gd"]), ("ln2b", dd["ln2bd"])):
        for g in range(2):
            S.op("sync", lambda e, d_=d_, g=g: e.dma_start(
                out=grow[:, :], in_=d_[:, g * 512:(g + 1) * 512]),
                waits=[(pe, None)], incs=[(ld, 16)])
            S.op("tensor", lambda e, g=g: e.matmul(
                p512c[:, :], ctx.ones_r[0:1, :], grow[:, :],
                start=True, stop=True),
                waits=[(ld, None), (dv, None)], incs=[(pe, 1)])
            S.op("vector", lambda e, nm=nm, g=g: e.tensor_copy(
                gb_t[nm][:, g * 512:(g + 1) * 512], p512c[:, :]),
                waits=[(pe, None)], incs=[(dv, 1)])
    for c in range(8):
        S.op("gpsimd", lambda e, c=c: e.dma_start(
            out=gw_sb[:, c * 16:c * 16 + 8],
            in_=dd["gwd"][c * 128:(c + 1) * 128, :]), incs=[(ldg, 16)])
    S.op("gpsimd", lambda e: e.dma_start(out=gb_sb[:, :], in_=dd["gbd"][:, :]),
         incs=[(ldg, 16)])
    S.op("sync", lambda e: e.dma_start(
        out=b1c[:, :], in_=dd["b1td"].ap().rearrange("(f p) e -> p f e", p=128)),
        waits=[(dv, None)], incs=[(ld, 16)])
    S.op("sync", lambda e: e.dma_start(out=b2_sb[:, :], in_=dd["b2d"][:, :]),
         incs=[(ld, 16)])
    for t in range(NT):
        pairs = [(xaT[kc][:, t * 128:(t + 1) * 128], gw_sb[:, kc * 16:kc * 16 + 8])
                 for kc in range(8)]
        mm_group(S, ctx, p512c[:, 0:8], pairs,
                 waits_first=[(ldg, None), (dv, None), (ld, None)], stop_last=False)
        S.op("tensor", lambda e: e.matmul(
            p512c[:, 0:8], ctx.ones_rb[:, :], gb_sb[:, :], start=False, stop=True),
            incs=[(pe, 1)])
        S.op("vector", lambda e: e.reduce_max(gred[:, 0:1], p512c[:, 0:8],
                                              axis=AX.X, negate=True),
             waits=[(pe, None)], incs=[(dv, 1)])
        S.op("scalar", lambda e: e.activation(
            gcol[:, :], p512c[:, 0:8], AF.Exp, bias=gred[:, 0:1],
            accum_out=gred[:, 1:2]),
            waits=[(dv, None)], incs=[(ac, 1)])
        S.op("vector", lambda e: e.reciprocal(gred[:, 2:3], gred[:, 1:2]),
             waits=[(ac, None)], incs=[(dv, 1)])
        S.op("vector", lambda e: e.tensor_scalar_mul(gcol[:, :], gcol[:, :],
                                                     gred[:, 2:3]),
             waits=[(dv, None)], incs=[(dv, 1)])
        S.op("vector", lambda e: e.reduce_max(gred[:, 2:3], gcol[:, :], axis=AX.X),
             waits=[(dv, None)], incs=[(dv, 1)])
        S.op("vector", lambda e: e.tensor_scalar(gtmp[:, :], gcol[:, :],
                                                 gred[:, 2:3], None, ALU.is_ge),
             waits=[(dv, None)], incs=[(dv, 1)])
        S.op("vector", lambda e: e.scalar_tensor_tensor(
            gtmp[:, :], gtmp[:, :], -1e30, gcol[:, :], ALU.mult, ALU.add),
            waits=[(dv, None)], incs=[(dv, 1)])
        S.op("vector", lambda e: e.reduce_max(gred[:, 3:4], gtmp[:, :], axis=AX.X),
             waits=[(dv, None)], incs=[(dv, 1)])
        S.op("vector", lambda e: e.tensor_scalar(gtmp[:, :], gcol[:, :],
                                                 gred[:, 3:4], None, ALU.is_ge),
             waits=[(dv, None)], incs=[(dv, 1)])
        S.op("vector", lambda e, t=t: e.tensor_tensor(
            w8[:, t * 8:(t + 1) * 8], gcol[:, :], gtmp[:, :], ALU.mult),
            waits=[(dv, None)], incs=[(dv, 1)])
    S.emit()

    for th in range(2):
        S = Script(nc)
        t0 = th * 8
        for tt in range(8):
            S.op("tensor", lambda e, tt=tt: e.transpose(
                ptpf[0:8, 0:128], w8[:, (t0 + tt) * 8:(t0 + tt + 1) * 8],
                ctx.idf[:, :]),
                waits=[(dv, None), (ac, None)], incs=[(pe, 1)])
            S.op("vector", lambda e: e.tensor_copy(w8tt[:, :], ptpf[0:8, 0:128]),
                 waits=[(pe, None)], incs=[(dv, 1)])
            S.op("tensor", lambda e, tt=tt: e.matmul(
                p512c[:, :], w8tt[:, :],
                b2_sb[:, 0:512], start=True, stop=True),
                waits=[(dv, None), (ac, None), (ld, None)], incs=[(pe, 1)])
            S.op("tensor", lambda e, tt=tt: e.matmul(
                p512d[:, :], w8tt[:, :],
                b2_sb[:, 512:1024], start=True, stop=True),
                incs=[(pe, 1)])
            S.op("scalar", lambda e, tt=tt: e.activation(
                accs[:, tt * 1024: tt * 1024 + 512], p512c[:, :], AF.Copy),
                waits=[(pe, None)], incs=[(ac, 1)])
            S.op("scalar", lambda e, tt=tt: e.activation(
                accs[:, tt * 1024 + 512:(tt + 1) * 1024], p512d[:, :], AF.Copy),
                waits=[(pe, None)], incs=[(ac, 1)])
        w1_guard = [pe.v, pe.v]
        w2_guard = [pe.v, pe.v]
        lsem = (ldg, ctx.lo)
        w2sem = (Sem(nc, f"s_w2a{th}"), Sem(nc, f"s_w2b{th}"))
        w1_ldg = [ldg.v, ctx.lo.v]
        w2_ldg = [0, 0]
        for e_i in range(8):
            for hh2 in range(2):
                for fq2 in range(2):
                    fq = hh2 * 2 + fq2
                    bi = fq % 2
                    buf = w1eb[bi]
                    for kc in range(8):
                        S.op("gpsimd", lambda e, e_i=e_i, kc=kc, fq=fq, buf=buf:
                             e.dma_start(
                                 out=buf[:, kc * 512:(kc + 1) * 512],
                                 in_=dd["w1d"][e_i, kc * 128:(kc + 1) * 128,
                                               fq * 512:(fq + 1) * 512]),
                             waits=([(pe, w1_guard[bi])] if kc == 0 else []),
                             incs=[(lsem[bi], 16)])
                    w1_ldg[bi] = lsem[bi].v
                    for ft4 in range(4):
                        ftl = fq2 * 4 + ft4
                        ftg = hh2 * 8 + ftl
                        for tb2 in range(2):
                            pbuf = p512a if tb2 == 0 else p512b
                            pairs = [(buf[:, kc * 512 + ft4 * 128:
                                          kc * 512 + (ft4 + 1) * 128],
                                      xaT[kc][:, (t0 + tb2 * 4) * 128:(t0 + (tb2 + 1) * 4) * 128])
                                     for kc in range(8)]
                            mm_group(S, ctx, pbuf[:, :], pairs,
                                     waits_first=[(lsem[bi], w1_ldg[bi]),
                                                  (ac, None), (dv, None)])
                            S.op("scalar", lambda e, ftl=ftl, ftg=ftg, tb2=tb2,
                                 e_i=e_i, pbuf=pbuf:
                                 e.activation(hts[:, ftl * 1024 + tb2 * 512:
                                                  ftl * 1024 + (tb2 + 1) * 512],
                                              pbuf[:, :], AF.Relu,
                                              bias=b1c[:, ftg * 8 + e_i:
                                                       ftg * 8 + e_i + 1]),
                                 waits=[(pe, None), (ld, None)], incs=[(ac, 1)])
                    w1_guard[bi] = pe.v
                for w2q in range(2):
                    bi = (hh2 * 2 + w2q) % 2
                    buf = w2eb[bi]
                    for fc4 in range(4):
                        fc = hh2 * 8 + w2q * 4 + fc4
                        S.op("gpsimd", lambda e, e_i=e_i, fc=fc, fc4=fc4,
                             buf=buf: e.dma_start(
                                 out=buf[:, fc4 * 1024:(fc4 + 1) * 1024],
                                 in_=dd["w2d"][e_i, fc * 128:(fc + 1) * 128, :]),
                             waits=([(pe, w2_guard[bi])] if fc4 == 0 else []),
                             incs=[(w2sem[bi], 16)])
                    w2_ldg[bi] = w2sem[bi].v
                    for tt in range(8):
                        for g in range(2):
                            pbuf = p512c if g == 0 else p512d
                            pairs = [(hts[:, (w2q * 4 + fc4) * 1024 + tt * 128:
                                          (w2q * 4 + fc4) * 1024 + (tt + 1) * 128],
                                      buf[:, fc4 * 1024 + g * 512:
                                          fc4 * 1024 + (g + 1) * 512])
                                     for fc4 in range(4)]
                            mm_group(S, ctx, pbuf[:, :], pairs,
                                     waits_first=[(w2sem[bi], w2_ldg[bi]),
                                                  (ac, None), (dv, None)])
                            S.op("vector", lambda e, tt=tt, g=g, e_i=e_i,
                                 pbuf=pbuf: e.scalar_tensor_tensor(
                                     accs[:, tt * 1024 + g * 512:
                                          tt * 1024 + (g + 1) * 512],
                                     pbuf[:, :], w8[:, (t0 + tt) * 8 + e_i:
                                                    (t0 + tt) * 8 + e_i + 1],
                                     accs[:, tt * 1024 + g * 512:
                                          tt * 1024 + (g + 1) * 512],
                                     ALU.mult, ALU.add),
                                 waits=[(pe, None)], incs=[(dv, 1)])
                    w2_guard[bi] = pe.v
        S.emit()

        S = Script(nc)
        for tt in range(8):
            t = t0 + tt
            tb = t % 2
            S.op("sync", lambda e, t=t: e.dma_start(
                out=xat[:, :], in_=dd["d_x1a"][t * 128:(t + 1) * 128, :]),
                waits=[(dv, None), (ac, None), (st, None)], incs=[(ld, 16)])
            S.op("sync", lambda e, t=t, tb=tb: e.dma_start(
                out=sxt[tb][:, :], in_=dd["d_x1"][t * 128:(t + 1) * 128, :]),
                waits=[(dv, None)], incs=[(ld, 16)])
            S.op("vector", lambda e, tt=tt: e.tensor_tensor(
                ytile[:, :], xat[:, :], accs[:, tt * 1024:(tt + 1) * 1024], ALU.add),
                waits=[(ld, None), (dv, None)], incs=[(dv, 1)])
            emit_ln(ctx, S, ytile[:, :], ytile[:, :], scr, x_waits=[(dv, None)],
                    gain_bias=(gb_t["ln2g"][:, :], gb_t["ln2b"][:, :]))
            emit_ln(ctx, S, ytile[:, :], ytile[:, :], scr, x_waits=[(dv, None)])
            S.op("vector", lambda e, tb=tb: e.tensor_tensor(
                xat[:, :], sxt[tb][:, :], ytile[:, :], ALU.add),
                waits=[(dv, None), (st, None)], incs=[(dv, 1)])
            S.op("gpsimd", lambda e, t=t: e.dma_start(
                out=dd["d_x2"][t * 128:(t + 1) * 128, :], in_=xat[:, :]),
                waits=[(dv, None)], incs=[(st, 16)])
            emit_ln(ctx, S, xat[:, :], sbf[:, :], scr, x_waits=[(dv, None)])
            for c in range(8):
                emit_tp(ctx, S, sbf[:, c * 128:(c + 1) * 128],
                        xaT[c][:, t * 128:(t + 1) * 128], ptpb)
        S.emit()

    S = Script(nc)
    for c in range(8):
        S.op("sync", lambda e, c=c: e.dma_start(
            out=bq3c[:, 8 * c:8 * c + 1], in_=dd["bq3d"][c * 128:(c + 1) * 128, :]),
            waits=([(dv, None)] if c == 0 else []), incs=[(ld, 16)])
        S.op("sync", lambda e, c=c: e.dma_start(
            out=bk3c[:, 8 * c:8 * c + 1], in_=dd["bk3d"][c * 128:(c + 1) * 128, :]),
            incs=[(ld, 16)])
    S.op("gpsimd", lambda e: e.dma_start(out=bv3s[:, :], in_=dd["bv3d"][:, :]),
         incs=[(ldg, 16)])
    for (wdram, biascols, odram, nr) in (
            (dd["wq3d"], bq3c, dd["d_qT3"], dd["d_qnn"]),
            (dd["wk3d"], bk3c, dd["d_kt3_c"], dd["d_knn_c"])):
        for oc in range(8):
            b = oc % 2
            for kc in range(8):
                S.op("gpsimd", lambda e, kc=kc, oc=oc, b=b, wdram=wdram: e.dma_start(
                    out=wfm3[b][:, kc * 128:(kc + 1) * 128],
                    in_=wdram[kc * 128:(kc + 1) * 128, oc * 128:oc * 128 + 128]),
                    waits=([(pe, None)] if kc == 0 else []), incs=[(ldg, 16)])
            for tb in range(4):
                pbuf = p512a if tb % 2 == 0 else p512b
                pairs = [(wfm3[b][:, kc * 128:(kc + 1) * 128],
                          xaT[kc][:, tb * 512:(tb + 1) * 512]) for kc in range(8)]
                mm_group(S, ctx, pbuf[:, :], pairs,
                         waits_first=[(ldg, None), (ac, None), (dv, None)])
                S.op("vector", lambda e, pbuf=pbuf, tb=tb, oc=oc, biascols=biascols:
                     e.tensor_scalar(stage3[tb][:, :], pbuf[:, :],
                                     biascols[:, 8 * oc:8 * oc + 1], None,
                                     ALU.add),
                     waits=[(pe, None), (st, None), (ac, None), (ld, None)],
                     incs=[(dv, 1)])
                S.op("gpsimd", lambda e, oc=oc, tb=tb, odram=odram: e.dma_start(
                    out=odram[oc * 128:(oc + 1) * 128, tb * 512:(tb + 1) * 512],
                    in_=stage3[tb][:, :]),
                    waits=[(dv, None)], incs=[(st, 16)])
                S.op("scalar", lambda e, tb=tb: e.activation(
                    sqf3[:, :], stage3[tb][:, :], AF.Square),
                    waits=[(dv, None)], incs=[(ac, 1)])
                for hh in range(2):
                    head = 2 * oc + hh
                    S.op("tensor", lambda e, hh=hh: e.matmul(
                        pbx[0:1, :], ctx.constcol(-0.5)[hh * 64:hh * 64 + 64, :],
                        sqf3[hh * 64:hh * 64 + 64, :], start=True, stop=True),
                        waits=[(ac, None), (dv, None), (st, None)], incs=[(pe, 1)])
                    S.op("vector", lambda e: e.tensor_copy(
                        nrow512[:, :], pbx[0:1, :]),
                        waits=[(pe, None)], incs=[(dv, 1)])
                    S.op("gpsimd", lambda e, head=head, tb=tb, nr=nr: e.dma_start(
                        out=nr[head:head + 1, tb * 512:(tb + 1) * 512],
                        in_=nrow512[:, :]),
                        waits=[(dv, None)], incs=[(st, 16)])
    for g in range(2):
        wb = wtm3[g % 2]
        for kc in range(8):
            S.op("gpsimd", lambda e, kc=kc, g=g, wb=wb: e.dma_start(
                out=wb[:, kc * 512:(kc + 1) * 512],
                in_=dd["wv3d"][kc * 128:(kc + 1) * 128, g * 512:g * 512 + 512]),
                waits=([(pe, None), (ac, None)] if kc == 0 else []),
                incs=[(ldg, 16)])
        for t in range(NT):
            pbuf = p512a if t % 2 == 0 else p512b
            pairs = [(xaT[kc][:, t * 128:(t + 1) * 128],
                      wb[:, kc * 512:(kc + 1) * 512]) for kc in range(8)]
            mm_group(S, ctx, pbuf[:, :], pairs,
                     waits_first=[(ldg, None), (ac, None), (dv, None)],
                     stop_last=False)
            S.op("tensor", lambda e, g=g, pbuf=pbuf: e.matmul(
                pbuf[:, :], ctx.ones_rb[:, :], bv3s[:, g * 512:(g + 1) * 512],
                start=False, stop=True), incs=[(pe, 1)])
            S.op("scalar", lambda e, t=t, pbuf=pbuf: e.activation(
                stage3[2 + t % 2][:, :], pbuf[:, :], AF.Copy),
                waits=[(pe, None), (st, None)], incs=[(ac, 1)])
            S.op("gpsimd", lambda e, t=t, g=g: e.dma_start(
                out=dd["d_v3_c"][t * 128:(t + 1) * 128, g * 512:(g + 1) * 512],
                in_=stage3[2 + t % 2][:, :]),
                waits=[(ac, None)], incs=[(st, 16)])
    for (ci, gi) in ((dd["d_kt3_c"], dd["d_kt3"]), (dd["d_v3_c"], dd["d_v3"]),
                     (dd["d_knn_c"], dd["d_knn"])):
        S.op("gpsimd", lambda e, ci=ci, gi=gi: e.collective_compute(
            "AllGather", ALU.bypass, replica_groups=RG,
            ins=[ci.ap()], outs=[gi.ap()]),
            waits=[(st, None), (ldg, None)], incs=[(cc, 1)])
    S.emit()


def phase5_l1(nc, ctx, cc, ar, dd, psum, scr):
    ld, ldg, st, pe, dv, ac = ctx.ld, ctx.ldg, ctx.st, ctx.pe, ctx.dv, ctx.ac
    (p512a, p512b, p512c, p512d, po, pbx, ptpb, ptpf) = psum
    pC = pbx
    d_qT3, d_kt3, d_v3 = dd["d_qT3"], dd["d_kt3"], dd["d_v3"]
    d_qnn, d_knn, d_x2, d_x3 = dd["d_qnn"], dd["d_knn"], dd["d_x2"], dd["d_x3"]
    o_x = dd["o_x"]

    ar.reset()
    qhat2 = ar.alloc(TOK, BF16)
    khat2 = ar.alloc(NSEQ, BF16)
    vhat = ar.alloc(NKT * 65, BF16)
    Mrun = ar.alloc(512, F32)
    lbuf = ar.alloc(512, F32)
    rsd = [ar.alloc(512, F32) for _ in range(2)]
    ubuf2 = [ar.alloc(512, F32) for _ in range(2)]
    ptile = [ar.alloc(512, BF16) for _ in range(2)]
    bcb = ar.alloc(512, F32)[0:64, :]
    rs_row = ar.alloc(512, F32)[0:1, :]
    crow = ar.alloc(512, F32)[0:1, :]
    fence = ar.alloc(1, F32)[0:1, :]
    ccol = ar.alloc(1, F32)
    hls = ar.alloc(1, F32)
    asb = ar.alloc(1, F32)[0:1, :]
    tst_off = ar.off
    tst = ar.alloc(16384, F32)
    attn_sb = [ar.alloc(TOK, BF16) for _ in range(8)]
    bout = ar.alloc(D, BF16)[0:1, :]
    ytile = ar.alloc(D, F32)
    x3t = ar.alloc(D, F32)
    sxt = [ar.alloc(D, F32) for _ in range(2)]
    sbf = ar.alloc(D, BF16)
    g7 = ar.alloc(7, F32)
    g7t = ar.alloc(7, F32)
    gred = ar.alloc(6, F32)
    wscol = ar.alloc(NT, F32)
    shtmp = ar.alloc(512, F32)
    s4t = [ar.at(j * 4096, TOK, BF16) for j in range(8)]
    wout = [ar.at(tst_off + c * 2048, D, BF16) for c in range(8)]
    wst = [ar.at(tst_off + 16384 + c * 2048, D, BF16) for c in range(8)]
    wet = [ar.at(tst_off + 32768 + c * 2048, D, BF16) for c in range(8)]
    wg_sb = ar.at(tst_off + 49152, 128, BF16)
    b4 = ar.at(tst_off + 49536, D, BF16)[0:1, :]

    S = Script(nc)
    S.op("sync", None, waits=[(cc, None), (st, None)])
    S.op("gpsimd", None, waits=[(cc, None), (st, None)])
    S.op("sync", lambda e: e.dma_start(out=asb[:, :], in_=dd["alphad"][:, :]),
         incs=[(ld, 16)])
    S.op("vector", lambda e: e.tensor_scalar_mul(asb[:, :], asb[:, :],
                                                 0.5 * LNBASE),
         waits=[(ld, None)], incs=[(dv, 1)])
    S.op("tensor", lambda e: e.matmul(pC[:, 0:1], ctx.ones_r[0:1, :], asb[:, :],
                                      start=True, stop=True),
         waits=[(dv, None)], incs=[(pe, 1)])
    S.op("vector", lambda e: e.tensor_copy(hls[:, :], pC[:, 0:1]),
         waits=[(pe, None)], incs=[(dv, 1)])
    S.op("vector", lambda e: e.memset(fence[:, :], 0.0), incs=[(dv, 1)])
    S.emit()

    for h in range(H):
        S = Script(nc)
        hw = 64 * (h % 2)
        S.op("sync", lambda e, h=h: e.dma_start(
            out=qhat2[0:64, :], in_=d_qT3[h * 64:(h + 1) * 64, :]),
            waits=[(pe, None), (ac, None), (dv, None)], incs=[(ld, 16)])
        for blk_ in range(2):
            S.op("sync", lambda e, h=h, blk_=blk_: e.dma_start(
                out=khat2[0:64, blk_ * TOK:(blk_ + 1) * TOK],
                in_=d_kt3[blk_ * D + h * 64:blk_ * D + (h + 1) * 64, :]),
                incs=[(ld, 16)])
        if h == 0:
            S.op("vector", lambda e: e.memset(
                vhat[:, :].rearrange("p (kt c) -> p kt c", c=65)[:, :, 64:65],
                1.0),
                incs=[(dv, 1)])
        S.op("sync", lambda e, h=h: e.dma_start(
            out=vhat[:, :].rearrange("p (kt c) -> p kt c", c=65)[:, :, 0:64],
            in_=d_v3.ap().rearrange("(kt p) d -> p kt d", p=128)[:, :, h * 64:(h + 1) * 64]),
            waits=[(dv, None)], incs=[(ld, 16)])
        S.op("vector", lambda e: e.memset(qhat2[64:66, :], 1.0), incs=[(dv, 1)])
        S.op("vector", lambda e: e.memset(khat2[64:66, :], 1.0), incs=[(dv, 1)])
        S.op("gpsimd", lambda e, h=h: e.dma_start(
            out=qhat2[64:65, :], in_=d_qnn[h:h + 1, :]),
            waits=[(dv, None)], incs=[(ldg, 16)])
        for blk_ in range(2):
            S.op("gpsimd", lambda e, h=h, blk_=blk_: e.dma_start(
                out=khat2[65:66, blk_ * TOK:(blk_ + 1) * TOK],
                in_=d_knn[blk_ * H + h:blk_ * H + h + 1, :]),
                waits=([(dv, None)] if blk_ == 0 else []), incs=[(ldg, 16)])

        for qb in range(4):
            qsl = slice(qb * 512, (qb + 1) * 512)

            def p1_mms(kt):
                pd = p512a if kt % 2 == 0 else p512b
                pn = p512c if kt % 2 == 0 else p512d
                S.op("tensor", lambda e, kt=kt, pd=pd, qsl=qsl: e.matmul(
                    pd[:, :], khat2[0:64, kt * 128:(kt + 1) * 128],
                    qhat2[0:64, qsl], start=True, stop=True),
                    waits=[(ld, None), (ldg, None), (dv, None)], incs=[(pe, 1)])
                S.op("tensor", lambda e, kt=kt, pn=pn, qsl=qsl: e.matmul(
                    pn[:, :], khat2[0:66, kt * 128:(kt + 1) * 128],
                    qhat2[0:66, qsl], start=True, stop=True),
                    incs=[(pe, 1)])

            mrun_guard = pe.v

            def mrun_upd(kt, certify):
                if kt == 0:
                    S.op("vector", lambda e: e.tensor_copy(
                        Mrun[:, :], tst[:, 0:512]),
                        waits=[(ac, certify), (pe, mrun_guard)],
                        incs=[(dv, 1)])
                else:
                    S.op("vector", lambda e, kt=kt: e.tensor_max(
                        Mrun[:, :], Mrun[:, :], tst[:, kt * 512:(kt + 1) * 512]),
                        waits=[(ac, certify)], incs=[(dv, 1)])

            def p1_ew(kt):
                pd = p512a if kt % 2 == 0 else p512b
                pn = p512c if kt % 2 == 0 else p512d
                S.op("scalar", lambda e, pn=pn: e.activation(
                    lbuf[:, :], pn[:, :], AF.Ln, bias=ctx.constcol(EPS),
                    scale=-2.0),
                    waits=[(pe, None), (dv, None)], incs=[(ac, 1)])
                if kt > 0:
                    mrun_upd(kt - 1, ac.v)
                S.op("scalar", lambda e, kt=kt: e.activation(
                    rsd[kt % 2][:, :], lbuf[:, :], AF.Exp, bias=hls[:, :],
                    scale=-0.5),
                    waits=[(ac, None), (dv, None)], incs=[(ac, 1)])
                S.op("vector", lambda e, kt=kt, pd=pd: e.tensor_tensor(
                    ubuf2[kt % 2][:, :], pd[:, :], rsd[kt % 2][:, :], ALU.mult),
                    waits=[(ac, None)], incs=[(dv, 1)])
                S.op("scalar", lambda e, kt=kt: e.activation(
                    tst[:, kt * 512:(kt + 1) * 512], ubuf2[kt % 2][:, :],
                    AF.Square),
                    waits=[(dv, None)], incs=[(ac, 1)])

            p1_mms(0)
            for kt in range(1, NKT):
                p1_mms(kt)
                p1_ew(kt - 1)
            p1_ew(NKT - 1)
            S.op("scalar", lambda e: e.activation(fence[:, :], fence[:, :],
                                                  AF.Square),
                 waits=[(ac, None)], incs=[(ac, 1)])
            mrun_upd(NKT - 1, ac.v)
            for j in range(4):
                S.op("tensor", lambda e, j=j: e.transpose(
                    ptpf[:, :], Mrun[:, j * 128:(j + 1) * 128], ctx.idf[:, :]),
                    waits=[(dv, None)], incs=[(pe, 1)])
                S.op("vector", lambda e: e.reduce_max(
                    ccol[:, :], ptpf[:, :], axis=AX.X),
                    waits=[(pe, None)], incs=[(dv, 1)])
                S.op("tensor", lambda e: e.transpose(
                    ptpf[0:1, 0:128], ccol[:, :], ctx.idf[:, :]),
                    waits=[(dv, None)], incs=[(pe, 1)])
                S.op("vector", lambda e, j=j: e.tensor_copy(
                    crow[:, j * 128:(j + 1) * 128], ptpf[0:1, 0:128]),
                    waits=[(pe, None)], incs=[(dv, 1)])
            S.op("tensor", lambda e: e.matmul(
                pC[:, :], ctx.ones_r[0:1, :], crow[:, :], start=True, stop=True),
                waits=[(dv, None)], incs=[(pe, 1)])
            pC_cnt = pe.v
            p1_ac = ac.v
            qb_guard = dv.v

            subcnt, expcnt, pvcnt = {}, {}, {}

            def p2_sub(kt):
                S.op("vector", lambda e, kt=kt: e.tensor_sub(
                    tst[:, kt * 512:(kt + 1) * 512],
                    tst[:, kt * 512:(kt + 1) * 512], pC[:, :]),
                    waits=[(ac, p1_ac), (pe, pC_cnt)],
                    incs=[(dv, 1)])
                subcnt[kt] = dv.v

            def p2_fence_dv():
                S.op("vector", lambda e: e.memset(fence[:, :], 0.0),
                     incs=[(dv, 1)])
                return dv.v

            def p2_exp(kt, certify):
                w = [(dv, certify)]
                if kt - 2 >= 0:
                    w.append((pe, pvcnt[kt - 2]))
                S.op("scalar", lambda e, kt=kt: e.activation(
                    ptile[kt % 2][:, :], tst[:, kt * 512:(kt + 1) * 512], AF.Exp),
                    waits=w, incs=[(ac, 1)])
                expcnt[kt] = ac.v

            def p2_fence_ac():
                S.op("scalar", lambda e: e.activation(fence[:, :], fence[:, :],
                                                      AF.Square),
                     incs=[(ac, 1)])
                return ac.v

            def p2_pv(kt, certify):
                w = [(ac, certify)]
                if kt == 0:
                    w.append((dv, qb_guard))
                S.op("tensor", lambda e, kt=kt: e.matmul(
                    po[:, :], vhat[:, kt * 65:(kt + 1) * 65], ptile[kt % 2][:, :],
                    start=(kt == 0), stop=(kt == NKT - 1)),
                    waits=w, incs=[(pe, 1)])
                pvcnt[kt] = pe.v

            p2_sub(0)
            p2_sub(1)
            p2_exp(0, subcnt[1])
            for kt in range(2, NKT):
                p2_sub(kt)
                p2_exp(kt - 1, subcnt[kt])
                p2_pv(kt - 2, expcnt[kt - 1])
            fdv = p2_fence_dv()
            p2_exp(NKT - 1, fdv)
            p2_pv(NKT - 2, expcnt[NKT - 1])
            fac = p2_fence_ac()
            p2_pv(NKT - 1, fac)
            S.op("vector", lambda e: e.reciprocal(rs_row[:, :], po[64:65, :]),
                 waits=[(pe, None)], incs=[(dv, 1)])
            S.op("tensor", lambda e: e.matmul(
                pC[0:64, :], ctx.ones_r[0:1, 0:64], rs_row[:, :],
                start=True, stop=True),
                waits=[(dv, None)], incs=[(pe, 1)])
            S.op("vector", lambda e: e.tensor_copy(bcb[:, :], pC[0:64, :]),
                 waits=[(pe, None)], incs=[(dv, 1)])
            S.op("vector", lambda e, h=h, qb=qb, hw=hw: e.tensor_tensor(
                attn_sb[h // 2][hw:hw + 64, qb * 512:(qb + 1) * 512],
                po[0:64, :], bcb[:, :], ALU.mult),
                waits=[(dv, None)], incs=[(dv, 1)])
        S.emit()

    S = Script(nc)
    for c in range(8):
        S.op("gpsimd", lambda e, c=c: e.dma_start(
            out=wout[c][:, :], in_=dd["wo4d"][c * 128:(c + 1) * 128, :]),
            waits=([(dv, None)] if c == 0 else []), incs=[(ldg, 16)])
    S.op("gpsimd", lambda e: e.dma_start(out=bout[:, :], in_=dd["bo4d"][:, :]),
         incs=[(ldg, 16)])
    for t in range(NT):
        tb = t % 2
        S.op("sync", lambda e, t=t, tb=tb: e.dma_start(
            out=sxt[tb][:, :], in_=d_x2[t * 128:(t + 1) * 128, :]),
            waits=[(dv, None)], incs=[(ld, 16)])
        for g in range(2):
            pbuf = p512a if g == 0 else p512b
            pairs = [(attn_sb[c][:, t * 128:(t + 1) * 128],
                      wout[c][:, g * 512:(g + 1) * 512]) for c in range(8)]
            mm_group(S, ctx, pbuf[:, :], pairs,
                     waits_first=[(ldg, None), (ac, None)], stop_last=False)
            S.op("tensor", lambda e, g=g, pbuf=pbuf: e.matmul(
                pbuf[:, :], ctx.ones_rb[:, :], bout[:, g * 512:(g + 1) * 512],
                start=False, stop=True), incs=[(pe, 1)])
            S.op("scalar", lambda e, g=g, pbuf=pbuf: e.activation(
                ytile[:, g * 512:(g + 1) * 512], pbuf[:, :], AF.Copy),
                waits=[(pe, None)], incs=[(ac, 1)])
        emit_ln(ctx, S, ytile[:, :], ytile[:, :], scr, x_waits=[(ac, None)])
        S.op("vector", lambda e, tb=tb: e.tensor_tensor(
            x3t[:, :], sxt[tb][:, :], ytile[:, :], ALU.add),
            waits=[(ld, None), (dv, None), (st, None)], incs=[(dv, 1)])
        S.op("gpsimd", lambda e, t=t: e.dma_start(
            out=d_x3[t * 128:(t + 1) * 128, :], in_=x3t[:, :]),
            waits=[(dv, None)], incs=[(st, 16)])
        emit_ln(ctx, S, x3t[:, :], sbf[:, :], scr, x_waits=[(dv, None)])
        for c in range(8):
            emit_tp(ctx, S, sbf[:, c * 128:(c + 1) * 128],
                    s4t[c][:, t * 128:(t + 1) * 128], ptpb)
    S.emit()

    S = Script(nc)
    for c in range(8):
        S.op("gpsimd", lambda e, c=c: e.dma_start(
            out=wst[c][:, :], in_=dd["wstd"][c * 128:(c + 1) * 128, :]),
            incs=[(ldg, 16)])
        S.op("gpsimd", lambda e, c=c: e.dma_start(
            out=wet[c][:, :], in_=dd["wetd"][c * 128:(c + 1) * 128, :]),
            incs=[(ldg, 16)])
        S.op("gpsimd", lambda e, c=c: e.dma_start(
            out=wg_sb[:, c * 16:c * 16 + 7],
            in_=dd["wgd"][c * 128:(c + 1) * 128, :]), incs=[(ldg, 16)])
    S.op("gpsimd", lambda e: e.dma_start(out=b4[:, :], in_=dd["bias4d"][:, :]),
         incs=[(ldg, 16)])
    for t in range(NT):
        pairs = [(s4t[kc][:, t * 128:(t + 1) * 128], wg_sb[:, kc * 16:kc * 16 + 7])
                 for kc in range(8)]
        mm_group(S, ctx, p512c[:, 0:7], pairs,
                 waits_first=[(ldg, None), (dv, None)])
        S.op("vector", lambda e: e.reduce_max(gred[:, 0:1], p512c[:, 0:7],
                                              axis=AX.X, negate=True),
             waits=[(pe, None)], incs=[(dv, 1)])
        S.op("scalar", lambda e: e.activation(
            g7[:, :], p512c[:, 0:7], AF.Exp, bias=gred[:, 0:1],
            accum_out=gred[:, 1:2]),
            waits=[(dv, None)], incs=[(ac, 1)])
        S.op("vector", lambda e: e.reciprocal(gred[:, 2:3], gred[:, 1:2]),
             waits=[(ac, None)], incs=[(dv, 1)])
        S.op("vector", lambda e: e.tensor_scalar_mul(g7[:, :], g7[:, :],
                                                     gred[:, 2:3]),
             waits=[(dv, None)], incs=[(dv, 1)])
        S.op("vector", lambda e: e.reduce_max(gred[:, 2:3], g7[:, :], axis=AX.X),
             waits=[(dv, None)], incs=[(dv, 1)])
        S.op("vector", lambda e: e.tensor_scalar(g7t[:, :], g7[:, :],
                                                 gred[:, 2:3], None, ALU.is_equal),
             waits=[(dv, None)], incs=[(dv, 1)])
        S.op("vector", lambda e: e.scalar_tensor_tensor(
            g7t[:, :], g7t[:, :], -1e30, g7[:, :], ALU.mult, ALU.add),
            waits=[(dv, None)], incs=[(dv, 1)])
        S.op("vector", lambda e: e.reduce_max(gred[:, 3:4], g7t[:, :], axis=AX.X),
             waits=[(dv, None)], incs=[(dv, 1)])
        S.op("vector", lambda e: e.tensor_tensor(gred[:, 4:5], gred[:, 2:3],
                                                 gred[:, 3:4], ALU.add),
             waits=[(dv, None)], incs=[(dv, 1)])
        S.op("vector", lambda e: e.reciprocal(gred[:, 5:6], gred[:, 4:5]),
             waits=[(dv, None)], incs=[(dv, 1)])
        S.op("vector", lambda e: e.tensor_tensor(gred[:, 4:5], gred[:, 2:3],
                                                 gred[:, 5:6], ALU.mult),
             waits=[(dv, None)], incs=[(dv, 1)])
        S.op("vector", lambda e: e.tensor_tensor(gred[:, 5:6], gred[:, 3:4],
                                                 gred[:, 5:6], ALU.mult),
             waits=[(dv, None)], incs=[(dv, 1)])
        S.op("vector", lambda e, t=t: e.tensor_tensor(
            wscol[:, t:t + 1], gred[:, 4:5], gred[:, 5:6], ALU.add),
            waits=[(dv, None)], incs=[(dv, 1)])
    S.op("sync", None, waits=[(st, None)])
    for t in range(NT):
        tb = t % 2
        S.op("sync", lambda e, t=t, tb=tb: e.dma_start(
            out=sxt[tb][:, :], in_=d_x3[t * 128:(t + 1) * 128, :]),
            waits=[(dv, None)], incs=[(ld, 16)])
        for g in range(2):
            pe_buf = p512a if g == 0 else p512b
            ps_buf = p512c if g == 0 else p512d
            pairs_e = [(s4t[kc][:, t * 128:(t + 1) * 128],
                        wet[kc][:, g * 512:(g + 1) * 512]) for kc in range(8)]
            mm_group(S, ctx, pe_buf[:, :], pairs_e,
                     waits_first=[(ldg, None), (dv, None)])
            pairs_s = [(s4t[kc][:, t * 128:(t + 1) * 128],
                        wst[kc][:, g * 512:(g + 1) * 512]) for kc in range(8)]
            mm_group(S, ctx, ps_buf[:, :], pairs_s,
                     waits_first=[(ac, None)], stop_last=False)
            S.op("tensor", lambda e, g=g, ps_buf=ps_buf: e.matmul(
                ps_buf[:, :], ctx.ones_rb[:, :], b4[:, g * 512:(g + 1) * 512],
                start=False, stop=True), incs=[(pe, 1)])
            S.op("scalar", lambda e, g=g, ps_buf=ps_buf: e.activation(
                shtmp[:, :], ps_buf[:, :], AF.Copy),
                waits=[(pe, None), (dv, None)], incs=[(ac, 1)])
            S.op("vector", lambda e, g=g, t=t, pe_buf=pe_buf: e.scalar_tensor_tensor(
                ytile[:, g * 512:(g + 1) * 512], pe_buf[:, :],
                wscol[:, t:t + 1], shtmp[:, :], ALU.mult, ALU.add),
                waits=[(ac, None)], incs=[(dv, 1)])
        emit_ln(ctx, S, ytile[:, :], ytile[:, :], scr, x_waits=[(dv, None)])
        S.op("vector", lambda e, tb=tb: e.tensor_tensor(
            x3t[:, :], sxt[tb][:, :], ytile[:, :], ALU.add),
            waits=[(ld, None), (dv, None), (st, None)], incs=[(dv, 1)])
        S.op("gpsimd", lambda e, t=t: e.dma_start(
            out=o_x[t * 128:(t + 1) * 128, :], in_=x3t[:, :]),
            waits=[(dv, None)], incs=[(st, 16)])
    S.op("gpsimd", None, waits=[(st, None), (ldg, None), (ld, None)])
    S.emit()


def build_full():
    nc, ctx, hd = build_fused()
    dd = hd["dram"]
    masks = (hd["tri"], hd["trit"], hd["lowt"], hd["low01"], hd["id8"])
    phase1_l0a(nc, ctx, hd["cc"], hd["ar"], dd, masks, hd["psum"], hd["flg"])
    phase2_outproj(nc, ctx, hd["cc"], hd["ar"], dd, hd["psum"], hd["scr"],
                   hd["flg"])
    phase3_l0f(nc, ctx, hd["cc"], hd["ar"], dd, hd["psum"], hd["scr"])
    phase4_moe(nc, ctx, hd["cc"], hd["ar"], dd, hd["psum"], hd["scr"])
    phase5_l1(nc, ctx, hd["cc"], hd["ar"], dd, hd["psum"], hd["scr"])
    return nc, ctx


def make_in_maps(kw, cores):
    f32 = np.float32
    X = np.asarray(kw["L0_attn_X"], f32)

    def row(v):
        return np.ascontiguousarray(np.asarray(v, f32).reshape(1, -1))

    def col(v):
        return np.ascontiguousarray(np.asarray(v, f32).reshape(-1, 1))

    w_kv = np.ascontiguousarray(np.asarray(kw["L0_attn_w_kvkv"], f32)[:, :2 * D])
    ropeQc, ropeQs = rope_tables(np.arange(NSEQ), 0.125)
    ropeQc = np.ascontiguousarray(ropeQc[:, :HD])
    ropeQs = np.ascontiguousarray(ropeQs[:, :HD])
    ropeK = [rope_tables(np.arange(h * TOK, (h + 1) * TOK), 1.0)
             for h in range(2)]
    shared = dict(
        w_qkv=np.asarray(kw["L0_attn_w_qkv"], f32), w_kv=w_kv,
        ropecq=ropeQc, ropesq=ropeQs,
        w_out=np.asarray(kw["L0_attn_w_out"], f32),
        b_out=row(kw["L0_attn_b_out"]),
        wq2=np.asarray(kw["L0_ffn_Wq"], f32), wk2=np.asarray(kw["L0_ffn_Wk"], f32),
        wv2=np.asarray(kw["L0_ffn_Wv"], f32),
        bq2c=col(kw["L0_ffn_bq"]), bk2c=col(kw["L0_ffn_bk"]),
        bv2=row(kw["L0_ffn_bv"]),
        wo2=np.asarray(kw["L0_ffn_Wo"], f32), bo2=row(kw["L0_ffn_bo"]),
        gW=np.asarray(kw["L0_ffn_gW"], f32), gb=row(kw["L0_ffn_gb"]),
        W1=np.asarray(kw["L0_ffn_W1"], f32),
        b1T=np.ascontiguousarray(np.asarray(kw["L0_ffn_b1"], f32).T),
        W2=np.asarray(kw["L0_ffn_W2"], f32), b2=np.asarray(kw["L0_ffn_b2"], f32),
        ln1g=row(kw["L0_ffn_ln1_g"]), ln1b=row(kw["L0_ffn_ln1_b"]),
        ln2g=row(kw["L0_ffn_ln2_g"]), ln2b=row(kw["L0_ffn_ln2_b"]),
        wq3=np.asarray(kw["L1_attn_wq"], f32), wk3=np.asarray(kw["L1_attn_wk"], f32),
        wv3=np.asarray(kw["L1_attn_wv"], f32),
        bq3c=col(kw["L1_attn_bq"]), bk3c=col(kw["L1_attn_bk"]),
        bv3=row(kw["L1_attn_bv"]),
        wo4=np.asarray(kw["L1_attn_wo"], f32), bo4=row(kw["L1_attn_bo"]),
        alpha=np.asarray(kw["L1_attn_alpha"], f32).reshape(1, 1),
        WsT=np.ascontiguousarray(np.asarray(kw["L1_ffn_W_shared"], f32).T),
        WeT=np.ascontiguousarray(np.asarray(kw["L1_ffn_W_expert"], f32).T),
        Wg=np.asarray(kw["L1_ffn_W_gate"], f32),
        bias4=row(kw["L1_ffn_bias"]))
    flags_h = []
    for h in range(2):
        fl = np.zeros((128, 2), f32)
        fl[:, h] = 1.0
        flags_h.append(fl)
    in_maps = []
    for c in cores:
        b, h = c // 2, c % 2
        m = dict(shared)
        m["xs"] = np.ascontiguousarray(X[b, h * TOK:(h + 1) * TOK])
        m["ropec"] = ropeK[h][0]
        m["ropes"] = ropeK[h][1]
        m["flags"] = flags_h[h]
        in_maps.append(m)
    return in_maps


from concourse.bass_utils import run_bass_kernel_spmd

_FPROG = {}


def kernel(L0_attn_X, L0_attn_w_qkv, L0_attn_w_kvkv, L0_attn_w_out,
           L0_attn_b_out, L0_ffn_Wq, L0_ffn_bq, L0_ffn_Wk, L0_ffn_bk,
           L0_ffn_Wv, L0_ffn_bv, L0_ffn_Wo, L0_ffn_bo, L0_ffn_gW, L0_ffn_gb,
           L0_ffn_W1, L0_ffn_b1, L0_ffn_W2, L0_ffn_b2, L0_ffn_ln1_g,
           L0_ffn_ln1_b, L0_ffn_ln2_g, L0_ffn_ln2_b, L1_attn_wq, L1_attn_bq,
           L1_attn_wk, L1_attn_bk, L1_attn_wv, L1_attn_bv, L1_attn_wo,
           L1_attn_bo, L1_attn_alpha, L1_ffn_cond, L1_ffn_mask,
           L1_ffn_W_shared, L1_ffn_W_expert, L1_ffn_W_gate, L1_ffn_bias):
    kw = dict(
        L0_attn_X=L0_attn_X, L0_attn_w_qkv=L0_attn_w_qkv,
        L0_attn_w_kvkv=L0_attn_w_kvkv, L0_attn_w_out=L0_attn_w_out,
        L0_attn_b_out=L0_attn_b_out, L0_ffn_Wq=L0_ffn_Wq, L0_ffn_bq=L0_ffn_bq,
        L0_ffn_Wk=L0_ffn_Wk, L0_ffn_bk=L0_ffn_bk, L0_ffn_Wv=L0_ffn_Wv,
        L0_ffn_bv=L0_ffn_bv, L0_ffn_Wo=L0_ffn_Wo, L0_ffn_bo=L0_ffn_bo,
        L0_ffn_gW=L0_ffn_gW, L0_ffn_gb=L0_ffn_gb, L0_ffn_W1=L0_ffn_W1,
        L0_ffn_b1=L0_ffn_b1, L0_ffn_W2=L0_ffn_W2, L0_ffn_b2=L0_ffn_b2,
        L0_ffn_ln1_g=L0_ffn_ln1_g, L0_ffn_ln1_b=L0_ffn_ln1_b,
        L0_ffn_ln2_g=L0_ffn_ln2_g, L0_ffn_ln2_b=L0_ffn_ln2_b,
        L1_attn_wq=L1_attn_wq, L1_attn_bq=L1_attn_bq, L1_attn_wk=L1_attn_wk,
        L1_attn_bk=L1_attn_bk, L1_attn_wv=L1_attn_wv, L1_attn_bv=L1_attn_bv,
        L1_attn_wo=L1_attn_wo, L1_attn_bo=L1_attn_bo,
        L1_attn_alpha=L1_attn_alpha,
        L1_ffn_W_shared=L1_ffn_W_shared, L1_ffn_W_expert=L1_ffn_W_expert,
        L1_ffn_W_gate=L1_ffn_W_gate, L1_ffn_bias=L1_ffn_bias)
    if "full" not in _FPROG:
        _FPROG["full"] = build_full()
    nc, ctx = _FPROG["full"]
    cores = list(range(8))
    in_maps = make_in_maps(kw, cores)
    res = run_bass_kernel_spmd(nc, in_maps, core_ids=cores).results
    B = np.asarray(L0_attn_X).shape[0]
    out = np.zeros((B, NSEQ, D), np.float32)
    for c in cores:
        b, h = c // 2, c % 2
        out[b, h * TOK:(h + 1) * TOK] = res[c]["xout"]
    return out
